# version 10
# baseline (speedup 1.0000x reference)
import math
from contextlib import ExitStack

import numpy as np
import ml_dtypes

import concourse.bass as bass
import concourse.mybir as mybir
from concourse.bass_utils import run_bass_kernel_spmd

F32 = mybir.dt.float32
BF16 = mybir.dt.bfloat16
I32 = mybir.dt.int32
AF = mybir.ActivationFunctionType
ALU = mybir.AluOpType
AX = mybir.AxisListType

P = 128
D = 1024
ND = D // P
NCORES = 8
LN_EPS = 1e-5

ENGS = ("pe", "act", "dve", "pool", "sp")

SIM_OPEN_BANKS = True


def bfc(x):
    return np.ascontiguousarray(x, dtype=ml_dtypes.bfloat16)


def f32c(x):
    return np.ascontiguousarray(x, dtype=np.float32)


class Sem:
    __slots__ = ("h", "n", "name")

    def __init__(self, h, name):
        self.h = h
        self.n = 0
        self.name = name

    def bump(self, k=1):
        self.n += k
        assert self.n < 2 ** 31
        return self.n


class B:

    def __init__(self, name, t_tokens):
        self.T = t_tokens
        self.NT = t_tokens // P
        self.nc = bass.Bass("TRN2", target_bir_lowering=False)
        self.stack = ExitStack()
        self.ops = {e: [] for e in ENGS}
        self._sems = {}
        self._sem_i = 0
        self.consts = {}
        self._sbuf_mark = None
        self._resident_top = None
        self.pb = [self.nc.alloc_psum_tensor(f"pb{i}", [P, 512], F32)
                   for i in range(3)]
        self.pbS = self.nc.alloc_psum_tensor("pbS", [P, 2048], F32)
        for i in range(4):
            self.pb.append(PBView(self.pbS.ap()[:, i * 512:(i + 1) * 512]))
        self.pb.append(self.nc.alloc_psum_tensor("pb7", [P, 512], F32))
        self.ph = self.sem("phase")
        self._uid = 0

    def uid(self):
        self._uid += 1
        return self._uid

    def sem(self, name):
        cname = "".join(ch for ch in name if not ch.isdigit())
        if cname not in self._sems:
            h = self.stack.enter_context(self.nc.semaphore(f"s_{cname}"))
            self._sems[cname] = Sem(h, cname)
        return self._sems[cname]

    def sbuf(self, name, shape, dtype):
        return self.nc.alloc_sbuf_tensor(name, shape, dtype)

    def mark(self):
        return self.nc.sbuf_base

    def reset(self, m):
        self.nc.sbuf_base = m

    def const(self, name, arr):
        if name not in self.consts:
            self.consts[name] = np.ascontiguousarray(arr)
            dt = {np.dtype(np.float32): F32, np.dtype(ml_dtypes.bfloat16): BF16,
                  np.dtype(np.int32): I32}[arr.dtype]
            self._decl_input(name, list(arr.shape), dt)
        return self._inputs[name]

    def _decl_input(self, name, shape, dt):
        if not hasattr(self, "_inputs"):
            self._inputs = {}
        self._inputs[name] = self.nc.dram_tensor(name, shape, dt, kind="ExternalInput").ap()
        return self._inputs[name]

    def inp(self, name, shape, dt=F32):
        if not hasattr(self, "_inputs"):
            self._inputs = {}
        if name not in self._inputs:
            self._decl_input(name, shape, dt)
        return self._inputs[name]

    def out(self, name, shape, dt=F32):
        return self.nc.dram_tensor(name, shape, dt, kind="ExternalOutput").ap()

    def scratch(self, name, shape, dt):
        return self.nc.dram_tensor(name, shape, dt, kind="Internal").ap()

    def emit(self, eng, fn, waits=(), inc=None, inck=1):
        wl = tuple((s, (s.n if v is None else v)) for s, v in waits)
        if inc is not None:
            inc.bump(inck)
        def closure(e, fn=fn, wl=wl, inc=inc, inck=inck):
            for s, v in wl:
                if v > 0:
                    e.wait_ge(s.h, v)
            ins = fn(e)
            if inc is not None:
                assert ins is not None
                ins.then_inc(inc.h, inck)
        self.ops[eng].append(closure)
        if inc is not None:
            return (inc, inc.n)
        return None

    def dma(self, eng, dst, src, waits=(), inc=None):
        self.emit(eng, lambda e, d=dst, s=src: e.dma_start(out=d, in_=s),
                  waits=waits, inc=inc, inck=16)
        if inc is not None:
            return (inc, inc.n)
        return None

    def chan(self, name, eng="sp", nslots=2):
        if not hasattr(self, "_chans"):
            self._chans = {}
        key = (eng, nslots)
        if key not in self._chans:
            self._chans[key] = Chan(self, f"gch_{eng}_{chr(107 + nslots)}",
                                    eng, nslots)
        return self._chans[key]


    def wait_toks(self, eng, toks):
        for s, v in toks:
            self.emit(eng, lambda e, s=s, v=v: e.wait_ge(s.h, v))

    def barrier(self, toks=()):
        self.wait_toks("sp", toks)
        k = self.ph.n + len(ENGS)
        for eng in ENGS:
            self.emit(eng, lambda e: e.drain(), inc=self.ph)
        for eng in ENGS:
            self.emit(eng, lambda e, k=k: e.wait_ge(self.ph.h, k))

    def finalize(self):
        nc = self.nc
        with nc.Block() as block:
            block.tensor(lambda t: [f(t) for f in self.ops["pe"]])
            block.scalar(lambda s: [f(s) for f in self.ops["act"]])
            block.vector(lambda v: [f(v) for f in self.ops["dve"]])
            block.gpsimd(lambda g: [f(g) for f in self.ops["pool"]])
            block.sync(lambda q: [f(q) for f in self.ops["sp"]])
        self.stack.close()
        return nc


class PBView:

    def __init__(self, ap_):
        self._ap = ap_

    def __getitem__(self, k):
        return self._ap[k]

    def ap(self):
        return self._ap


class Chan:

    def __init__(self, b, name, eng, nslots):
        self.b = b
        self.eng = eng
        self.sems = [b.sem(f"{name}_c{chr(97 + s)}") for s in range(nslots)]
        self.last = [None] * nslots
        self.k = 0

    def issue(self, dst, src, waits=()):
        si = self.k % len(self.sems)
        s = self.sems[si]
        w = list(waits)
        if self.last[si] is not None:
            w.append(self.last[si])
        tok = self.b.dma(self.eng, dst, src, waits=w, inc=s)
        self.last[si] = tok
        self.k += 1
        return tok


def ident_const(b):
    return b.const("c_ident", np.eye(P, dtype=ml_dtypes.bfloat16))


def ones_row_const(b):
    return b.const("c_ones_row", np.ones((1, P), dtype=ml_dtypes.bfloat16))


class Resident:

    def __init__(self, b):
        self.ident = b.sbuf("r_ident", [P, P], BF16)
        self.identf = b.sbuf("r_identf", [P, P], F32)
        self.ones = b.sbuf("r_ones", [1, P], BF16)
        self.onesw = b.sbuf("r_onesw", [1, 512], BF16)
        self.zrow = b.sbuf("r_zrow", [1, 512], BF16)
        self.mu = b.sbuf("r_mu", [P, b.NT], F32)
        self.rstd = b.sbuf("r_rstd", [P, b.NT], F32)
        self.ssum = b.sbuf("r_ssum", [P, b.NT], F32)
        self.ssq = b.sbuf("r_ssq", [P, b.NT], F32)
        self.xT = [b.sbuf(f"r_xT{c}", [P, max(b.T, 4096)], BF16)
                   for c in range(ND)]
        ld = b.sem("res_ld")
        b.dma("sp", self.ident[:], ident_const(b), inc=ld)
        b.dma("sp", self.identf[:], b.const("c_identf", np.eye(P, dtype=np.float32)), inc=ld)
        b.dma("sp", self.ones[:], ones_row_const(b), inc=ld)
        b.dma("sp", self.onesw[:], b.const("c_onesw", np.ones((1, 512), dtype=ml_dtypes.bfloat16)), inc=ld)
        b.dma("sp", self.zrow[:], b.const("c_zrow", np.zeros((1, 512), dtype=ml_dtypes.bfloat16)), inc=ld)
        b.emit("sp", lambda e: e.wait_ge(ld.h, 80))
        b.barrier()


def ln_stats(b, src_hbm, sem_pref="st"):
    res = b.res
    NT = b.NT
    m = b.mark()
    u = b.uid()
    xa = [b.sbuf(f"ls{u}_x{j}", [P, D], F32) for j in range(2)]
    sq = b.sbuf(f"ls{u}_sq", [P, D], F32)
    sums = b.sbuf(f"ls{u}_sums", [P, NT], F32)
    sumsq = b.sbuf(f"ls{u}_sumsq", [P, NT], F32)
    var = b.sbuf(f"ls{u}_var", [P, NT], F32)
    ch = b.chan("ld")
    dna = b.sem("lns_dna")
    dnd = b.sem("lns_dnd")
    ta, td = {}, {}
    for i in range(NT):
        j = i % 2
        w = [ta[i - 2], td[i - 2]] if i >= 2 else []
        tok = ch.issue(xa[j][:], src_hbm[i * P:(i + 1) * P, :], waits=w)
        ta[i] = b.emit("act", lambda e, j=j, i=i: e.activation(
            sq[:], xa[j][:], AF.Square, accum_out=sumsq[:, i:i + 1]),
            waits=[tok] + ([ta[i - 1]] if i >= 1 else []), inc=dna)
        td[i] = b.emit("dve", lambda e, j=j, i=i: e.reduce_sum(
            sums[:, i:i + 1], xa[j][:], axis=AX.X),
            waits=[tok], inc=dnd)
    st = b.sem("lns_fin")
    t = b.emit("dve", lambda e: e.tensor_scalar(res.mu[:], sums[:], 1.0 / D, None,
                                                ALU.mult),
               waits=[ta[NT - 1], td[NT - 1]], inc=st)
    t = b.emit("dve", lambda e: e.tensor_scalar(var[:], sumsq[:], 1.0 / D, None,
                                                ALU.mult), waits=[t], inc=st)
    t = b.emit("dve", lambda e: e.tensor_tensor(sq[:, 0:NT], res.mu[:], res.mu[:],
                                                ALU.mult), waits=[t], inc=st)
    t = b.emit("dve", lambda e: e.tensor_sub(var[:], var[:], sq[:, 0:NT]),
               waits=[t], inc=st)
    t = b.emit("dve", lambda e: e.tensor_scalar(var[:], var[:], float(LN_EPS), None,
                                                ALU.add), waits=[t], inc=st)
    t = b.emit("act", lambda e: e.activation(var[:], var[:], AF.Sqrt),
               waits=[t], inc=st)
    b.emit("dve", lambda e: e.reciprocal(res.rstd[:], var[:]), waits=[t], inc=st)
    b.barrier()
    b.reset(m)


def to_fm(b, src_hbm, dstT, sem_pref, mu=None, rstd=None, evac=None,
          col_off=0, src_bf16=False):
    res = b.res
    NT = b.NT
    m = b.mark()
    sem_pref = f"{sem_pref}{b.uid()}"
    xa = [b.sbuf(f"{sem_pref}_x{j}", [P, D], BF16 if src_bf16 else F32)
          for j in range(2)]
    xh = xa if src_bf16 else [b.sbuf(f"{sem_pref}_h{j}", [P, D], BF16)
                              for j in range(2)]
    ch = b.chan("ld")
    ap = b.sem("tofm_ap")
    tr = b.sem("tofm_tr")
    ev = b.sem("tofm_ev")
    tap, ttr, tev = {}, {}, {}
    for i in range(NT):
        j = i % 2
        w = ([tap[i - 2]] if not src_bf16 else
             [ttr[(i - 2, ND - 1)]]) if i >= 2 else []
        tok = ch.issue(xa[j][:], src_hbm[i * P:(i + 1) * P, :], waits=w)
        hw = [ttr[(i - 1, ND - 1)]] if i >= 1 else []
        if src_bf16:
            tap[i] = tok
        elif mu is not None:
            tap[i] = b.emit("pool", lambda e, j=j, i=i: e.tensor_scalar(
                xh[j][:], xa[j][:], mu[:, i:i + 1], rstd[:, i:i + 1],
                ALU.subtract, ALU.mult), waits=[tok] + hw, inc=ap)
        else:
            tap[i] = b.emit("pool", lambda e, j=j: e.tensor_copy(xh[j][:], xa[j][:]),
                            waits=[tok] + hw, inc=ap)
        for c in range(ND):
            bank = b.pb[c % 4].ap().bitcast(BF16)
            prev = (i, c - 4) if c >= 4 else ((i - 1, c + 4) if i >= 1 else None)
            bw = [tev[prev]] if prev else []
            ttr[(i, c)] = b.emit("pe", lambda e, j=j, c=c, bank=bank: e.transpose(
                bank[:, 0:P], xh[j][:, c * P:(c + 1) * P], res.ident[:]),
                waits=[tap[i]] + bw, inc=tr)
            dst = dstT[c][:, col_off + i * P:col_off + (i + 1) * P]
            if evac is None:
                eng_ = "dve" if c % 2 == 0 else "act"
                evs = ev if c % 2 == 0 else b.sem("tofm_eva")
                tev[(i, c)] = b.emit(eng_, lambda e, dst=dst, bank=bank:
                    (e.tensor_copy(dst, bank[:, 0:P])
                     if hasattr(e, "tensor_copy") else
                     e.copy(dst, bank[:, 0:P])),
                    waits=[ttr[(i, c)]] + ([tev[(i, c - 2)]] if c >= 2 else
                                           ([tev[(i - 1, c + 6)]] if i >= 1 else [])),
                    inc=evs)
            elif evac[0] == "gelu":
                kind, gam, bet = evac
                tev[(i, c)] = b.emit("act", lambda e, dst=dst, bank=bank, c=c:
                    e.activation(dst, bank[:, 0:P], AF.Gelu,
                                 bias=bet[:, c:c + 1], scale=gam[:, c:c + 1]),
                    waits=[ttr[(i, c)]], inc=ev)
            else:
                kind, gam, bet = evac
                tev[(i, c)] = b.emit("dve", lambda e, dst=dst, bank=bank, c=c:
                    e.tensor_scalar(dst, bank[:, 0:P], gam[:, c:c + 1],
                                    bet[:, c:c + 1], ALU.mult, ALU.add),
                    waits=[ttr[(i, c)]], inc=ev)
    b.barrier()
    b.reset(m)


def ln_apply_transpose(b, src_hbm, sem_pref="la"):
    to_fm(b, src_hbm, b.res.xT, sem_pref, mu=b.res.mu, rstd=b.res.rstd)


def load_weight_chunks(b, w_hbm, name, rows, cols, sem, eng="pool"):
    nr = (rows + P - 1) // P
    tiles = []
    for c in range(nr):
        r0, r1 = c * P, min((c + 1) * P, rows)
        t = b.sbuf(f"{name}_{c}", [r1 - r0, cols], BF16)
        b.dma(eng, t[:], w_hbm[r0:r1, :], inc=sem)
        tiles.append(t)
    return tiles


def bias_row(b, bias_hbm, name, cols, sem):
    t = b.sbuf(name, [1, cols], BF16)
    b.dma("pool", t[:], bias_hbm.rearrange("(a c) -> a c", a=1), inc=sem)
    return t



def linear_tm(b, pref, xT, w_tiles, bias_tile, epi, kchunks=None, n_out=D,
              extra_mm=None, xt_off=0):
    res = b.res
    NT = b.NT
    kchunks = list(range(len(xT))) if kchunks is None else list(kchunks)
    NH = n_out // 512
    mm = b.sem("lin_mm")
    evtoks = {}
    for i in range(NT):
        for n in range(NH):
            bank = b.pb[(i % 2) * NH + n]
            def mmf(e, i=i, n=n, bank=bank):
                ins = None
                last_plain = (extra_mm is None and bias_tile is None)
                for ci, c in enumerate(kchunks):
                    ins = e.matmul(bank[:], xT[c][:, xt_off + i * P:xt_off + (i + 1) * P],
                                   w_tiles[c][:, n * 512:(n + 1) * 512],
                                   start=(ci == 0),
                                   stop=(last_plain and ci == len(kchunks) - 1))
                if extra_mm is not None:
                    ins = extra_mm(e, bank, i, n, stop=(bias_tile is None)) or ins
                if bias_tile is not None:
                    ins = e.matmul(bank[:], res.ones[0:1, 0:P],
                                   bias_tile[0:1, n * 512:(n + 1) * 512],
                                   start=False, stop=True)
                return ins
            bw = [evtoks[(i - 2, n)]] if i >= 2 else []
            mmtok = b.emit("pe", mmf, waits=bw, inc=mm)
            evtoks[(i, n)] = epi(i, n, bank, mmtok)
    return evtoks


class ResidualEpi:

    def __init__(self, b, pref, x_hbm, dst_hbm, coef, n_out=D, stats=None):
        self.b = b
        self.pref = pref
        self.x_hbm = x_hbm
        self.dst_hbm = dst_hbm
        self.coef = coef
        self.NH = n_out // 512
        self.n_out = n_out
        self.stats = stats
        self.xa = [b.sbuf(f"{pref}_exa{j}", [P, n_out], F32) for j in range(2)]
        self.xn = [b.sbuf(f"{pref}_exn{j}", [P, n_out], F32) for j in range(2)]
        self.sq = b.sbuf(f"{pref}_esq", [P, n_out], F32) if stats is not None else None
        self.ch = b.chan("eld")
        self.cst = b.chan("est", nslots=2)
        self.dnd = b.sem("epi_dnd")
        self.dna = b.sem("epi_dna")
        self.ev = b.sem("epi_ev")
        self.sttoks = {}
        self.ldtoks = {}
        self.evtoks = {}
        self.statoks = {}

    def __call__(self, i, n, bank, mmtok):
        b = self.b
        j = i % 2
        NH = self.NH
        if n == 0:
            w = [self.evtoks[(i - 2, nn)] for nn in range(NH)] if i >= 2 else []
            self.ldtoks[i] = self.ch.issue(
                self.xa[j][:], self.x_hbm[i * P:(i + 1) * P, 0:self.n_out], waits=w)
        sw = [self.sttoks[i - 2]] if i >= 2 else []
        if i >= 2 and self.stats is not None:
            sw.append(self.statoks[i - 2])
        t = b.emit("dve", lambda e, j=j, n=n, bank=bank: e.scalar_tensor_tensor(
            self.xn[j][:, n * 512:(n + 1) * 512], bank[:], float(self.coef),
            self.xa[j][:, n * 512:(n + 1) * 512], ALU.mult, ALU.add),
            waits=[mmtok, self.ldtoks[i]] + sw, inc=self.ev)
        self.evtoks[(i, n)] = t
        if n == NH - 1:
            self.sttoks[i] = self.cst.issue(
                self.dst_hbm[i * P:(i + 1) * P, 0:self.n_out], self.xn[j][:],
                waits=[t])
            if self.stats is not None:
                sums, sumsq = self.stats
                ta = b.emit("act", lambda e, j=j, i=i: e.activation(
                    self.sq[:], self.xn[j][:], AF.Square,
                    accum_out=sumsq[:, i:i + 1]),
                    waits=[t] + ([self.statoks[i - 1]] if i >= 1 else []),
                    inc=self.dna)
                b.emit("dve", lambda e, j=j, i=i: e.reduce_sum(
                    sums[:, i:i + 1], self.xn[j][:], axis=AX.X),
                    waits=[t], inc=self.dnd)
                self.statoks[i] = ta
        return t

    def finish(self):
        NT = self.b.NT
        return [self.sttoks[i] for i in (NT - 2, NT - 1) if i in self.sttoks]


class StoreStatsEpi:

    def __init__(self, b, pref, dst_hbm, stats):
        self.b = b
        self.dst_hbm = dst_hbm
        self.stats = stats
        self.xn = [b.sbuf(f"{pref}_exn{j}", [P, D], F32) for j in range(2)]
        self.sq = b.sbuf(f"{pref}_esq", [P, D], F32)
        self.cst = b.chan("est", nslots=2)
        self.dna = b.sem("epi_dna")
        self.dnd = b.sem("epi_dnd")
        self.ev = b.sem("epi_ev")
        self.sttoks = {}
        self.statoks = {}

    def __call__(self, i, n, bank, mmtok):
        b = self.b
        j = i % 2
        sw = [self.sttoks[i - 2], self.statoks[i - 2]] if i >= 2 else []
        t = b.emit("dve", lambda e, j=j, n=n, bank=bank: e.tensor_copy(
            self.xn[j][:, n * 512:(n + 1) * 512], bank[:]),
            waits=[mmtok] + sw, inc=self.ev)
        if n == 1:
            self.sttoks[i] = self.cst.issue(
                self.dst_hbm[i * P:(i + 1) * P, :], self.xn[j][:], waits=[t])
            sums, sumsq = self.stats
            self.statoks[i] = b.emit("act", lambda e, j=j, i=i: e.activation(
                self.sq[:], self.xn[j][:], AF.Square, accum_out=sumsq[:, i:i + 1]),
                waits=[t] + ([self.statoks[i - 1]] if i >= 1 else []), inc=self.dna)
            b.emit("dve", lambda e, j=j, i=i: e.reduce_sum(
                sums[:, i:i + 1], self.xn[j][:], axis=AX.X), waits=[t], inc=self.dnd)
        return t

    def finish(self):
        NT = self.b.NT
        return [self.sttoks[i] for i in (NT - 2, NT - 1) if i in self.sttoks]


class QlFinalEpi:

    def __init__(self, b, pref, blk, x_hbm, h_hbm, dst_hbm, hT, use_xhat,
                 stats=None):
        self.b = b
        self.pref = pref
        self.blk = blk
        self.x_hbm = x_hbm
        self.h_hbm = h_hbm
        self.dst_hbm = dst_hbm
        self.hT = hT
        self.use_xhat = use_xhat
        self.last = (blk == 2)
        self.xa = [b.sbuf(f"{pref}_exa{j}", [P, D], F32) for j in range(2)]
        self.xh = [b.sbuf(f"{pref}_exh{j}", [P, D], F32) for j in range(2)]
        self.xn = [b.sbuf(f"{pref}_exn{j}", [P, D], F32) for j in range(2)]
        self.hb = [b.sbuf(f"{pref}_ehb{j}", [P, D], BF16) for j in range(2)]
        self.ch = b.chan("eld")
        self.chx = b.chan("eldx") if self.last else None
        self.cst = b.chan("est", nslots=2)
        self.sttoks = {}
        self.ldtoks = {}
        self.evtoks = {}
        self.cbtoks = {}
        self.trtoks = {}
        self.tvtoks = {}
        self.xstoks = {}
        self.tr = b.sem("epi_tr")
        self.evt = b.sem("epi_evt")
        self.cb = b.sem("epi_cb")
        self.xs = b.sem("epi_xs")
        self.ev = b.sem("epi_ev")
        self.stats = stats
        self.sq = b.sbuf(f"{pref}_esq", [P, D], F32) if stats is not None else None
        self.dna = b.sem("epi_dna")
        self.dnd = b.sem("epi_dnd")
        self.statoks = {}
        self.hsrc = self.xh if use_xhat else self.xa

    def __call__(self, i, n, bank, mmtok):
        b = self.b
        res = b.res
        j = i % 2
        if n == 0:
            w = ([self.evtoks[(i - 2, nn)] for nn in range(2)] +
                 [self.cbtoks[i - 2]]) if i >= 2 else []
            srcb = self.x_hbm if self.use_xhat else self.h_hbm
            t1 = self.ch.issue(self.xa[j][:], srcb[i * P:(i + 1) * P, :], waits=w)
            if self.use_xhat:
                self.xstoks[i] = b.emit("dve", lambda e, j=j, i=i: e.tensor_scalar(
                    self.xh[j][:], self.xa[j][:], res.mu[:, i:i + 1],
                    res.rstd[:, i:i + 1], ALU.subtract, ALU.mult), waits=[t1],
                    inc=self.xs)
            self.ldtoks[i] = t1
            if self.last:
                self.ldtoks[(i, "x")] = self.chx.issue(
                    self.xh[j][:], self.x_hbm[i * P:(i + 1) * P, :], waits=w)
        sw = [self.sttoks[i - 2]] if i >= 2 else []
        if self.stats is not None and i >= 2 and (i - 2) in self.statoks:
            sw = sw + [self.statoks[i - 2]]
        t = b.emit("dve", lambda e, j=j, n=n, bank=bank: e.scalar_tensor_tensor(
            self.xn[j][:, n * 512:(n + 1) * 512], bank[:], 1.0,
            self.hsrc[j][:, n * 512:(n + 1) * 512], ALU.mult, ALU.add),
            waits=[mmtok, self.ldtoks[i]] + sw +
                  ([self.xstoks[i]] if self.use_xhat else []) +
                  ([self.ldtoks[(i, "x")]] if self.last else []), inc=self.ev)
        self.evtoks[(i, n)] = t
        if n == 1:
            if self.last:
                t2 = b.emit("dve", lambda e, j=j: e.scalar_tensor_tensor(
                    self.xn[j][:], self.xn[j][:], 0.5, self.xh[j][:],
                    ALU.mult, ALU.add), waits=[t] +
                    ([self.statoks[i - 2]] if (self.stats is not None and i >= 2)
                     else []), inc=self.cb)
                self.cbtoks[i] = t2
                self.sttoks[i] = self.cst.issue(
                    self.dst_hbm[i * P:(i + 1) * P, :], self.xn[j][:], waits=[t2])
                if self.stats is not None:
                    sums, sumsq = self.stats
                    ta = b.emit("act", lambda e, j=j, i=i: e.activation(
                        self.sq[:], self.xn[j][:], AF.Square,
                        accum_out=sumsq[:, i:i + 1]),
                        waits=[t2] + ([self.statoks[i - 1]] if i >= 1 else []),
                        inc=self.dna)
                    b.emit("dve", lambda e, j=j, i=i: e.reduce_sum(
                        sums[:, i:i + 1], self.xn[j][:], axis=AX.X),
                        waits=[t2], inc=self.dnd)
                    self.statoks[i] = ta
            else:
                self.sttoks[i] = self.cst.issue(
                    self.h_hbm[i * P:(i + 1) * P, :], self.xn[j][:], waits=[t])
                hww = [self.trtoks[(i - 1, ND - 1)]] if i >= 1 else []
                tc = b.emit("dve", lambda e, j=j: e.tensor_copy(
                    self.hb[j][:], self.xn[j][:]), waits=[t] + hww, inc=self.cb)
                self.cbtoks[i] = tc
                for c in range(ND):
                    pbank = b.pb[4 + c % 4].ap().bitcast(BF16)
                    prev = (i, c - 4) if c >= 4 else \
                        ((i - 1, c + 4) if i >= 1 else None)
                    bw = [self.tvtoks[prev]] if prev else []
                    tt = b.emit("pe", lambda e, j=j, c=c, pbank=pbank: e.transpose(
                        pbank[:, 0:P], self.hb[j][:, c * P:(c + 1) * P],
                        res.ident[:]), waits=[tc] + bw, inc=self.tr)
                    self.trtoks[(i, c)] = tt
                    self.tvtoks[(i, c)] = b.emit(
                        "dve", lambda e, i=i, c=c, pbank=pbank: e.tensor_copy(
                            self.hT[c][:, i * P:(i + 1) * P], pbank[:, 0:P]),
                        waits=[tt], inc=self.evt)
        return t

    def finish(self):
        NT = self.b.NT
        return [self.sttoks[i] for i in (NT - 2, NT - 1) if i in self.sttoks]


def stats_finalize(b, sums, sumsq, mu, rstd, pref, nt=None):
    nt = b.NT if nt is None else nt
    m = b.mark()
    u = b.uid()
    var = b.sbuf(f"{pref}{u}_var", [P, nt], F32)
    msq = b.sbuf(f"{pref}{u}_msq", [P, nt], F32)
    st = b.sem(pref + "_fin")
    b.emit("dve", lambda e: e.tensor_scalar(mu[:, 0:nt], sums[:, 0:nt], 1.0 / D,
                                            None, ALU.mult), inc=st)
    b.emit("dve", lambda e: e.tensor_scalar(var[:], sumsq[:, 0:nt], 1.0 / D,
                                            None, ALU.mult), inc=st)
    b.emit("dve", lambda e: e.tensor_tensor(msq[:], mu[:, 0:nt], mu[:, 0:nt],
                                            ALU.mult), waits=[(st, None)], inc=st)
    b.emit("dve", lambda e: e.tensor_sub(var[:], var[:], msq[:]),
           waits=[(st, None)], inc=st)
    b.emit("dve", lambda e: e.tensor_scalar(var[:], var[:], float(LN_EPS), None,
                                            ALU.add), waits=[(st, None)], inc=st)
    b.emit("act", lambda e: e.activation(var[:], var[:], AF.Sqrt),
           waits=[(st, None)], inc=st)
    b.emit("dve", lambda e: e.reciprocal(rstd[:, 0:nt], var[:]),
           waits=[(st, None)], inc=st)
    b.barrier()
    b.reset(m)


def linear_tm_epilogue(b, w_tiles, bias_tile, coef, x_hbm, dst_hbm, sem_pref,
                       kchunks=None, xT=None, extra_mm=None, n_out=D,
                       out_stats=None):
    if xT is None:
        xT = b.res.xT
    epi = ResidualEpi(b, sem_pref, x_hbm, dst_hbm, coef, n_out=n_out,
                      stats=out_stats)
    linear_tm(b, sem_pref, xT, w_tiles, bias_tile, epi, kchunks=kchunks,
              n_out=n_out, extra_mm=extra_mm)
    b.barrier(toks=epi.finish())



def col_layout(b, vec_hbm_1d, name, psum_bank=6):
    res = b.res
    out = b.sbuf(name, [P, ND], F32)
    m = b.mark()
    rows = b.sbuf(name + "_r", [ND, P], F32)
    s = b.sem("colay_s")
    lds = b.sem("colay_dl")
    tok = b.dma("sp", rows[:], vec_hbm_1d.rearrange("(c p) -> c p", p=P), inc=lds)
    t = b.emit("pe", lambda e: e.transpose(b.pb[psum_bank][:, 0:ND], rows[:],
                                           res.identf[0:ND, 0:ND]),
               waits=[tok], inc=s)
    b.emit("dve", lambda e: e.tensor_copy(out[:], b.pb[psum_bank][:, 0:ND]),
           waits=[t], inc=s)
    b.barrier()
    b.reset(m)
    return out


def rows_to_colT(b, src_hbm, r, name, psum_bank=6):
    res = b.res
    outs = [b.sbuf(f"{name}_{c}", [P, r], BF16) for c in range(ND)]
    m = b.mark()
    rows = b.sbuf(name + "_r", [r, D], BF16)
    s = b.sem("rct_s")
    ev = b.sem("rct_e")
    lds = b.sem("rct_dl")
    tok = b.dma("pool", rows[:], src_hbm, inc=lds)
    tt, te = {}, {}
    for c in range(ND):
        bank = b.pb[psum_bank + (c % 2)].ap().bitcast(BF16)
        bw = [te[c - 2]] if c >= 2 else []
        tt[c] = b.emit("pe", lambda e, c=c, bank=bank, r=r: e.transpose(
            bank[:, 0:r], rows[:, c * P:(c + 1) * P], res.ident[0:r, 0:r]),
            waits=[tok] + bw, inc=s)
        te[c] = b.emit("dve", lambda e, c=c, bank=bank, r=r: e.tensor_copy(
            outs[c][:], bank[:, 0:r]), waits=[tt[c]], inc=ev)
    b.barrier()
    b.reset(m)
    return outs


def colchunks_to_rowT(b, src_hbm, r, name, psum_bank=6):
    res = b.res
    out = b.sbuf(name, [r, D], BF16)
    m = b.mark()
    chk = [b.sbuf(f"{name}_r{j}", [P, r], BF16) for j in range(2)]
    ch = b.chan("cct_l", eng="pool")
    s = b.sem("cct_s")
    ev = b.sem("cct_e")
    tt, te = {}, {}
    for c in range(ND):
        j = c % 2
        bank = b.pb[psum_bank + j].ap().bitcast(BF16)
        w = [tt[c - 2]] if c >= 2 else []
        tok = ch.issue(chk[j][:], src_hbm[c * P:(c + 1) * P, :], waits=w)
        bw = [te[c - 2]] if c >= 2 else []
        tt[c] = b.emit("pe", lambda e, j=j, bank=bank, r=r: e.transpose(
            bank[0:r, 0:P], chk[j][:], res.ident[:]), waits=[tok] + bw, inc=s)
        te[c] = b.emit("dve", lambda e, c=c, bank=bank, r=r: e.tensor_copy(
            out[:, c * P:(c + 1) * P], bank[0:r, 0:P]), waits=[tt[c]], inc=ev)
    b.barrier()
    b.reset(m)
    return out


def lora_mid(b, inT, laT, name):
    out = b.sbuf(name, [32, b.T], BF16)
    mm = b.sem("lmid_mm")
    ev = b.sem("lmid_ev")
    tm, te = {}, {}
    nb = b.T // 512
    for t in range(nb):
        bank = b.pb[t % 2]
        def mmf(e, t=t, bank=bank):
            ins = None
            for c in range(ND):
                ins = e.matmul(bank[0:32, :], laT[c][:], inT[c][:, t * 512:(t + 1) * 512],
                               start=(c == 0), stop=(c == ND - 1))
            return ins
        bw = [te[t - 2]] if t >= 2 else []
        tm[t] = b.emit("pe", mmf, waits=bw, inc=mm)
        te[t] = b.emit("dve", lambda e, t=t, bank=bank: e.tensor_copy(
            out[:, t * 512:(t + 1) * 512], bank[0:32, :]), waits=[tm[t]], inc=ev)
    b.barrier()
    return out


def dequant_w(b, q_hbm, sc_hbm, name):
    res = b.res
    outs = [b.sbuf(f"{name}_{c}", [P, D], BF16) for c in range(ND)]
    m = b.mark()
    qi = [b.sbuf(f"{name}_q{j}", [P, D], I32) for j in range(2)]
    sc = [b.sbuf(f"{name}_s{j}", [P, D // 16], F32) for j in range(2)]
    qf = b.sbuf(name + "_f", [P, D], F32)
    wb = [b.sbuf(f"{name}_w{j}", [P, D], BF16) for j in range(2)]
    chq = b.chan("dqw_lq")
    chs = b.chan("dqw_ls")
    dq = b.sem("dqw_dq")
    tr = b.sem("dqw_tr")
    ev = b.sem("dqw_ev")
    tmul, ttr, tev = {}, {}, {}
    tcv = {}
    for oc in range(ND):
        j = oc % 2
        w = [tmul[oc - 1]] if oc >= 1 else []
        wl = [tcv[oc - 2]] if oc >= 2 else []
        tq = chq.issue(qi[j][:], q_hbm[oc * P:(oc + 1) * P, :], waits=wl)
        ts = chs.issue(sc[j][:], sc_hbm[oc * P:(oc + 1) * P, :],
                       waits=[tmul[oc - 2]] if oc >= 2 else [])
        tcv[oc] = b.emit("pool", lambda e, j=j: e.tensor_copy(qf[:], qi[j][:]),
                         waits=[tq] + w, inc=dq)
        t2 = b.emit("pool", lambda e: e.tensor_scalar(qf[:], qf[:], 1.0 / 7.5, -1.0,
                                                      ALU.mult, ALU.add),
                    waits=[tcv[oc]], inc=dq)
        def mul(e, j=j):
            scv = sc[j][:]
            sc0 = bass.AP(tensor=scv.tensor, offset=scv.offset,
                          ap=[[scv.ap[0][0], P], [1, 64], [0, 16]])
            return e.tensor_tensor(wb[j].ap().rearrange("p (g w) -> p g w", w=16),
                                   qf.ap().rearrange("p (g w) -> p g w", w=16),
                                   sc0, ALU.mult)
        hw = [ttr[(oc - 1, ND - 1)]] if oc >= 1 else []
        tmul[oc] = b.emit("pool", mul, waits=[ts, t2] + hw, inc=dq)
        for c in range(ND):
            bank = b.pb[4 + c % 4].ap().bitcast(BF16)
            prev = (oc, c - 4) if c >= 4 else ((oc - 1, c + 4) if oc >= 1 else None)
            bw = [tev[prev]] if prev else []
            ttr[(oc, c)] = b.emit("pe", lambda e, j=j, c=c, bank=bank: e.transpose(
                bank[:, 0:P], wb[j][:, c * P:(c + 1) * P], res.ident[:]),
                waits=[tmul[oc]] + bw, inc=tr)
            tev[(oc, c)] = b.emit("dve", lambda e, c=c, oc=oc, bank=bank:
                e.tensor_copy(outs[c][:, oc * P:(oc + 1) * P], bank[:, 0:P]),
                waits=[ttr[(oc, c)]], inc=ev)
    b.barrier()
    b.reset(m)
    return outs


def lora_extra_mm(mid, lbT):
    def f(e, bank, i, n, stop):
        return e.matmul(bank[:], mid[:, i * P:(i + 1) * P],
                        lbT[:, n * 512:(n + 1) * 512], start=False, stop=stop)
    return f


def sublayer_l1ffn(b, ctx, x_hbm, dst_hbm, out_stats=None):
    ctx["out_stats"] = out_stats
    """x_new = x + 0.5 * qlora_stack(LN(x)). Needs ln_stats+ln_apply done."""
    n = ctx["n"]
    qw = ctx["L1_ffn_qweights"]
    sc = ctx["L1_ffn_scales"]
    bi = ctx["L1_ffn_biases"]
    la = ctx["L1_ffn_lora_a"]
    lb = ctx["L1_ffn_lora_b"]
    lng = ctx["L1_ffn_ln_gamma"]
    lnb = ctx["L1_ffn_ln_beta"]
    h_hbm = ctx["l1f_h"]
    y_hbm = ctx["l1f_y"]
    sub_m = b.mark()
    mu_y = b.sbuf(f"l1mu{n}", [P, b.NT], F32)
    rs_y = b.sbuf(f"l1rs{n}", [P, b.NT], F32)
    sums = b.sbuf(f"l1sm{n}", [P, b.NT], F32)
    sumsq = b.sbuf(f"l1sq{n}", [P, b.NT], F32)
    cur_hT = b.res.xT
    for blk in range(3):
        i0, i1 = 2 * blk, 2 * blk + 1
        blk_m = b.mark()
        gcol = col_layout(b, lng[blk], f"l1gc{n}_{blk}")
        bcol = col_layout(b, lnb[blk], f"l1bc{n}_{blk}")
        wT1 = dequant_w(b, qw[i1], sc[i1], f"l1w{n}_{blk}b")
        laT1 = rows_to_colT(b, la[i1], 32, f"l1la{n}_{blk}b")
        lbT1 = colchunks_to_rowT(b, lb[i1], 32, f"l1lb{n}_{blk}b")
        ldb = b.sem("l1bl")
        bt0 = bias_row(b, bi[i0], f"l1b{n}_{blk}a", D, ldb)
        bt1 = bias_row(b, bi[i1], f"l1b{n}_{blk}b", D, ldb)
        b.emit("pe", lambda e, v=ldb.n: e.wait_ge(ldb.h, v))
        m_a = b.mark()
        wT0 = dequant_w(b, qw[i0], sc[i0], f"l1w{n}_{blk}a")
        laT0 = rows_to_colT(b, la[i0], 32, f"l1la{n}_{blk}a")
        lbT0 = colchunks_to_rowT(b, lb[i0], 32, f"l1lb{n}_{blk}a")
        mid0 = lora_mid(b, cur_hT, laT0, f"l1m{n}_{blk}a")
        epi0 = StoreStatsEpi(b, f"l1A{n}_{blk}", y_hbm, (sums, sumsq))
        linear_tm(b, f"l1A{n}_{blk}", cur_hT, wT0, bt0, epi0,
                  extra_mm=lora_extra_mm(mid0, lbT0))
        b.barrier(toks=epi0.finish())
        stats_finalize(b, sums, sumsq, mu_y, rs_y, f"l1sf{n}_{blk}")
        b.reset(m_a)
        g_fm = [b.sbuf(f"l1g{n}_{blk}_{c}", [P, b.T], BF16) for c in range(ND)]
        to_fm(b, y_hbm, g_fm, f"l1T{n}_{blk}", mu=mu_y, rstd=rs_y,
              evac=("gelu", gcol, bcol))
        mid1 = lora_mid(b, g_fm, laT1, f"l1m{n}_{blk}b")
        epiF = QlFinalEpi(b, f"l1B{n}_{blk}", blk, x_hbm, h_hbm, dst_hbm,
                          cur_hT, use_xhat=(blk == 0),
                          stats=(ctx.get("out_stats") if blk == 2 else None))
        linear_tm(b, f"l1B{n}_{blk}", g_fm, wT1, bt1, epiF,
                  extra_mm=lora_extra_mm(mid1, lbT1))
        b.barrier(toks=epiF.finish())
        b.reset(blk_m)
    b.reset(sub_m)
    ctx["n"] += 1



class StoreEpi:

    def __init__(self, b, pref, dst_hbm, dt=F32):
        self.b = b
        self.dst_hbm = dst_hbm
        self.xn = [b.sbuf(f"{pref}_exn{j}", [P, D], dt) for j in range(2)]
        self.cst = b.chan("est", nslots=2)
        self.ev = b.sem("epi_ev")
        self.sttoks = {}

    def __call__(self, i, n, bank, mmtok):
        b = self.b
        j = i % 2
        sw = [self.sttoks[i - 2]] if i >= 2 else []
        t = b.emit("dve", lambda e, j=j, n=n, bank=bank: e.tensor_copy(
            self.xn[j][:, n * 512:(n + 1) * 512], bank[:]),
            waits=[mmtok] + sw, inc=self.ev)
        if n == 1:
            self.sttoks[i] = self.cst.issue(
                self.dst_hbm[i * P:(i + 1) * P, :], self.xn[j][:], waits=[t])
        return t

    def finish(self):
        NT = self.b.NT
        return [self.sttoks[i] for i in (NT - 2, NT - 1) if i in self.sttoks]


class VEpi:

    def __init__(self, b, pref, v_res):
        self.b = b
        self.v_res = v_res
        self.ev = b.sem("epi_ev")

    def __call__(self, i, n, bank, mmtok):
        b = self.b
        vr = self.v_res[:]
        off = i * 1056 + n * 8 * 66
        dst = bass.AP(tensor=vr.tensor, offset=vr.offset + off,
                      ap=[[vr.ap[0][0], P], [66, 8], [1, 64]])
        t = b.emit("dve", lambda e, bank=bank, dst=dst: e.tensor_copy(
            dst, bank.ap().rearrange("p (h w) -> p h w", w=64)),
            waits=[mmtok], inc=self.ev)
        return t

    def finish(self):
        return []


def qk_fm_pass(b, src_hbm, pref, transform, dst_sbufT=None, dst_hbm=None,
               norm_out=None, post=None, dst_hbm_lo=None):
    res = b.res
    NT = b.NT
    m = b.mark()
    srcbf = (transform is None or transform[0] != "ln")
    xa = [b.sbuf(f"{pref}_x{j}", [P, D], BF16 if srcbf else F32)
          for j in range(2)]
    xh = xa if srcbf else [b.sbuf(f"{pref}_h{j}", [P, D], BF16)
                           for j in range(2)]
    ss = b.sbuf(f"{pref}_ss", [P, 16], F32)
    scr = b.sbuf(f"{pref}_sc", [P, D], F32)
    strip = [b.sbuf(f"{pref}_st{j}", [P, D], BF16) for j in range(2)] \
        if dst_hbm is not None else None
    strip_lo = [b.sbuf(f"{pref}_sl{j}", [P, D], BF16) for j in range(2)] \
        if dst_hbm_lo is not None else None
    xl = [b.sbuf(f"{pref}_xl{j}", [P, D], BF16) for j in range(2)] \
        if dst_hbm_lo is not None else None
    afscr = b.sbuf(f"{pref}_af", [P, P], F32) if dst_hbm_lo is not None else None
    afsc2 = b.sbuf(f"{pref}_ag", [P, P], F32) if dst_hbm_lo is not None else None
    ch = b.chan("ld")
    cstt = b.chan("tst", nslots=2) if dst_hbm is not None else None
    cstl = b.chan("tsl", nslots=2) if dst_hbm_lo is not None else None
    ap_ = b.sem("qkf_ap")
    tr = b.sem("qkf_tr")
    ev = b.sem("qkf_ev")
    nrm = b.sem("qkf_nrm")
    tz = None
    if norm_out is not None:
        tz = b.emit("dve", lambda e: e.memset(norm_out[:], 0.0), inc=nrm)
    tap, ttr, tev, tst, tnm = {}, {}, {}, {}, {}

    def hview(ap, dsub=64):
        return bass.AP(tensor=ap.tensor, offset=ap.offset,
                       ap=[[ap.ap[0][0], P], [64, 16], [1, dsub]])

    for i in range(NT):
        j = i % 2
        if srcbf:
            w = [ttr[(i - 2, ND - 1)]] if i >= 2 else []
        else:
            w = [tap[i - 2]] if i >= 2 else []
        if i >= 2 and norm_out is not None:
            w.append(tnm[i - 2])
        tok = ch.issue(xa[j][:], src_hbm[i * P:(i + 1) * P, :], waits=w)
        hw = [ttr[(i - 1, ND - 1)]] if i >= 1 else []
        nw = [tnm[i - 1]] if (i >= 1 and norm_out is not None) else []
        tt = tok
        if transform is not None and transform[0] == "rope":
            _, cosc, sinc = transform
            def pview(ap, off):
                return bass.AP(tensor=ap.tensor, offset=ap.offset + off,
                               ap=[[ap.ap[0][0], P], [64, 16], [2, 32]])
            def sview(t_):
                a = t_[:]
                return bass.AP(tensor=a.tensor, offset=a.offset,
                               ap=[[a.ap[0][0], P], [32, 16], [1, 32]])
            def cv(cs):
                a = cs[:]
                return bass.AP(tensor=a.tensor, offset=a.offset + i * 32,
                               ap=[[a.ap[0][0], P], [0, 16], [1, 32]])
            s1v = sview(scr)
            s2v = bass.AP(tensor=scr[:].tensor, offset=scr[:].offset + 512,
                          ap=[[scr[:].ap[0][0], P], [32, 16], [1, 32]])
            xe = pview(xa[j][:], 0)
            xo = pview(xa[j][:], 1)
            cosv, sinv = cv(cosc), cv(sinc)
            t1 = b.emit("pool", lambda e, xe=xe, cosv=cosv, s1v=s1v:
                e.tensor_tensor(s1v, xe, cosv, ALU.mult),
                waits=[tt] + nw + hw, inc=ap_)
            t2 = b.emit("pool", lambda e, xo=xo, sinv=sinv, s2v=s2v:
                e.tensor_tensor(s2v, xo, sinv, ALU.mult), waits=[t1], inc=ap_)
            t3 = b.emit("pool", lambda e, s1v=s1v, s2v=s2v:
                e.tensor_sub(s1v, s1v, s2v), waits=[t2], inc=ap_)
            t4 = b.emit("pool", lambda e, xe=xe, sinv=sinv, s2v=s2v:
                e.tensor_tensor(s2v, xe, sinv, ALU.mult), waits=[t3], inc=ap_)
            t5 = b.emit("pool", lambda e, xo=xo, cosv=cosv:
                e.tensor_tensor(xo, xo, cosv, ALU.mult), waits=[t4], inc=ap_)
            t6 = b.emit("pool", lambda e, xo=xo, s2v=s2v:
                e.tensor_add(xo, xo, s2v), waits=[t5], inc=ap_)
            tt = b.emit("pool", lambda e, xe=xe, s1v=s1v:
                e.tensor_copy(xe, s1v), waits=[t6], inc=ap_)
        elif transform is not None and transform[0] == "rmsnorm":
            _, qnw_b = transform
            t1 = b.emit("act", lambda e, j=j: e.activation(
                scr[:], xa[j][:], AF.Square), waits=[tt] + nw + hw, inc=ap_)
            t2 = b.emit("dve", lambda e: e.tensor_reduce(
                ss[:], hview(scr[:]), axis=AX.X, op=ALU.add),
                waits=[t1], inc=ap_)
            t3 = b.emit("dve", lambda e: e.tensor_scalar(
                ss[:], ss[:], 1.0 / 64, 1e-6, ALU.mult, ALU.add),
                waits=[t2], inc=ap_)
            t4 = b.emit("act", lambda e: e.activation(ss[:], ss[:], AF.Sqrt),
                        waits=[t3], inc=ap_)
            t5 = b.emit("dve", lambda e: e.reciprocal(ss[:], ss[:]),
                        waits=[t4], inc=ap_)
            rrv = bass.AP(tensor=ss[:].tensor, offset=ss[:].offset,
                          ap=[[ss[:].ap[0][0], P], [1, 16], [0, 64]])
            t6 = b.emit("dve", lambda e, j=j, rrv=rrv: e.tensor_tensor(
                hview(xa[j][:]), hview(xa[j][:]), rrv, ALU.mult),
                waits=[t5], inc=ap_)
            tt = b.emit("dve", lambda e, j=j: e.tensor_mul(
                xa[j][:], xa[j][:], qnw_b[:]), waits=[t6], inc=ap_)
        elif transform is not None and transform[0] == "ln":
            _, muv, rsv = transform
            tt = b.emit("pool", lambda e, j=j, i=i: e.tensor_scalar(
                xa[j][:], xa[j][:], muv[:, i:i + 1], rsv[:, i:i + 1],
                ALU.subtract, ALU.mult), waits=[tt] + nw + hw, inc=ap_)
        else:
            tt = b.emit("dve", lambda e: e.engine_nop(), waits=[tt] + nw + hw,
                        inc=ap_)
        if srcbf:
            tcast = tt
        else:
            tcast = b.emit("dve", lambda e, j=j: e.tensor_copy(xh[j][:], xa[j][:]),
                           waits=[tt], inc=ap_)
        if dst_hbm_lo is not None:
            tcast = b.emit("dve", lambda e, j=j: e.tensor_sub(
                xl[j][:], xa[j][:], xh[j][:]), waits=[tcast], inc=ap_)
        tap[i] = tcast
        if norm_out is not None:
            t1 = b.emit("act", lambda e, j=j: e.activation(
                scr[:], xa[j][:], AF.Square), waits=[tcast], inc=ap_)
            t2 = b.emit("dve", lambda e: e.tensor_reduce(
                ss[:], hview(scr[:]), axis=AX.X, op=ALU.add),
                waits=[t1], inc=ap_)
            tnm[i] = b.emit("dve", lambda e: e.tensor_max(
                norm_out[:], norm_out[:], ss[:]),
                waits=[t2, tnm[i - 1] if i >= 1 else tz], inc=nrm)
        for c in range(ND):
            bank = b.pb[c % 4].ap().bitcast(BF16)
            prev = (i, c - 4) if c >= 4 else ((i - 1, c + 4) if i >= 1 else None)
            bw = [tev[prev]] if prev else []
            ttr[(i, c)] = b.emit("pe", lambda e, j=j, c=c, bank=bank: e.transpose(
                bank[:, 0:P], xh[j][:, c * P:(c + 1) * P], res.ident[:]),
                waits=[tcast] + bw, inc=tr)
            if dst_sbufT is not None:
                if post is None:
                    eng_ = "dve" if c % 2 == 0 else "act"
                    evs = ev if c % 2 == 0 else b.sem("qkf_eva")
                    tev[(i, c)] = b.emit(eng_, lambda e, i=i, c=c, bank=bank:
                        (e.tensor_copy(dst_sbufT[c][:, i * P:(i + 1) * P],
                                       bank[:, 0:P])
                         if hasattr(e, "tensor_copy") else
                         e.copy(dst_sbufT[c][:, i * P:(i + 1) * P], bank[:, 0:P])),
                        waits=[ttr[(i, c)]] + ([tev[(i, c - 2)]] if c >= 2 else
                                               ([tev[(i - 1, c + 6)]] if i >= 1
                                                else [])),
                        inc=evs)
                else:
                    _, gam, bet = post
                    tev[(i, c)] = b.emit("dve", lambda e, i=i, c=c, bank=bank:
                        e.tensor_scalar(dst_sbufT[c][:, i * P:(i + 1) * P],
                                        bank[:, 0:P], gam[:, c:c + 1],
                                        bet[:, c:c + 1], ALU.mult, ALU.add),
                        waits=[ttr[(i, c)]], inc=ev)
            else:
                sw2 = list(tst[i - 2]) if i >= 2 else []
                if post is None:
                    eng_ = "dve" if c % 2 == 0 else "act"
                    evs = ev if c % 2 == 0 else b.sem("qkf_eva")
                    tev[(i, c)] = b.emit(eng_, lambda e, j=j, c=c, bank=bank:
                        (e.tensor_copy(strip[j][:, c * P:(c + 1) * P],
                                       bank[:, 0:P])
                         if hasattr(e, "tensor_copy") else
                         e.copy(strip[j][:, c * P:(c + 1) * P], bank[:, 0:P])),
                        waits=[ttr[(i, c)]] + (sw2 if c == 0 else []) +
                              ([tev[(i, c - 2)]] if c >= 2 else
                               ([tev[(i - 1, c + 6)]] if i >= 1 else [])),
                        inc=evs)
                elif dst_hbm_lo is None:
                    _, gam, bet = post
                    tev[(i, c)] = b.emit("dve", lambda e, j=j, c=c, bank=bank:
                        e.tensor_scalar(strip[j][:, c * P:(c + 1) * P],
                                        bank[:, 0:P], gam[:, c:c + 1],
                                        bet[:, c:c + 1], ALU.mult, ALU.add),
                        waits=[ttr[(i, c)]] + (sw2 if c == 0 else []), inc=ev)
                else:
                    _, gam, bet = post
                    bank_lo = b.pb[4 + c % 4].ap().bitcast(BF16)
                    prev2 = (i, c - 4) if c >= 4 else \
                        ((i - 1, c + 4) if i >= 1 else None)
                    bw2 = [tev[prev2]] if prev2 else []
                    ttl = b.emit("pe", lambda e, j=j, c=c, bank_lo=bank_lo:
                        e.transpose(bank_lo[:, 0:P], xl[j][:, c * P:(c + 1) * P],
                                    res.ident[:]),
                        waits=[tap[i]] + bw2, inc=tr)
                    t0 = b.emit("dve", lambda e, c=c, bank=bank:
                        e.tensor_copy(afsc2[:], bank[:, 0:P]),
                        waits=[ttr[(i, c)]] +
                              ([tev[(i, c - 1)]] if c >= 1 else
                               ([tev[(i - 1, ND - 1)]] if i >= 1 else [])) +
                              (sw2 if c == 0 else []), inc=ev)
                    t1 = b.emit("dve", lambda e, c=c, bank_lo=bank_lo:
                        e.tensor_add(afsc2[:], afsc2[:], bank_lo[:, 0:P]),
                        waits=[t0, ttl], inc=ev)
                    t1 = b.emit("dve", lambda e, c=c:
                        e.tensor_scalar(afscr[:], afsc2[:], gam[:, c:c + 1],
                                        bet[:, c:c + 1], ALU.mult, ALU.add),
                        waits=[t1], inc=ev)
                    t2 = b.emit("dve", lambda e, j=j, c=c:
                        e.tensor_copy(strip[j][:, c * P:(c + 1) * P], afscr[:]),
                        waits=[t1], inc=ev)
                    t3 = b.emit("dve", lambda e, j=j, c=c:
                        e.tensor_sub(afscr[:], afscr[:],
                                     strip[j][:, c * P:(c + 1) * P]),
                        waits=[t2], inc=ev)
                    tev[(i, c)] = b.emit("dve", lambda e, j=j, c=c:
                        e.tensor_copy(strip_lo[j][:, c * P:(c + 1) * P],
                                      afscr[:]),
                        waits=[t3], inc=ev)
        if dst_hbm is not None:
            dv = dst_hbm
            dap = bass.AP(tensor=dv.tensor,
                          offset=dv.offset + i * P,
                          ap=[[b.T, P], [P * b.T, ND], [1, P]])
            toks_i = [cstt.issue(
                dap, strip[j].ap().rearrange("p (c t) -> p c t", c=ND),
                waits=[tev[(i, c2)] for c2 in range(ND)])]
            if dst_hbm_lo is not None:
                dv2 = dst_hbm_lo
                dap2 = bass.AP(tensor=dv2.tensor, offset=dv2.offset + i * P,
                               ap=[[b.T, P], [P * b.T, ND], [1, P]])
                toks_i.append(cstl.issue(
                    dap2, strip_lo[j].ap().rearrange("p (c t) -> p c t", c=ND),
                    waits=[tev[(i, c2)] for c2 in range(ND)]))
            tst[i] = toks_i
    if dst_hbm is not None:
        b.barrier(toks=[t for i in (NT - 2, NT - 1) if i in tst
                        for t in tst[i]])
    else:
        b.barrier()
    b.reset(m)


def bcast16(b, pref, src16, dst, psum_bank=7, extra_waits=()):
    res = b.res
    s = b.sem("bc_s")
    t = b.emit("pe", lambda e: e.matmul(b.pb[psum_bank][:, 0:16],
                                        res.ones[0:1, 0:P], src16[0:1, 0:16],
                                        start=True, stop=True),
               waits=list(extra_waits), inc=s)
    return b.emit("dve", lambda e: e.tensor_copy(dst[:], b.pb[psum_bank][:, 0:16]),
                  waits=[t], inc=s)


def mhat_phase(b, pref, maxq, maxk, mhat_neg, mpair_neg=None):
    res = b.res
    m = b.mark()
    s = b.sem("mh_s")
    t16 = b.sbuf(pref + "_t16", [16, P], F32)
    red = b.sbuf(pref + "_red", [16, 2], F32)
    prod = b.sbuf(pref + "_prod", [16, 1], F32)
    row = b.sbuf(pref + "_row", [1, 16], F32)
    rowb = b.sbuf(pref + "_rowb", [1, 16], BF16)
    t = b.emit("pe", lambda e: e.transpose(b.pb[7][0:16, 0:P], maxq[:],
                                           res.identf[:]), inc=s)
    t = b.emit("dve", lambda e: e.tensor_copy(t16[:], b.pb[7][0:16, 0:P]),
               waits=[t], inc=s)
    t = b.emit("dve", lambda e: e.reduce_max(red[:, 0:1], t16[:], axis=AX.X),
               waits=[t], inc=s)
    t = b.emit("pe", lambda e: e.transpose(b.pb[7][0:16, 0:P], maxk[:],
                                           res.identf[:]), waits=[t], inc=s)
    t = b.emit("dve", lambda e: e.tensor_copy(t16[:], b.pb[7][0:16, 0:P]),
               waits=[t], inc=s)
    t = b.emit("dve", lambda e: e.reduce_max(red[:, 1:2], t16[:], axis=AX.X),
               waits=[t], inc=s)
    t = b.emit("dve", lambda e: e.tensor_mul(prod[:], red[:, 0:1], red[:, 1:2]),
               waits=[t], inc=s)
    t = b.emit("act", lambda e: e.activation(prod[:], prod[:], AF.Sqrt),
               waits=[t], inc=s)
    t = b.emit("dve", lambda e: e.tensor_scalar(prod[:], prod[:], -0.125, None,
                                                ALU.mult), waits=[t], inc=s)
    t = b.emit("pe", lambda e: e.transpose(b.pb[7][0:1, 0:16], prod[:],
                                           res.identf[0:16, 0:16]),
               waits=[t], inc=s)
    t = b.emit("dve", lambda e: e.tensor_copy(row[:], b.pb[7][0:1, 0:16]),
               waits=[t], inc=s)
    t = b.emit("dve", lambda e: e.tensor_copy(rowb[:], row[:]), waits=[t], inc=s)
    t2 = bcast16(b, pref, rowb, mhat_neg, psum_bank=7, extra_waits=[t])
    if mpair_neg is not None:
        mv = mhat_neg[:]
        ev_ = bass.AP(tensor=mv.tensor, offset=mv.offset,
                      ap=[[mv.ap[0][0], P], [2, 8]])
        od_ = bass.AP(tensor=mv.tensor, offset=mv.offset + 1,
                      ap=[[mv.ap[0][0], P], [2, 8]])
        b.emit("dve", lambda e: e.tensor_tensor(mpair_neg[:], ev_, od_, ALU.min),
               waits=[t2], inc=s)
    b.barrier()
    b.reset(m)


def attn_core(b, pref, q_buf, k_resT, v_res, mhat_neg, mpair_neg, o_buf,
              window_tiles=None, causal=False, diagmask=None):
    res = b.res
    NT = b.NT
    m = b.mark()
    qt = [b.sbuf(f"{pref}_qt{j}", [P, D], BF16) for j in range(2)]
    pt = [b.sbuf(f"{pref}_pt{j}", [P, 1024], BF16) for j in range(2)]
    ot = [b.sbuf(f"{pref}_ot{j}", [P, D], BF16) for j in range(2)]
    linv = b.sbuf(f"{pref}_li", [P, 16], F32)
    chq = b.chan("atq")
    cso = b.chan("ato", nslots=2)
    smm = b.sem("atc_smm")
    smk = b.sem("atc_smk")
    sex = b.sem("atc_sex")
    sav = b.sem("atc_sav")
    sov = b.sem("atc_sov")
    otoks = {}
    oevt = {}
    expt = {}
    ptuse = {}
    qtok = {}
    def oslot(h):
        return b.pb[h // 6], (h % 6) * 66
    for qs in range(NT):
        j = qs % 2
        if causal:
            kts = list(range(0, qs + 1))
        elif window_tiles is not None:
            kts = list(range(max(0, qs - window_tiles), qs + 1))
        else:
            kts = list(range(NT))
        groups = [kts[g:g + 4] for g in range(0, len(kts), 4)]
        qw = [qtok[qs - 2]] if qs >= 2 else []
        qv = q_buf
        qap = bass.AP(tensor=qv.tensor, offset=qv.offset + qs * P,
                      ap=[[b.T, P], [P * b.T, ND], [1, P]])
        qtok[qs] = chq.issue(qt[j].ap().rearrange("p (c t) -> p c t", c=ND),
                             qap, waits=qw)
        ow0 = [oevt[qs - 1]] if qs >= 1 else []
        if SIM_OPEN_BANKS:
            def ozero(e):
                ins = None
                for bk in range(3):
                    ins = e.matmul(b.pb[bk][:, 0:512], res.ones[0:1, 0:P],
                                   res.zrow[0:1, 0:512], start=True, stop=False)
                return ins
            toz = b.emit("pe", ozero, waits=ow0, inc=sav)
        else:
            first_bank = {0: True, 1: True, 2: True}
        lastuse = {}
        for gi, grp in enumerate(groups):
            for hp2 in range(8):
                for hh2 in (2 * hp2, 2 * hp2 + 1):
                    lastuse[hh2 // 6] = (gi, hh2, len(grp) - 1)
        units = [(gi, grp, hp) for gi, grp in enumerate(groups)
                 for hp in range(8)]

        def emit_scores(u):
            gi, grp, hp = units[u]
            h0, h1 = 2 * hp, 2 * hp + 1
            st_ = u % 2
            pp = pt[u % 2]
            ng = len(grp)
            tmms = []
            for hh, bi_ in ((h0, 0), (h1, 1)):
                sbank = b.pb[3 + st_ * 2 + bi_]
                ch_ = hh // 2
                base = (hh % 2) * 64
                tp = (base, 0)
                def scmm(e, hh=hh, sbank=sbank, grp=grp, ch_=ch_,
                         base=base, tp=tp, j=j):
                    ins = None
                    if SIM_OPEN_BANKS:
                        e.matmul(sbank[:, 0:512], res.ones[0:1, 0:P],
                                 res.zrow[0:1, 0:512], start=True, stop=False)
                    for idx, kt in enumerate(grp):
                        off = idx * P
                        ins = e.matmul(
                            sbank[:, off:off + P],
                            k_resT[ch_][base:base + 64, kt * P:(kt + 1) * P],
                            qt[j][base:base + 64, ch_ * P:(ch_ + 1) * P],
                            start=(not SIM_OPEN_BANKS and idx == 0),
                            stop=(idx == len(grp) - 1),
                            tile_position=tp)
                    return ins
                key = u - 2
                bw = [expt[(qs, key)]] if key >= 0 else \
                    ([expt[qs - 1]] if qs >= 1 else [])
                tmm = b.emit("pe", scmm, waits=[qtok[qs]] + bw, inc=smm)
                if causal and diagmask is not None and grp[-1] == qs:
                    idx = len(grp) - 1
                    off = idx * P
                    tmm = b.emit("dve", lambda e, sbank=sbank, off=off:
                        e.tensor_tensor(sbank[:, off:off + P],
                                        sbank[:, off:off + P], diagmask[:],
                                        ALU.add), waits=[tmm], inc=smk)
                if (not causal) and window_tiles is not None and diagmask:
                    for idx, kt in enumerate(grp):
                        dlt = qs - kt
                        if dlt in diagmask:
                            off = idx * P
                            tmm = b.emit("dve",
                                lambda e, sbank=sbank, off=off, dlt=dlt:
                                e.tensor_tensor(sbank[:, off:off + P],
                                                sbank[:, off:off + P],
                                                diagmask[dlt][:], ALU.add),
                                waits=[tmm], inc=smk)
                tmms.append(tmm)
            pw = [ptuse[pp.name]] if pp.name in ptuse else []
            sv = b.pbS.ap()
            src_ap = bass.AP(tensor=sv.tensor, offset=sv.offset + st_ * 1024,
                             ap=[[sv.ap[0][0], P], [512, 2], [1, ng * P]])
            ppv = pp.ap()
            dst_ap = bass.AP(tensor=ppv.tensor, offset=ppv.offset,
                             ap=[[ppv.ap[0][0], P], [512, 2], [1, ng * P]])
            tex = b.emit("act", lambda e, src_ap=src_ap, dst_ap=dst_ap, hp=hp:
                e.activation(dst_ap, src_ap, AF.Exp,
                             bias=mpair_neg[:, hp:hp + 1], scale=0.125),
                waits=tmms + pw, inc=sex)
            expt[(qs, u)] = tex

        def emit_av(u):
            gi, grp, hp = units[u]
            h0, h1 = 2 * hp, 2 * hp + 1
            pp = pt[u % 2]
            tex = expt[(qs, u)]
            tav = None
            for hh, poff in ((h0, 0), (h1, 512)):
                obank, ooff = oslot(hh)
                lu = lastuse[hh // 6]
                if SIM_OPEN_BANKS:
                    fb_open = False
                else:
                    fb_open = first_bank[hh // 6] and gi == 0
                    if gi == 0:
                        first_bank[hh // 6] = False
                def avmm(e, hh=hh, grp=grp, pp=pp, poff=poff, obank=obank,
                         ooff=ooff, gi=gi, lu=lu, fb_open=fb_open):
                    ins = None
                    for idx, kt in enumerate(grp):
                        ins = e.matmul(
                            obank[:, ooff:ooff + 65],
                            pp[:, poff + idx * P:poff + (idx + 1) * P],
                            v_res[:, kt * 1056 + 66 * hh:kt * 1056 + 66 * hh + 65],
                            start=(fb_open and idx == 0),
                            stop=(lu == (gi, hh, idx)))
                    return ins
                ow2 = [oevt[qs - 1]] if (qs >= 1 and not SIM_OPEN_BANKS and
                                         fb_open) else []
                tav = b.emit("pe", avmm, waits=[tex] + ow2, inc=sav)
                otoks[(qs, hh)] = tav
            ptuse[pp.name] = tav

        for u in range(len(units)):
            emit_scores(u)
            if u >= 1:
                emit_av(u - 1)
        emit_av(len(units) - 1)
        lw = [otoks[(qs, hh)] for hh in range(16)]
        tprev = [oevt[qs - 1]] if qs >= 1 else []
        tl = None
        for bk in range(3):
            nheads = 6 if bk < 2 else 4
            bank = b.pb[bk]
            lap = bass.AP(tensor=bank[:].tensor, offset=bank[:].offset + 64,
                          ap=[[bank[:].ap[0][0], P], [66, nheads]])
            tl = b.emit("dve", lambda e, lap=lap, bk=bk, nheads=nheads:
                e.reciprocal(linv[:, bk * 6:bk * 6 + nheads], lap),
                waits=lw + tprev + ([tl] if tl else []), inc=sov)
        tno = []
        for hh in range(16):
            obank, ooff = oslot(hh)
            tno.append(b.emit("dve", lambda e, hh=hh, obank=obank, ooff=ooff, j=j:
                e.tensor_scalar(ot[j][:, 64 * hh:64 * hh + 64],
                                obank[:, ooff:ooff + 64], linv[:, hh:hh + 1],
                                None, ALU.mult),
                waits=[tl] + ([oevt[qs - 2], otoks[qs - 2]] if qs >= 2 else []),
                inc=sov))
        oevt[qs] = tno[-1]
        otoks[qs] = cso.issue(o_buf[qs * P:(qs + 1) * P, :], ot[j][:],
                              waits=[tno[-1]])
        expt[qs] = tno[-1]
    b.barrier(toks=[otoks[i] for i in (NT - 2, NT - 1) if i in otoks])
    b.reset(m)


def bcast_row_tile(b, pref, row_hbm_1d, width, reps, dst):
    res = b.res
    m = b.mark()
    r1 = b.sbuf(pref + "_r1", [1, width], F32)
    r2 = b.sbuf(pref + "_r2", [1, width * reps], BF16)
    s = b.sem("brt_s")
    lds = b.sem("brt_dl")
    tok = b.dma("sp", r1[:], row_hbm_1d.rearrange("(a c) -> a c", a=1), inc=lds)
    src_ap = bass.AP(tensor=r1[:].tensor, offset=r1[:].offset,
                     ap=[[r1[:].ap[0][0], 1], [0, reps], [1, width]])
    t = b.emit("dve", lambda e: e.tensor_copy(
        r2.ap().rearrange("a (r w) -> a r w", r=reps), src_ap),
        waits=[tok], inc=s)
    tot = width * reps
    tm1 = b.emit("pe", lambda e: e.matmul(b.pb[7][:, 0:min(tot, 512)],
                                          res.ones[0:1, 0:P],
                                          r2[0:1, 0:min(tot, 512)],
                                          start=True, stop=True),
                 waits=[t], inc=s)
    t2 = b.emit("dve", lambda e: e.tensor_copy(dst[:, 0:min(tot, 512)],
                                               b.pb[7][:, 0:min(tot, 512)]),
                waits=[tm1], inc=s)
    if tot > 512:
        tm2 = b.emit("pe", lambda e: e.matmul(b.pb[6][:, 0:tot - 512],
                                              res.ones[0:1, 0:P],
                                              r2[0:1, 512:tot],
                                              start=True, stop=True),
                     waits=[t], inc=s)
        t2 = b.emit("dve", lambda e: e.tensor_copy(dst[:, 512:tot],
                                                   b.pb[6][:, 0:tot - 512]),
                    waits=[tm2], inc=s)
    b.barrier()
    b.reset(m)


def load_weight_chunks_cols(b, w_hbm, name, col0, sem, rows=D, cols=D):
    nr = rows // P
    tiles = []
    for c in range(nr):
        t = b.sbuf(f"{name}_{c}", [P, cols], BF16)
        b.dma("pool", t[:], w_hbm[c * P:(c + 1) * P, col0:col0 + cols], inc=sem)
        tiles.append(t)
    return tiles


def attn_softmax(b, ctx, pref, x_hbm, dst_hbm, wq, wk, wv, wo, bq, bk, bv, bo,
                 transform_q=None, transform_k=None, window_tiles=None,
                 causal=False, coef=1.0, diagmask=None, qkv_col0=None,
                 wlayout="do", out_stats=None):
    res = b.res
    n = ctx["n"]
    qtm = ctx["at_q"]
    ktm = ctx["at_k"]
    qfm = ctx["at_qf"]
    obuf = ctx["at_o"]
    am = b.mark()
    v_res = b.sbuf(f"at{n}_v", [P, b.NT * 1056], BF16)
    maxq = b.sbuf(f"at{n}_mq", [P, 16], F32)
    maxk = b.sbuf(f"at{n}_mk", [P, 16], F32)
    mhat_neg = b.sbuf(f"at{n}_mh", [P, 16], F32)
    mpair_neg = b.sbuf(f"at{n}_mp", [P, 8], F32)
    cq, ck, cv = qkv_col0 if qkv_col0 is not None else (0, 0, 0)
    m1 = b.mark()
    ldw = b.sem("atw_ld")
    wtls = {}
    btls = {}
    for (wt, bt, which) in ((wq, bq, "q"), (wk, bk, "k"), (wv, bv, "v")):
        col0 = {"q": cq, "k": ck, "v": cv}[which]
        if wlayout == "od":
            wtls[which] = transpose_weight(b, wt, f"at{n}w{which}")
        else:
            wtls[which] = load_weight_chunks_cols(b, wt, f"at{n}w{which}",
                                                  col0, ldw)
        btls[which] = bias_row(b, bt, f"at{n}b{which}", D, ldw) \
            if bt is not None else None
    b.emit("pe", lambda e, v=ldw.n: e.wait_ge(ldw.h, v))
    vz = b.sem("at_vz")
    tz = b.emit("dve", lambda e: e.memset(v_res[:], 0.0), inc=vz)
    vap = v_res[:]
    onesap = bass.AP(tensor=vap.tensor, offset=vap.offset + 64,
                     ap=[[vap.ap[0][0], P], [1056, b.NT], [66, 16]])
    tz = b.emit("dve", lambda e, onesap=onesap: e.memset(onesap, 1.0),
                waits=[tz], inc=vz)
    b.wait_toks("dve", [tz])
    epis = {"q": StoreEpi(b, f"at{n}Eq", qtm, dt=BF16),
            "k": StoreEpi(b, f"at{n}Ek", ktm, dt=BF16),
            "v": VEpi(b, f"at{n}Ev", v_res)}
    mm = b.sem("lin_mm")
    evtoks = {}
    NT_ = b.NT
    for i in range(NT_):
        for wi, which in enumerate(("q", "k", "v")):
            wtl = wtls[which]
            btl = btls[which]
            for nn in range(2):
                bank = b.pb[(i % 2) * 3 + wi][:, 0:512] if False else \
                    b.pb[(i % 2) * 4 + nn + (2 if wi == 1 else 0)] \
                    if False else b.pb[((i * 3 + wi) % 4) * 2 + nn]
                def mmf(e, i=i, nn=nn, wtl=wtl, btl=btl, bank=bank):
                    ins = None
                    for c in range(ND):
                        ins = e.matmul(bank[:], res.xT[c][:, i * P:(i + 1) * P],
                                       wtl[c][:, nn * 512:(nn + 1) * 512],
                                       start=(c == 0),
                                       stop=(btl is None and c == ND - 1))
                    if btl is not None:
                        ins = e.matmul(bank[:], res.ones[0:1, 0:P],
                                       btl[0:1, nn * 512:(nn + 1) * 512],
                                       start=False, stop=True)
                    return ins
                key = (i * 3 + wi - 4, nn)
                bw = [evtoks[key]] if key in evtoks else []
                mmtok = b.emit("pe", mmf, waits=bw, inc=mm)
                evtoks[(i * 3 + wi, nn)] = epis[which](i, nn, bank, mmtok)
    b.barrier(toks=epis["q"].finish() + epis["k"].finish())
    b.reset(m1)
    qk_fm_pass(b, qtm, f"at{n}Fq", transform_q, dst_hbm=qfm, norm_out=maxq)
    qk_fm_pass(b, ktm, f"at{n}Fk", transform_k, dst_sbufT=res.xT, norm_out=maxk)
    mhat_phase(b, f"at{n}M", maxq, maxk, mhat_neg, mpair_neg)
    attn_core(b, f"at{n}C", qfm, res.xT, v_res, mhat_neg, mpair_neg, obuf,
              window_tiles=window_tiles, causal=causal, diagmask=diagmask)
    b.reset(am)
    to_fm(b, obuf, res.xT, f"at{n}To", src_bf16=True)
    m5 = b.mark()
    ldw = b.sem("atw_ld")
    if wlayout == "od":
        wol = transpose_weight(b, wo, f"at{n}wo")
    else:
        wol = load_weight_chunks(b, wo, f"at{n}wo", D, D, ldw)
    bol = bias_row(b, bo, f"at{n}bo", D, ldw) if bo is not None else None
    b.emit("pe", lambda e, v=ldw.n: e.wait_ge(ldw.h, v))
    epi = ResidualEpi(b, f"at{n}Eo", x_hbm, dst_hbm, coef, stats=out_stats)
    linear_tm(b, f"at{n}Lo", res.xT, wol, bol, epi)
    b.barrier(toks=epi.finish())
    b.reset(m5)
    ctx["n"] += 1


def transpose_weight(b, w_hbm, name):
    res = b.res
    outs = [b.sbuf(f"{name}_{c}", [P, D], BF16) for c in range(ND)]
    m = b.mark()
    wt = [b.sbuf(f"{name}_w{j}", [P, D], BF16) for j in range(2)]
    ch = b.chan("twt_l", eng="pool")
    tr = b.sem("twt_tr")
    ev = b.sem("twt_ev")
    ttr, tev = {}, {}
    for oc in range(ND):
        j = oc % 2
        w = [ttr[(oc - 2, ND - 1)]] if oc >= 2 else []
        tok = ch.issue(wt[j][:], w_hbm[oc * P:(oc + 1) * P, :], waits=w)
        for c in range(ND):
            bank = b.pb[4 + c % 4].ap().bitcast(BF16)
            prev = (oc, c - 4) if c >= 4 else ((oc - 1, c + 4) if oc >= 1 else None)
            bw = [tev[prev]] if prev else []
            ttr[(oc, c)] = b.emit("pe", lambda e, j=j, c=c, bank=bank: e.transpose(
                bank[:, 0:P], wt[j][:, c * P:(c + 1) * P], res.ident[:]),
                waits=[tok] + bw, inc=tr)
            tev[(oc, c)] = b.emit("dve", lambda e, c=c, oc=oc, bank=bank:
                e.tensor_copy(outs[c][:, oc * P:(oc + 1) * P], bank[:, 0:P]),
                waits=[ttr[(oc, c)]], inc=ev)
    b.barrier()
    b.reset(m)
    return outs


class VPlainEpi:

    def __init__(self, b, pref, v_res):
        self.b = b
        self.v_res = v_res
        self.ev = b.sem("epi_ev")

    def __call__(self, i, n, bank, mmtok):
        t = self.b.emit("dve", lambda e, bank=bank, i=i, n=n: e.tensor_copy(
            self.v_res[:, i * D + n * 512:i * D + (n + 1) * 512], bank[:]),
            waits=[mmtok], inc=self.ev)
        return t

    def finish(self):
        return []


def retention_consts(b):
    g = 0.96875
    outs = []
    kp = np.arange(P)[:, None]
    qf = np.arange(P)[None, :]
    for d in range(9):
        expo = qf + 128 * d - kp
        v = np.where(expo >= 0, np.power(np.float64(g), expo), 0.0)
        outs.append(b.const(f"c_ret{d}", f32c(v)))
    return outs


def sublayer_l3attn(b, ctx, x_hbm, dst_hbm, out_stats=None):
    n = ctx["n"]
    res = b.res
    NT = b.NT
    qtm, ktm, qfm = ctx["at_q"], ctx["at_k"], ctx["at_qf"]
    am = b.mark()
    v_res = b.sbuf(f"r3v{n}", [P, NT * D], BF16)
    dct = [b.sbuf(f"r3d{n}_{d}", [P, P], F32) for d in range(9)]
    ldd = b.sem("r3_ldd")
    for d in range(9):
        b.dma("sp", dct[d][:], ctx["ret_consts"][d], inc=ldd)
    b.emit("dve", lambda e, v=ldd.n: e.wait_ge(ldd.h, v))
    for which, wname, bname in (("q", "L3_attn_Wq", "L3_attn_bq"),
                                ("k", "L3_attn_Wk", "L3_attn_bk"),
                                ("v", "L3_attn_Wv", None)):
        m1 = b.mark()
        wT = transpose_weight(b, ctx[wname], f"r3w{n}{which}")
        if bname is not None:
            ldb = b.sem("r3_bl")
            bt = bias_row(b, ctx[bname], f"r3b{n}{which}", D, ldb)
            b.emit("pe", lambda e, v=ldb.n: e.wait_ge(ldb.h, v))
        else:
            bt = None
        if which == "q":
            epi = StoreEpi(b, f"r3Eq{n}", qtm, dt=BF16)
        elif which == "k":
            epi = StoreEpi(b, f"r3Ek{n}", ktm, dt=BF16)
        else:
            epi = VPlainEpi(b, f"r3Ev{n}", v_res)
        linear_tm(b, f"r3L{n}{which}", res.xT, wT, bt, epi)
        b.barrier(toks=epi.finish())
        b.reset(m1)
    qk_fm_pass(b, qtm, f"r3Fq{n}", None, dst_hbm=qfm)
    qk_fm_pass(b, ktm, f"r3Fk{n}", None, dst_sbufT=res.xT)
    m2 = b.mark()
    qt = [b.sbuf(f"r3qt{n}_{j}", [P, D], BF16) for j in range(2)]
    pt = [b.sbuf(f"r3pt{n}_{j}", [P, 1280], BF16) for j in range(2)]
    xa = [b.sbuf(f"r3xa{n}_{j}", [P, D], F32) for j in range(2)]
    xn = [b.sbuf(f"r3xn{n}_{j}", [P, D], F32) for j in range(2)]
    chq = b.chan("r3_q")
    chx = b.chan("r3_x")
    cso = b.chan("r3_o", nslots=2)
    smm = b.sem("r3_smm")
    sdk = b.sem("r3_sdk")
    sav = b.sem("r3_sav")
    sev = b.sem("r3_sev")
    qtok, xtok, stok, devt, oevt, statoks = {}, {}, {}, {}, {}, {}
    for qs in range(NT):
        j = qs % 2
        kts = list(range(max(0, qs - 8), qs + 1))
        qw = [qtok[qs - 2]] if qs >= 2 else []
        qv = qfm
        qap = bass.AP(tensor=qv.tensor, offset=qv.offset + qs * P,
                      ap=[[b.T, P], [P * b.T, ND], [1, P]])
        qtok[qs] = chq.issue(qt[j].ap().rearrange("p (c t) -> p c t", c=ND),
                             qap, waits=qw)
        xw = [oevt[qs - 2]] if qs >= 2 else []
        xtok[qs] = chx.issue(xa[j][:], x_hbm[qs * P:(qs + 1) * P, :], waits=xw)
        nk = len(kts)
        nbk = (nk + 3) // 4
        def scm(e, kts=kts, j=j, qs=qs):
            ins = None
            for bb in range((len(kts) + 3) // 4):
                if SIM_OPEN_BANKS:
                    e.matmul(b.pb[3 + bb][:, 0:512], res.ones[0:1, 0:P],
                             res.zrow[0:1, 0:512], start=True, stop=False)
            for idx, kt in enumerate(kts):
                bank = b.pb[3 + idx // 4]
                off = (idx % 4) * P
                for c in range(ND):
                    ins = e.matmul(bank[:, off:off + P],
                                   res.xT[c][:, kt * P:(kt + 1) * P],
                                   qt[j][:, c * P:(c + 1) * P],
                                   start=(not SIM_OPEN_BANKS and c == 0 and
                                          idx % 4 == 0),
                                   stop=(c == ND - 1 and (idx % 4 == 3 or
                                                          idx == len(kts) - 1)))
            return ins
        bw = [devt[qs - 1]] if qs >= 1 else []
        tmm = b.emit("pe", scm, waits=[qtok[qs]] + bw, inc=smm)
        tdk = None
        for idx, kt in enumerate(kts):
            bank = b.pb[3 + idx // 4]
            off = (idx % 4) * P
            dd = qs - kt
            pw = [devt[qs - 1]] if (qs >= 1 and idx == 0) else []
            tdk = b.emit("dve", lambda e, bank=bank, off=off, dd=dd, idx=idx, j=j:
                e.tensor_mul(pt[j][:, idx * P:(idx + 1) * P],
                             bank[:, off:off + P], dct[dd][:]),
                waits=[tmm] + pw + ([tdk] if tdk else []), inc=sdk)
        devt[qs] = tdk
        def avm(e, kts=kts, j=j):
            ins = None
            for bb in range(2):
                if SIM_OPEN_BANKS:
                    e.matmul(b.pb[6 + bb][:, 0:512], res.ones[0:1, 0:P],
                             res.zrow[0:1, 0:512], start=True, stop=False)
            for idx, kt in enumerate(kts):
                for nn in range(2):
                    ins = e.matmul(b.pb[6 + nn][:, 0:512],
                                   pt[j][:, idx * P:(idx + 1) * P],
                                   v_res[:, kt * D + nn * 512:kt * D + (nn + 1) * 512],
                                   start=(not SIM_OPEN_BANKS and idx == 0),
                                   stop=(idx == len(kts) - 1))
            return ins
        aw = [oevt[qs - 1]] if qs >= 1 else []
        tav = b.emit("pe", avm, waits=[tdk] + aw, inc=sav)
        tn = None
        for nn in range(2):
            sw = [stok[qs - 2]] if qs >= 2 else []
            if qs >= 2 and out_stats is not None:
                sw = sw + [statoks[qs - 2]]
            tn = b.emit("dve", lambda e, j=j, nn=nn: e.tensor_add(
                xn[j][:, nn * 512:(nn + 1) * 512], b.pb[6 + nn][:],
                xa[j][:, nn * 512:(nn + 1) * 512]),
                waits=[tav, xtok[qs]] + sw + ([tn] if tn else []), inc=sev)
        oevt[qs] = tn
        stok[qs] = cso.issue(dst_hbm[qs * P:(qs + 1) * P, :], xn[j][:],
                             waits=[tn])
        if out_stats is not None:
            sums, sumsq = out_stats
            trd = b.emit("dve", lambda e, j=j, qs=qs: e.reduce_sum(
                sums[:, qs:qs + 1], xn[j][:], axis=AX.X), waits=[tn],
                inc=b.sem("epi_dnd"))
            statoks[qs] = b.emit("act", lambda e, j=j, qs=qs: e.activation(
                xn[j][:], xn[j][:], AF.Square, accum_out=sumsq[:, qs:qs + 1]),
                waits=[trd, stok[qs]] + ([statoks[qs - 1]] if qs >= 1 else []),
                inc=b.sem("epi_dna"))
    b.barrier(toks=[stok[i] for i in (NT - 2, NT - 1) if i in stok])
    b.reset(m2)
    b.reset(am)
    ctx["n"] += 1


def ln_materialize(b, src_hbm, dst_hbm, pref):
    res = b.res
    NT = b.NT
    m = b.mark()
    u = b.uid()
    xa = [b.sbuf(f"lm{u}_x{j}", [P, D], F32) for j in range(2)]
    xn = [b.sbuf(f"lm{u}_n{j}", [P, D], F32) for j in range(2)]
    ch = b.chan("ld")
    cst = b.chan("est", nslots=2)
    ap_ = b.sem("lnm_ap")
    ts, tt = {}, {}
    for i in range(NT):
        j = i % 2
        w = [tt[i - 2]] if i >= 2 else []
        tok = ch.issue(xa[j][:], src_hbm[i * P:(i + 1) * P, :], waits=w)
        sw = [ts[i - 2]] if i >= 2 else []
        tt[i] = b.emit("dve", lambda e, j=j, i=i: e.tensor_scalar(
            xn[j][:], xa[j][:], res.mu[:, i:i + 1], res.rstd[:, i:i + 1],
            ALU.subtract, ALU.mult), waits=[tok] + sw, inc=ap_)
        ts[i] = cst.issue(dst_hbm[i * P:(i + 1) * P, :], xn[j][:], waits=[tt[i]])
    b.barrier(toks=[ts[i] for i in (NT - 2, NT - 1) if i in ts])
    b.reset(m)


def sublayer_l3ffn(b, ctx, x_hbm, dst_hbm, out_stats=None):
    n = ctx["n"]
    res = b.res
    NT = b.NT
    ntt = b.T // 512
    sub_m = b.mark()
    g1 = col_layout(b, ctx["L3_ffn_ln1_g"], f"f3g1{n}")
    b1c = col_layout(b, ctx["L3_ffn_ln1_b"], f"f3b1{n}")
    g2 = col_layout(b, ctx["L3_ffn_ln2_g"], f"f3g2{n}")
    b2c = col_layout(b, ctx["L3_ffn_ln2_b"], f"f3b2{n}")
    halfmask = b.sbuf(f"f3hm{n}", [P, NT * 8], F32)
    x2 = ctx["f3_x2"]
    xh2 = ctx["f3_xh2"]
    acc = ctx["f3_acc"]
    xln = ctx["f3_xln"]
    ln_materialize(b, x_hbm, xln, f"f3L{n}")
    ln_stats(b, xln, f"f3sl{n}")
    to_fm(b, xln, res.xT, f"f3A{n}", mu=res.mu, rstd=res.rstd,
          evac=("affine", g1, b1c))
    attn_softmax(b, ctx, f"f3T{n}", xln, x2, ctx["L3_ffn_wq"],
                 ctx["L3_ffn_wk"], ctx["L3_ffn_wv"], ctx["L3_ffn_wo"],
                 ctx["L3_ffn_bq"], ctx["L3_ffn_bk"], ctx["L3_ffn_bv"],
                 ctx["L3_ffn_bo"], coef=1.0)
    ln_stats(b, x2, f"f3s{n}")
    qk_fm_pass(b, x2, f"f3F{n}", ("ln", res.mu, res.rstd), dst_hbm=xh2,
               post=("affine", g2, b2c), dst_hbm_lo=ctx["f3_xh2lo"])
    mg = b.mark()
    gw = [b.sbuf(f"f3gw{n}_{c}", [P, 8], BF16) for c in range(ND)]
    gwlo = [b.sbuf(f"f3gl{n}_{c}", [P, 8], BF16) for c in range(ND)]
    gwf = b.sbuf(f"f3gf{n}", [P, 8], F32)
    gb = b.sbuf(f"f3gb{n}", [1, 8], BF16)
    xt = [b.sbuf(f"f3xt{n}_{j}", [P, D], BF16) for j in range(2)]
    xtl = [b.sbuf(f"f3xl{n}_{j}", [P, D], BF16) for j in range(2)]
    lg = b.sbuf(f"f3lg{n}", [P, 8], F32)
    mx = b.sbuf(f"f3mx{n}", [P, 1], F32)
    ldg = b.sem("f3_ldg")
    sgp = b.sem("f3_sgp")
    chgw = b.chan("f3_gw", eng="pool", nslots=4)
    chgf = b.chan("f3_gf", eng="sp", nslots=1)
    sgp0 = sgp.n
    for c in range(ND):
        tga = chgw.issue(gw[c][:], ctx["L3_ffn_gate_w"][c * P:(c + 1) * P, :])
        tgb = chgf.issue(gwf[:], ctx["L3_ffn_gate_w"][c * P:(c + 1) * P, :],
                         waits=[(sgp, sgp0 + 2 * c)])
        t1 = b.emit("dve", lambda e, c=c: e.tensor_sub(gwf[:], gwf[:], gw[c][:]),
                    waits=[tga, tgb], inc=sgp)
        b.emit("dve", lambda e, c=c: e.tensor_copy(gwlo[c][:], gwf[:]),
               waits=[t1], inc=sgp)
    b.dma("pool", gb[:], ctx["L3_ffn_gate_b"].rearrange("(a c) -> a c", a=1),
          inc=ldg)
    ldgv = ldg.n
    sgpv = sgp.n
    gw_toks = [tga]
    chg = b.chan("f3_gq")
    sg_ = b.sem("f3_sg")
    sgd = b.sem("f3_sgd")
    gtok, gevt = {}, {}
    for i in range(NT):
        j = i % 2
        qv = xh2
        qap = bass.AP(tensor=qv.tensor, offset=qv.offset + i * P,
                      ap=[[b.T, P], [P * b.T, ND], [1, P]])
        w = [gevt[i - 2]] if i >= 2 else []
        gtok[i] = chg.issue(xt[j].ap().rearrange("p (c t) -> p c t", c=ND),
                            qap, waits=w)
        qvl = ctx["f3_xh2lo"]
        qapl = bass.AP(tensor=qvl.tensor, offset=qvl.offset + i * P,
                       ap=[[b.T, P], [P * b.T, ND], [1, P]])
        gtok2 = chg.issue(xtl[j].ap().rearrange("p (c t) -> p c t", c=ND),
                          qapl, waits=w)
        bank = b.pb[i % 2]
        def gmm(e, j=j, bank=bank):
            for c in range(ND):
                e.matmul(bank[:, 0:8], xt[j][:, c * P:(c + 1) * P], gw[c][:],
                         start=(c == 0), stop=False)
                e.matmul(bank[:, 0:8], xt[j][:, c * P:(c + 1) * P], gwlo[c][:],
                         start=False, stop=False)
                e.matmul(bank[:, 0:8], xtl[j][:, c * P:(c + 1) * P], gw[c][:],
                         start=False, stop=False)
            return e.matmul(bank[:, 0:8], res.ones[0:1, 0:P], gb[0:1, :],
                            start=False, stop=True)
        bw = [gevt[i - 2]] if i >= 2 else []
        t1 = b.emit("pe", gmm, waits=[(ldg, ldgv), (sgp, sgpv), gtok[i], gtok2]
                    + bw + gw_toks, inc=sg_)
        t2 = b.emit("dve", lambda e, bank=bank: e.tensor_copy(lg[:], bank[:, 0:8]),
                    waits=[t1] + ([gevt[i - 1]] if i >= 1 else []), inc=sgd)
        t3 = b.emit("dve", lambda e: e.reduce_max(mx[:], lg[:], axis=AX.X),
                    waits=[t2], inc=sgd)
        t4 = b.emit("dve", lambda e, i=i: e.tensor_scalar(
            halfmask[:, i * 8:(i + 1) * 8], lg[:], mx[:, 0:1], None,
            ALU.is_equal), waits=[t3], inc=sgd)
        gevt[i] = b.emit("dve", lambda e, i=i: e.tensor_scalar(
            halfmask[:, i * 8:(i + 1) * 8], halfmask[:, i * 8:(i + 1) * 8],
            0.5, None, ALU.mult), waits=[t4], inc=sgd)
    b.barrier()
    b.reset(mg)
    mm_ = b.mark()
    w2t = [b.sbuf(f"f3w2{n}_{fc}", [P, D], BF16) for fc in range(32)]
    midt = [b.sbuf(f"f3md{n}_{fc}", [P, 512], BF16) for fc in range(32)]
    b1r = b.sbuf(f"f3b1r{n}", [1, 4096], BF16)
    b2r = b.sbuf(f"f3b2r{n}", [1, D], BF16)
    xh2t = [b.sbuf(f"f3h2{n}_{jj}", [P, ND * 512], BF16) for jj in range(2)]
    xac = [b.sbuf(f"f3xa{n}_{j}", [P, D], F32) for j in range(2)]
    xno = [b.sbuf(f"f3xn{n}_{j}", [P, D], F32) for j in range(2)]
    w1t = res.xT
    chw = b.chan("f3_w", eng="pool", nslots=8)
    chh = b.chan("f3_h")
    chx = b.chan("f3_xl")
    cst = b.chan("f3_st", nslots=2)
    smm = b.sem("f3_smm")
    sev = b.sem("f3_sev")
    smid = b.sem("f3_smid")
    prev_store = {}
    statoks = {}
    prev_mid_mm = None
    prev_w1_use = None
    prev_w2_use = None
    prev_acc_tok = {}
    for e in range(8):
        w1toks = []
        for c in range(ND):
            w = [prev_w1_use] if (e > 0 and c == 0) else []
            w1toks.append(chw.issue(w1t[c][:, 0:4096],
                                    ctx["L3_ffn_w1"][e][c * P:(c + 1) * P, :],
                                    waits=w))
        w2toks = []
        for fc in range(32):
            w = [prev_w2_use] if (e > 0 and fc == 0) else []
            w2toks.append(chw.issue(w2t[fc][:],
                                    ctx["L3_ffn_w2"][e][fc * P:(fc + 1) * P, :],
                                    waits=w))
        tb1 = chw.issue(b1r[:], ctx["L3_ffn_b1"][e].rearrange("(a c) -> a c", a=1),
                        waits=[prev_w1_use] if e > 0 else [])
        tb2 = chw.issue(b2r[:], ctx["L3_ffn_b2"][e].rearrange("(a c) -> a c", a=1),
                        waits=[prev_w1_use] if e > 0 else [])
        wall = w1toks + w2toks + [tb1, tb2]
        htok, mide, ymm = {}, {}, {}
        for tt in range(ntt):
            j = tt % 2
            hw = [mide.get((tt - 2, 31))] if tt >= 2 else \
                ([prev_mid_mm] if e > 0 else [])
            hw = [t for t in hw if t]
            hv = xh2
            hap = bass.AP(tensor=hv.tensor, offset=hv.offset + tt * 512,
                          ap=[[b.T, P], [P * b.T, ND], [1, 512]])
            htok[tt] = chh.issue(
                xh2t[j].ap().rearrange("p (c t) -> p c t", c=ND), hap, waits=hw)
            hts = [htok[tt]]
            for fc in range(32):
                bank = b.pb[fc % 2]
                def mmid(e2, fc=fc, j=j, bank=bank):
                    for dc in range(ND):
                        e2.matmul(bank[:], w1t[dc][:, fc * P:(fc + 1) * P],
                                  xh2t[j][:, dc * 512:(dc + 1) * 512],
                                  start=(dc == 0), stop=False)
                    return e2.matmul(bank[:], b1r[0:1, fc * P:(fc + 1) * P],
                                     res.onesw[0:1, 0:512], start=False, stop=True)
                k = fc
                bw = [mide[(tt, fc - 2)]] if fc >= 2 else \
                    ([mide[(tt - 1, 31)]] if tt >= 1 else [])
                if tt == 0 and fc == 0:
                    wts = hts + w1toks + [tb1]
                elif fc == 0:
                    wts = hts
                else:
                    wts = []
                mwy = [ymm[tt - 1]] if (fc == 0 and tt >= 1) else \
                    ([prev_mid_mm] if (e > 0 and tt == 0 and fc == 0) else [])
                t1 = b.emit("pe", mmid, waits=wts + bw + mwy, inc=smm)
                if tt == 0 and fc == 0:
                    prev_w1_use_new = t1
                mide[(tt, fc)] = b.emit("act", lambda e2, fc=fc, bank=bank:
                    e2.activation(midt[fc][:], bank[:], AF.Relu),
                    waits=[t1], inc=smid)
            for tsub in range(4):
                ti = tt * 4 + tsub
                jj = ti % 2
                aw = [prev_acc_tok.get((ti - 2, 1))]
                aw = [t for t in aw if t]
                srcb = x2 if e == 0 else acc
                atok = chx.issue(xac[jj][:], srcb[ti * P:(ti + 1) * P, :],
                                 waits=aw)
                if e == 0:
                    atok2 = chx.issue(xno[jj][:], x_hbm[ti * P:(ti + 1) * P, :],
                                      waits=[prev_store.get(ti - 2)]
                                      if (ti - 2) in prev_store else [])
                    atok3 = b.emit("dve", lambda e2, jj=jj: e2.scalar_tensor_tensor(
                        xac[jj][:], xac[jj][:], 0.5, xno[jj][:],
                        ALU.mult, ALU.add), waits=[atok, atok2], inc=sev)
                    atok = atok3
                for nh in range(2):
                    bank = b.pb[2 + (tsub % 2) * 2 + nh]
                    def ymf(e2, tsub=tsub, nh=nh, bank=bank):
                        for fc in range(32):
                            e2.matmul(bank[:], midt[fc][:, tsub * P:(tsub + 1) * P],
                                      w2t[fc][:, nh * 512:(nh + 1) * 512],
                                      start=(fc == 0), stop=False)
                        return e2.matmul(bank[:], res.ones[0:1, 0:P],
                                         b2r[0:1, nh * 512:(nh + 1) * 512],
                                         start=False, stop=True)
                    yw = [mide[(tt, 31)]]
                    if tt == 0 and tsub == 0 and nh == 0:
                        yw += w2toks + [tb2]
                    bw = [prev_acc_tok.get((ti - 2, nh))]
                    bw = [t for t in bw if t]
                    t1 = b.emit("pe", ymf, waits=yw + bw, inc=smm)
                    ymm[tt] = t1
                    sw = [prev_store[ti - 2]] if ti >= 2 and (ti - 2) in \
                        prev_store else []
                    if e == 7 and out_stats is not None and (ti - 2) in statoks:
                        sw = sw + [statoks[ti - 2]]
                    t2 = b.emit("dve", lambda e2, jj=jj, nh=nh, bank=bank,
                                ti=ti, e=e:
                        e2.scalar_tensor_tensor(
                            xno[jj][:, nh * 512:(nh + 1) * 512], bank[:],
                            halfmask[:, ti * 8 + e:ti * 8 + e + 1],
                            xac[jj][:, nh * 512:(nh + 1) * 512],
                            ALU.mult, ALU.add),
                        waits=[t1, atok] + sw, inc=sev)
                    prev_acc_tok[(ti, nh)] = t2

                dstb = dst_hbm if e == 7 else acc
                prev_acc_tok[ti] = prev_acc_tok[(ti, 1)]
                prev_store[ti] = cst.issue(dstb[ti * P:(ti + 1) * P, :],
                                           xno[jj][:],
                                           waits=[prev_acc_tok[(ti, 0)],
                                                  prev_acc_tok[(ti, 1)]])
                if e == 7 and out_stats is not None:
                    sums, sumsq = out_stats
                    trd = b.emit("dve", lambda e2, jj=jj, ti=ti: e2.reduce_sum(
                        sums[:, ti:ti + 1], xno[jj][:], axis=AX.X),
                        waits=[prev_acc_tok[(ti, 0)], prev_acc_tok[(ti, 1)]],
                        inc=b.sem("epi_dnd"))
                    statoks[ti] = b.emit("act", lambda e2, jj=jj, ti=ti:
                        e2.activation(xno[jj][:], xno[jj][:], AF.Square,
                                      accum_out=sumsq[:, ti:ti + 1]),
                        waits=[trd, prev_store[ti]] +
                              ([statoks[ti - 1]] if (ti - 1) in statoks else []),
                        inc=b.sem("epi_dna"))
        prev_mid_mm = ymm[ntt - 1]
        prev_w1_use = prev_mid_mm
        prev_w2_use = prev_mid_mm
        b.barrier(toks=[prev_store[t] for t in (NT * 4 // 4 - 2, NT * 4 // 4 - 1)
                        if t in prev_store])
    b.barrier(toks=[prev_store[t] for t in (NT * 4 // 4 - 2, NT * 4 // 4 - 1)
                    if t in prev_store])
    b.reset(mm_)
    b.reset(sub_m)
    ctx["n"] += 1



def sublayer_l0ffn(b, ctx, x_hbm, dst_hbm, out_stats=None):
    n = ctx["n"]
    res = b.res
    ntt = b.T // 512
    sub_m = b.mark()
    maskPP = b.sbuf(f"l0mk{n}", [P, 8], F32)
    m1 = b.mark()
    ld = b.sem("l0_ld")
    loss_parts = b.sbuf(f"l0lp{n}", [P, 512], F32)
    lossPP = b.sbuf(f"l0le{n}", [P, 8], F32)
    mask8b = b.sbuf(f"l0m8{n}", [1, 8], BF16)
    he = [b.sbuf(f"l0h{n}_{hc}", [P, b.T], BF16) for hc in range(2)]
    sq_scrap = b.sbuf(f"l0sqs{n}", [P, 512], F32)
    diff = [b.sbuf(f"l0df{n}_{j}", [P, 512], F32) for j in range(2)]
    wd1 = [b.sbuf(f"l0w1{n}_{c}", [P, 256], BF16) for c in range(ND)]
    wd2 = [b.sbuf(f"l0w2{n}_{hc}", [P, D], BF16) for hc in range(2)]
    zt = b.emit("dve", lambda e: e.memset(loss_parts[:], 0.0), inc=b.sem("l0_z"))
    mmh = b.sem("l0_mmh")
    evh = b.sem("l0_evh")
    mmr = b.sem("l0_mmr")
    evr = b.sem("l0_evr")
    sqs = b.sem("l0_sqs")
    prev_sq_last = None
    prev_hev = None
    prev_dv = {}
    prev_sq = {}
    for e in range(8):
        wl1 = [prev_hev] if e > 0 else []
        wl2 = [prev_sq_last] if e > 0 else []
        for c in range(ND):
            b.dma("pool", wd1[c][:], ctx["L0_ffn_Wd1"][e][c * P:(c + 1) * P, :],
                  waits=wl1 if c == 0 else [], inc=ld)
        for hc in range(2):
            b.dma("pool", wd2[hc][:], ctx["L0_ffn_Wd2"][e][hc * P:(hc + 1) * P, :],
                  waits=wl2 if hc == 0 else [], inc=ld)
        bt1 = bias_row(b, ctx["L0_ffn_bd1"][e], f"l0b1{n}_{e}", 256, ld)
        bt2 = bias_row(b, ctx["L0_ffn_bd2"][e], f"l0b2{n}_{e}", D, ld)
        ldv = ld.n
        hev = {}
        for hc in range(2):
            for tt in range(ntt):
                k = hc * ntt + tt
                bank = b.pb[k % 2]
                def mf(e2, hc=hc, tt=tt, bank=bank, bt1=bt1):
                    for c in range(ND):
                        e2.matmul(bank[:], wd1[c][:, hc * P:(hc + 1) * P],
                                  res.xT[c][:, tt * 512:(tt + 1) * 512],
                                  start=(c == 0), stop=False)
                    return e2.matmul(bank[:], bt1[0:1, hc * P:(hc + 1) * P],
                                     res.onesw[0:1, 0:512], start=False, stop=True)
                bw = [hev[k - 2]] if k >= 2 else \
                    ([prev_hev] if e > 0 else [])
                t1 = b.emit("pe", mf, waits=[(ld, ldv)] + bw, inc=mmh)
                hev[k] = b.emit("act", lambda e2, hc=hc, tt=tt, bank=bank:
                    e2.activation(he[hc][:, tt * 512:(tt + 1) * 512], bank[:],
                                  AF.Relu), waits=[t1], inc=evh)
        prev_hev = hev[2 * ntt - 1]
        for c in range(ND):
            for tt in range(ntt):
                k = c * ntt + tt
                bank = b.pb[2 + k % 2]
                def mf2(e2, c=c, tt=tt, bank=bank, bt2=bt2):
                    for hc in range(2):
                        e2.matmul(bank[:], wd2[hc][:, c * P:(c + 1) * P],
                                  he[hc][:, tt * 512:(tt + 1) * 512],
                                  start=(hc == 0), stop=False)
                    return e2.matmul(bank[:], bt2[0:1, c * P:(c + 1) * P],
                                     res.onesw[0:1, 0:512], start=False, stop=True)
                bw = [prev_dv[divmod(k - 2, ntt)]] if k >= 2 else \
                    ([prev_dv[(ND - 1, ntt - 1)]] if e > 0 else [])
                t1 = b.emit("pe", mf2,
                            waits=[(ld, ldv), hev[2 * ntt - 1]] + bw, inc=mmr)
                j = k % 2
                dwv = [prev_sq[divmod(k - 2, ntt)]] if k >= 2 else \
                    ([prev_sq[(ND - 1, ntt - 1)]] if e > 0 else [])
                t2 = b.emit("dve", lambda e2, c=c, tt=tt, j=j, bank=bank:
                    e2.scalar_tensor_tensor(
                        diff[j][:], bank[:], 1.0,
                        res.xT[c][:, tt * 512:(tt + 1) * 512],
                        ALU.mult, ALU.subtract),
                    waits=[t1] + dwv, inc=evr)
                prev_dv[(c, tt)] = t2
                col = e * 64 + k
                sqw = [prev_sq[divmod(k - 1, ntt)]] if k >= 1 else \
                    ([prev_sq[(ND - 1, ntt - 1)]] if e > 0 else [zt])
                prev_sq[(c, tt)] = b.emit("act", lambda e2, j=j, col=col:
                    e2.activation(sq_scrap[:], diff[j][:], AF.Square,
                                  accum_out=loss_parts[:, col:col + 1]),
                    waits=[t2] + sqw, inc=sqs)
        prev_sq_last = prev_sq[(ND - 1, ntt - 1)]
    b.barrier()
    s2 = b.sem("l0_s2")
    lt8 = b.sbuf(f"l0lt{n}", [8, P], F32)
    le8 = b.sbuf(f"l0l8{n}", [8, 1], F32)
    lrow = b.sbuf(f"l0lr{n}", [1, 8], F32)
    lmin = b.sbuf(f"l0lm{n}", [1, 1], F32)
    t = None
    for e in range(8):
        t = b.emit("dve", lambda e2, e=e: e2.reduce_sum(
            lossPP[:, e:e + 1], loss_parts[:, e * 64:(e + 1) * 64], axis=AX.X),
            waits=[t] if t else [], inc=s2)
    t = b.emit("pe", lambda e2: e2.transpose(b.pb[4][0:8, 0:P], lossPP[:],
                                             res.identf[:]), waits=[t], inc=s2)
    t = b.emit("dve", lambda e2: e2.tensor_copy(lt8[:], b.pb[4][0:8, 0:P]),
               waits=[t], inc=s2)
    t = b.emit("dve", lambda e2: e2.reduce_sum(le8[:], lt8[:], axis=AX.X),
               waits=[t], inc=s2)
    t = b.emit("pe", lambda e2: e2.transpose(b.pb[5][0:1, 0:8], le8[:],
                                             res.identf[0:8, 0:8]),
               waits=[t], inc=s2)
    t = b.emit("dve", lambda e2: e2.tensor_copy(lrow[:], b.pb[5][0:1, 0:8]),
               waits=[t], inc=s2)
    t = b.emit("dve", lambda e2: e2.tensor_reduce(lmin[:], lrow[:], axis=AX.X,
                                                  op=ALU.min), waits=[t], inc=s2)
    t = b.emit("dve", lambda e2: e2.tensor_scalar(lrow[:], lrow[:],
                                                  lmin[0:1, 0:1], None,
                                                  ALU.is_equal), waits=[t], inc=s2)
    t = b.emit("dve", lambda e2: e2.tensor_copy(mask8b[:], lrow[:]),
               waits=[t], inc=s2)
    t = b.emit("pe", lambda e2: e2.matmul(b.pb[6][:, 0:8], res.ones[0:1, 0:P],
                                          mask8b[0:1, :], start=True, stop=True),
               waits=[t], inc=s2)
    t = b.emit("dve", lambda e2: e2.tensor_copy(maskPP[:], b.pb[6][:, 0:8]),
               waits=[t], inc=s2)
    b.barrier()
    b.reset(m1)
    m3 = b.mark()
    wdc = [b.sbuf(f"l0wdc{n}_{c}", [P, 64], BF16) for c in range(ND)]
    wuc = b.sbuf(f"l0wuc{n}", [64, D], BF16)
    bcomb = b.sbuf(f"l0bc{n}", [1, 64 + D], BF16)
    acc = b.sbuf(f"l0acc{n}", [P, 64], F32)
    accu = b.sbuf(f"l0accu{n}", [64, D], F32)
    accb = b.sbuf(f"l0accb{n}", [1, 64 + D], F32)
    wtile = [b.sbuf(f"l0wt{n}_{j}", [P, 64], BF16) for j in range(2)]
    utile = [b.sbuf(f"l0ut{n}_{j}", [64, D], BF16) for j in range(2)]
    btile = [b.sbuf(f"l0btl{n}_{j}", [1, 64 + D], BF16) for j in range(2)]
    bbase = b.sbuf(f"l0bb{n}", [1, D], BF16)
    ch3 = b.chan("l0_c3", eng="pool")
    s3 = b.sem("l0_s3")
    ld3 = b.sem("l0_ld3")
    tcast = {}
    for c in range(ND):
        t = None
        for e in range(8):
            j = e % 2
            lw = [tcast[c - 1]] if (e < 2 and c >= 1) else []
            tok = ch3.issue(wtile[j][:],
                            ctx["L0_ffn_Wa_down"][e][c * P:(c + 1) * P, :],
                            waits=([t] if e >= 2 else []) + lw)
            if e == 0:
                t = b.emit("dve", lambda e2, j=j, e=e: e2.tensor_scalar(
                    acc[:], wtile[j][:], maskPP[:, e:e + 1], None, ALU.mult),
                    waits=[tok] + ([tcast[c - 1]] if c >= 1 else []), inc=s3)
            else:
                t = b.emit("dve", lambda e2, j=j, e=e: e2.scalar_tensor_tensor(
                    acc[:], wtile[j][:], maskPP[:, e:e + 1], acc[:],
                    ALU.mult, ALU.add), waits=[tok, t], inc=s3)
        tcast[c] = b.emit("dve", lambda e2, c=c: e2.tensor_copy(wdc[c][:], acc[:]),
                          waits=[t], inc=s3)
    t = None
    tcastu = None
    for e in range(8):
        j = e % 2
        tok = ch3.issue(utile[j][:], ctx["L0_ffn_Wa_up"][e],
                        waits=[t] if e >= 2 else [tcast[ND - 1]])
        if e == 0:
            t = b.emit("dve", lambda e2, j=j, e=e: e2.tensor_scalar(
                accu[:], utile[j][:], maskPP[0:64, e:e + 1], None, ALU.mult),
                waits=[tok], inc=s3)
        else:
            t = b.emit("dve", lambda e2, j=j, e=e: e2.scalar_tensor_tensor(
                accu[:], utile[j][:], maskPP[0:64, e:e + 1], accu[:],
                ALU.mult, ALU.add), waits=[tok, t], inc=s3)
    tcastu = b.emit("dve", lambda e2: e2.tensor_copy(wuc[:], accu[:]),
                    waits=[t], inc=s3)
    t = None
    for e in range(8):
        j = e % 2
        tokd = ch3.issue(btile[j][0:1, 0:64],
                         ctx["L0_ffn_ba_down"][e].rearrange("(a c) -> a c", a=1),
                         waits=[t] if e >= 1 else [tcastu])
        toku = ch3.issue(btile[j][0:1, 64:64 + D],
                         ctx["L0_ffn_ba_up"][e].rearrange("(a c) -> a c", a=1),
                         waits=[t] if e >= 1 else [tcastu])
        if e == 0:
            t = b.emit("dve", lambda e2, j=j, e=e: e2.tensor_scalar(
                accb[:], btile[j][:], maskPP[0:1, e:e + 1], None, ALU.mult),
                waits=[tokd, toku], inc=s3)
        else:
            t = b.emit("dve", lambda e2, j=j, e=e: e2.scalar_tensor_tensor(
                accb[:], btile[j][:], maskPP[0:1, e:e + 1], accb[:],
                ALU.mult, ALU.add), waits=[tokd, toku, t], inc=s3)
    tbb = b.dma("pool", bbase[:], ctx["L0_ffn_b_base"].rearrange("(a c) -> a c", a=1),
                inc=ld3)
    t = b.emit("dve", lambda e2: e2.tensor_add(accb[0:1, 64:64 + D],
                                               accb[0:1, 64:64 + D], bbase[0:1, :]),
               waits=[t, tbb], inc=s3)
    t = b.emit("dve", lambda e2: e2.tensor_copy(bcomb[:], accb[:]),
               waits=[t], inc=s3)
    b.barrier()
    a_fm = b.sbuf(f"l0a{n}", [64, b.T], BF16)
    m4 = b.mark()
    mm4 = b.sem("l0_mm4")
    ev4 = b.sem("l0_ev4")
    tev = {}
    for tt in range(ntt):
        bank = b.pb[tt % 2]
        def mf4(e2, tt=tt, bank=bank):
            for c in range(ND):
                e2.matmul(bank[0:64, :], wdc[c][:],
                          res.xT[c][:, tt * 512:(tt + 1) * 512],
                          start=(c == 0), stop=False)
            return e2.matmul(bank[0:64, :], bcomb[0:1, 0:64],
                             res.onesw[0:1, 0:512], start=False, stop=True)
        bw = [tev[tt - 2]] if tt >= 2 else []
        t1 = b.emit("pe", mf4, waits=bw, inc=mm4)
        tev[tt] = b.emit("act", lambda e2, tt=tt, bank=bank: e2.activation(
            a_fm[:, tt * 512:(tt + 1) * 512], bank[0:64, :], AF.Relu),
            waits=[t1], inc=ev4)
    b.barrier()
    b.reset(m4)
    m5 = b.mark()
    ld5 = b.sem("l0_ld5")
    wbase = load_weight_chunks(b, ctx["L0_ffn_W_base"], f"l0wb{n}", D, D, ld5)
    b.emit("pe", lambda e, v=ld5.n: e.wait_ge(ld5.h, v))
    def extra(e2, bank, i, nn, stop):
        return e2.matmul(bank[:], a_fm[:, i * P:(i + 1) * P],
                         wuc[:, nn * 512:(nn + 1) * 512], start=False, stop=stop)
    epi = ResidualEpi(b, f"l0E{n}", x_hbm, dst_hbm, 0.5, stats=out_stats)
    linear_tm(b, f"l0E{n}", res.xT, wbase, bcomb[0:1, 64:64 + D],
              epi, extra_mm=extra)
    b.barrier(toks=epi.finish())
    b.reset(sub_m)
    ctx["n"] += 1


def dft_consts(b):
    n = np.arange(D, dtype=np.float64)
    ang = 2.0 * np.pi * np.outer(n, n) / D
    Fr = np.cos(ang)
    Fi = -np.sin(ang)
    Gr = np.cos(ang) / D
    Gi = np.sin(ang) / D
    return (b.const("c_Fr", f32c(Fr)), b.const("c_Fi", f32c(Fi)),
            b.const("c_Fin", f32c(-Fi)), b.const("c_Gr", f32c(Gr)),
            b.const("c_Gin", f32c(-Gi)))


def l2ffn_precompute_M(b, wr_hbm, wi_hbm, m_scratch):
    cFr, cFi, cFin, cGr, cGin = dft_consts(b)
    m = b.mark()
    ld = b.sem("l2p_ld")
    mm = b.sem("l2p_mm")
    ev = b.sem("l2p_ev")
    wrt = load_weight_chunks(b, wr_hbm, "l2p_wr", D, D, ld)
    wit = load_weight_chunks(b, wi_hbm, "l2p_wi", D, D, ld)
    Frt = load_weight_chunks(b, cFr, "l2p_Fr", D, D, ld)
    Fit = load_weight_chunks(b, cFi, "l2p_Fi", D, D, ld)
    Fnt = load_weight_chunks(b, cFin, "l2p_Fn", D, D, ld)
    ctr = [b.sbuf(f"l2p_cr{c}", [P, D], BF16) for c in range(ND)]
    cti = [b.sbuf(f"l2p_ci{c}", [P, D], BF16) for c in range(ND)]
    nld = ld.n
    tev = {}
    for o in range(ND):
        for n2 in range(2):
            bank_r = b.pb[(o % 2) * 2 + n2]
            bank_i = b.pb[4 + (o % 2) * 2 + n2]
            def mmr(e, o=o, n2=n2, bank=bank_r):
                ins = None
                for k in range(ND):
                    e.matmul(bank[:], wrt[k][:, o * P:(o + 1) * P],
                             Frt[k][:, n2 * 512:(n2 + 1) * 512], start=(k == 0), stop=False)
                    ins = e.matmul(bank[:], wit[k][:, o * P:(o + 1) * P],
                                   Fnt[k][:, n2 * 512:(n2 + 1) * 512],
                                   start=False, stop=(k == ND - 1))
                return ins
            def mmi(e, o=o, n2=n2, bank=bank_i):
                ins = None
                for k in range(ND):
                    e.matmul(bank[:], wrt[k][:, o * P:(o + 1) * P],
                             Fit[k][:, n2 * 512:(n2 + 1) * 512], start=(k == 0), stop=False)
                    ins = e.matmul(bank[:], wit[k][:, o * P:(o + 1) * P],
                                   Frt[k][:, n2 * 512:(n2 + 1) * 512],
                                   start=False, stop=(k == ND - 1))
                return ins
            bw = ([tev[(o - 2, n2, "r")], tev[(o - 2, n2, "i")]] if o >= 2 else [])
            t1 = b.emit("pe", mmr, waits=[(ld, nld)] + bw, inc=mm)
            t2 = b.emit("pe", mmi, waits=[(ld, nld)] + bw, inc=mm)
            tev[(o, n2, "r")] = b.emit("dve", lambda e, o=o, n2=n2, bank=bank_r:
                e.tensor_copy(ctr[o][:, n2 * 512:(n2 + 1) * 512], bank[:]),
                waits=[t1], inc=ev)
            tev[(o, n2, "i")] = b.emit("dve", lambda e, o=o, n2=n2, bank=bank_i:
                e.tensor_copy(cti[o][:, n2 * 512:(n2 + 1) * 512], bank[:]),
                waits=[t2], inc=ev)
    b.barrier()
    b.reset(m)
    m2 = b.mark()
    ld2 = b.sem("l2p_ld2")
    mm2 = b.sem("l2p_mm2")
    ev2 = b.sem("l2p_ev2")
    cstm = b.chan("l2p_st2", nslots=2)
    Grt = load_weight_chunks(b, cGr, "l2p_Gr", D, D, ld2)
    Gnt = load_weight_chunks(b, cGin, "l2p_Gn", D, D, ld2)
    mt = [b.sbuf(f"l2p_m{j}", [P, D], BF16) for j in range(2)]
    mtoks = {}
    tev2 = {}
    nld2 = ld2.n
    for n in range(ND):
        j = n % 2
        for m2i in range(2):
            bank = b.pb[(n % 2) * 2 + m2i]
            def mmm(e, n=n, m2i=m2i, bank=bank):
                ins = None
                for o in range(ND):
                    e.matmul(bank[:], ctr[o][:, n * P:(n + 1) * P],
                             Grt[o][:, m2i * 512:(m2i + 1) * 512], start=(o == 0), stop=False)
                    ins = e.matmul(bank[:], cti[o][:, n * P:(n + 1) * P],
                                   Gnt[o][:, m2i * 512:(m2i + 1) * 512],
                                   start=False, stop=(o == ND - 1))
                return ins
            bw = [tev2[(n - 2, m2i)]] if n >= 2 else []
            t1 = b.emit("pe", mmm, waits=[(ld2, nld2)] + bw, inc=mm2)
            sw = [mtoks[n - 2]] if n >= 2 else []
            tev2[(n, m2i)] = b.emit("dve", lambda e, j=j, m2i=m2i, bank=bank:
                e.tensor_copy(mt[j][:, m2i * 512:(m2i + 1) * 512], bank[:]),
                waits=[t1] + sw, inc=ev2)
        mtoks[n] = cstm.issue(m_scratch[n * P:(n + 1) * P, :], mt[j][:],
                              waits=[tev2[(n, 0)], tev2[(n, 1)]])
    b.barrier(toks=[mtoks[ND - 2], mtoks[ND - 1]])
    b.reset(m2)


def l2ffn_apply(b, m_scratch, bias_hbm, x_hbm, dst_hbm, tag, out_stats=None):
    mk = b.mark()
    ld = b.sem(f"{tag}_wld")
    mtiles = []
    for c in range(ND):
        t = b.sbuf(f"{tag}_m{c}", [P, D], BF16)
        b.dma("pool", t[:], m_scratch[c * P:(c + 1) * P, :], inc=ld)
        mtiles.append(t)
    bt = bias_row(b, bias_hbm, f"{tag}_b", D, ld)
    b.emit("pe", lambda e: e.wait_ge(ld.h, ld.n))
    linear_tm_epilogue(b, mtiles, bt, 0.5, x_hbm, dst_hbm, tag,
                       out_stats=out_stats)
    b.reset(mk)



def sublayer_l2ffn(b, ctx, x_hbm, dst_hbm, first):
    if first:
        l2ffn_precompute_M(b, ctx["L2_ffn_w_freq_real"], ctx["L2_ffn_w_freq_imag"],
                           ctx["m_scratch"])
    ln_stats(b, x_hbm, sem_pref=f"s{ctx['n']}")
    ln_apply_transpose(b, x_hbm, sem_pref=f"a{ctx['n']}")
    l2ffn_apply(b, ctx["m_scratch"], ctx["L2_ffn_bias"], x_hbm, dst_hbm,
                f"l2f{ctx['n']}")
    ctx["n"] += 1



def build_l2ffn_test(t_tokens):
    b = B("l2ffn", t_tokens)
    x_in = b.inp("x", [t_tokens, D], F32)
    wr = b.inp("L2_ffn_w_freq_real", [D, D], F32)
    wi = b.inp("L2_ffn_w_freq_imag", [D, D], F32)
    bias = b.inp("L2_ffn_bias", [D], F32)
    x_out = b.out("x_out", [t_tokens, D], F32)
    m_scratch = b.scratch("m_scratch", [D, D], BF16)
    b.res = Resident(b)
    ctx = {"L2_ffn_w_freq_real": wr, "L2_ffn_w_freq_imag": wi,
           "L2_ffn_bias": bias, "m_scratch": m_scratch, "n": 0}
    sublayer_l2ffn(b, ctx, x_in, x_out, first=True)
    return b


def build_l1ffn_test(t_tokens):
    b = B("l1ffn", t_tokens)
    x_in = b.inp("x", [t_tokens, D])
    x_out = b.out("x_out", [t_tokens, D])
    b.res = Resident(b)
    ctx = {
        "L1_ffn_qweights": b.inp("L1_ffn_qweights", [6, D, D], I32),
        "L1_ffn_scales": b.inp("L1_ffn_scales", [6, D, 64]),
        "L1_ffn_biases": b.inp("L1_ffn_biases", [6, D]),
        "L1_ffn_lora_a": b.inp("L1_ffn_lora_a", [6, 32, D]),
        "L1_ffn_lora_b": b.inp("L1_ffn_lora_b", [6, D, 32]),
        "L1_ffn_ln_gamma": b.inp("L1_ffn_ln_gamma", [3, D]),
        "L1_ffn_ln_beta": b.inp("L1_ffn_ln_beta", [3, D]),
        "l1f_h": b.scratch("l1f_h", [t_tokens, D], F32),
        "l1f_y": b.scratch("l1f_y", [t_tokens, D], F32),
        "n": 0,
    }
    ln_stats(b, x_in, "s0")
    ln_apply_transpose(b, x_in, "a0")
    sublayer_l1ffn(b, ctx, x_in, x_out)
    return b


def build_l0ffn_test(t_tokens):
    b = B("l0ffn", t_tokens)
    x_in = b.inp("x", [t_tokens, D])
    x_out = b.out("x_out", [t_tokens, D])
    b.res = Resident(b)
    ctx = {
        "L0_ffn_W_base": b.inp("L0_ffn_W_base", [D, D]),
        "L0_ffn_b_base": b.inp("L0_ffn_b_base", [D]),
        "L0_ffn_Wd1": b.inp("L0_ffn_Wd1", [8, D, 256]),
        "L0_ffn_bd1": b.inp("L0_ffn_bd1", [8, 256]),
        "L0_ffn_Wd2": b.inp("L0_ffn_Wd2", [8, 256, D]),
        "L0_ffn_bd2": b.inp("L0_ffn_bd2", [8, D]),
        "L0_ffn_Wa_down": b.inp("L0_ffn_Wa_down", [8, D, 64]),
        "L0_ffn_ba_down": b.inp("L0_ffn_ba_down", [8, 64]),
        "L0_ffn_Wa_up": b.inp("L0_ffn_Wa_up", [8, 64, D]),
        "L0_ffn_ba_up": b.inp("L0_ffn_ba_up", [8, D]),
        "n": 0,
    }
    ln_stats(b, x_in, "s0")
    ln_apply_transpose(b, x_in, "a0")
    sublayer_l0ffn(b, ctx, x_in, x_out)
    return b


def build_l2attn_test(t_tokens):
    b = B("l2attn", t_tokens)
    x_in = b.inp("x", [t_tokens, D])
    x_out = b.out("x_out", [t_tokens, D])
    wqkv = b.inp("L2_attn_W_qkv", [D, 3 * D])
    bqkv = b.inp("L2_attn_b_qkv", [3 * D])
    wproj = b.inp("L2_attn_W_proj", [D, D])
    bproj = b.inp("L2_attn_b_proj", [D])
    qnw = b.inp("L2_attn_qn_w", [64])
    knw = b.inp("L2_attn_kn_w", [64])
    b.res = Resident(b)
    ctx = {
        "at_q": b.scratch("at_q", [t_tokens, D], BF16),
        "at_k": b.scratch("at_k", [t_tokens, D], BF16),
        "at_qf": b.scratch("at_qf", [D, t_tokens], BF16),
        "at_o": b.scratch("at_o", [t_tokens, D], BF16),
        "n": 0,
    }
    ln_stats(b, x_in, "s0")
    ln_apply_transpose(b, x_in, "a0")
    qnb = b.sbuf("l2qnb", [P, D], F32)
    knb = b.sbuf("l2knb", [P, D], F32)
    bcast_row_tile(b, "l2qn", qnw, 64, 16, qnb)
    bcast_row_tile(b, "l2kn", knw, 64, 16, knb)
    attn_softmax(b, ctx, "a", x_in, x_out, wqkv, wqkv, wqkv, wproj,
                 bqkv[0:D], bqkv[D:2 * D], bqkv[2 * D:3 * D], bproj,
                 transform_q=("rmsnorm", qnb), transform_k=("rmsnorm", knb),
                 coef=1.0, qkv_col0=(0, D, 2 * D))
    return b


def build_l3attn_test(t_tokens):
    b = B("l3a", t_tokens)
    x_in = b.inp("x", [t_tokens, D])
    x_out = b.out("x_out", [t_tokens, D])
    ctx = {"n": 0,
           "L3_attn_Wq": b.inp("L3_attn_Wq", [D, D]),
           "L3_attn_bq": b.inp("L3_attn_bq", [D]),
           "L3_attn_Wk": b.inp("L3_attn_Wk", [D, D]),
           "L3_attn_bk": b.inp("L3_attn_bk", [D]),
           "L3_attn_Wv": b.inp("L3_attn_Wv", [D, D]),
           "at_q": b.scratch("at_q", [t_tokens, D], BF16),
           "at_k": b.scratch("at_k", [t_tokens, D], BF16),
           "at_qf": b.scratch("at_qf", [D, t_tokens], BF16)}
    b.res = Resident(b)
    ctx["ret_consts"] = retention_consts(b)
    ln_stats(b, x_in, "s0")
    ln_apply_transpose(b, x_in, "a0")
    sublayer_l3attn(b, ctx, x_in, x_out)
    return b


def build_l3ffn_test(t_tokens):
    b = B("l3f", t_tokens)
    x_in = b.inp("x", [t_tokens, D])
    x_out = b.out("x_out", [t_tokens, D])
    ctx = {"n": 0}
    for nm, shp, dt in MODULE_INPUTS:
        if nm.startswith("L3_ffn"):
            ctx[nm] = b.inp(nm, shp, dt)
    for nm, shp, dt in (("at_q", [t_tokens, D], BF16),
                        ("at_k", [t_tokens, D], BF16),
                        ("at_qf", [D, t_tokens], BF16),
                        ("at_o", [t_tokens, D], BF16),
                        ("f3_x2", [t_tokens, D], F32),
                        ("f3_acc", [t_tokens, D], F32),
                        ("f3_xh2", [D, t_tokens], BF16),
                        ("f3_xh2lo", [D, t_tokens], BF16),
                        ("f3_xln", [t_tokens, D], F32)):
        ctx[nm] = b.scratch(nm, shp, dt)
    b.res = Resident(b)
    ln_stats(b, x_in, "s0")
    sublayer_l3ffn(b, ctx, x_in, x_out)
    return b


def build_l1attn_test(t_tokens):
    b = B("l1a", t_tokens)
    x_in = b.inp("x", [t_tokens, D])
    x_out = b.out("x_out", [t_tokens, D])
    ctx = {"n": 0,
           "L1_attn_wq": b.inp("L1_attn_wq", [D, D]),
           "L1_attn_wk": b.inp("L1_attn_wk", [D, D]),
           "L1_attn_wv": b.inp("L1_attn_wv", [D, D]),
           "L1_attn_wo": b.inp("L1_attn_wo", [D, D]),
           "at_q": b.scratch("at_q", [t_tokens, D], BF16),
           "at_k": b.scratch("at_k", [t_tokens, D], BF16),
           "at_qf": b.scratch("at_qf", [D, t_tokens], BF16),
           "at_o": b.scratch("at_o", [t_tokens, D], BF16)}
    b.res = Resident(b)
    ctri_h, _ = mask_consts(b)
    crc, crs = rope_consts(b, t_tokens)
    mtri = b.sbuf("m_tri", [P, P], F32)
    ropec = b.sbuf("m_ropec", [P, b.NT * 32], F32)
    ropes = b.sbuf("m_ropes", [P, b.NT * 32], F32)
    ldm = b.sem("mod_ld")
    b.dma("sp", mtri[:], ctri_h, inc=ldm)
    b.dma("sp", ropec[:], crc, inc=ldm)
    b.dma("sp", ropes[:], crs, inc=ldm)
    b.emit("sp", lambda e, v=ldm.n: e.wait_ge(ldm.h, v))
    b.barrier()
    ln_stats(b, x_in, "s0")
    ln_apply_transpose(b, x_in, "a0")
    attn_softmax(b, ctx, "a", x_in, x_out, ctx["L1_attn_wq"], ctx["L1_attn_wk"],
                 ctx["L1_attn_wv"], ctx["L1_attn_wo"], None, None, None, None,
                 transform_q=("rope", ropec, ropes),
                 transform_k=("rope", ropec, ropes),
                 causal=True, diagmask=mtri, coef=1.0, wlayout="od")
    return b


def rope_consts(b, t_tokens):
    NT = t_tokens // P
    inv = 1.0 / (10000.0 ** (np.arange(0, 64, 2, dtype=np.float64) / 64.0))
    pos = np.arange(t_tokens, dtype=np.float64)
    fr = pos[:, None] * inv[None, :]
    cs = np.cos(fr).reshape(NT, P, 32).transpose(1, 0, 2).reshape(P, NT * 32)
    sn = np.sin(fr).reshape(NT, P, 32).transpose(1, 0, 2).reshape(P, NT * 32)
    return b.const("c_ropec", f32c(cs)), b.const("c_ropes", f32c(sn))


def mask_consts(b):
    kp = np.arange(P)[:, None]
    qf = np.arange(P)[None, :]
    tri = np.where(qf >= kp, 0.0, -2.0e38)
    low = np.where(kp >= qf, 0.0, -2.0e38)
    return b.const("c_mtri", f32c(tri)), b.const("c_mlow", f32c(low))


MODULE_INPUTS = [
    ("L0_attn_Wq", [D, D], F32), ("L0_attn_bq", [D], F32),
    ("L0_attn_Wk", [D, D], F32), ("L0_attn_bk", [D], F32),
    ("L0_attn_Wv", [D, D], F32), ("L0_attn_bv", [D], F32),
    ("L0_attn_Wo", [D, D], F32), ("L0_attn_bo", [D], F32),
    ("L0_ffn_W_base", [D, D], F32), ("L0_ffn_b_base", [D], F32),
    ("L0_ffn_Wd1", [8, D, 256], F32), ("L0_ffn_bd1", [8, 256], F32),
    ("L0_ffn_Wd2", [8, 256, D], F32), ("L0_ffn_bd2", [8, D], F32),
    ("L0_ffn_Wa_down", [8, D, 64], F32), ("L0_ffn_ba_down", [8, 64], F32),
    ("L0_ffn_Wa_up", [8, 64, D], F32), ("L0_ffn_ba_up", [8, D], F32),
    ("L1_attn_wq", [D, D], F32), ("L1_attn_wk", [D, D], F32),
    ("L1_attn_wv", [D, D], F32), ("L1_attn_wo", [D, D], F32),
    ("L1_ffn_qweights", [6, D, D], I32), ("L1_ffn_scales", [6, D, 64], F32),
    ("L1_ffn_biases", [6, D], F32), ("L1_ffn_lora_a", [6, 32, D], F32),
    ("L1_ffn_lora_b", [6, D, 32], F32), ("L1_ffn_ln_gamma", [3, D], F32),
    ("L1_ffn_ln_beta", [3, D], F32),
    ("L2_attn_W_qkv", [D, 3 * D], F32), ("L2_attn_b_qkv", [3 * D], F32),
    ("L2_attn_W_proj", [D, D], F32), ("L2_attn_b_proj", [D], F32),
    ("L2_attn_qn_w", [64], F32), ("L2_attn_kn_w", [64], F32),
    ("L2_ffn_w_freq_real", [D, D], F32), ("L2_ffn_w_freq_imag", [D, D], F32),
    ("L2_ffn_bias", [D], F32),
    ("L3_attn_Wq", [D, D], F32), ("L3_attn_bq", [D], F32),
    ("L3_attn_Wk", [D, D], F32), ("L3_attn_bk", [D], F32),
    ("L3_attn_Wv", [D, D], F32),
    ("L3_ffn_ln1_g", [D], F32), ("L3_ffn_ln1_b", [D], F32),
    ("L3_ffn_ln2_g", [D], F32), ("L3_ffn_ln2_b", [D], F32),
    ("L3_ffn_wq", [D, D], F32), ("L3_ffn_bq", [D], F32),
    ("L3_ffn_wk", [D, D], F32), ("L3_ffn_bk", [D], F32),
    ("L3_ffn_wv", [D, D], F32), ("L3_ffn_bv", [D], F32),
    ("L3_ffn_wo", [D, D], F32), ("L3_ffn_bo", [D], F32),
    ("L3_ffn_gate_w", [D, 8], F32), ("L3_ffn_gate_b", [8], F32),
    ("L3_ffn_w1", [8, D, 4096], F32), ("L3_ffn_b1", [8, 4096], F32),
    ("L3_ffn_w2", [8, 4096, D], F32), ("L3_ffn_b2", [8, D], F32),
]


def build_module(t_tokens):
    b = B("mk", t_tokens)
    x_in = b.inp("L0_attn_x", [t_tokens, D])
    out = b.out("out", [t_tokens, D])
    ctx = {"n": 0}
    for nm, shp, dt in MODULE_INPUTS:
        ctx[nm] = b.inp(nm, shp, dt)
    xs = b.scratch("x_state", [t_tokens, D], F32)
    for nm, shp, dt in (("at_q", [t_tokens, D], BF16),
                        ("at_k", [t_tokens, D], BF16),
                        ("at_qf", [D, t_tokens], BF16),
                        ("at_o", [t_tokens, D], BF16),
                        ("l1f_h", [t_tokens, D], F32),
                        ("l1f_y", [t_tokens, D], F32),
                        ("m_scratch", [D, D], BF16),
                        ("f3_x2", [t_tokens, D], F32),
                        ("f3_acc", [t_tokens, D], F32),
                        ("f3_xh2", [D, t_tokens], BF16),
                        ("f3_xh2lo", [D, t_tokens], BF16),
                        ("f3_xln", [t_tokens, D], F32)):
        ctx[nm] = b.scratch(nm, shp, dt)
    b.res = Resident(b)
    ctx["ret_consts"] = retention_consts(b)
    ctri_h, clow_h = mask_consts(b)
    crc, crs = rope_consts(b, t_tokens)

    def load_tiles(specs):
        ldm = b.sem("mod_ld")
        outs = []
        for name, shape, href in specs:
            t = b.sbuf(name + str(b.uid()), shape, F32)
            b.dma("sp", t[:], href, inc=ldm)
            outs.append(t)
        b.emit("sp", lambda e, v=ldm.n: e.wait_ge(ldm.h, v))
        b.barrier()
        return outs

    cur = x_in
    res = b.res
    OS = (res.ssum, res.ssq)

    def pre(first=False):
        if first:
            ln_stats(b, cur, "MS")
        else:
            stats_finalize(b, res.ssum, res.ssq, res.mu, res.rstd, "MSF")
        ln_apply_transpose(b, cur, "MA")

    pre(first=True)
    sublayer_l0ffn(b, ctx, cur, xs, out_stats=OS)
    cur = xs
    pre()
    mk0 = b.mark()
    mtri, mlow = load_tiles([("m_tri", [P, P], ctri_h),
                             ("m_low", [P, P], clow_h)])
    attn_softmax(b, ctx, "a0", cur, cur, ctx["L0_attn_Wq"], ctx["L0_attn_Wk"],
                 ctx["L0_attn_Wv"], ctx["L0_attn_Wo"], ctx["L0_attn_bq"],
                 ctx["L0_attn_bk"], ctx["L0_attn_bv"], ctx["L0_attn_bo"],
                 window_tiles=2, diagmask={0: mtri, 2: mlow}, coef=1.0,
                 out_stats=OS)
    b.reset(mk0)
    pre()
    sublayer_l0ffn(b, ctx, cur, cur, out_stats=OS)
    pre()
    sublayer_l1ffn(b, ctx, cur, cur, out_stats=OS)
    pre()
    mk1 = b.mark()
    mtri, ropec, ropes = load_tiles([
        ("m_tri", [P, P], ctri_h),
        ("m_ropec", [P, b.NT * 32], crc), ("m_ropes", [P, b.NT * 32], crs)])
    attn_softmax(b, ctx, "a1", cur, cur, ctx["L1_attn_wq"], ctx["L1_attn_wk"],
                 ctx["L1_attn_wv"], ctx["L1_attn_wo"], None, None, None, None,
                 transform_q=("rope", ropec, ropes),
                 transform_k=("rope", ropec, ropes),
                 causal=True, diagmask=mtri, coef=1.0, wlayout="od",
                 out_stats=OS)
    b.reset(mk1)
    pre()
    sublayer_l1ffn(b, ctx, cur, cur, out_stats=OS)
    l2ffn_precompute_M(b, ctx["L2_ffn_w_freq_real"], ctx["L2_ffn_w_freq_imag"],
                       ctx["m_scratch"])
    pre()
    l2ffn_apply(b, ctx["m_scratch"], ctx["L2_ffn_bias"], cur, cur, "l2fA",
                out_stats=OS)
    pre()
    mk2 = b.mark()
    qnb = b.sbuf("m_qnb", [P, D], F32)
    knb = b.sbuf("m_knb", [P, D], F32)
    bcast_row_tile(b, "l2qn", ctx["L2_attn_qn_w"], 64, 16, qnb)
    bcast_row_tile(b, "l2kn", ctx["L2_attn_kn_w"], 64, 16, knb)
    attn_softmax(b, ctx, "a2", cur, cur, ctx["L2_attn_W_qkv"],
                 ctx["L2_attn_W_qkv"], ctx["L2_attn_W_qkv"],
                 ctx["L2_attn_W_proj"], ctx["L2_attn_b_qkv"][0:D],
                 ctx["L2_attn_b_qkv"][D:2 * D], ctx["L2_attn_b_qkv"][2 * D:3 * D],
                 ctx["L2_attn_b_proj"],
                 transform_q=("rmsnorm", qnb), transform_k=("rmsnorm", knb),
                 coef=1.0, qkv_col0=(0, D, 2 * D), out_stats=OS)
    b.reset(mk2)
    pre()
    l2ffn_apply(b, ctx["m_scratch"], ctx["L2_ffn_bias"], cur, cur, "l2fB",
                out_stats=OS)
    stats_finalize(b, res.ssum, res.ssq, res.mu, res.rstd, "MSF")
    sublayer_l3ffn(b, ctx, cur, cur, out_stats=OS)
    pre()
    sublayer_l3attn(b, ctx, cur, cur, out_stats=OS)
    stats_finalize(b, res.ssum, res.ssq, res.mu, res.rstd, "MSF")
    sublayer_l3ffn(b, ctx, cur, out)
    return b


def kernel(**inputs):
    global SIM_OPEN_BANKS
    SIM_OPEN_BANKS = False
    t_tokens = 4096
    b = build_module(t_tokens)
    nc = b.finalize()
    base = {}
    for nm, shp, dt in MODULE_INPUTS:
        base[nm] = np.ascontiguousarray(inputs[nm])
    base.update(b.consts)
    in_maps = []
    for core in range(NCORES):
        m = dict(base)
        m["L0_attn_x"] = np.ascontiguousarray(inputs["L0_attn_x"][core])
        in_maps.append(m)
    res = run_bass_kernel_spmd(nc, in_maps, core_ids=list(range(NCORES)))
    return np.stack([np.asarray(r["out"]).astype(np.float32)
                     for r in res.results])


# version 11
# speedup vs baseline: 1.1948x; 1.1948x over previous
import math
from contextlib import ExitStack

import numpy as np
import ml_dtypes

import concourse.bass as bass
import concourse.mybir as mybir
from concourse.bass_utils import run_bass_kernel_spmd

F32 = mybir.dt.float32
BF16 = mybir.dt.bfloat16
I32 = mybir.dt.int32
AF = mybir.ActivationFunctionType
ALU = mybir.AluOpType
AX = mybir.AxisListType

P = 128
D = 1024
ND = D // P
NCORES = 8
LN_EPS = 1e-5

ENGS = ("pe", "act", "dve", "pool", "sp")

SIM_OPEN_BANKS = True


def bfc(x):
    return np.ascontiguousarray(x, dtype=ml_dtypes.bfloat16)


def f32c(x):
    return np.ascontiguousarray(x, dtype=np.float32)


class Sem:
    __slots__ = ("h", "n", "name")

    def __init__(self, h, name):
        self.h = h
        self.n = 0
        self.name = name

    def bump(self, k=1):
        self.n += k
        assert self.n < 2 ** 31
        return self.n


class B:

    def __init__(self, name, t_tokens):
        self.T = t_tokens
        self.NT = t_tokens // P
        self.nc = bass.Bass("TRN2", target_bir_lowering=False)
        self.stack = ExitStack()
        self.ops = {e: [] for e in ENGS}
        self._sems = {}
        self._sem_i = 0
        self.consts = {}
        self._sbuf_mark = None
        self._resident_top = None
        self.pb = [self.nc.alloc_psum_tensor(f"pb{i}", [P, 512], F32)
                   for i in range(3)]
        self.pbS = self.nc.alloc_psum_tensor("pbS", [P, 2048], F32)
        for i in range(4):
            self.pb.append(PBView(self.pbS.ap()[:, i * 512:(i + 1) * 512]))
        self.pb.append(self.nc.alloc_psum_tensor("pb7", [P, 512], F32))
        self.ph = self.sem("phase")
        self._uid = 0

    def uid(self):
        self._uid += 1
        return self._uid

    def sem(self, name):
        cname = "".join(ch for ch in name if not ch.isdigit())
        if cname not in self._sems:
            h = self.stack.enter_context(self.nc.semaphore(f"s_{cname}"))
            self._sems[cname] = Sem(h, cname)
        return self._sems[cname]

    def sbuf(self, name, shape, dtype):
        return self.nc.alloc_sbuf_tensor(name, shape, dtype)

    def mark(self):
        return self.nc.sbuf_base

    def reset(self, m):
        self.nc.sbuf_base = m

    def const(self, name, arr):
        if name not in self.consts:
            self.consts[name] = np.ascontiguousarray(arr)
            dt = {np.dtype(np.float32): F32, np.dtype(ml_dtypes.bfloat16): BF16,
                  np.dtype(np.int32): I32}[arr.dtype]
            self._decl_input(name, list(arr.shape), dt)
        return self._inputs[name]

    def _decl_input(self, name, shape, dt):
        if not hasattr(self, "_inputs"):
            self._inputs = {}
        self._inputs[name] = self.nc.dram_tensor(name, shape, dt, kind="ExternalInput").ap()
        return self._inputs[name]

    def inp(self, name, shape, dt=F32):
        if not hasattr(self, "_inputs"):
            self._inputs = {}
        if name not in self._inputs:
            self._decl_input(name, shape, dt)
        return self._inputs[name]

    def out(self, name, shape, dt=F32):
        return self.nc.dram_tensor(name, shape, dt, kind="ExternalOutput").ap()

    def scratch(self, name, shape, dt):
        return self.nc.dram_tensor(name, shape, dt, kind="Internal").ap()

    def emit(self, eng, fn, waits=(), inc=None, inck=1):
        wl = tuple((s, (s.n if v is None else v)) for s, v in waits)
        if inc is not None:
            inc.bump(inck)
        def closure(e, fn=fn, wl=wl, inc=inc, inck=inck):
            for s, v in wl:
                if v > 0:
                    e.wait_ge(s.h, v)
            ins = fn(e)
            if inc is not None:
                assert ins is not None
                ins.then_inc(inc.h, inck)
        self.ops[eng].append(closure)
        if inc is not None:
            return (inc, inc.n)
        return None

    def dma(self, eng, dst, src, waits=(), inc=None):
        self.emit(eng, lambda e, d=dst, s=src: e.dma_start(out=d, in_=s),
                  waits=waits, inc=inc, inck=16)
        if inc is not None:
            return (inc, inc.n)
        return None

    def chan(self, name, eng="sp", nslots=2):
        if not hasattr(self, "_chans"):
            self._chans = {}
        key = (eng, nslots)
        if key not in self._chans:
            self._chans[key] = Chan(self, f"gch_{eng}_{chr(107 + nslots)}",
                                    eng, nslots)
        return self._chans[key]


    def wait_toks(self, eng, toks):
        for s, v in toks:
            self.emit(eng, lambda e, s=s, v=v: e.wait_ge(s.h, v))

    def barrier(self, toks=()):
        self.wait_toks("sp", toks)
        k = self.ph.n + len(ENGS)
        for eng in ENGS:
            self.emit(eng, lambda e: e.drain(), inc=self.ph)
        for eng in ENGS:
            self.emit(eng, lambda e, k=k: e.wait_ge(self.ph.h, k))

    def finalize(self):
        nc = self.nc
        with nc.Block() as block:
            block.tensor(lambda t: [f(t) for f in self.ops["pe"]])
            block.scalar(lambda s: [f(s) for f in self.ops["act"]])
            block.vector(lambda v: [f(v) for f in self.ops["dve"]])
            block.gpsimd(lambda g: [f(g) for f in self.ops["pool"]])
            block.sync(lambda q: [f(q) for f in self.ops["sp"]])
        self.stack.close()
        return nc


class PBView:

    def __init__(self, ap_):
        self._ap = ap_

    def __getitem__(self, k):
        return self._ap[k]

    def ap(self):
        return self._ap


class Chan:

    def __init__(self, b, name, eng, nslots):
        self.b = b
        self.eng = eng
        self.sems = [b.sem(f"{name}_c{chr(97 + s)}") for s in range(nslots)]
        self.last = [None] * nslots
        self.k = 0

    def issue(self, dst, src, waits=()):
        si = self.k % len(self.sems)
        s = self.sems[si]
        w = list(waits)
        if self.last[si] is not None:
            w.append(self.last[si])
        tok = self.b.dma(self.eng, dst, src, waits=w, inc=s)
        self.last[si] = tok
        self.k += 1
        return tok


def ident_const(b):
    return b.const("c_ident", np.eye(P, dtype=ml_dtypes.bfloat16))


def ones_row_const(b):
    return b.const("c_ones_row", np.ones((1, P), dtype=ml_dtypes.bfloat16))


class Resident:

    def __init__(self, b):
        self.ident = b.sbuf("r_ident", [P, P], BF16)
        self.identf = b.sbuf("r_identf", [P, P], F32)
        self.ones = b.sbuf("r_ones", [1, P], BF16)
        self.onesw = b.sbuf("r_onesw", [1, 512], BF16)
        self.zrow = b.sbuf("r_zrow", [1, 512], BF16)
        self.mu = b.sbuf("r_mu", [P, b.NT], F32)
        self.rstd = b.sbuf("r_rstd", [P, b.NT], F32)
        self.ssum = b.sbuf("r_ssum", [P, b.NT], F32)
        self.ssq = b.sbuf("r_ssq", [P, b.NT], F32)
        self.xT = [b.sbuf(f"r_xT{c}", [P, max(b.T, 4096)], BF16)
                   for c in range(ND)]
        ld = b.sem("res_ld")
        b.dma("sp", self.ident[:], ident_const(b), inc=ld)
        b.dma("sp", self.identf[:], b.const("c_identf", np.eye(P, dtype=np.float32)), inc=ld)
        b.dma("sp", self.ones[:], ones_row_const(b), inc=ld)
        b.dma("sp", self.onesw[:], b.const("c_onesw", np.ones((1, 512), dtype=ml_dtypes.bfloat16)), inc=ld)
        b.dma("sp", self.zrow[:], b.const("c_zrow", np.zeros((1, 512), dtype=ml_dtypes.bfloat16)), inc=ld)
        b.emit("sp", lambda e: e.wait_ge(ld.h, 80))
        b.barrier()


def ln_stats(b, src_hbm, sem_pref="st"):
    res = b.res
    NT = b.NT
    m = b.mark()
    u = b.uid()
    xa = [b.sbuf(f"ls{u}_x{j}", [P, D], F32) for j in range(2)]
    sq = b.sbuf(f"ls{u}_sq", [P, D], F32)
    sums = b.sbuf(f"ls{u}_sums", [P, NT], F32)
    sumsq = b.sbuf(f"ls{u}_sumsq", [P, NT], F32)
    var = b.sbuf(f"ls{u}_var", [P, NT], F32)
    ch = b.chan("ld")
    dna = b.sem("lns_dna")
    dnd = b.sem("lns_dnd")
    ta, td = {}, {}
    for i in range(NT):
        j = i % 2
        w = [ta[i - 2], td[i - 2]] if i >= 2 else []
        tok = ch.issue(xa[j][:], src_hbm[i * P:(i + 1) * P, :], waits=w)
        ta[i] = b.emit("act", lambda e, j=j, i=i: e.activation(
            sq[:], xa[j][:], AF.Square, accum_out=sumsq[:, i:i + 1]),
            waits=[tok] + ([ta[i - 1]] if i >= 1 else []), inc=dna)
        td[i] = b.emit("dve", lambda e, j=j, i=i: e.reduce_sum(
            sums[:, i:i + 1], xa[j][:], axis=AX.X),
            waits=[tok], inc=dnd)
    st = b.sem("lns_fin")
    t = b.emit("dve", lambda e: e.tensor_scalar(res.mu[:], sums[:], 1.0 / D, None,
                                                ALU.mult),
               waits=[ta[NT - 1], td[NT - 1]], inc=st)
    t = b.emit("dve", lambda e: e.tensor_scalar(var[:], sumsq[:], 1.0 / D, None,
                                                ALU.mult), waits=[t], inc=st)
    t = b.emit("dve", lambda e: e.tensor_tensor(sq[:, 0:NT], res.mu[:], res.mu[:],
                                                ALU.mult), waits=[t], inc=st)
    t = b.emit("dve", lambda e: e.tensor_sub(var[:], var[:], sq[:, 0:NT]),
               waits=[t], inc=st)
    t = b.emit("dve", lambda e: e.tensor_scalar(var[:], var[:], float(LN_EPS), None,
                                                ALU.add), waits=[t], inc=st)
    t = b.emit("act", lambda e: e.activation(var[:], var[:], AF.Sqrt),
               waits=[t], inc=st)
    b.emit("dve", lambda e: e.reciprocal(res.rstd[:], var[:]), waits=[t], inc=st)
    b.barrier()
    b.reset(m)


def to_fm(b, src_hbm, dstT, sem_pref, mu=None, rstd=None, evac=None,
          col_off=0, src_bf16=False):
    res = b.res
    NT = b.NT
    m = b.mark()
    sem_pref = f"{sem_pref}{b.uid()}"
    xa = [b.sbuf(f"{sem_pref}_x{j}", [P, D], BF16 if src_bf16 else F32)
          for j in range(2)]
    xh = xa if src_bf16 else [b.sbuf(f"{sem_pref}_h{j}", [P, D], BF16)
                              for j in range(2)]
    ch = b.chan("ld")
    ap = b.sem("tofm_ap")
    tr = b.sem("tofm_tr")
    ev = b.sem("tofm_ev")
    tap, ttr, tev = {}, {}, {}
    for i in range(NT):
        j = i % 2
        w = ([tap[i - 2]] if not src_bf16 else
             [ttr[(i - 2, ND - 1)]]) if i >= 2 else []
        tok = ch.issue(xa[j][:], src_hbm[i * P:(i + 1) * P, :], waits=w)
        hw = [ttr[(i - 1, ND - 1)]] if i >= 1 else []
        if src_bf16:
            tap[i] = tok
        elif mu is not None:
            tap[i] = b.emit("dve", lambda e, j=j, i=i: e.tensor_scalar(
                xh[j][:], xa[j][:], mu[:, i:i + 1], rstd[:, i:i + 1],
                ALU.subtract, ALU.mult), waits=[tok] + hw, inc=ap)
        else:
            tap[i] = b.emit("dve", lambda e, j=j: e.tensor_copy(xh[j][:], xa[j][:]),
                            waits=[tok] + hw, inc=ap)
        for c in range(ND):
            bank = b.pb[c % 4].ap().bitcast(BF16)
            prev = (i, c - 4) if c >= 4 else ((i - 1, c + 4) if i >= 1 else None)
            bw = [tev[prev]] if prev else []
            ttr[(i, c)] = b.emit("pe", lambda e, j=j, c=c, bank=bank: e.transpose(
                bank[:, 0:P], xh[j][:, c * P:(c + 1) * P], res.ident[:]),
                waits=[tap[i]] + bw, inc=tr)
            dst = dstT[c][:, col_off + i * P:col_off + (i + 1) * P]
            if evac is None:
                eng_ = "dve" if c % 2 == 0 else "act"
                evs = ev if c % 2 == 0 else b.sem("tofm_eva")
                tev[(i, c)] = b.emit(eng_, lambda e, dst=dst, bank=bank:
                    (e.tensor_copy(dst, bank[:, 0:P])
                     if hasattr(e, "tensor_copy") else
                     e.copy(dst, bank[:, 0:P])),
                    waits=[ttr[(i, c)]] + ([tev[(i, c - 2)]] if c >= 2 else
                                           ([tev[(i - 1, c + 6)]] if i >= 1 else [])),
                    inc=evs)
            elif evac[0] == "gelu":
                kind, gam, bet = evac
                tev[(i, c)] = b.emit("act", lambda e, dst=dst, bank=bank, c=c:
                    e.activation(dst, bank[:, 0:P], AF.Gelu,
                                 bias=bet[:, c:c + 1], scale=gam[:, c:c + 1]),
                    waits=[ttr[(i, c)]], inc=ev)
            else:
                kind, gam, bet = evac
                tev[(i, c)] = b.emit("dve", lambda e, dst=dst, bank=bank, c=c:
                    e.tensor_scalar(dst, bank[:, 0:P], gam[:, c:c + 1],
                                    bet[:, c:c + 1], ALU.mult, ALU.add),
                    waits=[ttr[(i, c)]], inc=ev)
    b.barrier()
    b.reset(m)


def ln_apply_transpose(b, src_hbm, sem_pref="la"):
    to_fm(b, src_hbm, b.res.xT, sem_pref, mu=b.res.mu, rstd=b.res.rstd)


def load_weight_chunks(b, w_hbm, name, rows, cols, sem, eng="pool"):
    nr = (rows + P - 1) // P
    tiles = []
    for c in range(nr):
        r0, r1 = c * P, min((c + 1) * P, rows)
        t = b.sbuf(f"{name}_{c}", [r1 - r0, cols], BF16)
        b.dma(eng, t[:], w_hbm[r0:r1, :], inc=sem)
        tiles.append(t)
    return tiles


def bias_row(b, bias_hbm, name, cols, sem):
    t = b.sbuf(name, [1, cols], BF16)
    b.dma("pool", t[:], bias_hbm.rearrange("(a c) -> a c", a=1), inc=sem)
    return t



def linear_tm(b, pref, xT, w_tiles, bias_tile, epi, kchunks=None, n_out=D,
              extra_mm=None, xt_off=0):
    res = b.res
    NT = b.NT
    kchunks = list(range(len(xT))) if kchunks is None else list(kchunks)
    NH = n_out // 512
    mm = b.sem("lin_mm")
    evtoks = {}
    for i in range(NT):
        for n in range(NH):
            bank = b.pb[(i % 2) * NH + n]
            def mmf(e, i=i, n=n, bank=bank):
                ins = None
                last_plain = (extra_mm is None and bias_tile is None)
                for ci, c in enumerate(kchunks):
                    ins = e.matmul(bank[:], xT[c][:, xt_off + i * P:xt_off + (i + 1) * P],
                                   w_tiles[c][:, n * 512:(n + 1) * 512],
                                   start=(ci == 0),
                                   stop=(last_plain and ci == len(kchunks) - 1))
                if extra_mm is not None:
                    ins = extra_mm(e, bank, i, n, stop=(bias_tile is None)) or ins
                if bias_tile is not None:
                    ins = e.matmul(bank[:], res.ones[0:1, 0:P],
                                   bias_tile[0:1, n * 512:(n + 1) * 512],
                                   start=False, stop=True)
                return ins
            bw = [evtoks[(i - 2, n)]] if i >= 2 else []
            mmtok = b.emit("pe", mmf, waits=bw, inc=mm)
            evtoks[(i, n)] = epi(i, n, bank, mmtok)
    return evtoks


class ResidualEpi:

    def __init__(self, b, pref, x_hbm, dst_hbm, coef, n_out=D, stats=None):
        self.b = b
        self.pref = pref
        self.x_hbm = x_hbm
        self.dst_hbm = dst_hbm
        self.coef = coef
        self.NH = n_out // 512
        self.n_out = n_out
        self.stats = stats
        self.xa = [b.sbuf(f"{pref}_exa{j}", [P, n_out], F32) for j in range(2)]
        self.xn = [b.sbuf(f"{pref}_exn{j}", [P, n_out], F32) for j in range(2)]
        self.sq = b.sbuf(f"{pref}_esq", [P, n_out], F32) if stats is not None else None
        self.ch = b.chan("eld")
        self.cst = b.chan("est", nslots=2)
        self.dnd = b.sem("epi_dnd")
        self.dna = b.sem("epi_dna")
        self.ev = b.sem("epi_ev")
        self.sttoks = {}
        self.ldtoks = {}
        self.evtoks = {}
        self.statoks = {}

    def __call__(self, i, n, bank, mmtok):
        b = self.b
        j = i % 2
        NH = self.NH
        if n == 0:
            w = [self.evtoks[(i - 2, nn)] for nn in range(NH)] if i >= 2 else []
            self.ldtoks[i] = self.ch.issue(
                self.xa[j][:], self.x_hbm[i * P:(i + 1) * P, 0:self.n_out], waits=w)
        sw = [self.sttoks[i - 2]] if i >= 2 else []
        if i >= 2 and self.stats is not None:
            sw.append(self.statoks[i - 2])
        t = b.emit("dve", lambda e, j=j, n=n, bank=bank: e.scalar_tensor_tensor(
            self.xn[j][:, n * 512:(n + 1) * 512], bank[:], float(self.coef),
            self.xa[j][:, n * 512:(n + 1) * 512], ALU.mult, ALU.add),
            waits=[mmtok, self.ldtoks[i]] + sw, inc=self.ev)
        self.evtoks[(i, n)] = t
        if n == NH - 1:
            self.sttoks[i] = self.cst.issue(
                self.dst_hbm[i * P:(i + 1) * P, 0:self.n_out], self.xn[j][:],
                waits=[t])
            if self.stats is not None:
                sums, sumsq = self.stats
                ta = b.emit("act", lambda e, j=j, i=i: e.activation(
                    self.sq[:], self.xn[j][:], AF.Square,
                    accum_out=sumsq[:, i:i + 1]),
                    waits=[t] + ([self.statoks[i - 1]] if i >= 1 else []),
                    inc=self.dna)
                b.emit("dve", lambda e, j=j, i=i: e.reduce_sum(
                    sums[:, i:i + 1], self.xn[j][:], axis=AX.X),
                    waits=[t], inc=self.dnd)
                self.statoks[i] = ta
        return t

    def finish(self):
        NT = self.b.NT
        return [self.sttoks[i] for i in (NT - 2, NT - 1) if i in self.sttoks]


class StoreStatsEpi:

    def __init__(self, b, pref, dst_hbm, stats):
        self.b = b
        self.dst_hbm = dst_hbm
        self.stats = stats
        self.xn = [b.sbuf(f"{pref}_exn{j}", [P, D], F32) for j in range(2)]
        self.sq = b.sbuf(f"{pref}_esq", [P, D], F32)
        self.cst = b.chan("est", nslots=2)
        self.dna = b.sem("epi_dna")
        self.dnd = b.sem("epi_dnd")
        self.ev = b.sem("epi_ev")
        self.sttoks = {}
        self.statoks = {}

    def __call__(self, i, n, bank, mmtok):
        b = self.b
        j = i % 2
        sw = [self.sttoks[i - 2], self.statoks[i - 2]] if i >= 2 else []
        t = b.emit("dve", lambda e, j=j, n=n, bank=bank: e.tensor_copy(
            self.xn[j][:, n * 512:(n + 1) * 512], bank[:]),
            waits=[mmtok] + sw, inc=self.ev)
        if n == 1:
            self.sttoks[i] = self.cst.issue(
                self.dst_hbm[i * P:(i + 1) * P, :], self.xn[j][:], waits=[t])
            sums, sumsq = self.stats
            self.statoks[i] = b.emit("act", lambda e, j=j, i=i: e.activation(
                self.sq[:], self.xn[j][:], AF.Square, accum_out=sumsq[:, i:i + 1]),
                waits=[t] + ([self.statoks[i - 1]] if i >= 1 else []), inc=self.dna)
            b.emit("dve", lambda e, j=j, i=i: e.reduce_sum(
                sums[:, i:i + 1], self.xn[j][:], axis=AX.X), waits=[t], inc=self.dnd)
        return t

    def finish(self):
        NT = self.b.NT
        return [self.sttoks[i] for i in (NT - 2, NT - 1) if i in self.sttoks]


class QlFinalEpi:

    def __init__(self, b, pref, blk, x_hbm, h_hbm, dst_hbm, hT, use_xhat,
                 stats=None):
        self.b = b
        self.pref = pref
        self.blk = blk
        self.x_hbm = x_hbm
        self.h_hbm = h_hbm
        self.dst_hbm = dst_hbm
        self.hT = hT
        self.use_xhat = use_xhat
        self.last = (blk == 2)
        self.xa = [b.sbuf(f"{pref}_exa{j}", [P, D], F32) for j in range(2)]
        self.xh = [b.sbuf(f"{pref}_exh{j}", [P, D], F32) for j in range(2)]
        self.xn = [b.sbuf(f"{pref}_exn{j}", [P, D], F32) for j in range(2)]
        self.hb = [b.sbuf(f"{pref}_ehb{j}", [P, D], BF16) for j in range(2)]
        self.ch = b.chan("eld")
        self.chx = b.chan("eldx") if self.last else None
        self.cst = b.chan("est", nslots=2)
        self.sttoks = {}
        self.ldtoks = {}
        self.evtoks = {}
        self.cbtoks = {}
        self.trtoks = {}
        self.tvtoks = {}
        self.xstoks = {}
        self.tr = b.sem("epi_tr")
        self.evt = b.sem("epi_evt")
        self.cb = b.sem("epi_cb")
        self.xs = b.sem("epi_xs")
        self.ev = b.sem("epi_ev")
        self.stats = stats
        self.sq = b.sbuf(f"{pref}_esq", [P, D], F32) if stats is not None else None
        self.dna = b.sem("epi_dna")
        self.dnd = b.sem("epi_dnd")
        self.statoks = {}
        self.hsrc = self.xh if use_xhat else self.xa

    def __call__(self, i, n, bank, mmtok):
        b = self.b
        res = b.res
        j = i % 2
        if n == 0:
            w = ([self.evtoks[(i - 2, nn)] for nn in range(2)] +
                 [self.cbtoks[i - 2]]) if i >= 2 else []
            srcb = self.x_hbm if self.use_xhat else self.h_hbm
            t1 = self.ch.issue(self.xa[j][:], srcb[i * P:(i + 1) * P, :], waits=w)
            if self.use_xhat:
                self.xstoks[i] = b.emit("dve", lambda e, j=j, i=i: e.tensor_scalar(
                    self.xh[j][:], self.xa[j][:], res.mu[:, i:i + 1],
                    res.rstd[:, i:i + 1], ALU.subtract, ALU.mult), waits=[t1],
                    inc=self.xs)
            self.ldtoks[i] = t1
            if self.last:
                self.ldtoks[(i, "x")] = self.chx.issue(
                    self.xh[j][:], self.x_hbm[i * P:(i + 1) * P, :], waits=w)
        sw = [self.sttoks[i - 2]] if i >= 2 else []
        if self.stats is not None and i >= 2 and (i - 2) in self.statoks:
            sw = sw + [self.statoks[i - 2]]
        t = b.emit("dve", lambda e, j=j, n=n, bank=bank: e.scalar_tensor_tensor(
            self.xn[j][:, n * 512:(n + 1) * 512], bank[:], 1.0,
            self.hsrc[j][:, n * 512:(n + 1) * 512], ALU.mult, ALU.add),
            waits=[mmtok, self.ldtoks[i]] + sw +
                  ([self.xstoks[i]] if self.use_xhat else []) +
                  ([self.ldtoks[(i, "x")]] if self.last else []), inc=self.ev)
        self.evtoks[(i, n)] = t
        if n == 1:
            if self.last:
                t2 = b.emit("dve", lambda e, j=j: e.scalar_tensor_tensor(
                    self.xn[j][:], self.xn[j][:], 0.5, self.xh[j][:],
                    ALU.mult, ALU.add), waits=[t] +
                    ([self.statoks[i - 2]] if (self.stats is not None and i >= 2)
                     else []), inc=self.cb)
                self.cbtoks[i] = t2
                self.sttoks[i] = self.cst.issue(
                    self.dst_hbm[i * P:(i + 1) * P, :], self.xn[j][:], waits=[t2])
                if self.stats is not None:
                    sums, sumsq = self.stats
                    ta = b.emit("act", lambda e, j=j, i=i: e.activation(
                        self.sq[:], self.xn[j][:], AF.Square,
                        accum_out=sumsq[:, i:i + 1]),
                        waits=[t2] + ([self.statoks[i - 1]] if i >= 1 else []),
                        inc=self.dna)
                    b.emit("dve", lambda e, j=j, i=i: e.reduce_sum(
                        sums[:, i:i + 1], self.xn[j][:], axis=AX.X),
                        waits=[t2], inc=self.dnd)
                    self.statoks[i] = ta
            else:
                self.sttoks[i] = self.cst.issue(
                    self.h_hbm[i * P:(i + 1) * P, :], self.xn[j][:], waits=[t])
                hww = [self.trtoks[(i - 1, ND - 1)]] if i >= 1 else []
                tc = b.emit("dve", lambda e, j=j: e.tensor_copy(
                    self.hb[j][:], self.xn[j][:]), waits=[t] + hww, inc=self.cb)
                self.cbtoks[i] = tc
                for c in range(ND):
                    pbank = b.pb[4 + c % 4].ap().bitcast(BF16)
                    prev = (i, c - 4) if c >= 4 else \
                        ((i - 1, c + 4) if i >= 1 else None)
                    bw = [self.tvtoks[prev]] if prev else []
                    tt = b.emit("pe", lambda e, j=j, c=c, pbank=pbank: e.transpose(
                        pbank[:, 0:P], self.hb[j][:, c * P:(c + 1) * P],
                        res.ident[:]), waits=[tc] + bw, inc=self.tr)
                    self.trtoks[(i, c)] = tt
                    self.tvtoks[(i, c)] = b.emit(
                        "dve", lambda e, i=i, c=c, pbank=pbank: e.tensor_copy(
                            self.hT[c][:, i * P:(i + 1) * P], pbank[:, 0:P]),
                        waits=[tt], inc=self.evt)
        return t

    def finish(self):
        NT = self.b.NT
        return [self.sttoks[i] for i in (NT - 2, NT - 1) if i in self.sttoks]


def stats_finalize(b, sums, sumsq, mu, rstd, pref, nt=None):
    nt = b.NT if nt is None else nt
    m = b.mark()
    u = b.uid()
    var = b.sbuf(f"{pref}{u}_var", [P, nt], F32)
    msq = b.sbuf(f"{pref}{u}_msq", [P, nt], F32)
    st = b.sem(pref + "_fin")
    b.emit("dve", lambda e: e.tensor_scalar(mu[:, 0:nt], sums[:, 0:nt], 1.0 / D,
                                            None, ALU.mult), inc=st)
    b.emit("dve", lambda e: e.tensor_scalar(var[:], sumsq[:, 0:nt], 1.0 / D,
                                            None, ALU.mult), inc=st)
    b.emit("dve", lambda e: e.tensor_tensor(msq[:], mu[:, 0:nt], mu[:, 0:nt],
                                            ALU.mult), waits=[(st, None)], inc=st)
    b.emit("dve", lambda e: e.tensor_sub(var[:], var[:], msq[:]),
           waits=[(st, None)], inc=st)
    b.emit("dve", lambda e: e.tensor_scalar(var[:], var[:], float(LN_EPS), None,
                                            ALU.add), waits=[(st, None)], inc=st)
    b.emit("act", lambda e: e.activation(var[:], var[:], AF.Sqrt),
           waits=[(st, None)], inc=st)
    b.emit("dve", lambda e: e.reciprocal(rstd[:, 0:nt], var[:]),
           waits=[(st, None)], inc=st)
    b.barrier()
    b.reset(m)


def linear_tm_epilogue(b, w_tiles, bias_tile, coef, x_hbm, dst_hbm, sem_pref,
                       kchunks=None, xT=None, extra_mm=None, n_out=D,
                       out_stats=None):
    if xT is None:
        xT = b.res.xT
    epi = ResidualEpi(b, sem_pref, x_hbm, dst_hbm, coef, n_out=n_out,
                      stats=out_stats)
    linear_tm(b, sem_pref, xT, w_tiles, bias_tile, epi, kchunks=kchunks,
              n_out=n_out, extra_mm=extra_mm)
    b.barrier(toks=epi.finish())



def col_layout(b, vec_hbm_1d, name, psum_bank=6):
    res = b.res
    out = b.sbuf(name, [P, ND], F32)
    m = b.mark()
    rows = b.sbuf(name + "_r", [ND, P], F32)
    s = b.sem("colay_s")
    lds = b.sem("colay_dl")
    tok = b.dma("sp", rows[:], vec_hbm_1d.rearrange("(c p) -> c p", p=P), inc=lds)
    t = b.emit("pe", lambda e: e.transpose(b.pb[psum_bank][:, 0:ND], rows[:],
                                           res.identf[0:ND, 0:ND]),
               waits=[tok], inc=s)
    b.emit("dve", lambda e: e.tensor_copy(out[:], b.pb[psum_bank][:, 0:ND]),
           waits=[t], inc=s)
    b.barrier()
    b.reset(m)
    return out


def rows_to_colT(b, src_hbm, r, name, psum_bank=6):
    res = b.res
    outs = [b.sbuf(f"{name}_{c}", [P, r], BF16) for c in range(ND)]
    m = b.mark()
    rows = b.sbuf(name + "_r", [r, D], BF16)
    s = b.sem("rct_s")
    ev = b.sem("rct_e")
    lds = b.sem("rct_dl")
    tok = b.dma("pool", rows[:], src_hbm, inc=lds)
    tt, te = {}, {}
    for c in range(ND):
        bank = b.pb[psum_bank + (c % 2)].ap().bitcast(BF16)
        bw = [te[c - 2]] if c >= 2 else []
        tt[c] = b.emit("pe", lambda e, c=c, bank=bank, r=r: e.transpose(
            bank[:, 0:r], rows[:, c * P:(c + 1) * P], res.ident[0:r, 0:r]),
            waits=[tok] + bw, inc=s)
        te[c] = b.emit("dve", lambda e, c=c, bank=bank, r=r: e.tensor_copy(
            outs[c][:], bank[:, 0:r]), waits=[tt[c]], inc=ev)
    b.barrier()
    b.reset(m)
    return outs


def colchunks_to_rowT(b, src_hbm, r, name, psum_bank=6):
    res = b.res
    out = b.sbuf(name, [r, D], BF16)
    m = b.mark()
    chk = [b.sbuf(f"{name}_r{j}", [P, r], BF16) for j in range(2)]
    ch = b.chan("cct_l", eng="pool")
    s = b.sem("cct_s")
    ev = b.sem("cct_e")
    tt, te = {}, {}
    for c in range(ND):
        j = c % 2
        bank = b.pb[psum_bank + j].ap().bitcast(BF16)
        w = [tt[c - 2]] if c >= 2 else []
        tok = ch.issue(chk[j][:], src_hbm[c * P:(c + 1) * P, :], waits=w)
        bw = [te[c - 2]] if c >= 2 else []
        tt[c] = b.emit("pe", lambda e, j=j, bank=bank, r=r: e.transpose(
            bank[0:r, 0:P], chk[j][:], res.ident[:]), waits=[tok] + bw, inc=s)
        te[c] = b.emit("dve", lambda e, c=c, bank=bank, r=r: e.tensor_copy(
            out[:, c * P:(c + 1) * P], bank[0:r, 0:P]), waits=[tt[c]], inc=ev)
    b.barrier()
    b.reset(m)
    return out


def lora_mid(b, inT, laT, name):
    out = b.sbuf(name, [32, b.T], BF16)
    mm = b.sem("lmid_mm")
    ev = b.sem("lmid_ev")
    tm, te = {}, {}
    nb = b.T // 512
    for t in range(nb):
        bank = b.pb[t % 2]
        def mmf(e, t=t, bank=bank):
            ins = None
            for c in range(ND):
                ins = e.matmul(bank[0:32, :], laT[c][:], inT[c][:, t * 512:(t + 1) * 512],
                               start=(c == 0), stop=(c == ND - 1))
            return ins
        bw = [te[t - 2]] if t >= 2 else []
        tm[t] = b.emit("pe", mmf, waits=bw, inc=mm)
        te[t] = b.emit("dve", lambda e, t=t, bank=bank: e.tensor_copy(
            out[:, t * 512:(t + 1) * 512], bank[0:32, :]), waits=[tm[t]], inc=ev)
    b.barrier()
    return out


def dequant_w(b, q_hbm, sc_hbm, name):
    res = b.res
    outs = [b.sbuf(f"{name}_{c}", [P, D], BF16) for c in range(ND)]
    m = b.mark()
    qi = [b.sbuf(f"{name}_q{j}", [P, D], I32) for j in range(2)]
    sc = [b.sbuf(f"{name}_s{j}", [P, D // 16], F32) for j in range(2)]
    qf = b.sbuf(name + "_f", [P, D], F32)
    wb = [b.sbuf(f"{name}_w{j}", [P, D], BF16) for j in range(2)]
    chq = b.chan("dqw_lq")
    chs = b.chan("dqw_ls")
    dq = b.sem("dqw_dq")
    tr = b.sem("dqw_tr")
    ev = b.sem("dqw_ev")
    tmul, ttr, tev = {}, {}, {}
    tcv = {}
    for oc in range(ND):
        j = oc % 2
        w = [tmul[oc - 1]] if oc >= 1 else []
        wl = [tcv[oc - 2]] if oc >= 2 else []
        tq = chq.issue(qi[j][:], q_hbm[oc * P:(oc + 1) * P, :], waits=wl)
        ts = chs.issue(sc[j][:], sc_hbm[oc * P:(oc + 1) * P, :],
                       waits=[tmul[oc - 2]] if oc >= 2 else [])
        tcv[oc] = b.emit("dve", lambda e, j=j: e.tensor_copy(qf[:], qi[j][:]),
                         waits=[tq] + w, inc=dq)
        t2 = b.emit("dve", lambda e: e.tensor_scalar(qf[:], qf[:], 1.0 / 7.5, -1.0,
                                                     ALU.mult, ALU.add),
                    waits=[tcv[oc]], inc=dq)
        def mul(e, j=j):
            scv = sc[j][:]
            sc0 = bass.AP(tensor=scv.tensor, offset=scv.offset,
                          ap=[[scv.ap[0][0], P], [1, 64], [0, 16]])
            return e.tensor_tensor(wb[j].ap().rearrange("p (g w) -> p g w", w=16),
                                   qf.ap().rearrange("p (g w) -> p g w", w=16),
                                   sc0, ALU.mult)
        hw = [ttr[(oc - 1, ND - 1)]] if oc >= 1 else []
        tmul[oc] = b.emit("dve", mul, waits=[ts, t2] + hw, inc=dq)
        for c in range(ND):
            bank = b.pb[4 + c % 4].ap().bitcast(BF16)
            prev = (oc, c - 4) if c >= 4 else ((oc - 1, c + 4) if oc >= 1 else None)
            bw = [tev[prev]] if prev else []
            ttr[(oc, c)] = b.emit("pe", lambda e, j=j, c=c, bank=bank: e.transpose(
                bank[:, 0:P], wb[j][:, c * P:(c + 1) * P], res.ident[:]),
                waits=[tmul[oc]] + bw, inc=tr)
            tev[(oc, c)] = b.emit("dve", lambda e, c=c, oc=oc, bank=bank:
                e.tensor_copy(outs[c][:, oc * P:(oc + 1) * P], bank[:, 0:P]),
                waits=[ttr[(oc, c)]], inc=ev)
    b.barrier()
    b.reset(m)
    return outs


def lora_extra_mm(mid, lbT):
    def f(e, bank, i, n, stop):
        return e.matmul(bank[:], mid[:, i * P:(i + 1) * P],
                        lbT[:, n * 512:(n + 1) * 512], start=False, stop=stop)
    return f


def sublayer_l1ffn(b, ctx, x_hbm, dst_hbm, out_stats=None):
    ctx["out_stats"] = out_stats
    """x_new = x + 0.5 * qlora_stack(LN(x)). Needs ln_stats+ln_apply done."""
    n = ctx["n"]
    qw = ctx["L1_ffn_qweights"]
    sc = ctx["L1_ffn_scales"]
    bi = ctx["L1_ffn_biases"]
    la = ctx["L1_ffn_lora_a"]
    lb = ctx["L1_ffn_lora_b"]
    lng = ctx["L1_ffn_ln_gamma"]
    lnb = ctx["L1_ffn_ln_beta"]
    h_hbm = ctx["l1f_h"]
    y_hbm = ctx["l1f_y"]
    sub_m = b.mark()
    mu_y = b.sbuf(f"l1mu{n}", [P, b.NT], F32)
    rs_y = b.sbuf(f"l1rs{n}", [P, b.NT], F32)
    sums = b.sbuf(f"l1sm{n}", [P, b.NT], F32)
    sumsq = b.sbuf(f"l1sq{n}", [P, b.NT], F32)
    cur_hT = b.res.xT
    for blk in range(3):
        i0, i1 = 2 * blk, 2 * blk + 1
        blk_m = b.mark()
        gcol = col_layout(b, lng[blk], f"l1gc{n}_{blk}")
        bcol = col_layout(b, lnb[blk], f"l1bc{n}_{blk}")
        wT1 = dequant_w(b, qw[i1], sc[i1], f"l1w{n}_{blk}b")
        laT1 = rows_to_colT(b, la[i1], 32, f"l1la{n}_{blk}b")
        lbT1 = colchunks_to_rowT(b, lb[i1], 32, f"l1lb{n}_{blk}b")
        ldb = b.sem("l1bl")
        bt0 = bias_row(b, bi[i0], f"l1b{n}_{blk}a", D, ldb)
        bt1 = bias_row(b, bi[i1], f"l1b{n}_{blk}b", D, ldb)
        b.emit("pe", lambda e, v=ldb.n: e.wait_ge(ldb.h, v))
        m_a = b.mark()
        wT0 = dequant_w(b, qw[i0], sc[i0], f"l1w{n}_{blk}a")
        laT0 = rows_to_colT(b, la[i0], 32, f"l1la{n}_{blk}a")
        lbT0 = colchunks_to_rowT(b, lb[i0], 32, f"l1lb{n}_{blk}a")
        mid0 = lora_mid(b, cur_hT, laT0, f"l1m{n}_{blk}a")
        epi0 = StoreStatsEpi(b, f"l1A{n}_{blk}", y_hbm, (sums, sumsq))
        linear_tm(b, f"l1A{n}_{blk}", cur_hT, wT0, bt0, epi0,
                  extra_mm=lora_extra_mm(mid0, lbT0))
        b.barrier(toks=epi0.finish())
        stats_finalize(b, sums, sumsq, mu_y, rs_y, f"l1sf{n}_{blk}")
        b.reset(m_a)
        g_fm = [b.sbuf(f"l1g{n}_{blk}_{c}", [P, b.T], BF16) for c in range(ND)]
        to_fm(b, y_hbm, g_fm, f"l1T{n}_{blk}", mu=mu_y, rstd=rs_y,
              evac=("gelu", gcol, bcol))
        mid1 = lora_mid(b, g_fm, laT1, f"l1m{n}_{blk}b")
        epiF = QlFinalEpi(b, f"l1B{n}_{blk}", blk, x_hbm, h_hbm, dst_hbm,
                          cur_hT, use_xhat=(blk == 0),
                          stats=(ctx.get("out_stats") if blk == 2 else None))
        linear_tm(b, f"l1B{n}_{blk}", g_fm, wT1, bt1, epiF,
                  extra_mm=lora_extra_mm(mid1, lbT1))
        b.barrier(toks=epiF.finish())
        b.reset(blk_m)
    b.reset(sub_m)
    ctx["n"] += 1



class StoreEpi:

    def __init__(self, b, pref, dst_hbm, dt=F32):
        self.b = b
        self.dst_hbm = dst_hbm
        self.xn = [b.sbuf(f"{pref}_exn{j}", [P, D], dt) for j in range(2)]
        self.cst = b.chan("est", nslots=2)
        self.ev = b.sem("epi_ev")
        self.sttoks = {}

    def __call__(self, i, n, bank, mmtok):
        b = self.b
        j = i % 2
        sw = [self.sttoks[i - 2]] if i >= 2 else []
        t = b.emit("dve", lambda e, j=j, n=n, bank=bank: e.tensor_copy(
            self.xn[j][:, n * 512:(n + 1) * 512], bank[:]),
            waits=[mmtok] + sw, inc=self.ev)
        if n == 1:
            self.sttoks[i] = self.cst.issue(
                self.dst_hbm[i * P:(i + 1) * P, :], self.xn[j][:], waits=[t])
        return t

    def finish(self):
        NT = self.b.NT
        return [self.sttoks[i] for i in (NT - 2, NT - 1) if i in self.sttoks]


class VEpi:

    def __init__(self, b, pref, v_res):
        self.b = b
        self.v_res = v_res
        self.ev = b.sem("epi_ev")

    def __call__(self, i, n, bank, mmtok):
        b = self.b
        vr = self.v_res[:]
        off = i * 1056 + n * 8 * 66
        dst = bass.AP(tensor=vr.tensor, offset=vr.offset + off,
                      ap=[[vr.ap[0][0], P], [66, 8], [1, 64]])
        t = b.emit("dve", lambda e, bank=bank, dst=dst: e.tensor_copy(
            dst, bank.ap().rearrange("p (h w) -> p h w", w=64)),
            waits=[mmtok], inc=self.ev)
        return t

    def finish(self):
        return []


def qk_fm_pass(b, src_hbm, pref, transform, dst_sbufT=None, dst_hbm=None,
               norm_out=None, post=None, dst_hbm_lo=None,
               dst_tile_major=True):
    res = b.res
    NT = b.NT
    m = b.mark()
    srcbf = (transform is None or transform[0] != "ln")
    xa = [b.sbuf(f"{pref}_x{j}", [P, D], BF16 if srcbf else F32)
          for j in range(2)]
    xh = xa if srcbf else [b.sbuf(f"{pref}_h{j}", [P, D], BF16)
                           for j in range(2)]
    ss = b.sbuf(f"{pref}_ss", [P, 16], F32)
    scr = b.sbuf(f"{pref}_sc", [P, D], F32)
    strip = [b.sbuf(f"{pref}_st{j}", [P, D], BF16) for j in range(2)] \
        if dst_hbm is not None else None
    strip_lo = [b.sbuf(f"{pref}_sl{j}", [P, D], BF16) for j in range(2)] \
        if dst_hbm_lo is not None else None
    xl = [b.sbuf(f"{pref}_xl{j}", [P, D], BF16) for j in range(2)] \
        if dst_hbm_lo is not None else None
    afscr = b.sbuf(f"{pref}_af", [P, P], F32) if dst_hbm_lo is not None else None
    afsc2 = b.sbuf(f"{pref}_ag", [P, P], F32) if dst_hbm_lo is not None else None
    ch = b.chan("ld")
    cstt = b.chan("tst", nslots=2) if dst_hbm is not None else None
    cstl = b.chan("tsl", nslots=2) if dst_hbm_lo is not None else None
    ap_ = b.sem("qkf_ap")
    tr = b.sem("qkf_tr")
    ev = b.sem("qkf_ev")
    nrm = b.sem("qkf_nrm")
    tz = None
    if norm_out is not None:
        tz = b.emit("dve", lambda e: e.memset(norm_out[:], 0.0), inc=nrm)
    tap, ttr, tev, tst, tnm = {}, {}, {}, {}, {}

    def hview(ap, dsub=64):
        return bass.AP(tensor=ap.tensor, offset=ap.offset,
                       ap=[[ap.ap[0][0], P], [64, 16], [1, dsub]])

    for i in range(NT):
        j = i % 2
        if srcbf:
            w = [ttr[(i - 2, ND - 1)]] if i >= 2 else []
        else:
            w = [tap[i - 2]] if i >= 2 else []
        if i >= 2 and norm_out is not None:
            w.append(tnm[i - 2])
        tok = ch.issue(xa[j][:], src_hbm[i * P:(i + 1) * P, :], waits=w)
        hw = [ttr[(i - 1, ND - 1)]] if i >= 1 else []
        nw = [tnm[i - 1]] if (i >= 1 and norm_out is not None) else []
        tt = tok
        if transform is not None and transform[0] == "rope":
            _, cosc, sinc = transform
            def pview(ap, off):
                return bass.AP(tensor=ap.tensor, offset=ap.offset + off,
                               ap=[[ap.ap[0][0], P], [64, 16], [2, 32]])
            def sview(t_):
                a = t_[:]
                return bass.AP(tensor=a.tensor, offset=a.offset,
                               ap=[[a.ap[0][0], P], [32, 16], [1, 32]])
            def cv(cs):
                a = cs[:]
                return bass.AP(tensor=a.tensor, offset=a.offset + i * 32,
                               ap=[[a.ap[0][0], P], [0, 16], [1, 32]])
            s1v = sview(scr)
            s2v = bass.AP(tensor=scr[:].tensor, offset=scr[:].offset + 512,
                          ap=[[scr[:].ap[0][0], P], [32, 16], [1, 32]])
            xe = pview(xa[j][:], 0)
            xo = pview(xa[j][:], 1)
            cosv, sinv = cv(cosc), cv(sinc)
            t1 = b.emit("dve", lambda e, xe=xe, cosv=cosv, s1v=s1v:
                e.tensor_tensor(s1v, xe, cosv, ALU.mult),
                waits=[tt] + nw + hw, inc=ap_)
            t2 = b.emit("dve", lambda e, xo=xo, sinv=sinv, s2v=s2v:
                e.tensor_tensor(s2v, xo, sinv, ALU.mult), waits=[t1], inc=ap_)
            t3 = b.emit("dve", lambda e, s1v=s1v, s2v=s2v:
                e.tensor_sub(s1v, s1v, s2v), waits=[t2], inc=ap_)
            t4 = b.emit("dve", lambda e, xe=xe, sinv=sinv, s2v=s2v:
                e.tensor_tensor(s2v, xe, sinv, ALU.mult), waits=[t3], inc=ap_)
            t5 = b.emit("dve", lambda e, xo=xo, cosv=cosv:
                e.tensor_tensor(xo, xo, cosv, ALU.mult), waits=[t4], inc=ap_)
            t6 = b.emit("dve", lambda e, xo=xo, s2v=s2v:
                e.tensor_add(xo, xo, s2v), waits=[t5], inc=ap_)
            tt = b.emit("dve", lambda e, xe=xe, s1v=s1v:
                e.tensor_copy(xe, s1v), waits=[t6], inc=ap_)
        elif transform is not None and transform[0] == "rmsnorm":
            _, qnw_b = transform
            t1 = b.emit("act", lambda e, j=j: e.activation(
                scr[:], xa[j][:], AF.Square), waits=[tt] + nw + hw, inc=ap_)
            t2 = b.emit("dve", lambda e: e.tensor_reduce(
                ss[:], hview(scr[:]), axis=AX.X, op=ALU.add),
                waits=[t1], inc=ap_)
            t3 = b.emit("dve", lambda e: e.tensor_scalar(
                ss[:], ss[:], 1.0 / 64, 1e-6, ALU.mult, ALU.add),
                waits=[t2], inc=ap_)
            t4 = b.emit("act", lambda e: e.activation(ss[:], ss[:], AF.Sqrt),
                        waits=[t3], inc=ap_)
            t5 = b.emit("dve", lambda e: e.reciprocal(ss[:], ss[:]),
                        waits=[t4], inc=ap_)
            rrv = bass.AP(tensor=ss[:].tensor, offset=ss[:].offset,
                          ap=[[ss[:].ap[0][0], P], [1, 16], [0, 64]])
            t6 = b.emit("dve", lambda e, j=j, rrv=rrv: e.tensor_tensor(
                hview(xa[j][:]), hview(xa[j][:]), rrv, ALU.mult),
                waits=[t5], inc=ap_)
            tt = b.emit("dve", lambda e, j=j: e.tensor_mul(
                xa[j][:], xa[j][:], qnw_b[:]), waits=[t6], inc=ap_)
        elif transform is not None and transform[0] == "ln":
            _, muv, rsv = transform
            tt = b.emit("dve", lambda e, j=j, i=i: e.tensor_scalar(
                xa[j][:], xa[j][:], muv[:, i:i + 1], rsv[:, i:i + 1],
                ALU.subtract, ALU.mult), waits=[tt] + nw + hw, inc=ap_)
        else:
            tt = b.emit("dve", lambda e: e.engine_nop(), waits=[tt] + nw + hw,
                        inc=ap_)
        if srcbf:
            tcast = tt
        else:
            tcast = b.emit("dve", lambda e, j=j: e.tensor_copy(xh[j][:], xa[j][:]),
                           waits=[tt], inc=ap_)
        if dst_hbm_lo is not None:
            tcast = b.emit("dve", lambda e, j=j: e.tensor_sub(
                xl[j][:], xa[j][:], xh[j][:]), waits=[tcast], inc=ap_)
        tap[i] = tcast
        if norm_out is not None:
            t1 = b.emit("act", lambda e, j=j: e.activation(
                scr[:], xa[j][:], AF.Square), waits=[tcast], inc=ap_)
            t2 = b.emit("dve", lambda e: e.tensor_reduce(
                ss[:], hview(scr[:]), axis=AX.X, op=ALU.add),
                waits=[t1], inc=ap_)
            tnm[i] = b.emit("dve", lambda e: e.tensor_max(
                norm_out[:], norm_out[:], ss[:]),
                waits=[t2, tnm[i - 1] if i >= 1 else tz], inc=nrm)
        for c in range(ND):
            bank = b.pb[c % 4].ap().bitcast(BF16)
            prev = (i, c - 4) if c >= 4 else ((i - 1, c + 4) if i >= 1 else None)
            bw = [tev[prev]] if prev else []
            ttr[(i, c)] = b.emit("pe", lambda e, j=j, c=c, bank=bank: e.transpose(
                bank[:, 0:P], xh[j][:, c * P:(c + 1) * P], res.ident[:]),
                waits=[tcast] + bw, inc=tr)
            if dst_sbufT is not None:
                if post is None:
                    eng_ = "dve" if c % 2 == 0 else "act"
                    evs = ev if c % 2 == 0 else b.sem("qkf_eva")
                    tev[(i, c)] = b.emit(eng_, lambda e, i=i, c=c, bank=bank:
                        (e.tensor_copy(dst_sbufT[c][:, i * P:(i + 1) * P],
                                       bank[:, 0:P])
                         if hasattr(e, "tensor_copy") else
                         e.copy(dst_sbufT[c][:, i * P:(i + 1) * P], bank[:, 0:P])),
                        waits=[ttr[(i, c)]] + ([tev[(i, c - 2)]] if c >= 2 else
                                               ([tev[(i - 1, c + 6)]] if i >= 1
                                                else [])),
                        inc=evs)
                else:
                    _, gam, bet = post
                    tev[(i, c)] = b.emit("dve", lambda e, i=i, c=c, bank=bank:
                        e.tensor_scalar(dst_sbufT[c][:, i * P:(i + 1) * P],
                                        bank[:, 0:P], gam[:, c:c + 1],
                                        bet[:, c:c + 1], ALU.mult, ALU.add),
                        waits=[ttr[(i, c)]], inc=ev)
            else:
                sw2 = list(tst[i - 2]) if i >= 2 else []
                if post is None:
                    eng_ = "dve" if c % 2 == 0 else "act"
                    evs = ev if c % 2 == 0 else b.sem("qkf_eva")
                    tev[(i, c)] = b.emit(eng_, lambda e, j=j, c=c, bank=bank:
                        (e.tensor_copy(strip[j][:, c * P:(c + 1) * P],
                                       bank[:, 0:P])
                         if hasattr(e, "tensor_copy") else
                         e.copy(strip[j][:, c * P:(c + 1) * P], bank[:, 0:P])),
                        waits=[ttr[(i, c)]] + (sw2 if c == 0 else []) +
                              ([tev[(i, c - 2)]] if c >= 2 else
                               ([tev[(i - 1, c + 6)]] if i >= 1 else [])),
                        inc=evs)
                elif dst_hbm_lo is None:
                    _, gam, bet = post
                    tev[(i, c)] = b.emit("dve", lambda e, j=j, c=c, bank=bank:
                        e.tensor_scalar(strip[j][:, c * P:(c + 1) * P],
                                        bank[:, 0:P], gam[:, c:c + 1],
                                        bet[:, c:c + 1], ALU.mult, ALU.add),
                        waits=[ttr[(i, c)]] + (sw2 if c == 0 else []), inc=ev)
                else:
                    _, gam, bet = post
                    bank_lo = b.pb[4 + c % 4].ap().bitcast(BF16)
                    prev2 = (i, c - 4) if c >= 4 else \
                        ((i - 1, c + 4) if i >= 1 else None)
                    bw2 = [tev[prev2]] if prev2 else []
                    ttl = b.emit("pe", lambda e, j=j, c=c, bank_lo=bank_lo:
                        e.transpose(bank_lo[:, 0:P], xl[j][:, c * P:(c + 1) * P],
                                    res.ident[:]),
                        waits=[tap[i]] + bw2, inc=tr)
                    t0 = b.emit("dve", lambda e, c=c, bank=bank:
                        e.tensor_copy(afsc2[:], bank[:, 0:P]),
                        waits=[ttr[(i, c)]] +
                              ([tev[(i, c - 1)]] if c >= 1 else
                               ([tev[(i - 1, ND - 1)]] if i >= 1 else [])) +
                              (sw2 if c == 0 else []), inc=ev)
                    t1 = b.emit("dve", lambda e, c=c, bank_lo=bank_lo:
                        e.tensor_add(afsc2[:], afsc2[:], bank_lo[:, 0:P]),
                        waits=[t0, ttl], inc=ev)
                    t1 = b.emit("dve", lambda e, c=c:
                        e.tensor_scalar(afscr[:], afsc2[:], gam[:, c:c + 1],
                                        bet[:, c:c + 1], ALU.mult, ALU.add),
                        waits=[t1], inc=ev)
                    t2 = b.emit("dve", lambda e, j=j, c=c:
                        e.tensor_copy(strip[j][:, c * P:(c + 1) * P], afscr[:]),
                        waits=[t1], inc=ev)
                    t3 = b.emit("dve", lambda e, j=j, c=c:
                        e.tensor_sub(afscr[:], afscr[:],
                                     strip[j][:, c * P:(c + 1) * P]),
                        waits=[t2], inc=ev)
                    tev[(i, c)] = b.emit("dve", lambda e, j=j, c=c:
                        e.tensor_copy(strip_lo[j][:, c * P:(c + 1) * P],
                                      afscr[:]),
                        waits=[t3], inc=ev)
        if dst_hbm is not None and dst_tile_major:
            toks_i = [cstt.issue(
                dst_hbm[i * P:(i + 1) * P, :], strip[j][:],
                waits=[tev[(i, c2)] for c2 in range(ND)])]
            if dst_hbm_lo is not None:
                toks_i.append(cstl.issue(
                    dst_hbm_lo[i * P:(i + 1) * P, :], strip_lo[j][:],
                    waits=[tev[(i, c2)] for c2 in range(ND)]))
            tst[i] = toks_i
        elif dst_hbm is not None:
            dv = dst_hbm
            dap = bass.AP(tensor=dv.tensor,
                          offset=dv.offset + i * P,
                          ap=[[b.T, P], [P * b.T, ND], [1, P]])
            toks_i = [cstt.issue(
                dap, strip[j].ap().rearrange("p (c t) -> p c t", c=ND),
                waits=[tev[(i, c2)] for c2 in range(ND)])]
            if dst_hbm_lo is not None:
                dv2 = dst_hbm_lo
                dap2 = bass.AP(tensor=dv2.tensor, offset=dv2.offset + i * P,
                               ap=[[b.T, P], [P * b.T, ND], [1, P]])
                toks_i.append(cstl.issue(
                    dap2, strip_lo[j].ap().rearrange("p (c t) -> p c t", c=ND),
                    waits=[tev[(i, c2)] for c2 in range(ND)]))
            tst[i] = toks_i
    if dst_hbm is not None:
        b.barrier(toks=[t for i in (NT - 2, NT - 1) if i in tst
                        for t in tst[i]])
    else:
        b.barrier()
    b.reset(m)


def bcast16(b, pref, src16, dst, psum_bank=7, extra_waits=()):
    res = b.res
    s = b.sem("bc_s")
    t = b.emit("pe", lambda e: e.matmul(b.pb[psum_bank][:, 0:16],
                                        res.ones[0:1, 0:P], src16[0:1, 0:16],
                                        start=True, stop=True),
               waits=list(extra_waits), inc=s)
    return b.emit("dve", lambda e: e.tensor_copy(dst[:], b.pb[psum_bank][:, 0:16]),
                  waits=[t], inc=s)


def mhat_phase(b, pref, maxq, maxk, mhat_neg, mpair_neg=None):
    res = b.res
    m = b.mark()
    s = b.sem("mh_s")
    t16 = b.sbuf(pref + "_t16", [16, P], F32)
    red = b.sbuf(pref + "_red", [16, 2], F32)
    prod = b.sbuf(pref + "_prod", [16, 1], F32)
    row = b.sbuf(pref + "_row", [1, 16], F32)
    rowb = b.sbuf(pref + "_rowb", [1, 16], BF16)
    t = b.emit("pe", lambda e: e.transpose(b.pb[7][0:16, 0:P], maxq[:],
                                           res.identf[:]), inc=s)
    t = b.emit("dve", lambda e: e.tensor_copy(t16[:], b.pb[7][0:16, 0:P]),
               waits=[t], inc=s)
    t = b.emit("dve", lambda e: e.reduce_max(red[:, 0:1], t16[:], axis=AX.X),
               waits=[t], inc=s)
    t = b.emit("pe", lambda e: e.transpose(b.pb[7][0:16, 0:P], maxk[:],
                                           res.identf[:]), waits=[t], inc=s)
    t = b.emit("dve", lambda e: e.tensor_copy(t16[:], b.pb[7][0:16, 0:P]),
               waits=[t], inc=s)
    t = b.emit("dve", lambda e: e.reduce_max(red[:, 1:2], t16[:], axis=AX.X),
               waits=[t], inc=s)
    t = b.emit("dve", lambda e: e.tensor_mul(prod[:], red[:, 0:1], red[:, 1:2]),
               waits=[t], inc=s)
    t = b.emit("act", lambda e: e.activation(prod[:], prod[:], AF.Sqrt),
               waits=[t], inc=s)
    t = b.emit("dve", lambda e: e.tensor_scalar(prod[:], prod[:], -0.125, None,
                                                ALU.mult), waits=[t], inc=s)
    t = b.emit("pe", lambda e: e.transpose(b.pb[7][0:1, 0:16], prod[:],
                                           res.identf[0:16, 0:16]),
               waits=[t], inc=s)
    t = b.emit("dve", lambda e: e.tensor_copy(row[:], b.pb[7][0:1, 0:16]),
               waits=[t], inc=s)
    t = b.emit("dve", lambda e: e.tensor_copy(rowb[:], row[:]), waits=[t], inc=s)
    t2 = bcast16(b, pref, rowb, mhat_neg, psum_bank=7, extra_waits=[t])
    if mpair_neg is not None:
        mv = mhat_neg[:]
        ev_ = bass.AP(tensor=mv.tensor, offset=mv.offset,
                      ap=[[mv.ap[0][0], P], [2, 8]])
        od_ = bass.AP(tensor=mv.tensor, offset=mv.offset + 1,
                      ap=[[mv.ap[0][0], P], [2, 8]])
        b.emit("dve", lambda e: e.tensor_tensor(mpair_neg[:], ev_, od_, ALU.min),
               waits=[t2], inc=s)
    b.barrier()
    b.reset(m)


def attn_core(b, pref, q_buf, k_resT, v_res, mhat_neg, mpair_neg, o_buf,
              window_tiles=None, causal=False, diagmask=None):
    res = b.res
    NT = b.NT
    m = b.mark()
    qt = [b.sbuf(f"{pref}_qt{j}", [P, D], BF16) for j in range(2)]
    pt = [b.sbuf(f"{pref}_pt{j}", [P, 1024], BF16) for j in range(2)]
    ot = [b.sbuf(f"{pref}_ot{j}", [P, D], BF16) for j in range(2)]
    linv = b.sbuf(f"{pref}_li", [P, 16], F32)
    chq = b.chan("atq")
    cso = b.chan("ato", nslots=2)
    smm = b.sem("atc_smm")
    smk = b.sem("atc_smk")
    sex = b.sem("atc_sex")
    sav = b.sem("atc_sav")
    sov = b.sem("atc_sov")
    otoks = {}
    oevt = {}
    expt = {}
    ptuse = {}
    qtok = {}
    def oslot(h):
        return b.pb[h // 6], (h % 6) * 66
    for qs in range(NT):
        j = qs % 2
        if causal:
            kts = list(range(0, qs + 1))
        elif window_tiles is not None:
            kts = list(range(max(0, qs - window_tiles), qs + 1))
        else:
            kts = list(range(NT))
        groups = [kts[g:g + 4] for g in range(0, len(kts), 4)]
        qw = [qtok[qs - 2]] if qs >= 2 else []
        qtok[qs] = chq.issue(qt[j][:], q_buf[qs * P:(qs + 1) * P, :], waits=qw)
        ow0 = [oevt[qs - 1]] if qs >= 1 else []
        if SIM_OPEN_BANKS:
            def ozero(e):
                ins = None
                for bk in range(3):
                    ins = e.matmul(b.pb[bk][:, 0:512], res.ones[0:1, 0:P],
                                   res.zrow[0:1, 0:512], start=True, stop=False)
                return ins
            toz = b.emit("pe", ozero, waits=ow0, inc=sav)
        else:
            first_bank = {0: True, 1: True, 2: True}
        lastuse = {}
        for gi, grp in enumerate(groups):
            for hp2 in range(8):
                for hh2 in (2 * hp2, 2 * hp2 + 1):
                    lastuse[hh2 // 6] = (gi, hh2, len(grp) - 1)
        units = [(gi, grp, hp) for gi, grp in enumerate(groups)
                 for hp in range(8)]

        def emit_scores(u):
            gi, grp, hp = units[u]
            h0, h1 = 2 * hp, 2 * hp + 1
            st_ = u % 2
            pp = pt[u % 2]
            ng = len(grp)
            tmms = []
            for hh, bi_ in ((h0, 0), (h1, 1)):
                sbank = b.pb[3 + st_ * 2 + bi_]
                ch_ = hh // 2
                base = (hh % 2) * 64
                tp = (base, 0)
                def scmm(e, hh=hh, sbank=sbank, grp=grp, ch_=ch_,
                         base=base, tp=tp, j=j):
                    ins = None
                    if SIM_OPEN_BANKS:
                        e.matmul(sbank[:, 0:512], res.ones[0:1, 0:P],
                                 res.zrow[0:1, 0:512], start=True, stop=False)
                    for idx, kt in enumerate(grp):
                        off = idx * P
                        ins = e.matmul(
                            sbank[:, off:off + P],
                            k_resT[ch_][base:base + 64, kt * P:(kt + 1) * P],
                            qt[j][base:base + 64, ch_ * P:(ch_ + 1) * P],
                            start=(not SIM_OPEN_BANKS and idx == 0),
                            stop=(idx == len(grp) - 1),
                            tile_position=tp)
                    return ins
                key = u - 2
                bw = [expt[(qs, key)]] if key >= 0 else \
                    ([expt[qs - 1]] if qs >= 1 else [])
                tmm = b.emit("pe", scmm, waits=[qtok[qs]] + bw, inc=smm)
                if causal and diagmask is not None and grp[-1] == qs:
                    idx = len(grp) - 1
                    off = idx * P
                    tmm = b.emit("dve", lambda e, sbank=sbank, off=off:
                        e.tensor_tensor(sbank[:, off:off + P],
                                        sbank[:, off:off + P], diagmask[:],
                                        ALU.add), waits=[tmm], inc=smk)
                if (not causal) and window_tiles is not None and diagmask:
                    for idx, kt in enumerate(grp):
                        dlt = qs - kt
                        if dlt in diagmask:
                            off = idx * P
                            tmm = b.emit("dve",
                                lambda e, sbank=sbank, off=off, dlt=dlt:
                                e.tensor_tensor(sbank[:, off:off + P],
                                                sbank[:, off:off + P],
                                                diagmask[dlt][:], ALU.add),
                                waits=[tmm], inc=smk)
                tmms.append(tmm)
            pw = [ptuse[pp.name]] if pp.name in ptuse else []
            sv = b.pbS.ap()
            src_ap = bass.AP(tensor=sv.tensor, offset=sv.offset + st_ * 1024,
                             ap=[[sv.ap[0][0], P], [512, 2], [1, ng * P]])
            ppv = pp.ap()
            dst_ap = bass.AP(tensor=ppv.tensor, offset=ppv.offset,
                             ap=[[ppv.ap[0][0], P], [512, 2], [1, ng * P]])
            tex = b.emit("act", lambda e, src_ap=src_ap, dst_ap=dst_ap, hp=hp:
                e.activation(dst_ap, src_ap, AF.Exp,
                             bias=mpair_neg[:, hp:hp + 1], scale=0.125),
                waits=tmms + pw, inc=sex)
            expt[(qs, u)] = tex

        def emit_av(u):
            gi, grp, hp = units[u]
            h0, h1 = 2 * hp, 2 * hp + 1
            pp = pt[u % 2]
            tex = expt[(qs, u)]
            tav = None
            for hh, poff in ((h0, 0), (h1, 512)):
                obank, ooff = oslot(hh)
                lu = lastuse[hh // 6]
                if SIM_OPEN_BANKS:
                    fb_open = False
                else:
                    fb_open = first_bank[hh // 6] and gi == 0
                    if gi == 0:
                        first_bank[hh // 6] = False
                def avmm(e, hh=hh, grp=grp, pp=pp, poff=poff, obank=obank,
                         ooff=ooff, gi=gi, lu=lu, fb_open=fb_open):
                    ins = None
                    for idx, kt in enumerate(grp):
                        ins = e.matmul(
                            obank[:, ooff:ooff + 65],
                            pp[:, poff + idx * P:poff + (idx + 1) * P],
                            v_res[:, kt * 1056 + 66 * hh:kt * 1056 + 66 * hh + 65],
                            start=(fb_open and idx == 0),
                            stop=(lu == (gi, hh, idx)))
                    return ins
                ow2 = [oevt[qs - 1]] if (qs >= 1 and not SIM_OPEN_BANKS and
                                         fb_open) else []
                tav = b.emit("pe", avmm, waits=[tex] + ow2, inc=sav)
                otoks[(qs, hh)] = tav
            ptuse[pp.name] = tav

        for u in range(len(units)):
            emit_scores(u)
            if u >= 1:
                emit_av(u - 1)
        emit_av(len(units) - 1)
        lw = [otoks[(qs, hh)] for hh in range(16)]
        tprev = [oevt[qs - 1]] if qs >= 1 else []
        tl = None
        for bk in range(3):
            nheads = 6 if bk < 2 else 4
            bank = b.pb[bk]
            lap = bass.AP(tensor=bank[:].tensor, offset=bank[:].offset + 64,
                          ap=[[bank[:].ap[0][0], P], [66, nheads]])
            tl = b.emit("dve", lambda e, lap=lap, bk=bk, nheads=nheads:
                e.reciprocal(linv[:, bk * 6:bk * 6 + nheads], lap),
                waits=lw + tprev + ([tl] if tl else []), inc=sov)
        tno = []
        for hh in range(16):
            obank, ooff = oslot(hh)
            tno.append(b.emit("dve", lambda e, hh=hh, obank=obank, ooff=ooff, j=j:
                e.tensor_scalar(ot[j][:, 64 * hh:64 * hh + 64],
                                obank[:, ooff:ooff + 64], linv[:, hh:hh + 1],
                                None, ALU.mult),
                waits=[tl] + ([oevt[qs - 2], otoks[qs - 2]] if qs >= 2 else []),
                inc=sov))
        oevt[qs] = tno[-1]
        otoks[qs] = cso.issue(o_buf[qs * P:(qs + 1) * P, :], ot[j][:],
                              waits=[tno[-1]])
        expt[qs] = tno[-1]
    b.barrier(toks=[otoks[i] for i in (NT - 2, NT - 1) if i in otoks])
    b.reset(m)


def bcast_row_tile(b, pref, row_hbm_1d, width, reps, dst):
    res = b.res
    m = b.mark()
    r1 = b.sbuf(pref + "_r1", [1, width], F32)
    r2 = b.sbuf(pref + "_r2", [1, width * reps], BF16)
    s = b.sem("brt_s")
    lds = b.sem("brt_dl")
    tok = b.dma("sp", r1[:], row_hbm_1d.rearrange("(a c) -> a c", a=1), inc=lds)
    src_ap = bass.AP(tensor=r1[:].tensor, offset=r1[:].offset,
                     ap=[[r1[:].ap[0][0], 1], [0, reps], [1, width]])
    t = b.emit("dve", lambda e: e.tensor_copy(
        r2.ap().rearrange("a (r w) -> a r w", r=reps), src_ap),
        waits=[tok], inc=s)
    tot = width * reps
    tm1 = b.emit("pe", lambda e: e.matmul(b.pb[7][:, 0:min(tot, 512)],
                                          res.ones[0:1, 0:P],
                                          r2[0:1, 0:min(tot, 512)],
                                          start=True, stop=True),
                 waits=[t], inc=s)
    t2 = b.emit("dve", lambda e: e.tensor_copy(dst[:, 0:min(tot, 512)],
                                               b.pb[7][:, 0:min(tot, 512)]),
                waits=[tm1], inc=s)
    if tot > 512:
        tm2 = b.emit("pe", lambda e: e.matmul(b.pb[6][:, 0:tot - 512],
                                              res.ones[0:1, 0:P],
                                              r2[0:1, 512:tot],
                                              start=True, stop=True),
                     waits=[t], inc=s)
        t2 = b.emit("dve", lambda e: e.tensor_copy(dst[:, 512:tot],
                                                   b.pb[6][:, 0:tot - 512]),
                    waits=[tm2], inc=s)
    b.barrier()
    b.reset(m)


def load_weight_chunks_cols(b, w_hbm, name, col0, sem, rows=D, cols=D):
    nr = rows // P
    tiles = []
    for c in range(nr):
        t = b.sbuf(f"{name}_{c}", [P, cols], BF16)
        b.dma("pool", t[:], w_hbm[c * P:(c + 1) * P, col0:col0 + cols], inc=sem)
        tiles.append(t)
    return tiles


def attn_softmax(b, ctx, pref, x_hbm, dst_hbm, wq, wk, wv, wo, bq, bk, bv, bo,
                 transform_q=None, transform_k=None, window_tiles=None,
                 causal=False, coef=1.0, diagmask=None, qkv_col0=None,
                 wlayout="do", out_stats=None):
    res = b.res
    n = ctx["n"]
    qtm = ctx["at_q"]
    ktm = ctx["at_k"]
    qfm = ctx["at_qf"]
    obuf = ctx["at_o"]
    am = b.mark()
    v_res = b.sbuf(f"at{n}_v", [P, b.NT * 1056], BF16)
    maxq = b.sbuf(f"at{n}_mq", [P, 16], F32)
    maxk = b.sbuf(f"at{n}_mk", [P, 16], F32)
    mhat_neg = b.sbuf(f"at{n}_mh", [P, 16], F32)
    mpair_neg = b.sbuf(f"at{n}_mp", [P, 8], F32)
    cq, ck, cv = qkv_col0 if qkv_col0 is not None else (0, 0, 0)
    m1 = b.mark()
    ldw = b.sem("atw_ld")
    wtls = {}
    btls = {}
    for (wt, bt, which) in ((wq, bq, "q"), (wk, bk, "k"), (wv, bv, "v")):
        col0 = {"q": cq, "k": ck, "v": cv}[which]
        if wlayout == "od":
            wtls[which] = transpose_weight(b, wt, f"at{n}w{which}")
        else:
            wtls[which] = load_weight_chunks_cols(b, wt, f"at{n}w{which}",
                                                  col0, ldw)
        btls[which] = bias_row(b, bt, f"at{n}b{which}", D, ldw) \
            if bt is not None else None
    b.emit("pe", lambda e, v=ldw.n: e.wait_ge(ldw.h, v))
    vz = b.sem("at_vz")
    tz = b.emit("dve", lambda e: e.memset(v_res[:], 0.0), inc=vz)
    vap = v_res[:]
    onesap = bass.AP(tensor=vap.tensor, offset=vap.offset + 64,
                     ap=[[vap.ap[0][0], P], [1056, b.NT], [66, 16]])
    tz = b.emit("dve", lambda e, onesap=onesap: e.memset(onesap, 1.0),
                waits=[tz], inc=vz)
    b.wait_toks("dve", [tz])
    epis = {"q": StoreEpi(b, f"at{n}Eq", qtm, dt=BF16),
            "k": StoreEpi(b, f"at{n}Ek", ktm, dt=BF16),
            "v": VEpi(b, f"at{n}Ev", v_res)}
    mm = b.sem("lin_mm")
    evtoks = {}
    NT_ = b.NT
    for i in range(NT_):
        for wi, which in enumerate(("q", "k", "v")):
            wtl = wtls[which]
            btl = btls[which]
            for nn in range(2):
                bank = b.pb[(i % 2) * 3 + wi][:, 0:512] if False else \
                    b.pb[(i % 2) * 4 + nn + (2 if wi == 1 else 0)] \
                    if False else b.pb[((i * 3 + wi) % 4) * 2 + nn]
                def mmf(e, i=i, nn=nn, wtl=wtl, btl=btl, bank=bank):
                    ins = None
                    for c in range(ND):
                        ins = e.matmul(bank[:], res.xT[c][:, i * P:(i + 1) * P],
                                       wtl[c][:, nn * 512:(nn + 1) * 512],
                                       start=(c == 0),
                                       stop=(btl is None and c == ND - 1))
                    if btl is not None:
                        ins = e.matmul(bank[:], res.ones[0:1, 0:P],
                                       btl[0:1, nn * 512:(nn + 1) * 512],
                                       start=False, stop=True)
                    return ins
                key = (i * 3 + wi - 4, nn)
                bw = [evtoks[key]] if key in evtoks else []
                mmtok = b.emit("pe", mmf, waits=bw, inc=mm)
                evtoks[(i * 3 + wi, nn)] = epis[which](i, nn, bank, mmtok)
    b.barrier(toks=epis["q"].finish() + epis["k"].finish())
    b.reset(m1)
    qk_fm_pass(b, qtm, f"at{n}Fq", transform_q, dst_hbm=qfm, norm_out=maxq)
    qk_fm_pass(b, ktm, f"at{n}Fk", transform_k, dst_sbufT=res.xT, norm_out=maxk)
    mhat_phase(b, f"at{n}M", maxq, maxk, mhat_neg, mpair_neg)
    attn_core(b, f"at{n}C", qfm, res.xT, v_res, mhat_neg, mpair_neg, obuf,
              window_tiles=window_tiles, causal=causal, diagmask=diagmask)
    b.reset(am)
    to_fm(b, obuf, res.xT, f"at{n}To", src_bf16=True)
    m5 = b.mark()
    ldw = b.sem("atw_ld")
    if wlayout == "od":
        wol = transpose_weight(b, wo, f"at{n}wo")
    else:
        wol = load_weight_chunks(b, wo, f"at{n}wo", D, D, ldw)
    bol = bias_row(b, bo, f"at{n}bo", D, ldw) if bo is not None else None
    b.emit("pe", lambda e, v=ldw.n: e.wait_ge(ldw.h, v))
    epi = ResidualEpi(b, f"at{n}Eo", x_hbm, dst_hbm, coef, stats=out_stats)
    linear_tm(b, f"at{n}Lo", res.xT, wol, bol, epi)
    b.barrier(toks=epi.finish())
    b.reset(m5)
    ctx["n"] += 1


def transpose_weight(b, w_hbm, name):
    res = b.res
    outs = [b.sbuf(f"{name}_{c}", [P, D], BF16) for c in range(ND)]
    m = b.mark()
    wt = [b.sbuf(f"{name}_w{j}", [P, D], BF16) for j in range(2)]
    ch = b.chan("twt_l", eng="pool")
    tr = b.sem("twt_tr")
    ev = b.sem("twt_ev")
    ttr, tev = {}, {}
    for oc in range(ND):
        j = oc % 2
        w = [ttr[(oc - 2, ND - 1)]] if oc >= 2 else []
        tok = ch.issue(wt[j][:], w_hbm[oc * P:(oc + 1) * P, :], waits=w)
        for c in range(ND):
            bank = b.pb[4 + c % 4].ap().bitcast(BF16)
            prev = (oc, c - 4) if c >= 4 else ((oc - 1, c + 4) if oc >= 1 else None)
            bw = [tev[prev]] if prev else []
            ttr[(oc, c)] = b.emit("pe", lambda e, j=j, c=c, bank=bank: e.transpose(
                bank[:, 0:P], wt[j][:, c * P:(c + 1) * P], res.ident[:]),
                waits=[tok] + bw, inc=tr)
            tev[(oc, c)] = b.emit("dve", lambda e, c=c, oc=oc, bank=bank:
                e.tensor_copy(outs[c][:, oc * P:(oc + 1) * P], bank[:, 0:P]),
                waits=[ttr[(oc, c)]], inc=ev)
    b.barrier()
    b.reset(m)
    return outs


class VPlainEpi:

    def __init__(self, b, pref, v_res):
        self.b = b
        self.v_res = v_res
        self.ev = b.sem("epi_ev")

    def __call__(self, i, n, bank, mmtok):
        t = self.b.emit("dve", lambda e, bank=bank, i=i, n=n: e.tensor_copy(
            self.v_res[:, i * D + n * 512:i * D + (n + 1) * 512], bank[:]),
            waits=[mmtok], inc=self.ev)
        return t

    def finish(self):
        return []


def retention_consts(b):
    g = 0.96875
    outs = []
    kp = np.arange(P)[:, None]
    qf = np.arange(P)[None, :]
    for d in range(9):
        expo = qf + 128 * d - kp
        v = np.where(expo >= 0, np.power(np.float64(g), expo), 0.0)
        outs.append(b.const(f"c_ret{d}", f32c(v)))
    return outs


def sublayer_l3attn(b, ctx, x_hbm, dst_hbm, out_stats=None):
    n = ctx["n"]
    res = b.res
    NT = b.NT
    qtm, ktm, qfm = ctx["at_q"], ctx["at_k"], ctx["at_qf"]
    am = b.mark()
    v_res = b.sbuf(f"r3v{n}", [P, NT * D], BF16)
    dct = [b.sbuf(f"r3d{n}_{d}", [P, P], F32) for d in range(9)]
    ldd = b.sem("r3_ldd")
    for d in range(9):
        b.dma("sp", dct[d][:], ctx["ret_consts"][d], inc=ldd)
    b.emit("dve", lambda e, v=ldd.n: e.wait_ge(ldd.h, v))
    for which, wname, bname in (("q", "L3_attn_Wq", "L3_attn_bq"),
                                ("k", "L3_attn_Wk", "L3_attn_bk"),
                                ("v", "L3_attn_Wv", None)):
        m1 = b.mark()
        wT = transpose_weight(b, ctx[wname], f"r3w{n}{which}")
        if bname is not None:
            ldb = b.sem("r3_bl")
            bt = bias_row(b, ctx[bname], f"r3b{n}{which}", D, ldb)
            b.emit("pe", lambda e, v=ldb.n: e.wait_ge(ldb.h, v))
        else:
            bt = None
        if which == "q":
            epi = StoreEpi(b, f"r3Eq{n}", qtm, dt=BF16)
        elif which == "k":
            epi = StoreEpi(b, f"r3Ek{n}", ktm, dt=BF16)
        else:
            epi = VPlainEpi(b, f"r3Ev{n}", v_res)
        linear_tm(b, f"r3L{n}{which}", res.xT, wT, bt, epi)
        b.barrier(toks=epi.finish())
        b.reset(m1)
    qk_fm_pass(b, qtm, f"r3Fq{n}", None, dst_hbm=qfm)
    qk_fm_pass(b, ktm, f"r3Fk{n}", None, dst_sbufT=res.xT)
    m2 = b.mark()
    qt = [b.sbuf(f"r3qt{n}_{j}", [P, D], BF16) for j in range(2)]
    pt = [b.sbuf(f"r3pt{n}_{j}", [P, 1280], BF16) for j in range(2)]
    xa = [b.sbuf(f"r3xa{n}_{j}", [P, D], F32) for j in range(2)]
    xn = [b.sbuf(f"r3xn{n}_{j}", [P, D], F32) for j in range(2)]
    chq = b.chan("r3_q")
    chx = b.chan("r3_x")
    cso = b.chan("r3_o", nslots=2)
    smm = b.sem("r3_smm")
    sdk = b.sem("r3_sdk")
    sav = b.sem("r3_sav")
    sev = b.sem("r3_sev")
    qtok, xtok, stok, devt, oevt, statoks = {}, {}, {}, {}, {}, {}
    for qs in range(NT):
        j = qs % 2
        kts = list(range(max(0, qs - 8), qs + 1))
        qw = [qtok[qs - 2]] if qs >= 2 else []
        qtok[qs] = chq.issue(qt[j][:], qfm[qs * P:(qs + 1) * P, :], waits=qw)
        xw = [oevt[qs - 2]] if qs >= 2 else []
        xtok[qs] = chx.issue(xa[j][:], x_hbm[qs * P:(qs + 1) * P, :], waits=xw)
        nk = len(kts)
        nbk = (nk + 3) // 4
        def scm(e, kts=kts, j=j, qs=qs):
            ins = None
            for bb in range((len(kts) + 3) // 4):
                if SIM_OPEN_BANKS:
                    e.matmul(b.pb[3 + bb][:, 0:512], res.ones[0:1, 0:P],
                             res.zrow[0:1, 0:512], start=True, stop=False)
            for idx, kt in enumerate(kts):
                bank = b.pb[3 + idx // 4]
                off = (idx % 4) * P
                for c in range(ND):
                    ins = e.matmul(bank[:, off:off + P],
                                   res.xT[c][:, kt * P:(kt + 1) * P],
                                   qt[j][:, c * P:(c + 1) * P],
                                   start=(not SIM_OPEN_BANKS and c == 0 and
                                          idx % 4 == 0),
                                   stop=(c == ND - 1 and (idx % 4 == 3 or
                                                          idx == len(kts) - 1)))
            return ins
        bw = [devt[qs - 1]] if qs >= 1 else []
        tmm = b.emit("pe", scm, waits=[qtok[qs]] + bw, inc=smm)
        tdk = None
        for idx, kt in enumerate(kts):
            bank = b.pb[3 + idx // 4]
            off = (idx % 4) * P
            dd = qs - kt
            pw = [devt[qs - 1]] if (qs >= 1 and idx == 0) else []
            tdk = b.emit("dve", lambda e, bank=bank, off=off, dd=dd, idx=idx, j=j:
                e.tensor_mul(pt[j][:, idx * P:(idx + 1) * P],
                             bank[:, off:off + P], dct[dd][:]),
                waits=[tmm] + pw + ([tdk] if tdk else []), inc=sdk)
        devt[qs] = tdk
        def avm(e, kts=kts, j=j):
            ins = None
            for bb in range(2):
                if SIM_OPEN_BANKS:
                    e.matmul(b.pb[6 + bb][:, 0:512], res.ones[0:1, 0:P],
                             res.zrow[0:1, 0:512], start=True, stop=False)
            for idx, kt in enumerate(kts):
                for nn in range(2):
                    ins = e.matmul(b.pb[6 + nn][:, 0:512],
                                   pt[j][:, idx * P:(idx + 1) * P],
                                   v_res[:, kt * D + nn * 512:kt * D + (nn + 1) * 512],
                                   start=(not SIM_OPEN_BANKS and idx == 0),
                                   stop=(idx == len(kts) - 1))
            return ins
        aw = [oevt[qs - 1]] if qs >= 1 else []
        tav = b.emit("pe", avm, waits=[tdk] + aw, inc=sav)
        tn = None
        for nn in range(2):
            sw = [stok[qs - 2]] if qs >= 2 else []
            if qs >= 2 and out_stats is not None:
                sw = sw + [statoks[qs - 2]]
            tn = b.emit("dve", lambda e, j=j, nn=nn: e.tensor_add(
                xn[j][:, nn * 512:(nn + 1) * 512], b.pb[6 + nn][:],
                xa[j][:, nn * 512:(nn + 1) * 512]),
                waits=[tav, xtok[qs]] + sw + ([tn] if tn else []), inc=sev)
        oevt[qs] = tn
        stok[qs] = cso.issue(dst_hbm[qs * P:(qs + 1) * P, :], xn[j][:],
                             waits=[tn])
        if out_stats is not None:
            sums, sumsq = out_stats
            trd = b.emit("dve", lambda e, j=j, qs=qs: e.reduce_sum(
                sums[:, qs:qs + 1], xn[j][:], axis=AX.X), waits=[tn],
                inc=b.sem("epi_dnd"))
            statoks[qs] = b.emit("act", lambda e, j=j, qs=qs: e.activation(
                xn[j][:], xn[j][:], AF.Square, accum_out=sumsq[:, qs:qs + 1]),
                waits=[trd, stok[qs]] + ([statoks[qs - 1]] if qs >= 1 else []),
                inc=b.sem("epi_dna"))
    b.barrier(toks=[stok[i] for i in (NT - 2, NT - 1) if i in stok])
    b.reset(m2)
    b.reset(am)
    ctx["n"] += 1


def ln_materialize(b, src_hbm, dst_hbm, pref):
    res = b.res
    NT = b.NT
    m = b.mark()
    u = b.uid()
    xa = [b.sbuf(f"lm{u}_x{j}", [P, D], F32) for j in range(2)]
    xn = [b.sbuf(f"lm{u}_n{j}", [P, D], F32) for j in range(2)]
    ch = b.chan("ld")
    cst = b.chan("est", nslots=2)
    ap_ = b.sem("lnm_ap")
    ts, tt = {}, {}
    for i in range(NT):
        j = i % 2
        w = [tt[i - 2]] if i >= 2 else []
        tok = ch.issue(xa[j][:], src_hbm[i * P:(i + 1) * P, :], waits=w)
        sw = [ts[i - 2]] if i >= 2 else []
        tt[i] = b.emit("dve", lambda e, j=j, i=i: e.tensor_scalar(
            xn[j][:], xa[j][:], res.mu[:, i:i + 1], res.rstd[:, i:i + 1],
            ALU.subtract, ALU.mult), waits=[tok] + sw, inc=ap_)
        ts[i] = cst.issue(dst_hbm[i * P:(i + 1) * P, :], xn[j][:], waits=[tt[i]])
    b.barrier(toks=[ts[i] for i in (NT - 2, NT - 1) if i in ts])
    b.reset(m)


def sublayer_l3ffn(b, ctx, x_hbm, dst_hbm, out_stats=None):
    n = ctx["n"]
    res = b.res
    NT = b.NT
    ntt = b.T // 512
    sub_m = b.mark()
    g1 = col_layout(b, ctx["L3_ffn_ln1_g"], f"f3g1{n}")
    b1c = col_layout(b, ctx["L3_ffn_ln1_b"], f"f3b1{n}")
    g2 = col_layout(b, ctx["L3_ffn_ln2_g"], f"f3g2{n}")
    b2c = col_layout(b, ctx["L3_ffn_ln2_b"], f"f3b2{n}")
    halfmask = b.sbuf(f"f3hm{n}", [P, NT * 8], F32)
    x2 = ctx["f3_x2"]
    xh2 = ctx["f3_xh2"]
    acc = ctx["f3_acc"]
    xln = ctx["f3_xln"]
    ln_materialize(b, x_hbm, xln, f"f3L{n}")
    ln_stats(b, xln, f"f3sl{n}")
    to_fm(b, xln, res.xT, f"f3A{n}", mu=res.mu, rstd=res.rstd,
          evac=("affine", g1, b1c))
    attn_softmax(b, ctx, f"f3T{n}", xln, x2, ctx["L3_ffn_wq"],
                 ctx["L3_ffn_wk"], ctx["L3_ffn_wv"], ctx["L3_ffn_wo"],
                 ctx["L3_ffn_bq"], ctx["L3_ffn_bk"], ctx["L3_ffn_bv"],
                 ctx["L3_ffn_bo"], coef=1.0)
    ln_stats(b, x2, f"f3s{n}")
    qk_fm_pass(b, x2, f"f3F{n}", ("ln", res.mu, res.rstd), dst_hbm=xh2,
               post=("affine", g2, b2c), dst_hbm_lo=ctx["f3_xh2lo"],
               dst_tile_major=False)
    mg = b.mark()
    gw = [b.sbuf(f"f3gw{n}_{c}", [P, 8], BF16) for c in range(ND)]
    gwlo = [b.sbuf(f"f3gl{n}_{c}", [P, 8], BF16) for c in range(ND)]
    gwf = b.sbuf(f"f3gf{n}", [P, 8], F32)
    gb = b.sbuf(f"f3gb{n}", [1, 8], BF16)
    xt = [b.sbuf(f"f3xt{n}_{j}", [P, D], BF16) for j in range(2)]
    xtl = [b.sbuf(f"f3xl{n}_{j}", [P, D], BF16) for j in range(2)]
    lg = b.sbuf(f"f3lg{n}", [P, 8], F32)
    mx = b.sbuf(f"f3mx{n}", [P, 1], F32)
    ldg = b.sem("f3_ldg")
    sgp = b.sem("f3_sgp")
    chgw = b.chan("f3_gw", eng="pool", nslots=4)
    chgf = b.chan("f3_gf", eng="sp", nslots=1)
    sgp0 = sgp.n
    for c in range(ND):
        tga = chgw.issue(gw[c][:], ctx["L3_ffn_gate_w"][c * P:(c + 1) * P, :])
        tgb = chgf.issue(gwf[:], ctx["L3_ffn_gate_w"][c * P:(c + 1) * P, :],
                         waits=[(sgp, sgp0 + 2 * c)])
        t1 = b.emit("dve", lambda e, c=c: e.tensor_sub(gwf[:], gwf[:], gw[c][:]),
                    waits=[tga, tgb], inc=sgp)
        b.emit("dve", lambda e, c=c: e.tensor_copy(gwlo[c][:], gwf[:]),
               waits=[t1], inc=sgp)
    b.dma("pool", gb[:], ctx["L3_ffn_gate_b"].rearrange("(a c) -> a c", a=1),
          inc=ldg)
    ldgv = ldg.n
    sgpv = sgp.n
    gw_toks = [tga]
    chg = b.chan("f3_gq")
    sg_ = b.sem("f3_sg")
    sgd = b.sem("f3_sgd")
    gtok, gevt = {}, {}
    for i in range(NT):
        j = i % 2
        qv = xh2
        qap = bass.AP(tensor=qv.tensor, offset=qv.offset + i * P,
                      ap=[[b.T, P], [P * b.T, ND], [1, P]])
        w = [gevt[i - 2]] if i >= 2 else []
        gtok[i] = chg.issue(xt[j].ap().rearrange("p (c t) -> p c t", c=ND),
                            qap, waits=w)
        qvl = ctx["f3_xh2lo"]
        qapl = bass.AP(tensor=qvl.tensor, offset=qvl.offset + i * P,
                       ap=[[b.T, P], [P * b.T, ND], [1, P]])
        gtok2 = chg.issue(xtl[j].ap().rearrange("p (c t) -> p c t", c=ND),
                          qapl, waits=w)
        bank = b.pb[i % 2]
        def gmm(e, j=j, bank=bank):
            for c in range(ND):
                e.matmul(bank[:, 0:8], xt[j][:, c * P:(c + 1) * P], gw[c][:],
                         start=(c == 0), stop=False)
                e.matmul(bank[:, 0:8], xt[j][:, c * P:(c + 1) * P], gwlo[c][:],
                         start=False, stop=False)
                e.matmul(bank[:, 0:8], xtl[j][:, c * P:(c + 1) * P], gw[c][:],
                         start=False, stop=False)
            return e.matmul(bank[:, 0:8], res.ones[0:1, 0:P], gb[0:1, :],
                            start=False, stop=True)
        bw = [gevt[i - 2]] if i >= 2 else []
        t1 = b.emit("pe", gmm, waits=[(ldg, ldgv), (sgp, sgpv), gtok[i], gtok2]
                    + bw + gw_toks, inc=sg_)
        t2 = b.emit("dve", lambda e, bank=bank: e.tensor_copy(lg[:], bank[:, 0:8]),
                    waits=[t1] + ([gevt[i - 1]] if i >= 1 else []), inc=sgd)
        t3 = b.emit("dve", lambda e: e.reduce_max(mx[:], lg[:], axis=AX.X),
                    waits=[t2], inc=sgd)
        t4 = b.emit("dve", lambda e, i=i: e.tensor_scalar(
            halfmask[:, i * 8:(i + 1) * 8], lg[:], mx[:, 0:1], None,
            ALU.is_equal), waits=[t3], inc=sgd)
        gevt[i] = b.emit("dve", lambda e, i=i: e.tensor_scalar(
            halfmask[:, i * 8:(i + 1) * 8], halfmask[:, i * 8:(i + 1) * 8],
            0.5, None, ALU.mult), waits=[t4], inc=sgd)
    b.barrier()
    b.reset(mg)
    mm_ = b.mark()
    w2t = [b.sbuf(f"f3w2{n}_{fc}", [P, D], BF16) for fc in range(32)]
    midt = [b.sbuf(f"f3md{n}_{fc}", [P, 512], BF16) for fc in range(32)]
    b1r = b.sbuf(f"f3b1r{n}", [1, 4096], BF16)
    b2r = b.sbuf(f"f3b2r{n}", [1, D], BF16)
    xh2t = [b.sbuf(f"f3h2{n}_{jj}", [P, ND * 512], BF16) for jj in range(2)]
    xac = [b.sbuf(f"f3xa{n}_{j}", [P, D], F32) for j in range(2)]
    xno = [b.sbuf(f"f3xn{n}_{j}", [P, D], F32) for j in range(2)]
    w1t = res.xT
    chw = b.chan("f3_w", eng="pool", nslots=8)
    chh = b.chan("f3_h")
    chx = b.chan("f3_xl")
    cst = b.chan("f3_st", nslots=2)
    smm = b.sem("f3_smm")
    sev = b.sem("f3_sev")
    smid = b.sem("f3_smid")
    prev_store = {}
    statoks = {}
    prev_mid_mm = None
    prev_w1_use = None
    prev_w2_use = None
    prev_acc_tok = {}
    for e in range(8):
        w1toks = []
        for c in range(ND):
            w = [prev_w1_use] if (e > 0 and c == 0) else []
            w1toks.append(chw.issue(w1t[c][:, 0:4096],
                                    ctx["L3_ffn_w1"][e][c * P:(c + 1) * P, :],
                                    waits=w))
        w2toks = []
        for fc in range(32):
            w = [prev_w2_use] if (e > 0 and fc == 0) else []
            w2toks.append(chw.issue(w2t[fc][:],
                                    ctx["L3_ffn_w2"][e][fc * P:(fc + 1) * P, :],
                                    waits=w))
        tb1 = chw.issue(b1r[:], ctx["L3_ffn_b1"][e].rearrange("(a c) -> a c", a=1),
                        waits=[prev_w1_use] if e > 0 else [])
        tb2 = chw.issue(b2r[:], ctx["L3_ffn_b2"][e].rearrange("(a c) -> a c", a=1),
                        waits=[prev_w1_use] if e > 0 else [])
        wall = w1toks + w2toks + [tb1, tb2]
        htok, mide, ymm = {}, {}, {}
        for tt in range(ntt):
            j = tt % 2
            hw = [mide.get((tt - 2, 31))] if tt >= 2 else \
                ([prev_mid_mm] if e > 0 else [])
            hw = [t for t in hw if t]
            hv = xh2
            hap = bass.AP(tensor=hv.tensor, offset=hv.offset + tt * 512,
                          ap=[[b.T, P], [P * b.T, ND], [1, 512]])
            htok[tt] = chh.issue(
                xh2t[j].ap().rearrange("p (c t) -> p c t", c=ND), hap, waits=hw)
            hts = [htok[tt]]
            for fc in range(32):
                bank = b.pb[fc % 2]
                def mmid(e2, fc=fc, j=j, bank=bank):
                    for dc in range(ND):
                        e2.matmul(bank[:], w1t[dc][:, fc * P:(fc + 1) * P],
                                  xh2t[j][:, dc * 512:(dc + 1) * 512],
                                  start=(dc == 0), stop=False)
                    return e2.matmul(bank[:], b1r[0:1, fc * P:(fc + 1) * P],
                                     res.onesw[0:1, 0:512], start=False, stop=True)
                k = fc
                bw = [mide[(tt, fc - 2)]] if fc >= 2 else \
                    ([mide[(tt - 1, 31)]] if tt >= 1 else [])
                if tt == 0 and fc == 0:
                    wts = hts + w1toks + [tb1]
                elif fc == 0:
                    wts = hts
                else:
                    wts = []
                mwy = [ymm[tt - 1]] if (fc == 0 and tt >= 1) else \
                    ([prev_mid_mm] if (e > 0 and tt == 0 and fc == 0) else [])
                t1 = b.emit("pe", mmid, waits=wts + bw + mwy, inc=smm)
                if tt == 0 and fc == 0:
                    prev_w1_use_new = t1
                mide[(tt, fc)] = b.emit("act", lambda e2, fc=fc, bank=bank:
                    e2.activation(midt[fc][:], bank[:], AF.Relu),
                    waits=[t1], inc=smid)
            for tsub in range(4):
                ti = tt * 4 + tsub
                jj = ti % 2
                aw = [prev_acc_tok.get((ti - 2, 1))]
                aw = [t for t in aw if t]
                srcb = x2 if e == 0 else acc
                atok = chx.issue(xac[jj][:], srcb[ti * P:(ti + 1) * P, :],
                                 waits=aw)
                if e == 0:
                    atok2 = chx.issue(xno[jj][:], x_hbm[ti * P:(ti + 1) * P, :],
                                      waits=[prev_store.get(ti - 2)]
                                      if (ti - 2) in prev_store else [])
                    atok3 = b.emit("dve", lambda e2, jj=jj: e2.scalar_tensor_tensor(
                        xac[jj][:], xac[jj][:], 0.5, xno[jj][:],
                        ALU.mult, ALU.add), waits=[atok, atok2], inc=sev)
                    atok = atok3
                for nh in range(2):
                    bank = b.pb[2 + (tsub % 2) * 2 + nh]
                    def ymf(e2, tsub=tsub, nh=nh, bank=bank):
                        for fc in range(32):
                            e2.matmul(bank[:], midt[fc][:, tsub * P:(tsub + 1) * P],
                                      w2t[fc][:, nh * 512:(nh + 1) * 512],
                                      start=(fc == 0), stop=False)
                        return e2.matmul(bank[:], res.ones[0:1, 0:P],
                                         b2r[0:1, nh * 512:(nh + 1) * 512],
                                         start=False, stop=True)
                    yw = [mide[(tt, 31)]]
                    if tt == 0 and tsub == 0 and nh == 0:
                        yw += w2toks + [tb2]
                    bw = [prev_acc_tok.get((ti - 2, nh))]
                    bw = [t for t in bw if t]
                    t1 = b.emit("pe", ymf, waits=yw + bw, inc=smm)
                    ymm[tt] = t1
                    sw = [prev_store[ti - 2]] if ti >= 2 and (ti - 2) in \
                        prev_store else []
                    if e == 7 and out_stats is not None and (ti - 2) in statoks:
                        sw = sw + [statoks[ti - 2]]
                    t2 = b.emit("dve", lambda e2, jj=jj, nh=nh, bank=bank,
                                ti=ti, e=e:
                        e2.scalar_tensor_tensor(
                            xno[jj][:, nh * 512:(nh + 1) * 512], bank[:],
                            halfmask[:, ti * 8 + e:ti * 8 + e + 1],
                            xac[jj][:, nh * 512:(nh + 1) * 512],
                            ALU.mult, ALU.add),
                        waits=[t1, atok] + sw, inc=sev)
                    prev_acc_tok[(ti, nh)] = t2

                dstb = dst_hbm if e == 7 else acc
                prev_acc_tok[ti] = prev_acc_tok[(ti, 1)]
                prev_store[ti] = cst.issue(dstb[ti * P:(ti + 1) * P, :],
                                           xno[jj][:],
                                           waits=[prev_acc_tok[(ti, 0)],
                                                  prev_acc_tok[(ti, 1)]])
                if e == 7 and out_stats is not None:
                    sums, sumsq = out_stats
                    trd = b.emit("dve", lambda e2, jj=jj, ti=ti: e2.reduce_sum(
                        sums[:, ti:ti + 1], xno[jj][:], axis=AX.X),
                        waits=[prev_acc_tok[(ti, 0)], prev_acc_tok[(ti, 1)]],
                        inc=b.sem("epi_dnd"))
                    statoks[ti] = b.emit("act", lambda e2, jj=jj, ti=ti:
                        e2.activation(xno[jj][:], xno[jj][:], AF.Square,
                                      accum_out=sumsq[:, ti:ti + 1]),
                        waits=[trd, prev_store[ti]] +
                              ([statoks[ti - 1]] if (ti - 1) in statoks else []),
                        inc=b.sem("epi_dna"))
        prev_mid_mm = ymm[ntt - 1]
        prev_w1_use = prev_mid_mm
        prev_w2_use = prev_mid_mm
        b.barrier(toks=[prev_store[t] for t in (NT * 4 // 4 - 2, NT * 4 // 4 - 1)
                        if t in prev_store])
    b.barrier(toks=[prev_store[t] for t in (NT * 4 // 4 - 2, NT * 4 // 4 - 1)
                    if t in prev_store])
    b.reset(mm_)
    b.reset(sub_m)
    ctx["n"] += 1



def sublayer_l0ffn(b, ctx, x_hbm, dst_hbm, out_stats=None):
    n = ctx["n"]
    res = b.res
    ntt = b.T // 512
    sub_m = b.mark()
    maskPP = b.sbuf(f"l0mk{n}", [P, 8], F32)
    m1 = b.mark()
    ld = b.sem("l0_ld")
    loss_parts = b.sbuf(f"l0lp{n}", [P, 512], F32)
    lossPP = b.sbuf(f"l0le{n}", [P, 8], F32)
    mask8b = b.sbuf(f"l0m8{n}", [1, 8], BF16)
    he = [b.sbuf(f"l0h{n}_{hc}", [P, b.T], BF16) for hc in range(2)]
    sq_scrap = b.sbuf(f"l0sqs{n}", [P, 512], F32)
    diff = [b.sbuf(f"l0df{n}_{j}", [P, 512], F32) for j in range(2)]
    wd1 = [b.sbuf(f"l0w1{n}_{c}", [P, 256], BF16) for c in range(ND)]
    wd2 = [b.sbuf(f"l0w2{n}_{hc}", [P, D], BF16) for hc in range(2)]
    zt = b.emit("dve", lambda e: e.memset(loss_parts[:], 0.0), inc=b.sem("l0_z"))
    mmh = b.sem("l0_mmh")
    evh = b.sem("l0_evh")
    mmr = b.sem("l0_mmr")
    evr = b.sem("l0_evr")
    sqs = b.sem("l0_sqs")
    prev_sq_last = None
    prev_hev = None
    prev_dv = {}
    prev_sq = {}
    for e in range(8):
        wl1 = [prev_hev] if e > 0 else []
        wl2 = [prev_sq_last] if e > 0 else []
        for c in range(ND):
            b.dma("pool", wd1[c][:], ctx["L0_ffn_Wd1"][e][c * P:(c + 1) * P, :],
                  waits=wl1 if c == 0 else [], inc=ld)
        for hc in range(2):
            b.dma("pool", wd2[hc][:], ctx["L0_ffn_Wd2"][e][hc * P:(hc + 1) * P, :],
                  waits=wl2 if hc == 0 else [], inc=ld)
        bt1 = bias_row(b, ctx["L0_ffn_bd1"][e], f"l0b1{n}_{e}", 256, ld)
        bt2 = bias_row(b, ctx["L0_ffn_bd2"][e], f"l0b2{n}_{e}", D, ld)
        ldv = ld.n
        hev = {}
        for hc in range(2):
            for tt in range(ntt):
                k = hc * ntt + tt
                bank = b.pb[k % 2]
                def mf(e2, hc=hc, tt=tt, bank=bank, bt1=bt1):
                    for c in range(ND):
                        e2.matmul(bank[:], wd1[c][:, hc * P:(hc + 1) * P],
                                  res.xT[c][:, tt * 512:(tt + 1) * 512],
                                  start=(c == 0), stop=False)
                    return e2.matmul(bank[:], bt1[0:1, hc * P:(hc + 1) * P],
                                     res.onesw[0:1, 0:512], start=False, stop=True)
                bw = [hev[k - 2]] if k >= 2 else \
                    ([prev_hev] if e > 0 else [])
                t1 = b.emit("pe", mf, waits=[(ld, ldv)] + bw, inc=mmh)
                hev[k] = b.emit("act", lambda e2, hc=hc, tt=tt, bank=bank:
                    e2.activation(he[hc][:, tt * 512:(tt + 1) * 512], bank[:],
                                  AF.Relu), waits=[t1], inc=evh)
        prev_hev = hev[2 * ntt - 1]
        for c in range(ND):
            for tt in range(ntt):
                k = c * ntt + tt
                bank = b.pb[2 + k % 2]
                def mf2(e2, c=c, tt=tt, bank=bank, bt2=bt2):
                    for hc in range(2):
                        e2.matmul(bank[:], wd2[hc][:, c * P:(c + 1) * P],
                                  he[hc][:, tt * 512:(tt + 1) * 512],
                                  start=(hc == 0), stop=False)
                    return e2.matmul(bank[:], bt2[0:1, c * P:(c + 1) * P],
                                     res.onesw[0:1, 0:512], start=False, stop=True)
                bw = [prev_dv[divmod(k - 2, ntt)]] if k >= 2 else \
                    ([prev_dv[(ND - 1, ntt - 1)]] if e > 0 else [])
                t1 = b.emit("pe", mf2,
                            waits=[(ld, ldv), hev[2 * ntt - 1]] + bw, inc=mmr)
                j = k % 2
                dwv = [prev_sq[divmod(k - 2, ntt)]] if k >= 2 else \
                    ([prev_sq[(ND - 1, ntt - 1)]] if e > 0 else [])
                t2 = b.emit("dve", lambda e2, c=c, tt=tt, j=j, bank=bank:
                    e2.scalar_tensor_tensor(
                        diff[j][:], bank[:], 1.0,
                        res.xT[c][:, tt * 512:(tt + 1) * 512],
                        ALU.mult, ALU.subtract),
                    waits=[t1] + dwv, inc=evr)
                prev_dv[(c, tt)] = t2
                col = e * 64 + k
                sqw = [prev_sq[divmod(k - 1, ntt)]] if k >= 1 else \
                    ([prev_sq[(ND - 1, ntt - 1)]] if e > 0 else [zt])
                prev_sq[(c, tt)] = b.emit("act", lambda e2, j=j, col=col:
                    e2.activation(sq_scrap[:], diff[j][:], AF.Square,
                                  accum_out=loss_parts[:, col:col + 1]),
                    waits=[t2] + sqw, inc=sqs)
        prev_sq_last = prev_sq[(ND - 1, ntt - 1)]
    b.barrier()
    s2 = b.sem("l0_s2")
    lt8 = b.sbuf(f"l0lt{n}", [8, P], F32)
    le8 = b.sbuf(f"l0l8{n}", [8, 1], F32)
    lrow = b.sbuf(f"l0lr{n}", [1, 8], F32)
    lmin = b.sbuf(f"l0lm{n}", [1, 1], F32)
    t = None
    for e in range(8):
        t = b.emit("dve", lambda e2, e=e: e2.reduce_sum(
            lossPP[:, e:e + 1], loss_parts[:, e * 64:(e + 1) * 64], axis=AX.X),
            waits=[t] if t else [], inc=s2)
    t = b.emit("pe", lambda e2: e2.transpose(b.pb[4][0:8, 0:P], lossPP[:],
                                             res.identf[:]), waits=[t], inc=s2)
    t = b.emit("dve", lambda e2: e2.tensor_copy(lt8[:], b.pb[4][0:8, 0:P]),
               waits=[t], inc=s2)
    t = b.emit("dve", lambda e2: e2.reduce_sum(le8[:], lt8[:], axis=AX.X),
               waits=[t], inc=s2)
    t = b.emit("pe", lambda e2: e2.transpose(b.pb[5][0:1, 0:8], le8[:],
                                             res.identf[0:8, 0:8]),
               waits=[t], inc=s2)
    t = b.emit("dve", lambda e2: e2.tensor_copy(lrow[:], b.pb[5][0:1, 0:8]),
               waits=[t], inc=s2)
    t = b.emit("dve", lambda e2: e2.tensor_reduce(lmin[:], lrow[:], axis=AX.X,
                                                  op=ALU.min), waits=[t], inc=s2)
    t = b.emit("dve", lambda e2: e2.tensor_scalar(lrow[:], lrow[:],
                                                  lmin[0:1, 0:1], None,
                                                  ALU.is_equal), waits=[t], inc=s2)
    t = b.emit("dve", lambda e2: e2.tensor_copy(mask8b[:], lrow[:]),
               waits=[t], inc=s2)
    t = b.emit("pe", lambda e2: e2.matmul(b.pb[6][:, 0:8], res.ones[0:1, 0:P],
                                          mask8b[0:1, :], start=True, stop=True),
               waits=[t], inc=s2)
    t = b.emit("dve", lambda e2: e2.tensor_copy(maskPP[:], b.pb[6][:, 0:8]),
               waits=[t], inc=s2)
    b.barrier()
    b.reset(m1)
    m3 = b.mark()
    wdc = [b.sbuf(f"l0wdc{n}_{c}", [P, 64], BF16) for c in range(ND)]
    wuc = b.sbuf(f"l0wuc{n}", [64, D], BF16)
    bcomb = b.sbuf(f"l0bc{n}", [1, 64 + D], BF16)
    acc = b.sbuf(f"l0acc{n}", [P, 64], F32)
    accu = b.sbuf(f"l0accu{n}", [64, D], F32)
    accb = b.sbuf(f"l0accb{n}", [1, 64 + D], F32)
    wtile = [b.sbuf(f"l0wt{n}_{j}", [P, 64], BF16) for j in range(2)]
    utile = [b.sbuf(f"l0ut{n}_{j}", [64, D], BF16) for j in range(2)]
    btile = [b.sbuf(f"l0btl{n}_{j}", [1, 64 + D], BF16) for j in range(2)]
    bbase = b.sbuf(f"l0bb{n}", [1, D], BF16)
    ch3 = b.chan("l0_c3", eng="pool")
    s3 = b.sem("l0_s3")
    ld3 = b.sem("l0_ld3")
    tcast = {}
    for c in range(ND):
        t = None
        for e in range(8):
            j = e % 2
            lw = [tcast[c - 1]] if (e < 2 and c >= 1) else []
            tok = ch3.issue(wtile[j][:],
                            ctx["L0_ffn_Wa_down"][e][c * P:(c + 1) * P, :],
                            waits=([t] if e >= 2 else []) + lw)
            if e == 0:
                t = b.emit("dve", lambda e2, j=j, e=e: e2.tensor_scalar(
                    acc[:], wtile[j][:], maskPP[:, e:e + 1], None, ALU.mult),
                    waits=[tok] + ([tcast[c - 1]] if c >= 1 else []), inc=s3)
            else:
                t = b.emit("dve", lambda e2, j=j, e=e: e2.scalar_tensor_tensor(
                    acc[:], wtile[j][:], maskPP[:, e:e + 1], acc[:],
                    ALU.mult, ALU.add), waits=[tok, t], inc=s3)
        tcast[c] = b.emit("dve", lambda e2, c=c: e2.tensor_copy(wdc[c][:], acc[:]),
                          waits=[t], inc=s3)
    t = None
    tcastu = None
    for e in range(8):
        j = e % 2
        tok = ch3.issue(utile[j][:], ctx["L0_ffn_Wa_up"][e],
                        waits=[t] if e >= 2 else [tcast[ND - 1]])
        if e == 0:
            t = b.emit("dve", lambda e2, j=j, e=e: e2.tensor_scalar(
                accu[:], utile[j][:], maskPP[0:64, e:e + 1], None, ALU.mult),
                waits=[tok], inc=s3)
        else:
            t = b.emit("dve", lambda e2, j=j, e=e: e2.scalar_tensor_tensor(
                accu[:], utile[j][:], maskPP[0:64, e:e + 1], accu[:],
                ALU.mult, ALU.add), waits=[tok, t], inc=s3)
    tcastu = b.emit("dve", lambda e2: e2.tensor_copy(wuc[:], accu[:]),
                    waits=[t], inc=s3)
    t = None
    for e in range(8):
        j = e % 2
        tokd = ch3.issue(btile[j][0:1, 0:64],
                         ctx["L0_ffn_ba_down"][e].rearrange("(a c) -> a c", a=1),
                         waits=[t] if e >= 1 else [tcastu])
        toku = ch3.issue(btile[j][0:1, 64:64 + D],
                         ctx["L0_ffn_ba_up"][e].rearrange("(a c) -> a c", a=1),
                         waits=[t] if e >= 1 else [tcastu])
        if e == 0:
            t = b.emit("dve", lambda e2, j=j, e=e: e2.tensor_scalar(
                accb[:], btile[j][:], maskPP[0:1, e:e + 1], None, ALU.mult),
                waits=[tokd, toku], inc=s3)
        else:
            t = b.emit("dve", lambda e2, j=j, e=e: e2.scalar_tensor_tensor(
                accb[:], btile[j][:], maskPP[0:1, e:e + 1], accb[:],
                ALU.mult, ALU.add), waits=[tokd, toku, t], inc=s3)
    tbb = b.dma("pool", bbase[:], ctx["L0_ffn_b_base"].rearrange("(a c) -> a c", a=1),
                inc=ld3)
    t = b.emit("dve", lambda e2: e2.tensor_add(accb[0:1, 64:64 + D],
                                               accb[0:1, 64:64 + D], bbase[0:1, :]),
               waits=[t, tbb], inc=s3)
    t = b.emit("dve", lambda e2: e2.tensor_copy(bcomb[:], accb[:]),
               waits=[t], inc=s3)
    b.barrier()
    a_fm = b.sbuf(f"l0a{n}", [64, b.T], BF16)
    m4 = b.mark()
    mm4 = b.sem("l0_mm4")
    ev4 = b.sem("l0_ev4")
    tev = {}
    for tt in range(ntt):
        bank = b.pb[tt % 2]
        def mf4(e2, tt=tt, bank=bank):
            for c in range(ND):
                e2.matmul(bank[0:64, :], wdc[c][:],
                          res.xT[c][:, tt * 512:(tt + 1) * 512],
                          start=(c == 0), stop=False)
            return e2.matmul(bank[0:64, :], bcomb[0:1, 0:64],
                             res.onesw[0:1, 0:512], start=False, stop=True)
        bw = [tev[tt - 2]] if tt >= 2 else []
        t1 = b.emit("pe", mf4, waits=bw, inc=mm4)
        tev[tt] = b.emit("act", lambda e2, tt=tt, bank=bank: e2.activation(
            a_fm[:, tt * 512:(tt + 1) * 512], bank[0:64, :], AF.Relu),
            waits=[t1], inc=ev4)
    b.barrier()
    b.reset(m4)
    m5 = b.mark()
    ld5 = b.sem("l0_ld5")
    wbase = load_weight_chunks(b, ctx["L0_ffn_W_base"], f"l0wb{n}", D, D, ld5)
    b.emit("pe", lambda e, v=ld5.n: e.wait_ge(ld5.h, v))
    def extra(e2, bank, i, nn, stop):
        return e2.matmul(bank[:], a_fm[:, i * P:(i + 1) * P],
                         wuc[:, nn * 512:(nn + 1) * 512], start=False, stop=stop)
    epi = ResidualEpi(b, f"l0E{n}", x_hbm, dst_hbm, 0.5, stats=out_stats)
    linear_tm(b, f"l0E{n}", res.xT, wbase, bcomb[0:1, 64:64 + D],
              epi, extra_mm=extra)
    b.barrier(toks=epi.finish())
    b.reset(sub_m)
    ctx["n"] += 1


def dft_consts(b):
    n = np.arange(D, dtype=np.float64)
    ang = 2.0 * np.pi * np.outer(n, n) / D
    Fr = np.cos(ang)
    Fi = -np.sin(ang)
    Gr = np.cos(ang) / D
    Gi = np.sin(ang) / D
    return (b.const("c_Fr", f32c(Fr)), b.const("c_Fi", f32c(Fi)),
            b.const("c_Fin", f32c(-Fi)), b.const("c_Gr", f32c(Gr)),
            b.const("c_Gin", f32c(-Gi)))


def l2ffn_precompute_M(b, wr_hbm, wi_hbm, m_scratch):
    cFr, cFi, cFin, cGr, cGin = dft_consts(b)
    m = b.mark()
    ld = b.sem("l2p_ld")
    mm = b.sem("l2p_mm")
    ev = b.sem("l2p_ev")
    wrt = load_weight_chunks(b, wr_hbm, "l2p_wr", D, D, ld)
    wit = load_weight_chunks(b, wi_hbm, "l2p_wi", D, D, ld)
    Frt = load_weight_chunks(b, cFr, "l2p_Fr", D, D, ld)
    Fit = load_weight_chunks(b, cFi, "l2p_Fi", D, D, ld)
    Fnt = load_weight_chunks(b, cFin, "l2p_Fn", D, D, ld)
    ctr = [b.sbuf(f"l2p_cr{c}", [P, D], BF16) for c in range(ND)]
    cti = [b.sbuf(f"l2p_ci{c}", [P, D], BF16) for c in range(ND)]
    nld = ld.n
    tev = {}
    for o in range(ND):
        for n2 in range(2):
            bank_r = b.pb[(o % 2) * 2 + n2]
            bank_i = b.pb[4 + (o % 2) * 2 + n2]
            def mmr(e, o=o, n2=n2, bank=bank_r):
                ins = None
                for k in range(ND):
                    e.matmul(bank[:], wrt[k][:, o * P:(o + 1) * P],
                             Frt[k][:, n2 * 512:(n2 + 1) * 512], start=(k == 0), stop=False)
                    ins = e.matmul(bank[:], wit[k][:, o * P:(o + 1) * P],
                                   Fnt[k][:, n2 * 512:(n2 + 1) * 512],
                                   start=False, stop=(k == ND - 1))
                return ins
            def mmi(e, o=o, n2=n2, bank=bank_i):
                ins = None
                for k in range(ND):
                    e.matmul(bank[:], wrt[k][:, o * P:(o + 1) * P],
                             Fit[k][:, n2 * 512:(n2 + 1) * 512], start=(k == 0), stop=False)
                    ins = e.matmul(bank[:], wit[k][:, o * P:(o + 1) * P],
                                   Frt[k][:, n2 * 512:(n2 + 1) * 512],
                                   start=False, stop=(k == ND - 1))
                return ins
            bw = ([tev[(o - 2, n2, "r")], tev[(o - 2, n2, "i")]] if o >= 2 else [])
            t1 = b.emit("pe", mmr, waits=[(ld, nld)] + bw, inc=mm)
            t2 = b.emit("pe", mmi, waits=[(ld, nld)] + bw, inc=mm)
            tev[(o, n2, "r")] = b.emit("dve", lambda e, o=o, n2=n2, bank=bank_r:
                e.tensor_copy(ctr[o][:, n2 * 512:(n2 + 1) * 512], bank[:]),
                waits=[t1], inc=ev)
            tev[(o, n2, "i")] = b.emit("dve", lambda e, o=o, n2=n2, bank=bank_i:
                e.tensor_copy(cti[o][:, n2 * 512:(n2 + 1) * 512], bank[:]),
                waits=[t2], inc=ev)
    b.barrier()
    b.reset(m)
    m2 = b.mark()
    ld2 = b.sem("l2p_ld2")
    mm2 = b.sem("l2p_mm2")
    ev2 = b.sem("l2p_ev2")
    cstm = b.chan("l2p_st2", nslots=2)
    Grt = load_weight_chunks(b, cGr, "l2p_Gr", D, D, ld2)
    Gnt = load_weight_chunks(b, cGin, "l2p_Gn", D, D, ld2)
    mt = [b.sbuf(f"l2p_m{j}", [P, D], BF16) for j in range(2)]
    mtoks = {}
    tev2 = {}
    nld2 = ld2.n
    for n in range(ND):
        j = n % 2
        for m2i in range(2):
            bank = b.pb[(n % 2) * 2 + m2i]
            def mmm(e, n=n, m2i=m2i, bank=bank):
                ins = None
                for o in range(ND):
                    e.matmul(bank[:], ctr[o][:, n * P:(n + 1) * P],
                             Grt[o][:, m2i * 512:(m2i + 1) * 512], start=(o == 0), stop=False)
                    ins = e.matmul(bank[:], cti[o][:, n * P:(n + 1) * P],
                                   Gnt[o][:, m2i * 512:(m2i + 1) * 512],
                                   start=False, stop=(o == ND - 1))
                return ins
            bw = [tev2[(n - 2, m2i)]] if n >= 2 else []
            t1 = b.emit("pe", mmm, waits=[(ld2, nld2)] + bw, inc=mm2)
            sw = [mtoks[n - 2]] if n >= 2 else []
            tev2[(n, m2i)] = b.emit("dve", lambda e, j=j, m2i=m2i, bank=bank:
                e.tensor_copy(mt[j][:, m2i * 512:(m2i + 1) * 512], bank[:]),
                waits=[t1] + sw, inc=ev2)
        mtoks[n] = cstm.issue(m_scratch[n * P:(n + 1) * P, :], mt[j][:],
                              waits=[tev2[(n, 0)], tev2[(n, 1)]])
    b.barrier(toks=[mtoks[ND - 2], mtoks[ND - 1]])
    b.reset(m2)


def l2ffn_apply(b, m_scratch, bias_hbm, x_hbm, dst_hbm, tag, out_stats=None):
    mk = b.mark()
    ld = b.sem(f"{tag}_wld")
    mtiles = []
    for c in range(ND):
        t = b.sbuf(f"{tag}_m{c}", [P, D], BF16)
        b.dma("pool", t[:], m_scratch[c * P:(c + 1) * P, :], inc=ld)
        mtiles.append(t)
    bt = bias_row(b, bias_hbm, f"{tag}_b", D, ld)
    b.emit("pe", lambda e: e.wait_ge(ld.h, ld.n))
    linear_tm_epilogue(b, mtiles, bt, 0.5, x_hbm, dst_hbm, tag,
                       out_stats=out_stats)
    b.reset(mk)



def sublayer_l2ffn(b, ctx, x_hbm, dst_hbm, first):
    if first:
        l2ffn_precompute_M(b, ctx["L2_ffn_w_freq_real"], ctx["L2_ffn_w_freq_imag"],
                           ctx["m_scratch"])
    ln_stats(b, x_hbm, sem_pref=f"s{ctx['n']}")
    ln_apply_transpose(b, x_hbm, sem_pref=f"a{ctx['n']}")
    l2ffn_apply(b, ctx["m_scratch"], ctx["L2_ffn_bias"], x_hbm, dst_hbm,
                f"l2f{ctx['n']}")
    ctx["n"] += 1



def build_l2ffn_test(t_tokens):
    b = B("l2ffn", t_tokens)
    x_in = b.inp("x", [t_tokens, D], F32)
    wr = b.inp("L2_ffn_w_freq_real", [D, D], F32)
    wi = b.inp("L2_ffn_w_freq_imag", [D, D], F32)
    bias = b.inp("L2_ffn_bias", [D], F32)
    x_out = b.out("x_out", [t_tokens, D], F32)
    m_scratch = b.scratch("m_scratch", [D, D], BF16)
    b.res = Resident(b)
    ctx = {"L2_ffn_w_freq_real": wr, "L2_ffn_w_freq_imag": wi,
           "L2_ffn_bias": bias, "m_scratch": m_scratch, "n": 0}
    sublayer_l2ffn(b, ctx, x_in, x_out, first=True)
    return b


def build_l1ffn_test(t_tokens):
    b = B("l1ffn", t_tokens)
    x_in = b.inp("x", [t_tokens, D])
    x_out = b.out("x_out", [t_tokens, D])
    b.res = Resident(b)
    ctx = {
        "L1_ffn_qweights": b.inp("L1_ffn_qweights", [6, D, D], I32),
        "L1_ffn_scales": b.inp("L1_ffn_scales", [6, D, 64]),
        "L1_ffn_biases": b.inp("L1_ffn_biases", [6, D]),
        "L1_ffn_lora_a": b.inp("L1_ffn_lora_a", [6, 32, D]),
        "L1_ffn_lora_b": b.inp("L1_ffn_lora_b", [6, D, 32]),
        "L1_ffn_ln_gamma": b.inp("L1_ffn_ln_gamma", [3, D]),
        "L1_ffn_ln_beta": b.inp("L1_ffn_ln_beta", [3, D]),
        "l1f_h": b.scratch("l1f_h", [t_tokens, D], F32),
        "l1f_y": b.scratch("l1f_y", [t_tokens, D], F32),
        "n": 0,
    }
    ln_stats(b, x_in, "s0")
    ln_apply_transpose(b, x_in, "a0")
    sublayer_l1ffn(b, ctx, x_in, x_out)
    return b


def build_l0ffn_test(t_tokens):
    b = B("l0ffn", t_tokens)
    x_in = b.inp("x", [t_tokens, D])
    x_out = b.out("x_out", [t_tokens, D])
    b.res = Resident(b)
    ctx = {
        "L0_ffn_W_base": b.inp("L0_ffn_W_base", [D, D]),
        "L0_ffn_b_base": b.inp("L0_ffn_b_base", [D]),
        "L0_ffn_Wd1": b.inp("L0_ffn_Wd1", [8, D, 256]),
        "L0_ffn_bd1": b.inp("L0_ffn_bd1", [8, 256]),
        "L0_ffn_Wd2": b.inp("L0_ffn_Wd2", [8, 256, D]),
        "L0_ffn_bd2": b.inp("L0_ffn_bd2", [8, D]),
        "L0_ffn_Wa_down": b.inp("L0_ffn_Wa_down", [8, D, 64]),
        "L0_ffn_ba_down": b.inp("L0_ffn_ba_down", [8, 64]),
        "L0_ffn_Wa_up": b.inp("L0_ffn_Wa_up", [8, 64, D]),
        "L0_ffn_ba_up": b.inp("L0_ffn_ba_up", [8, D]),
        "n": 0,
    }
    ln_stats(b, x_in, "s0")
    ln_apply_transpose(b, x_in, "a0")
    sublayer_l0ffn(b, ctx, x_in, x_out)
    return b


def build_l2attn_test(t_tokens):
    b = B("l2attn", t_tokens)
    x_in = b.inp("x", [t_tokens, D])
    x_out = b.out("x_out", [t_tokens, D])
    wqkv = b.inp("L2_attn_W_qkv", [D, 3 * D])
    bqkv = b.inp("L2_attn_b_qkv", [3 * D])
    wproj = b.inp("L2_attn_W_proj", [D, D])
    bproj = b.inp("L2_attn_b_proj", [D])
    qnw = b.inp("L2_attn_qn_w", [64])
    knw = b.inp("L2_attn_kn_w", [64])
    b.res = Resident(b)
    ctx = {
        "at_q": b.scratch("at_q", [t_tokens, D], BF16),
        "at_k": b.scratch("at_k", [t_tokens, D], BF16),
        "at_qf": b.scratch("at_qf", [t_tokens, D], BF16),
        "at_o": b.scratch("at_o", [t_tokens, D], BF16),
        "n": 0,
    }
    ln_stats(b, x_in, "s0")
    ln_apply_transpose(b, x_in, "a0")
    qnb = b.sbuf("l2qnb", [P, D], F32)
    knb = b.sbuf("l2knb", [P, D], F32)
    bcast_row_tile(b, "l2qn", qnw, 64, 16, qnb)
    bcast_row_tile(b, "l2kn", knw, 64, 16, knb)
    attn_softmax(b, ctx, "a", x_in, x_out, wqkv, wqkv, wqkv, wproj,
                 bqkv[0:D], bqkv[D:2 * D], bqkv[2 * D:3 * D], bproj,
                 transform_q=("rmsnorm", qnb), transform_k=("rmsnorm", knb),
                 coef=1.0, qkv_col0=(0, D, 2 * D))
    return b


def build_l3attn_test(t_tokens):
    b = B("l3a", t_tokens)
    x_in = b.inp("x", [t_tokens, D])
    x_out = b.out("x_out", [t_tokens, D])
    ctx = {"n": 0,
           "L3_attn_Wq": b.inp("L3_attn_Wq", [D, D]),
           "L3_attn_bq": b.inp("L3_attn_bq", [D]),
           "L3_attn_Wk": b.inp("L3_attn_Wk", [D, D]),
           "L3_attn_bk": b.inp("L3_attn_bk", [D]),
           "L3_attn_Wv": b.inp("L3_attn_Wv", [D, D]),
           "at_q": b.scratch("at_q", [t_tokens, D], BF16),
           "at_k": b.scratch("at_k", [t_tokens, D], BF16),
           "at_qf": b.scratch("at_qf", [t_tokens, D], BF16)}
    b.res = Resident(b)
    ctx["ret_consts"] = retention_consts(b)
    ln_stats(b, x_in, "s0")
    ln_apply_transpose(b, x_in, "a0")
    sublayer_l3attn(b, ctx, x_in, x_out)
    return b


def build_l3ffn_test(t_tokens):
    b = B("l3f", t_tokens)
    x_in = b.inp("x", [t_tokens, D])
    x_out = b.out("x_out", [t_tokens, D])
    ctx = {"n": 0}
    for nm, shp, dt in MODULE_INPUTS:
        if nm.startswith("L3_ffn"):
            ctx[nm] = b.inp(nm, shp, dt)
    for nm, shp, dt in (("at_q", [t_tokens, D], BF16),
                        ("at_k", [t_tokens, D], BF16),
                        ("at_qf", [t_tokens, D], BF16),
                        ("at_o", [t_tokens, D], BF16),
                        ("f3_x2", [t_tokens, D], F32),
                        ("f3_acc", [t_tokens, D], F32),
                        ("f3_xh2", [D, t_tokens], BF16),
                        ("f3_xh2lo", [D, t_tokens], BF16),
                        ("f3_xln", [t_tokens, D], F32)):
        ctx[nm] = b.scratch(nm, shp, dt)
    b.res = Resident(b)
    ln_stats(b, x_in, "s0")
    sublayer_l3ffn(b, ctx, x_in, x_out)
    return b


def build_l1attn_test(t_tokens):
    b = B("l1a", t_tokens)
    x_in = b.inp("x", [t_tokens, D])
    x_out = b.out("x_out", [t_tokens, D])
    ctx = {"n": 0,
           "L1_attn_wq": b.inp("L1_attn_wq", [D, D]),
           "L1_attn_wk": b.inp("L1_attn_wk", [D, D]),
           "L1_attn_wv": b.inp("L1_attn_wv", [D, D]),
           "L1_attn_wo": b.inp("L1_attn_wo", [D, D]),
           "at_q": b.scratch("at_q", [t_tokens, D], BF16),
           "at_k": b.scratch("at_k", [t_tokens, D], BF16),
           "at_qf": b.scratch("at_qf", [t_tokens, D], BF16),
           "at_o": b.scratch("at_o", [t_tokens, D], BF16)}
    b.res = Resident(b)
    ctri_h, _ = mask_consts(b)
    crc, crs = rope_consts(b, t_tokens)
    mtri = b.sbuf("m_tri", [P, P], F32)
    ropec = b.sbuf("m_ropec", [P, b.NT * 32], F32)
    ropes = b.sbuf("m_ropes", [P, b.NT * 32], F32)
    ldm = b.sem("mod_ld")
    b.dma("sp", mtri[:], ctri_h, inc=ldm)
    b.dma("sp", ropec[:], crc, inc=ldm)
    b.dma("sp", ropes[:], crs, inc=ldm)
    b.emit("sp", lambda e, v=ldm.n: e.wait_ge(ldm.h, v))
    b.barrier()
    ln_stats(b, x_in, "s0")
    ln_apply_transpose(b, x_in, "a0")
    attn_softmax(b, ctx, "a", x_in, x_out, ctx["L1_attn_wq"], ctx["L1_attn_wk"],
                 ctx["L1_attn_wv"], ctx["L1_attn_wo"], None, None, None, None,
                 transform_q=("rope", ropec, ropes),
                 transform_k=("rope", ropec, ropes),
                 causal=True, diagmask=mtri, coef=1.0, wlayout="od")
    return b


def rope_consts(b, t_tokens):
    NT = t_tokens // P
    inv = 1.0 / (10000.0 ** (np.arange(0, 64, 2, dtype=np.float64) / 64.0))
    pos = np.arange(t_tokens, dtype=np.float64)
    fr = pos[:, None] * inv[None, :]
    cs = np.cos(fr).reshape(NT, P, 32).transpose(1, 0, 2).reshape(P, NT * 32)
    sn = np.sin(fr).reshape(NT, P, 32).transpose(1, 0, 2).reshape(P, NT * 32)
    return b.const("c_ropec", f32c(cs)), b.const("c_ropes", f32c(sn))


def mask_consts(b):
    kp = np.arange(P)[:, None]
    qf = np.arange(P)[None, :]
    tri = np.where(qf >= kp, 0.0, -2.0e38)
    low = np.where(kp >= qf, 0.0, -2.0e38)
    return b.const("c_mtri", f32c(tri)), b.const("c_mlow", f32c(low))


MODULE_INPUTS = [
    ("L0_attn_Wq", [D, D], F32), ("L0_attn_bq", [D], F32),
    ("L0_attn_Wk", [D, D], F32), ("L0_attn_bk", [D], F32),
    ("L0_attn_Wv", [D, D], F32), ("L0_attn_bv", [D], F32),
    ("L0_attn_Wo", [D, D], F32), ("L0_attn_bo", [D], F32),
    ("L0_ffn_W_base", [D, D], F32), ("L0_ffn_b_base", [D], F32),
    ("L0_ffn_Wd1", [8, D, 256], F32), ("L0_ffn_bd1", [8, 256], F32),
    ("L0_ffn_Wd2", [8, 256, D], F32), ("L0_ffn_bd2", [8, D], F32),
    ("L0_ffn_Wa_down", [8, D, 64], F32), ("L0_ffn_ba_down", [8, 64], F32),
    ("L0_ffn_Wa_up", [8, 64, D], F32), ("L0_ffn_ba_up", [8, D], F32),
    ("L1_attn_wq", [D, D], F32), ("L1_attn_wk", [D, D], F32),
    ("L1_attn_wv", [D, D], F32), ("L1_attn_wo", [D, D], F32),
    ("L1_ffn_qweights", [6, D, D], I32), ("L1_ffn_scales", [6, D, 64], F32),
    ("L1_ffn_biases", [6, D], F32), ("L1_ffn_lora_a", [6, 32, D], F32),
    ("L1_ffn_lora_b", [6, D, 32], F32), ("L1_ffn_ln_gamma", [3, D], F32),
    ("L1_ffn_ln_beta", [3, D], F32),
    ("L2_attn_W_qkv", [D, 3 * D], F32), ("L2_attn_b_qkv", [3 * D], F32),
    ("L2_attn_W_proj", [D, D], F32), ("L2_attn_b_proj", [D], F32),
    ("L2_attn_qn_w", [64], F32), ("L2_attn_kn_w", [64], F32),
    ("L2_ffn_w_freq_real", [D, D], F32), ("L2_ffn_w_freq_imag", [D, D], F32),
    ("L2_ffn_bias", [D], F32),
    ("L3_attn_Wq", [D, D], F32), ("L3_attn_bq", [D], F32),
    ("L3_attn_Wk", [D, D], F32), ("L3_attn_bk", [D], F32),
    ("L3_attn_Wv", [D, D], F32),
    ("L3_ffn_ln1_g", [D], F32), ("L3_ffn_ln1_b", [D], F32),
    ("L3_ffn_ln2_g", [D], F32), ("L3_ffn_ln2_b", [D], F32),
    ("L3_ffn_wq", [D, D], F32), ("L3_ffn_bq", [D], F32),
    ("L3_ffn_wk", [D, D], F32), ("L3_ffn_bk", [D], F32),
    ("L3_ffn_wv", [D, D], F32), ("L3_ffn_bv", [D], F32),
    ("L3_ffn_wo", [D, D], F32), ("L3_ffn_bo", [D], F32),
    ("L3_ffn_gate_w", [D, 8], F32), ("L3_ffn_gate_b", [8], F32),
    ("L3_ffn_w1", [8, D, 4096], F32), ("L3_ffn_b1", [8, 4096], F32),
    ("L3_ffn_w2", [8, 4096, D], F32), ("L3_ffn_b2", [8, D], F32),
]


def build_module(t_tokens):
    b = B("mk", t_tokens)
    x_in = b.inp("L0_attn_x", [t_tokens, D])
    out = b.out("out", [t_tokens, D])
    ctx = {"n": 0}
    for nm, shp, dt in MODULE_INPUTS:
        ctx[nm] = b.inp(nm, shp, dt)
    xs = b.scratch("x_state", [t_tokens, D], F32)
    for nm, shp, dt in (("at_q", [t_tokens, D], BF16),
                        ("at_k", [t_tokens, D], BF16),
                        ("at_qf", [t_tokens, D], BF16),
                        ("at_o", [t_tokens, D], BF16),
                        ("l1f_h", [t_tokens, D], F32),
                        ("l1f_y", [t_tokens, D], F32),
                        ("m_scratch", [D, D], BF16),
                        ("f3_x2", [t_tokens, D], F32),
                        ("f3_acc", [t_tokens, D], F32),
                        ("f3_xh2", [D, t_tokens], BF16),
                        ("f3_xh2lo", [D, t_tokens], BF16),
                        ("f3_xln", [t_tokens, D], F32)):
        ctx[nm] = b.scratch(nm, shp, dt)
    b.res = Resident(b)
    ctx["ret_consts"] = retention_consts(b)
    ctri_h, clow_h = mask_consts(b)
    crc, crs = rope_consts(b, t_tokens)

    def load_tiles(specs):
        ldm = b.sem("mod_ld")
        outs = []
        for name, shape, href in specs:
            t = b.sbuf(name + str(b.uid()), shape, F32)
            b.dma("sp", t[:], href, inc=ldm)
            outs.append(t)
        b.emit("sp", lambda e, v=ldm.n: e.wait_ge(ldm.h, v))
        b.barrier()
        return outs

    cur = x_in
    res = b.res
    OS = (res.ssum, res.ssq)

    def pre(first=False):
        if first:
            ln_stats(b, cur, "MS")
        else:
            stats_finalize(b, res.ssum, res.ssq, res.mu, res.rstd, "MSF")
        ln_apply_transpose(b, cur, "MA")

    pre(first=True)
    sublayer_l0ffn(b, ctx, cur, xs, out_stats=OS)
    cur = xs
    pre()
    mk0 = b.mark()
    mtri, mlow = load_tiles([("m_tri", [P, P], ctri_h),
                             ("m_low", [P, P], clow_h)])
    attn_softmax(b, ctx, "a0", cur, cur, ctx["L0_attn_Wq"], ctx["L0_attn_Wk"],
                 ctx["L0_attn_Wv"], ctx["L0_attn_Wo"], ctx["L0_attn_bq"],
                 ctx["L0_attn_bk"], ctx["L0_attn_bv"], ctx["L0_attn_bo"],
                 window_tiles=2, diagmask={0: mtri, 2: mlow}, coef=1.0,
                 out_stats=OS)
    b.reset(mk0)
    pre()
    sublayer_l0ffn(b, ctx, cur, cur, out_stats=OS)
    pre()
    sublayer_l1ffn(b, ctx, cur, cur, out_stats=OS)
    pre()
    mk1 = b.mark()
    mtri, ropec, ropes = load_tiles([
        ("m_tri", [P, P], ctri_h),
        ("m_ropec", [P, b.NT * 32], crc), ("m_ropes", [P, b.NT * 32], crs)])
    attn_softmax(b, ctx, "a1", cur, cur, ctx["L1_attn_wq"], ctx["L1_attn_wk"],
                 ctx["L1_attn_wv"], ctx["L1_attn_wo"], None, None, None, None,
                 transform_q=("rope", ropec, ropes),
                 transform_k=("rope", ropec, ropes),
                 causal=True, diagmask=mtri, coef=1.0, wlayout="od",
                 out_stats=OS)
    b.reset(mk1)
    pre()
    sublayer_l1ffn(b, ctx, cur, cur, out_stats=OS)
    l2ffn_precompute_M(b, ctx["L2_ffn_w_freq_real"], ctx["L2_ffn_w_freq_imag"],
                       ctx["m_scratch"])
    pre()
    l2ffn_apply(b, ctx["m_scratch"], ctx["L2_ffn_bias"], cur, cur, "l2fA",
                out_stats=OS)
    pre()
    mk2 = b.mark()
    qnb = b.sbuf("m_qnb", [P, D], F32)
    knb = b.sbuf("m_knb", [P, D], F32)
    bcast_row_tile(b, "l2qn", ctx["L2_attn_qn_w"], 64, 16, qnb)
    bcast_row_tile(b, "l2kn", ctx["L2_attn_kn_w"], 64, 16, knb)
    attn_softmax(b, ctx, "a2", cur, cur, ctx["L2_attn_W_qkv"],
                 ctx["L2_attn_W_qkv"], ctx["L2_attn_W_qkv"],
                 ctx["L2_attn_W_proj"], ctx["L2_attn_b_qkv"][0:D],
                 ctx["L2_attn_b_qkv"][D:2 * D], ctx["L2_attn_b_qkv"][2 * D:3 * D],
                 ctx["L2_attn_b_proj"],
                 transform_q=("rmsnorm", qnb), transform_k=("rmsnorm", knb),
                 coef=1.0, qkv_col0=(0, D, 2 * D), out_stats=OS)
    b.reset(mk2)
    pre()
    l2ffn_apply(b, ctx["m_scratch"], ctx["L2_ffn_bias"], cur, cur, "l2fB",
                out_stats=OS)
    stats_finalize(b, res.ssum, res.ssq, res.mu, res.rstd, "MSF")
    sublayer_l3ffn(b, ctx, cur, cur, out_stats=OS)
    pre()
    sublayer_l3attn(b, ctx, cur, cur, out_stats=OS)
    stats_finalize(b, res.ssum, res.ssq, res.mu, res.rstd, "MSF")
    sublayer_l3ffn(b, ctx, cur, out)
    return b


def kernel(**inputs):
    global SIM_OPEN_BANKS
    SIM_OPEN_BANKS = False
    t_tokens = 4096
    b = build_module(t_tokens)
    nc = b.finalize()
    base = {}
    for nm, shp, dt in MODULE_INPUTS:
        base[nm] = np.ascontiguousarray(inputs[nm])
    base.update(b.consts)
    in_maps = []
    for core in range(NCORES):
        m = dict(base)
        m["L0_attn_x"] = np.ascontiguousarray(inputs["L0_attn_x"][core])
        in_maps.append(m)
    res = run_bass_kernel_spmd(nc, in_maps, core_ids=list(range(NCORES)))
    return np.stack([np.asarray(r["out"]).astype(np.float32)
                     for r in res.results])


# version 12
# speedup vs baseline: 1.2377x; 1.0360x over previous
import math
from contextlib import ExitStack

import numpy as np
import ml_dtypes

import concourse.bass as bass
import concourse.mybir as mybir
from concourse.bass_utils import run_bass_kernel_spmd

F32 = mybir.dt.float32
BF16 = mybir.dt.bfloat16
I32 = mybir.dt.int32
AF = mybir.ActivationFunctionType
ALU = mybir.AluOpType
AX = mybir.AxisListType

P = 128
D = 1024
ND = D // P
NCORES = 8
LN_EPS = 1e-5

ENGS = ("pe", "act", "dve", "pool", "sp")

SIM_OPEN_BANKS = True


def bfc(x):
    return np.ascontiguousarray(x, dtype=ml_dtypes.bfloat16)


def f32c(x):
    return np.ascontiguousarray(x, dtype=np.float32)


class Sem:
    __slots__ = ("h", "n", "name")

    def __init__(self, h, name):
        self.h = h
        self.n = 0
        self.name = name

    def bump(self, k=1):
        self.n += k
        assert self.n < 2 ** 31
        return self.n


class B:

    def __init__(self, name, t_tokens):
        self.T = t_tokens
        self.NT = t_tokens // P
        self.nc = bass.Bass("TRN2", target_bir_lowering=False)
        self.stack = ExitStack()
        self.ops = {e: [] for e in ENGS}
        self._sems = {}
        self._sem_i = 0
        self.consts = {}
        self._sbuf_mark = None
        self._resident_top = None
        self.pb = [self.nc.alloc_psum_tensor(f"pb{i}", [P, 512], F32)
                   for i in range(3)]
        self.pbS = self.nc.alloc_psum_tensor("pbS", [P, 2048], F32)
        for i in range(4):
            self.pb.append(PBView(self.pbS.ap()[:, i * 512:(i + 1) * 512]))
        self.pb.append(self.nc.alloc_psum_tensor("pb7", [P, 512], F32))
        self.ph = self.sem("phase")
        self._uid = 0

    def uid(self):
        self._uid += 1
        return self._uid

    def sem(self, name):
        cname = "".join(ch for ch in name if not ch.isdigit())
        if cname not in self._sems:
            h = self.stack.enter_context(self.nc.semaphore(f"s_{cname}"))
            self._sems[cname] = Sem(h, cname)
        return self._sems[cname]

    def sbuf(self, name, shape, dtype):
        return self.nc.alloc_sbuf_tensor(name, shape, dtype)

    def mark(self):
        return self.nc.sbuf_base

    def reset(self, m):
        self.nc.sbuf_base = m

    def const(self, name, arr):
        if name not in self.consts:
            self.consts[name] = np.ascontiguousarray(arr)
            dt = {np.dtype(np.float32): F32, np.dtype(ml_dtypes.bfloat16): BF16,
                  np.dtype(np.int32): I32}[arr.dtype]
            self._decl_input(name, list(arr.shape), dt)
        return self._inputs[name]

    def _decl_input(self, name, shape, dt):
        if not hasattr(self, "_inputs"):
            self._inputs = {}
        self._inputs[name] = self.nc.dram_tensor(name, shape, dt, kind="ExternalInput").ap()
        return self._inputs[name]

    def inp(self, name, shape, dt=F32):
        if not hasattr(self, "_inputs"):
            self._inputs = {}
        if name not in self._inputs:
            self._decl_input(name, shape, dt)
        return self._inputs[name]

    def out(self, name, shape, dt=F32):
        return self.nc.dram_tensor(name, shape, dt, kind="ExternalOutput").ap()

    def scratch(self, name, shape, dt):
        return self.nc.dram_tensor(name, shape, dt, kind="Internal").ap()

    def emit(self, eng, fn, waits=(), inc=None, inck=1):
        wl = tuple((s, (s.n if v is None else v)) for s, v in waits)
        if inc is not None:
            inc.bump(inck)
        def closure(e, fn=fn, wl=wl, inc=inc, inck=inck):
            for s, v in wl:
                if v > 0:
                    e.wait_ge(s.h, v)
            ins = fn(e)
            if inc is not None:
                assert ins is not None
                ins.then_inc(inc.h, inck)
        self.ops[eng].append(closure)
        if inc is not None:
            return (inc, inc.n)
        return None

    def dma(self, eng, dst, src, waits=(), inc=None):
        self.emit(eng, lambda e, d=dst, s=src: e.dma_start(out=d, in_=s),
                  waits=waits, inc=inc, inck=16)
        if inc is not None:
            return (inc, inc.n)
        return None

    def chan(self, name, eng="sp", nslots=2):
        if not hasattr(self, "_chans"):
            self._chans = {}
        key = (eng, nslots)
        if key not in self._chans:
            self._chans[key] = Chan(self, f"gch_{eng}_{chr(107 + nslots)}",
                                    eng, nslots)
        return self._chans[key]


    def wait_toks(self, eng, toks):
        for s, v in toks:
            self.emit(eng, lambda e, s=s, v=v: e.wait_ge(s.h, v))

    def barrier(self, toks=()):
        self.wait_toks("sp", toks)
        k = self.ph.n + len(ENGS)
        for eng in ENGS:
            self.emit(eng, lambda e: e.drain(), inc=self.ph)
        for eng in ENGS:
            self.emit(eng, lambda e, k=k: e.wait_ge(self.ph.h, k))

    def finalize(self):
        nc = self.nc
        with nc.Block() as block:
            block.tensor(lambda t: [f(t) for f in self.ops["pe"]])
            block.scalar(lambda s: [f(s) for f in self.ops["act"]])
            block.vector(lambda v: [f(v) for f in self.ops["dve"]])
            block.gpsimd(lambda g: [f(g) for f in self.ops["pool"]])
            block.sync(lambda q: [f(q) for f in self.ops["sp"]])
        self.stack.close()
        return nc


class PBView:

    def __init__(self, ap_):
        self._ap = ap_

    def __getitem__(self, k):
        return self._ap[k]

    def ap(self):
        return self._ap


class Chan:

    def __init__(self, b, name, eng, nslots):
        self.b = b
        self.eng = eng
        self.sems = [b.sem(f"{name}_c{chr(97 + s)}") for s in range(nslots)]
        self.last = [None] * nslots
        self.k = 0

    def issue(self, dst, src, waits=()):
        si = self.k % len(self.sems)
        s = self.sems[si]
        w = list(waits)
        if self.last[si] is not None:
            w.append(self.last[si])
        tok = self.b.dma(self.eng, dst, src, waits=w, inc=s)
        self.last[si] = tok
        self.k += 1
        return tok


def ident_const(b):
    return b.const("c_ident", np.eye(P, dtype=ml_dtypes.bfloat16))


def ones_row_const(b):
    return b.const("c_ones_row", np.ones((1, P), dtype=ml_dtypes.bfloat16))


class Resident:

    def __init__(self, b):
        self.ident = b.sbuf("r_ident", [P, P], BF16)
        self.identf = b.sbuf("r_identf", [P, P], F32)
        self.ones = b.sbuf("r_ones", [1, P], BF16)
        self.onesw = b.sbuf("r_onesw", [1, 512], BF16)
        self.zrow = b.sbuf("r_zrow", [1, 512], BF16)
        self.mu = b.sbuf("r_mu", [P, b.NT], F32)
        self.rstd = b.sbuf("r_rstd", [P, b.NT], F32)
        self.ssum = b.sbuf("r_ssum", [P, b.NT], F32)
        self.ssq = b.sbuf("r_ssq", [P, b.NT], F32)
        self.xT = [b.sbuf(f"r_xT{c}", [P, max(b.T, 4096)], BF16)
                   for c in range(ND)]
        ld = b.sem("res_ld")
        b.dma("sp", self.ident[:], ident_const(b), inc=ld)
        b.dma("sp", self.identf[:], b.const("c_identf", np.eye(P, dtype=np.float32)), inc=ld)
        b.dma("sp", self.ones[:], ones_row_const(b), inc=ld)
        b.dma("sp", self.onesw[:], b.const("c_onesw", np.ones((1, 512), dtype=ml_dtypes.bfloat16)), inc=ld)
        b.dma("sp", self.zrow[:], b.const("c_zrow", np.zeros((1, 512), dtype=ml_dtypes.bfloat16)), inc=ld)
        b.emit("sp", lambda e: e.wait_ge(ld.h, 80))
        b.barrier()


def ln_stats(b, src_hbm, sem_pref="st"):
    res = b.res
    NT = b.NT
    m = b.mark()
    u = b.uid()
    xa = [b.sbuf(f"ls{u}_x{j}", [P, D], F32) for j in range(2)]
    sq = b.sbuf(f"ls{u}_sq", [P, D], F32)
    sums = b.sbuf(f"ls{u}_sums", [P, NT], F32)
    sumsq = b.sbuf(f"ls{u}_sumsq", [P, NT], F32)
    var = b.sbuf(f"ls{u}_var", [P, NT], F32)
    ch = b.chan("ld")
    dna = b.sem("lns_dna")
    dnd = b.sem("lns_dnd")
    ta, td = {}, {}
    for i in range(NT):
        j = i % 2
        w = [ta[i - 2], td[i - 2]] if i >= 2 else []
        tok = ch.issue(xa[j][:], src_hbm[i * P:(i + 1) * P, :], waits=w)
        ta[i] = b.emit("act", lambda e, j=j, i=i: e.activation(
            sq[:], xa[j][:], AF.Square, accum_out=sumsq[:, i:i + 1]),
            waits=[tok] + ([ta[i - 1]] if i >= 1 else []), inc=dna)
        td[i] = b.emit("dve", lambda e, j=j, i=i: e.reduce_sum(
            sums[:, i:i + 1], xa[j][:], axis=AX.X),
            waits=[tok], inc=dnd)
    st = b.sem("lns_fin")
    t = b.emit("dve", lambda e: e.tensor_scalar(res.mu[:], sums[:], 1.0 / D, None,
                                                ALU.mult),
               waits=[ta[NT - 1], td[NT - 1]], inc=st)
    t = b.emit("dve", lambda e: e.tensor_scalar(var[:], sumsq[:], 1.0 / D, None,
                                                ALU.mult), waits=[t], inc=st)
    t = b.emit("dve", lambda e: e.tensor_tensor(sq[:, 0:NT], res.mu[:], res.mu[:],
                                                ALU.mult), waits=[t], inc=st)
    t = b.emit("dve", lambda e: e.tensor_sub(var[:], var[:], sq[:, 0:NT]),
               waits=[t], inc=st)
    t = b.emit("dve", lambda e: e.tensor_scalar(var[:], var[:], float(LN_EPS), None,
                                                ALU.add), waits=[t], inc=st)
    t = b.emit("act", lambda e: e.activation(var[:], var[:], AF.Sqrt),
               waits=[t], inc=st)
    b.emit("dve", lambda e: e.reciprocal(res.rstd[:], var[:]), waits=[t], inc=st)
    b.barrier()
    b.reset(m)


def to_fm(b, src_hbm, dstT, sem_pref, mu=None, rstd=None, evac=None,
          col_off=0, src_bf16=False):
    res = b.res
    NT = b.NT
    m = b.mark()
    sem_pref = f"{sem_pref}{b.uid()}"
    xa = [b.sbuf(f"{sem_pref}_x{j}", [P, D], BF16 if src_bf16 else F32)
          for j in range(2)]
    xh = xa if src_bf16 else [b.sbuf(f"{sem_pref}_h{j}", [P, D], BF16)
                              for j in range(2)]
    ch = b.chan("ld")
    ap = b.sem("tofm_ap")
    tr = b.sem("tofm_tr")
    ev = b.sem("tofm_ev")
    tap, ttr, tev = {}, {}, {}
    for i in range(NT):
        j = i % 2
        w = ([tap[i - 2]] if not src_bf16 else
             [ttr[(i - 2, ND - 1)]]) if i >= 2 else []
        tok = ch.issue(xa[j][:], src_hbm[i * P:(i + 1) * P, :], waits=w)
        hw = [ttr[(i - 1, ND - 1)]] if i >= 1 else []
        if src_bf16:
            tap[i] = tok
        elif mu is not None:
            tap[i] = b.emit("dve", lambda e, j=j, i=i: e.tensor_scalar(
                xh[j][:], xa[j][:], mu[:, i:i + 1], rstd[:, i:i + 1],
                ALU.subtract, ALU.mult), waits=[tok] + hw, inc=ap)
        else:
            tap[i] = b.emit("dve", lambda e, j=j: e.tensor_copy(xh[j][:], xa[j][:]),
                            waits=[tok] + hw, inc=ap)
        for c in range(ND):
            bank = b.pb[c % 4].ap().bitcast(BF16)
            prev = (i, c - 4) if c >= 4 else ((i - 1, c + 4) if i >= 1 else None)
            bw = [tev[prev]] if prev else []
            ttr[(i, c)] = b.emit("pe", lambda e, j=j, c=c, bank=bank: e.transpose(
                bank[:, 0:P], xh[j][:, c * P:(c + 1) * P], res.ident[:]),
                waits=[tap[i]] + bw, inc=tr)
            dst = dstT[c][:, col_off + i * P:col_off + (i + 1) * P]
            if evac is None:
                eng_ = "dve" if c % 2 == 0 else "act"
                evs = ev if c % 2 == 0 else b.sem("tofm_eva")
                tev[(i, c)] = b.emit(eng_, lambda e, dst=dst, bank=bank:
                    (e.tensor_copy(dst, bank[:, 0:P])
                     if hasattr(e, "tensor_copy") else
                     e.copy(dst, bank[:, 0:P])),
                    waits=[ttr[(i, c)]] + ([tev[(i, c - 2)]] if c >= 2 else
                                           ([tev[(i - 1, c + 6)]] if i >= 1 else [])),
                    inc=evs)
            elif evac[0] == "gelu":
                kind, gam, bet = evac
                tev[(i, c)] = b.emit("act", lambda e, dst=dst, bank=bank, c=c:
                    e.activation(dst, bank[:, 0:P], AF.Gelu,
                                 bias=bet[:, c:c + 1], scale=gam[:, c:c + 1]),
                    waits=[ttr[(i, c)]], inc=ev)
            else:
                kind, gam, bet = evac
                tev[(i, c)] = b.emit("dve", lambda e, dst=dst, bank=bank, c=c:
                    e.tensor_scalar(dst, bank[:, 0:P], gam[:, c:c + 1],
                                    bet[:, c:c + 1], ALU.mult, ALU.add),
                    waits=[ttr[(i, c)]], inc=ev)
    b.barrier()
    b.reset(m)


def ln_apply_transpose(b, src_hbm, sem_pref="la"):
    to_fm(b, src_hbm, b.res.xT, sem_pref, mu=b.res.mu, rstd=b.res.rstd)


def load_weight_chunks(b, w_hbm, name, rows, cols, sem, eng="pool"):
    nr = (rows + P - 1) // P
    tiles = []
    for c in range(nr):
        r0, r1 = c * P, min((c + 1) * P, rows)
        t = b.sbuf(f"{name}_{c}", [r1 - r0, cols], BF16)
        b.dma(eng, t[:], w_hbm[r0:r1, :], inc=sem)
        tiles.append(t)
    return tiles


def bias_row(b, bias_hbm, name, cols, sem):
    t = b.sbuf(name, [1, cols], BF16)
    b.dma("pool", t[:], bias_hbm.rearrange("(a c) -> a c", a=1), inc=sem)
    return t



def linear_tm(b, pref, xT, w_tiles, bias_tile, epi, kchunks=None, n_out=D,
              extra_mm=None, xt_off=0):
    res = b.res
    NT = b.NT
    kchunks = list(range(len(xT))) if kchunks is None else list(kchunks)
    NH = n_out // 512
    mm = b.sem("lin_mm")
    evtoks = {}
    for i in range(NT):
        for n in range(NH):
            bank = b.pb[(i % 2) * NH + n]
            def mmf(e, i=i, n=n, bank=bank):
                ins = None
                last_plain = (extra_mm is None and bias_tile is None)
                for ci, c in enumerate(kchunks):
                    ins = e.matmul(bank[:], xT[c][:, xt_off + i * P:xt_off + (i + 1) * P],
                                   w_tiles[c][:, n * 512:(n + 1) * 512],
                                   start=(ci == 0),
                                   stop=(last_plain and ci == len(kchunks) - 1))
                if extra_mm is not None:
                    ins = extra_mm(e, bank, i, n, stop=(bias_tile is None)) or ins
                if bias_tile is not None:
                    ins = e.matmul(bank[:], res.ones[0:1, 0:P],
                                   bias_tile[0:1, n * 512:(n + 1) * 512],
                                   start=False, stop=True)
                return ins
            bw = [evtoks[(i - 2, n)]] if i >= 2 else []
            mmtok = b.emit("pe", mmf, waits=bw, inc=mm)
            evtoks[(i, n)] = epi(i, n, bank, mmtok)
    return evtoks


class ResidualEpi:

    def __init__(self, b, pref, x_hbm, dst_hbm, coef, n_out=D, stats=None):
        self.b = b
        self.pref = pref
        self.x_hbm = x_hbm
        self.dst_hbm = dst_hbm
        self.coef = coef
        self.NH = n_out // 512
        self.n_out = n_out
        self.stats = stats
        self.xa = [b.sbuf(f"{pref}_exa{j}", [P, n_out], F32) for j in range(2)]
        self.xn = [b.sbuf(f"{pref}_exn{j}", [P, n_out], F32) for j in range(2)]
        self.sq = b.sbuf(f"{pref}_esq", [P, n_out], F32) if stats is not None else None
        self.ch = b.chan("eld")
        self.cst = b.chan("est", eng="act", nslots=2)
        self.dnd = b.sem("epi_dnd")
        self.dna = b.sem("epi_dna")
        self.ev = b.sem("epi_ev")
        self.sttoks = {}
        self.ldtoks = {}
        self.evtoks = {}
        self.statoks = {}

    def __call__(self, i, n, bank, mmtok):
        b = self.b
        j = i % 2
        NH = self.NH
        if n == 0:
            w = [self.evtoks[(i - 2, nn)] for nn in range(NH)] if i >= 2 else []
            self.ldtoks[i] = self.ch.issue(
                self.xa[j][:], self.x_hbm[i * P:(i + 1) * P, 0:self.n_out], waits=w)
        sw = [self.sttoks[i - 2]] if i >= 2 else []
        if i >= 2 and self.stats is not None:
            sw.append(self.statoks[i - 2])
        t = b.emit("dve", lambda e, j=j, n=n, bank=bank: e.scalar_tensor_tensor(
            self.xn[j][:, n * 512:(n + 1) * 512], bank[:], float(self.coef),
            self.xa[j][:, n * 512:(n + 1) * 512], ALU.mult, ALU.add),
            waits=[mmtok, self.ldtoks[i]] + sw, inc=self.ev)
        self.evtoks[(i, n)] = t
        if n == NH - 1:
            self.sttoks[i] = self.cst.issue(
                self.dst_hbm[i * P:(i + 1) * P, 0:self.n_out], self.xn[j][:],
                waits=[t])
            if self.stats is not None:
                sums, sumsq = self.stats
                ta = b.emit("act", lambda e, j=j, i=i: e.activation(
                    self.sq[:], self.xn[j][:], AF.Square,
                    accum_out=sumsq[:, i:i + 1]),
                    waits=[t] + ([self.statoks[i - 1]] if i >= 1 else []),
                    inc=self.dna)
                b.emit("dve", lambda e, j=j, i=i: e.reduce_sum(
                    sums[:, i:i + 1], self.xn[j][:], axis=AX.X),
                    waits=[t], inc=self.dnd)
                self.statoks[i] = ta
        return t

    def finish(self):
        NT = self.b.NT
        return [self.sttoks[i] for i in (NT - 2, NT - 1) if i in self.sttoks]


class StoreStatsEpi:

    def __init__(self, b, pref, dst_hbm, stats):
        self.b = b
        self.dst_hbm = dst_hbm
        self.stats = stats
        self.xn = [b.sbuf(f"{pref}_exn{j}", [P, D], F32) for j in range(2)]
        self.sq = b.sbuf(f"{pref}_esq", [P, D], F32)
        self.cst = b.chan("est", eng="act", nslots=2)
        self.dna = b.sem("epi_dna")
        self.dnd = b.sem("epi_dnd")
        self.ev = b.sem("epi_ev")
        self.sttoks = {}
        self.statoks = {}

    def __call__(self, i, n, bank, mmtok):
        b = self.b
        j = i % 2
        sw = [self.sttoks[i - 2], self.statoks[i - 2]] if i >= 2 else []
        t = b.emit("dve", lambda e, j=j, n=n, bank=bank: e.tensor_copy(
            self.xn[j][:, n * 512:(n + 1) * 512], bank[:]),
            waits=[mmtok] + sw, inc=self.ev)
        if n == 1:
            self.sttoks[i] = self.cst.issue(
                self.dst_hbm[i * P:(i + 1) * P, :], self.xn[j][:], waits=[t])
            sums, sumsq = self.stats
            self.statoks[i] = b.emit("act", lambda e, j=j, i=i: e.activation(
                self.sq[:], self.xn[j][:], AF.Square, accum_out=sumsq[:, i:i + 1]),
                waits=[t] + ([self.statoks[i - 1]] if i >= 1 else []), inc=self.dna)
            b.emit("dve", lambda e, j=j, i=i: e.reduce_sum(
                sums[:, i:i + 1], self.xn[j][:], axis=AX.X), waits=[t], inc=self.dnd)
        return t

    def finish(self):
        NT = self.b.NT
        return [self.sttoks[i] for i in (NT - 2, NT - 1) if i in self.sttoks]


class QlFinalEpi:

    def __init__(self, b, pref, blk, x_hbm, h_hbm, dst_hbm, hT, use_xhat,
                 stats=None):
        self.b = b
        self.pref = pref
        self.blk = blk
        self.x_hbm = x_hbm
        self.h_hbm = h_hbm
        self.dst_hbm = dst_hbm
        self.hT = hT
        self.use_xhat = use_xhat
        self.last = (blk == 2)
        self.xa = [b.sbuf(f"{pref}_exa{j}", [P, D], F32) for j in range(2)]
        self.xh = [b.sbuf(f"{pref}_exh{j}", [P, D], F32) for j in range(2)]
        self.xn = [b.sbuf(f"{pref}_exn{j}", [P, D], F32) for j in range(2)]
        self.hb = [b.sbuf(f"{pref}_ehb{j}", [P, D], BF16) for j in range(2)]
        self.ch = b.chan("eld")
        self.chx = b.chan("eldx") if self.last else None
        self.cst = b.chan("est", eng="act", nslots=2)
        self.sttoks = {}
        self.ldtoks = {}
        self.evtoks = {}
        self.cbtoks = {}
        self.trtoks = {}
        self.tvtoks = {}
        self.xstoks = {}
        self.tr = b.sem("epi_tr")
        self.evt = b.sem("epi_evt")
        self.cb = b.sem("epi_cb")
        self.xs = b.sem("epi_xs")
        self.ev = b.sem("epi_ev")
        self.stats = stats
        self.sq = b.sbuf(f"{pref}_esq", [P, D], F32) if stats is not None else None
        self.dna = b.sem("epi_dna")
        self.dnd = b.sem("epi_dnd")
        self.statoks = {}
        self.hsrc = self.xh if use_xhat else self.xa

    def __call__(self, i, n, bank, mmtok):
        b = self.b
        res = b.res
        j = i % 2
        if n == 0:
            w = ([self.evtoks[(i - 2, nn)] for nn in range(2)] +
                 [self.cbtoks[i - 2]]) if i >= 2 else []
            srcb = self.x_hbm if self.use_xhat else self.h_hbm
            t1 = self.ch.issue(self.xa[j][:], srcb[i * P:(i + 1) * P, :], waits=w)
            if self.use_xhat:
                self.xstoks[i] = b.emit("dve", lambda e, j=j, i=i: e.tensor_scalar(
                    self.xh[j][:], self.xa[j][:], res.mu[:, i:i + 1],
                    res.rstd[:, i:i + 1], ALU.subtract, ALU.mult), waits=[t1],
                    inc=self.xs)
            self.ldtoks[i] = t1
            if self.last:
                self.ldtoks[(i, "x")] = self.chx.issue(
                    self.xh[j][:], self.x_hbm[i * P:(i + 1) * P, :], waits=w)
        sw = [self.sttoks[i - 2]] if i >= 2 else []
        if self.stats is not None and i >= 2 and (i - 2) in self.statoks:
            sw = sw + [self.statoks[i - 2]]
        t = b.emit("dve", lambda e, j=j, n=n, bank=bank: e.scalar_tensor_tensor(
            self.xn[j][:, n * 512:(n + 1) * 512], bank[:], 1.0,
            self.hsrc[j][:, n * 512:(n + 1) * 512], ALU.mult, ALU.add),
            waits=[mmtok, self.ldtoks[i]] + sw +
                  ([self.xstoks[i]] if self.use_xhat else []) +
                  ([self.ldtoks[(i, "x")]] if self.last else []), inc=self.ev)
        self.evtoks[(i, n)] = t
        if n == 1:
            if self.last:
                t2 = b.emit("dve", lambda e, j=j: e.scalar_tensor_tensor(
                    self.xn[j][:], self.xn[j][:], 0.5, self.xh[j][:],
                    ALU.mult, ALU.add), waits=[t] +
                    ([self.statoks[i - 2]] if (self.stats is not None and i >= 2)
                     else []), inc=self.cb)
                self.cbtoks[i] = t2
                self.sttoks[i] = self.cst.issue(
                    self.dst_hbm[i * P:(i + 1) * P, :], self.xn[j][:], waits=[t2])
                if self.stats is not None:
                    sums, sumsq = self.stats
                    ta = b.emit("act", lambda e, j=j, i=i: e.activation(
                        self.sq[:], self.xn[j][:], AF.Square,
                        accum_out=sumsq[:, i:i + 1]),
                        waits=[t2] + ([self.statoks[i - 1]] if i >= 1 else []),
                        inc=self.dna)
                    b.emit("dve", lambda e, j=j, i=i: e.reduce_sum(
                        sums[:, i:i + 1], self.xn[j][:], axis=AX.X),
                        waits=[t2], inc=self.dnd)
                    self.statoks[i] = ta
            else:
                self.sttoks[i] = self.cst.issue(
                    self.h_hbm[i * P:(i + 1) * P, :], self.xn[j][:], waits=[t])
                hww = [self.trtoks[(i - 1, ND - 1)]] if i >= 1 else []
                tc = b.emit("dve", lambda e, j=j: e.tensor_copy(
                    self.hb[j][:], self.xn[j][:]), waits=[t] + hww, inc=self.cb)
                self.cbtoks[i] = tc
                for c in range(ND):
                    pbank = b.pb[4 + c % 4].ap().bitcast(BF16)
                    prev = (i, c - 4) if c >= 4 else \
                        ((i - 1, c + 4) if i >= 1 else None)
                    bw = [self.tvtoks[prev]] if prev else []
                    tt = b.emit("pe", lambda e, j=j, c=c, pbank=pbank: e.transpose(
                        pbank[:, 0:P], self.hb[j][:, c * P:(c + 1) * P],
                        res.ident[:]), waits=[tc] + bw, inc=self.tr)
                    self.trtoks[(i, c)] = tt
                    self.tvtoks[(i, c)] = b.emit(
                        "dve", lambda e, i=i, c=c, pbank=pbank: e.tensor_copy(
                            self.hT[c][:, i * P:(i + 1) * P], pbank[:, 0:P]),
                        waits=[tt], inc=self.evt)
        return t

    def finish(self):
        NT = self.b.NT
        return [self.sttoks[i] for i in (NT - 2, NT - 1) if i in self.sttoks]


def stats_finalize(b, sums, sumsq, mu, rstd, pref, nt=None):
    nt = b.NT if nt is None else nt
    m = b.mark()
    u = b.uid()
    var = b.sbuf(f"{pref}{u}_var", [P, nt], F32)
    msq = b.sbuf(f"{pref}{u}_msq", [P, nt], F32)
    st = b.sem(pref + "_fin")
    b.emit("dve", lambda e: e.tensor_scalar(mu[:, 0:nt], sums[:, 0:nt], 1.0 / D,
                                            None, ALU.mult), inc=st)
    b.emit("dve", lambda e: e.tensor_scalar(var[:], sumsq[:, 0:nt], 1.0 / D,
                                            None, ALU.mult), inc=st)
    b.emit("dve", lambda e: e.tensor_tensor(msq[:], mu[:, 0:nt], mu[:, 0:nt],
                                            ALU.mult), waits=[(st, None)], inc=st)
    b.emit("dve", lambda e: e.tensor_sub(var[:], var[:], msq[:]),
           waits=[(st, None)], inc=st)
    b.emit("dve", lambda e: e.tensor_scalar(var[:], var[:], float(LN_EPS), None,
                                            ALU.add), waits=[(st, None)], inc=st)
    b.emit("act", lambda e: e.activation(var[:], var[:], AF.Sqrt),
           waits=[(st, None)], inc=st)
    b.emit("dve", lambda e: e.reciprocal(rstd[:, 0:nt], var[:]),
           waits=[(st, None)], inc=st)
    b.barrier()
    b.reset(m)


def linear_tm_epilogue(b, w_tiles, bias_tile, coef, x_hbm, dst_hbm, sem_pref,
                       kchunks=None, xT=None, extra_mm=None, n_out=D,
                       out_stats=None):
    if xT is None:
        xT = b.res.xT
    epi = ResidualEpi(b, sem_pref, x_hbm, dst_hbm, coef, n_out=n_out,
                      stats=out_stats)
    linear_tm(b, sem_pref, xT, w_tiles, bias_tile, epi, kchunks=kchunks,
              n_out=n_out, extra_mm=extra_mm)
    b.barrier(toks=epi.finish())



def col_layout(b, vec_hbm_1d, name, psum_bank=6):
    res = b.res
    out = b.sbuf(name, [P, ND], F32)
    m = b.mark()
    rows = b.sbuf(name + "_r", [ND, P], F32)
    s = b.sem("colay_s")
    lds = b.sem("colay_dl")
    tok = b.dma("sp", rows[:], vec_hbm_1d.rearrange("(c p) -> c p", p=P), inc=lds)
    t = b.emit("pe", lambda e: e.transpose(b.pb[psum_bank][:, 0:ND], rows[:],
                                           res.identf[0:ND, 0:ND]),
               waits=[tok], inc=s)
    b.emit("dve", lambda e: e.tensor_copy(out[:], b.pb[psum_bank][:, 0:ND]),
           waits=[t], inc=s)
    b.barrier()
    b.reset(m)
    return out


def rows_to_colT(b, src_hbm, r, name, psum_bank=6):
    res = b.res
    outs = [b.sbuf(f"{name}_{c}", [P, r], BF16) for c in range(ND)]
    m = b.mark()
    rows = b.sbuf(name + "_r", [r, D], BF16)
    s = b.sem("rct_s")
    ev = b.sem("rct_e")
    lds = b.sem("rct_dl")
    tok = b.dma("pool", rows[:], src_hbm, inc=lds)
    tt, te = {}, {}
    for c in range(ND):
        bank = b.pb[psum_bank + (c % 2)].ap().bitcast(BF16)
        bw = [te[c - 2]] if c >= 2 else []
        tt[c] = b.emit("pe", lambda e, c=c, bank=bank, r=r: e.transpose(
            bank[:, 0:r], rows[:, c * P:(c + 1) * P], res.ident[0:r, 0:r]),
            waits=[tok] + bw, inc=s)
        te[c] = b.emit("dve", lambda e, c=c, bank=bank, r=r: e.tensor_copy(
            outs[c][:], bank[:, 0:r]), waits=[tt[c]], inc=ev)
    b.barrier()
    b.reset(m)
    return outs


def colchunks_to_rowT(b, src_hbm, r, name, psum_bank=6):
    res = b.res
    out = b.sbuf(name, [r, D], BF16)
    m = b.mark()
    chk = [b.sbuf(f"{name}_r{j}", [P, r], BF16) for j in range(2)]
    ch = b.chan("cct_l", eng="pool")
    s = b.sem("cct_s")
    ev = b.sem("cct_e")
    tt, te = {}, {}
    for c in range(ND):
        j = c % 2
        bank = b.pb[psum_bank + j].ap().bitcast(BF16)
        w = [tt[c - 2]] if c >= 2 else []
        tok = ch.issue(chk[j][:], src_hbm[c * P:(c + 1) * P, :], waits=w)
        bw = [te[c - 2]] if c >= 2 else []
        tt[c] = b.emit("pe", lambda e, j=j, bank=bank, r=r: e.transpose(
            bank[0:r, 0:P], chk[j][:], res.ident[:]), waits=[tok] + bw, inc=s)
        te[c] = b.emit("dve", lambda e, c=c, bank=bank, r=r: e.tensor_copy(
            out[:, c * P:(c + 1) * P], bank[0:r, 0:P]), waits=[tt[c]], inc=ev)
    b.barrier()
    b.reset(m)
    return out


def lora_mid(b, inT, laT, name):
    out = b.sbuf(name, [32, b.T], BF16)
    mm = b.sem("lmid_mm")
    ev = b.sem("lmid_ev")
    tm, te = {}, {}
    nb = b.T // 512
    for t in range(nb):
        bank = b.pb[t % 2]
        def mmf(e, t=t, bank=bank):
            ins = None
            for c in range(ND):
                ins = e.matmul(bank[0:32, :], laT[c][:], inT[c][:, t * 512:(t + 1) * 512],
                               start=(c == 0), stop=(c == ND - 1))
            return ins
        bw = [te[t - 2]] if t >= 2 else []
        tm[t] = b.emit("pe", mmf, waits=bw, inc=mm)
        te[t] = b.emit("dve", lambda e, t=t, bank=bank: e.tensor_copy(
            out[:, t * 512:(t + 1) * 512], bank[0:32, :]), waits=[tm[t]], inc=ev)
    b.barrier()
    return out


def dequant_w(b, q_hbm, sc_hbm, name):
    res = b.res
    outs = [b.sbuf(f"{name}_{c}", [P, D], BF16) for c in range(ND)]
    m = b.mark()
    qi = [b.sbuf(f"{name}_q{j}", [P, D], I32) for j in range(2)]
    sc = [b.sbuf(f"{name}_s{j}", [P, D // 16], F32) for j in range(2)]
    qf = b.sbuf(name + "_f", [P, D], F32)
    wb = [b.sbuf(f"{name}_w{j}", [P, D], BF16) for j in range(2)]
    chq = b.chan("dqw_lq")
    chs = b.chan("dqw_ls")
    dq = b.sem("dqw_dq")
    tr = b.sem("dqw_tr")
    ev = b.sem("dqw_ev")
    tmul, ttr, tev = {}, {}, {}
    tcv = {}
    for oc in range(ND):
        j = oc % 2
        w = [tmul[oc - 1]] if oc >= 1 else []
        wl = [tcv[oc - 2]] if oc >= 2 else []
        tq = chq.issue(qi[j][:], q_hbm[oc * P:(oc + 1) * P, :], waits=wl)
        ts = chs.issue(sc[j][:], sc_hbm[oc * P:(oc + 1) * P, :],
                       waits=[tmul[oc - 2]] if oc >= 2 else [])
        tcv[oc] = b.emit("dve", lambda e, j=j: e.tensor_copy(qf[:], qi[j][:]),
                         waits=[tq] + w, inc=dq)
        t2 = b.emit("dve", lambda e: e.tensor_scalar(qf[:], qf[:], 1.0 / 7.5, -1.0,
                                                     ALU.mult, ALU.add),
                    waits=[tcv[oc]], inc=dq)
        def mul(e, j=j):
            scv = sc[j][:]
            sc0 = bass.AP(tensor=scv.tensor, offset=scv.offset,
                          ap=[[scv.ap[0][0], P], [1, 64], [0, 16]])
            return e.tensor_tensor(wb[j].ap().rearrange("p (g w) -> p g w", w=16),
                                   qf.ap().rearrange("p (g w) -> p g w", w=16),
                                   sc0, ALU.mult)
        hw = [ttr[(oc - 1, ND - 1)]] if oc >= 1 else []
        tmul[oc] = b.emit("dve", mul, waits=[ts, t2] + hw, inc=dq)
        for c in range(ND):
            bank = b.pb[4 + c % 4].ap().bitcast(BF16)
            prev = (oc, c - 4) if c >= 4 else ((oc - 1, c + 4) if oc >= 1 else None)
            bw = [tev[prev]] if prev else []
            ttr[(oc, c)] = b.emit("pe", lambda e, j=j, c=c, bank=bank: e.transpose(
                bank[:, 0:P], wb[j][:, c * P:(c + 1) * P], res.ident[:]),
                waits=[tmul[oc]] + bw, inc=tr)
            tev[(oc, c)] = b.emit("dve", lambda e, c=c, oc=oc, bank=bank:
                e.tensor_copy(outs[c][:, oc * P:(oc + 1) * P], bank[:, 0:P]),
                waits=[ttr[(oc, c)]], inc=ev)
    b.barrier()
    b.reset(m)
    return outs


def lora_extra_mm(mid, lbT):
    def f(e, bank, i, n, stop):
        return e.matmul(bank[:], mid[:, i * P:(i + 1) * P],
                        lbT[:, n * 512:(n + 1) * 512], start=False, stop=stop)
    return f


def sublayer_l1ffn(b, ctx, x_hbm, dst_hbm, out_stats=None):
    ctx["out_stats"] = out_stats
    """x_new = x + 0.5 * qlora_stack(LN(x)). Needs ln_stats+ln_apply done."""
    n = ctx["n"]
    qw = ctx["L1_ffn_qweights"]
    sc = ctx["L1_ffn_scales"]
    bi = ctx["L1_ffn_biases"]
    la = ctx["L1_ffn_lora_a"]
    lb = ctx["L1_ffn_lora_b"]
    lng = ctx["L1_ffn_ln_gamma"]
    lnb = ctx["L1_ffn_ln_beta"]
    h_hbm = ctx["l1f_h"]
    y_hbm = ctx["l1f_y"]
    sub_m = b.mark()
    mu_y = b.sbuf(f"l1mu{n}", [P, b.NT], F32)
    rs_y = b.sbuf(f"l1rs{n}", [P, b.NT], F32)
    sums = b.sbuf(f"l1sm{n}", [P, b.NT], F32)
    sumsq = b.sbuf(f"l1sq{n}", [P, b.NT], F32)
    cur_hT = b.res.xT
    for blk in range(3):
        i0, i1 = 2 * blk, 2 * blk + 1
        blk_m = b.mark()
        gcol = col_layout(b, lng[blk], f"l1gc{n}_{blk}")
        bcol = col_layout(b, lnb[blk], f"l1bc{n}_{blk}")
        wT1 = dequant_w(b, qw[i1], sc[i1], f"l1w{n}_{blk}b")
        laT1 = rows_to_colT(b, la[i1], 32, f"l1la{n}_{blk}b")
        lbT1 = colchunks_to_rowT(b, lb[i1], 32, f"l1lb{n}_{blk}b")
        ldb = b.sem("l1bl")
        bt0 = bias_row(b, bi[i0], f"l1b{n}_{blk}a", D, ldb)
        bt1 = bias_row(b, bi[i1], f"l1b{n}_{blk}b", D, ldb)
        b.emit("pe", lambda e, v=ldb.n: e.wait_ge(ldb.h, v))
        m_a = b.mark()
        wT0 = dequant_w(b, qw[i0], sc[i0], f"l1w{n}_{blk}a")
        laT0 = rows_to_colT(b, la[i0], 32, f"l1la{n}_{blk}a")
        lbT0 = colchunks_to_rowT(b, lb[i0], 32, f"l1lb{n}_{blk}a")
        mid0 = lora_mid(b, cur_hT, laT0, f"l1m{n}_{blk}a")
        epi0 = StoreStatsEpi(b, f"l1A{n}_{blk}", y_hbm, (sums, sumsq))
        linear_tm(b, f"l1A{n}_{blk}", cur_hT, wT0, bt0, epi0,
                  extra_mm=lora_extra_mm(mid0, lbT0))
        b.barrier(toks=epi0.finish())
        stats_finalize(b, sums, sumsq, mu_y, rs_y, f"l1sf{n}_{blk}")
        b.reset(m_a)
        g_fm = [b.sbuf(f"l1g{n}_{blk}_{c}", [P, b.T], BF16) for c in range(ND)]
        to_fm(b, y_hbm, g_fm, f"l1T{n}_{blk}", mu=mu_y, rstd=rs_y,
              evac=("gelu", gcol, bcol))
        mid1 = lora_mid(b, g_fm, laT1, f"l1m{n}_{blk}b")
        epiF = QlFinalEpi(b, f"l1B{n}_{blk}", blk, x_hbm, h_hbm, dst_hbm,
                          cur_hT, use_xhat=(blk == 0),
                          stats=(ctx.get("out_stats") if blk == 2 else None))
        linear_tm(b, f"l1B{n}_{blk}", g_fm, wT1, bt1, epiF,
                  extra_mm=lora_extra_mm(mid1, lbT1))
        b.barrier(toks=epiF.finish())
        b.reset(blk_m)
    b.reset(sub_m)
    ctx["n"] += 1



class StoreEpi:

    def __init__(self, b, pref, dst_hbm, dt=F32):
        self.b = b
        self.dst_hbm = dst_hbm
        self.xn = [b.sbuf(f"{pref}_exn{j}", [P, D], dt) for j in range(2)]
        self.cst = b.chan("est", eng="act", nslots=2)
        self.ev = b.sem("epi_ev")
        self.sttoks = {}

    def __call__(self, i, n, bank, mmtok):
        b = self.b
        j = i % 2
        sw = [self.sttoks[i - 2]] if i >= 2 else []
        t = b.emit("dve", lambda e, j=j, n=n, bank=bank: e.tensor_copy(
            self.xn[j][:, n * 512:(n + 1) * 512], bank[:]),
            waits=[mmtok] + sw, inc=self.ev)
        if n == 1:
            self.sttoks[i] = self.cst.issue(
                self.dst_hbm[i * P:(i + 1) * P, :], self.xn[j][:], waits=[t])
        return t

    def finish(self):
        NT = self.b.NT
        return [self.sttoks[i] for i in (NT - 2, NT - 1) if i in self.sttoks]


class VEpi:

    def __init__(self, b, pref, v_res):
        self.b = b
        self.v_res = v_res
        self.ev = b.sem("epi_ev")

    def __call__(self, i, n, bank, mmtok):
        b = self.b
        vr = self.v_res[:]
        off = i * 1056 + n * 8 * 66
        dst = bass.AP(tensor=vr.tensor, offset=vr.offset + off,
                      ap=[[vr.ap[0][0], P], [66, 8], [1, 64]])
        t = b.emit("dve", lambda e, bank=bank, dst=dst: e.tensor_copy(
            dst, bank.ap().rearrange("p (h w) -> p h w", w=64)),
            waits=[mmtok], inc=self.ev)
        return t

    def finish(self):
        return []


def qk_fm_pass(b, src_hbm, pref, transform, dst_sbufT=None, dst_hbm=None,
               norm_out=None, post=None, dst_hbm_lo=None):
    res = b.res
    NT = b.NT
    m = b.mark()
    srcbf = (transform is None or transform[0] != "ln")
    xa = [b.sbuf(f"{pref}_x{j}", [P, D], BF16 if srcbf else F32)
          for j in range(2)]
    xh = xa if srcbf else [b.sbuf(f"{pref}_h{j}", [P, D], BF16)
                           for j in range(2)]
    ss = b.sbuf(f"{pref}_ss", [P, 16], F32)
    scr = b.sbuf(f"{pref}_sc", [P, D], F32)
    strip = [b.sbuf(f"{pref}_st{j}", [P, D], BF16) for j in range(2)] \
        if dst_hbm is not None else None
    strip_lo = [b.sbuf(f"{pref}_sl{j}", [P, D], BF16) for j in range(2)] \
        if dst_hbm_lo is not None else None
    xl = [b.sbuf(f"{pref}_xl{j}", [P, D], BF16) for j in range(2)] \
        if dst_hbm_lo is not None else None
    afscr = b.sbuf(f"{pref}_af", [P, P], F32) if dst_hbm_lo is not None else None
    afsc2 = b.sbuf(f"{pref}_ag", [P, P], F32) if dst_hbm_lo is not None else None
    ch = b.chan("ld")
    cstt = b.chan("tst", eng="act", nslots=2) if dst_hbm is not None else None
    cstl = b.chan("tsl", eng="act", nslots=2) if dst_hbm_lo is not None else None
    ap_ = b.sem("qkf_ap")
    tr = b.sem("qkf_tr")
    ev = b.sem("qkf_ev")
    nrm = b.sem("qkf_nrm")
    tz = None
    if norm_out is not None:
        tz = b.emit("dve", lambda e: e.memset(norm_out[:], 0.0), inc=nrm)
    tap, ttr, tev, tst, tnm = {}, {}, {}, {}, {}

    def hview(ap, dsub=64):
        return bass.AP(tensor=ap.tensor, offset=ap.offset,
                       ap=[[ap.ap[0][0], P], [64, 16], [1, dsub]])

    for i in range(NT):
        j = i % 2
        if srcbf:
            w = [ttr[(i - 2, ND - 1)]] if i >= 2 else []
        else:
            w = [tap[i - 2]] if i >= 2 else []
        if i >= 2 and norm_out is not None:
            w.append(tnm[i - 2])
        tok = ch.issue(xa[j][:], src_hbm[i * P:(i + 1) * P, :], waits=w)
        hw = [ttr[(i - 1, ND - 1)]] if i >= 1 else []
        nw = [tnm[i - 1]] if (i >= 1 and norm_out is not None) else []
        tt = tok
        if transform is not None and transform[0] == "rope":
            _, cosc, sinc = transform
            def pview(ap, off):
                return bass.AP(tensor=ap.tensor, offset=ap.offset + off,
                               ap=[[ap.ap[0][0], P], [64, 16], [2, 32]])
            def sview(t_):
                a = t_[:]
                return bass.AP(tensor=a.tensor, offset=a.offset,
                               ap=[[a.ap[0][0], P], [32, 16], [1, 32]])
            def cv(cs):
                a = cs[:]
                return bass.AP(tensor=a.tensor, offset=a.offset + i * 32,
                               ap=[[a.ap[0][0], P], [0, 16], [1, 32]])
            s1v = sview(scr)
            s2v = bass.AP(tensor=scr[:].tensor, offset=scr[:].offset + 512,
                          ap=[[scr[:].ap[0][0], P], [32, 16], [1, 32]])
            xe = pview(xa[j][:], 0)
            xo = pview(xa[j][:], 1)
            cosv, sinv = cv(cosc), cv(sinc)
            t1 = b.emit("dve", lambda e, xe=xe, cosv=cosv, s1v=s1v:
                e.tensor_tensor(s1v, xe, cosv, ALU.mult),
                waits=[tt] + nw + hw, inc=ap_)
            t2 = b.emit("dve", lambda e, xo=xo, sinv=sinv, s2v=s2v:
                e.tensor_tensor(s2v, xo, sinv, ALU.mult), waits=[t1], inc=ap_)
            t3 = b.emit("dve", lambda e, s1v=s1v, s2v=s2v:
                e.tensor_sub(s1v, s1v, s2v), waits=[t2], inc=ap_)
            t4 = b.emit("dve", lambda e, xe=xe, sinv=sinv, s2v=s2v:
                e.tensor_tensor(s2v, xe, sinv, ALU.mult), waits=[t3], inc=ap_)
            t5 = b.emit("dve", lambda e, xo=xo, cosv=cosv:
                e.tensor_tensor(xo, xo, cosv, ALU.mult), waits=[t4], inc=ap_)
            t6 = b.emit("dve", lambda e, xo=xo, s2v=s2v:
                e.tensor_add(xo, xo, s2v), waits=[t5], inc=ap_)
            tt = b.emit("dve", lambda e, xe=xe, s1v=s1v:
                e.tensor_copy(xe, s1v), waits=[t6], inc=ap_)
        elif transform is not None and transform[0] == "rmsnorm":
            _, qnw_b = transform
            t1 = b.emit("act", lambda e, j=j: e.activation(
                scr[:], xa[j][:], AF.Square), waits=[tt] + nw + hw, inc=ap_)
            t2 = b.emit("dve", lambda e: e.tensor_reduce(
                ss[:], hview(scr[:]), axis=AX.X, op=ALU.add),
                waits=[t1], inc=ap_)
            t3 = b.emit("dve", lambda e: e.tensor_scalar(
                ss[:], ss[:], 1.0 / 64, 1e-6, ALU.mult, ALU.add),
                waits=[t2], inc=ap_)
            t4 = b.emit("act", lambda e: e.activation(ss[:], ss[:], AF.Sqrt),
                        waits=[t3], inc=ap_)
            t5 = b.emit("dve", lambda e: e.reciprocal(ss[:], ss[:]),
                        waits=[t4], inc=ap_)
            rrv = bass.AP(tensor=ss[:].tensor, offset=ss[:].offset,
                          ap=[[ss[:].ap[0][0], P], [1, 16], [0, 64]])
            t6 = b.emit("dve", lambda e, j=j, rrv=rrv: e.tensor_tensor(
                hview(xa[j][:]), hview(xa[j][:]), rrv, ALU.mult),
                waits=[t5], inc=ap_)
            tt = b.emit("dve", lambda e, j=j: e.tensor_mul(
                xa[j][:], xa[j][:], qnw_b[:]), waits=[t6], inc=ap_)
        elif transform is not None and transform[0] == "ln":
            _, muv, rsv = transform
            tt = b.emit("dve", lambda e, j=j, i=i: e.tensor_scalar(
                xa[j][:], xa[j][:], muv[:, i:i + 1], rsv[:, i:i + 1],
                ALU.subtract, ALU.mult), waits=[tt] + nw + hw, inc=ap_)
        else:
            tt = b.emit("dve", lambda e: e.engine_nop(), waits=[tt] + nw + hw,
                        inc=ap_)
        if srcbf:
            tcast = tt
        else:
            tcast = b.emit("dve", lambda e, j=j: e.tensor_copy(xh[j][:], xa[j][:]),
                           waits=[tt], inc=ap_)
        if dst_hbm_lo is not None:
            tcast = b.emit("dve", lambda e, j=j: e.tensor_sub(
                xl[j][:], xa[j][:], xh[j][:]), waits=[tcast], inc=ap_)
        tap[i] = tcast
        if norm_out is not None:
            t1 = b.emit("act", lambda e, j=j: e.activation(
                scr[:], xa[j][:], AF.Square), waits=[tcast], inc=ap_)
            t2 = b.emit("dve", lambda e: e.tensor_reduce(
                ss[:], hview(scr[:]), axis=AX.X, op=ALU.add),
                waits=[t1], inc=ap_)
            tnm[i] = b.emit("dve", lambda e: e.tensor_max(
                norm_out[:], norm_out[:], ss[:]),
                waits=[t2, tnm[i - 1] if i >= 1 else tz], inc=nrm)
        for c in range(ND):
            bank = b.pb[c % 4].ap().bitcast(BF16)
            prev = (i, c - 4) if c >= 4 else ((i - 1, c + 4) if i >= 1 else None)
            bw = [tev[prev]] if prev else []
            ttr[(i, c)] = b.emit("pe", lambda e, j=j, c=c, bank=bank: e.transpose(
                bank[:, 0:P], xh[j][:, c * P:(c + 1) * P], res.ident[:]),
                waits=[tcast] + bw, inc=tr)
            if dst_sbufT is not None:
                if post is None:
                    eng_ = "dve" if c % 2 == 0 else "act"
                    evs = ev if c % 2 == 0 else b.sem("qkf_eva")
                    tev[(i, c)] = b.emit(eng_, lambda e, i=i, c=c, bank=bank:
                        (e.tensor_copy(dst_sbufT[c][:, i * P:(i + 1) * P],
                                       bank[:, 0:P])
                         if hasattr(e, "tensor_copy") else
                         e.copy(dst_sbufT[c][:, i * P:(i + 1) * P], bank[:, 0:P])),
                        waits=[ttr[(i, c)]] + ([tev[(i, c - 2)]] if c >= 2 else
                                               ([tev[(i - 1, c + 6)]] if i >= 1
                                                else [])),
                        inc=evs)
                else:
                    _, gam, bet = post
                    tev[(i, c)] = b.emit("dve", lambda e, i=i, c=c, bank=bank:
                        e.tensor_scalar(dst_sbufT[c][:, i * P:(i + 1) * P],
                                        bank[:, 0:P], gam[:, c:c + 1],
                                        bet[:, c:c + 1], ALU.mult, ALU.add),
                        waits=[ttr[(i, c)]], inc=ev)
            else:
                sw2 = list(tst[i - 2]) if i >= 2 else []
                if post is None:
                    eng_ = "dve" if c % 2 == 0 else "act"
                    evs = ev if c % 2 == 0 else b.sem("qkf_eva")
                    tev[(i, c)] = b.emit(eng_, lambda e, j=j, c=c, bank=bank:
                        (e.tensor_copy(strip[j][:, c * P:(c + 1) * P],
                                       bank[:, 0:P])
                         if hasattr(e, "tensor_copy") else
                         e.copy(strip[j][:, c * P:(c + 1) * P], bank[:, 0:P])),
                        waits=[ttr[(i, c)]] + (sw2 if c == 0 else []) +
                              ([tev[(i, c - 2)]] if c >= 2 else
                               ([tev[(i - 1, c + 6)]] if i >= 1 else [])),
                        inc=evs)
                elif dst_hbm_lo is None:
                    _, gam, bet = post
                    tev[(i, c)] = b.emit("dve", lambda e, j=j, c=c, bank=bank:
                        e.tensor_scalar(strip[j][:, c * P:(c + 1) * P],
                                        bank[:, 0:P], gam[:, c:c + 1],
                                        bet[:, c:c + 1], ALU.mult, ALU.add),
                        waits=[ttr[(i, c)]] + (sw2 if c == 0 else []), inc=ev)
                else:
                    _, gam, bet = post
                    bank_lo = b.pb[4 + c % 4].ap().bitcast(BF16)
                    prev2 = (i, c - 4) if c >= 4 else \
                        ((i - 1, c + 4) if i >= 1 else None)
                    bw2 = [tev[prev2]] if prev2 else []
                    ttl = b.emit("pe", lambda e, j=j, c=c, bank_lo=bank_lo:
                        e.transpose(bank_lo[:, 0:P], xl[j][:, c * P:(c + 1) * P],
                                    res.ident[:]),
                        waits=[tap[i]] + bw2, inc=tr)
                    t0 = b.emit("dve", lambda e, c=c, bank=bank:
                        e.tensor_copy(afsc2[:], bank[:, 0:P]),
                        waits=[ttr[(i, c)]] +
                              ([tev[(i, c - 1)]] if c >= 1 else
                               ([tev[(i - 1, ND - 1)]] if i >= 1 else [])) +
                              (sw2 if c == 0 else []), inc=ev)
                    t1 = b.emit("dve", lambda e, c=c, bank_lo=bank_lo:
                        e.tensor_add(afsc2[:], afsc2[:], bank_lo[:, 0:P]),
                        waits=[t0, ttl], inc=ev)
                    t1 = b.emit("dve", lambda e, c=c:
                        e.tensor_scalar(afscr[:], afsc2[:], gam[:, c:c + 1],
                                        bet[:, c:c + 1], ALU.mult, ALU.add),
                        waits=[t1], inc=ev)
                    t2 = b.emit("dve", lambda e, j=j, c=c:
                        e.tensor_copy(strip[j][:, c * P:(c + 1) * P], afscr[:]),
                        waits=[t1], inc=ev)
                    t3 = b.emit("dve", lambda e, j=j, c=c:
                        e.tensor_sub(afscr[:], afscr[:],
                                     strip[j][:, c * P:(c + 1) * P]),
                        waits=[t2], inc=ev)
                    tev[(i, c)] = b.emit("dve", lambda e, j=j, c=c:
                        e.tensor_copy(strip_lo[j][:, c * P:(c + 1) * P],
                                      afscr[:]),
                        waits=[t3], inc=ev)
        if dst_hbm is not None:
            dv = dst_hbm
            dap = bass.AP(tensor=dv.tensor,
                          offset=dv.offset + i * P,
                          ap=[[b.T, P], [P * b.T, ND], [1, P]])
            toks_i = [cstt.issue(
                dap, strip[j].ap().rearrange("p (c t) -> p c t", c=ND),
                waits=[tev[(i, c2)] for c2 in range(ND)])]
            if dst_hbm_lo is not None:
                dv2 = dst_hbm_lo
                dap2 = bass.AP(tensor=dv2.tensor, offset=dv2.offset + i * P,
                               ap=[[b.T, P], [P * b.T, ND], [1, P]])
                toks_i.append(cstl.issue(
                    dap2, strip_lo[j].ap().rearrange("p (c t) -> p c t", c=ND),
                    waits=[tev[(i, c2)] for c2 in range(ND)]))
            tst[i] = toks_i
    if dst_hbm is not None:
        b.barrier(toks=[t for i in (NT - 2, NT - 1) if i in tst
                        for t in tst[i]])
    else:
        b.barrier()
    b.reset(m)


def bcast16(b, pref, src16, dst, psum_bank=7, extra_waits=()):
    res = b.res
    s = b.sem("bc_s")
    t = b.emit("pe", lambda e: e.matmul(b.pb[psum_bank][:, 0:16],
                                        res.ones[0:1, 0:P], src16[0:1, 0:16],
                                        start=True, stop=True),
               waits=list(extra_waits), inc=s)
    return b.emit("dve", lambda e: e.tensor_copy(dst[:], b.pb[psum_bank][:, 0:16]),
                  waits=[t], inc=s)


def mhat_phase(b, pref, maxq, maxk, mhat_neg, mpair_neg=None):
    res = b.res
    m = b.mark()
    s = b.sem("mh_s")
    t16 = b.sbuf(pref + "_t16", [16, P], F32)
    red = b.sbuf(pref + "_red", [16, 2], F32)
    prod = b.sbuf(pref + "_prod", [16, 1], F32)
    row = b.sbuf(pref + "_row", [1, 16], F32)
    rowb = b.sbuf(pref + "_rowb", [1, 16], BF16)
    t = b.emit("pe", lambda e: e.transpose(b.pb[7][0:16, 0:P], maxq[:],
                                           res.identf[:]), inc=s)
    t = b.emit("dve", lambda e: e.tensor_copy(t16[:], b.pb[7][0:16, 0:P]),
               waits=[t], inc=s)
    t = b.emit("dve", lambda e: e.reduce_max(red[:, 0:1], t16[:], axis=AX.X),
               waits=[t], inc=s)
    t = b.emit("pe", lambda e: e.transpose(b.pb[7][0:16, 0:P], maxk[:],
                                           res.identf[:]), waits=[t], inc=s)
    t = b.emit("dve", lambda e: e.tensor_copy(t16[:], b.pb[7][0:16, 0:P]),
               waits=[t], inc=s)
    t = b.emit("dve", lambda e: e.reduce_max(red[:, 1:2], t16[:], axis=AX.X),
               waits=[t], inc=s)
    t = b.emit("dve", lambda e: e.tensor_mul(prod[:], red[:, 0:1], red[:, 1:2]),
               waits=[t], inc=s)
    t = b.emit("act", lambda e: e.activation(prod[:], prod[:], AF.Sqrt),
               waits=[t], inc=s)
    t = b.emit("dve", lambda e: e.tensor_scalar(prod[:], prod[:], -0.125, None,
                                                ALU.mult), waits=[t], inc=s)
    t = b.emit("pe", lambda e: e.transpose(b.pb[7][0:1, 0:16], prod[:],
                                           res.identf[0:16, 0:16]),
               waits=[t], inc=s)
    t = b.emit("dve", lambda e: e.tensor_copy(row[:], b.pb[7][0:1, 0:16]),
               waits=[t], inc=s)
    t = b.emit("dve", lambda e: e.tensor_copy(rowb[:], row[:]), waits=[t], inc=s)
    t2 = bcast16(b, pref, rowb, mhat_neg, psum_bank=7, extra_waits=[t])
    if mpair_neg is not None:
        mv = mhat_neg[:]
        ev_ = bass.AP(tensor=mv.tensor, offset=mv.offset,
                      ap=[[mv.ap[0][0], P], [2, 8]])
        od_ = bass.AP(tensor=mv.tensor, offset=mv.offset + 1,
                      ap=[[mv.ap[0][0], P], [2, 8]])
        b.emit("dve", lambda e: e.tensor_tensor(mpair_neg[:], ev_, od_, ALU.min),
               waits=[t2], inc=s)
    b.barrier()
    b.reset(m)


def attn_core(b, pref, q_buf, k_resT, v_res, mhat_neg, mpair_neg, o_buf,
              window_tiles=None, causal=False, diagmask=None):
    res = b.res
    NT = b.NT
    m = b.mark()
    qt = [b.sbuf(f"{pref}_qt{j}", [P, D], BF16) for j in range(2)]
    pt = [b.sbuf(f"{pref}_pt{j}", [P, 1024], BF16) for j in range(2)]
    ot = [b.sbuf(f"{pref}_ot{j}", [P, D], BF16) for j in range(2)]
    linv = b.sbuf(f"{pref}_li", [P, 16], F32)
    chq = b.chan("atq")
    cso = b.chan("ato", eng="act", nslots=2)
    smm = b.sem("atc_smm")
    smk = b.sem("atc_smk")
    sex = b.sem("atc_sex")
    sav = b.sem("atc_sav")
    sov = b.sem("atc_sov")
    otoks = {}
    oevt = {}
    expt = {}
    ptuse = {}
    qtok = {}
    def oslot(h):
        return b.pb[h // 6], (h % 6) * 66
    for qs in range(NT):
        j = qs % 2
        if causal:
            kts = list(range(0, qs + 1))
        elif window_tiles is not None:
            kts = list(range(max(0, qs - window_tiles), qs + 1))
        else:
            kts = list(range(NT))
        groups = [kts[g:g + 4] for g in range(0, len(kts), 4)]
        qw = [qtok[qs - 2]] if qs >= 2 else []
        qv = q_buf
        qap = bass.AP(tensor=qv.tensor, offset=qv.offset + qs * P,
                      ap=[[b.T, P], [P * b.T, ND], [1, P]])
        qtok[qs] = chq.issue(qt[j].ap().rearrange("p (c t) -> p c t", c=ND),
                             qap, waits=qw)
        ow0 = [oevt[qs - 1]] if qs >= 1 else []
        if SIM_OPEN_BANKS:
            def ozero(e):
                ins = None
                for bk in range(3):
                    ins = e.matmul(b.pb[bk][:, 0:512], res.ones[0:1, 0:P],
                                   res.zrow[0:1, 0:512], start=True, stop=False)
                return ins
            toz = b.emit("pe", ozero, waits=ow0, inc=sav)
        else:
            first_bank = {0: True, 1: True, 2: True}
        lastuse = {}
        for gi, grp in enumerate(groups):
            for hp2 in range(8):
                for hh2 in (2 * hp2, 2 * hp2 + 1):
                    lastuse[hh2 // 6] = (gi, hh2, len(grp) - 1)
        units = [(gi, grp, hp) for gi, grp in enumerate(groups)
                 for hp in range(8)]

        def emit_scores(u):
            gi, grp, hp = units[u]
            h0, h1 = 2 * hp, 2 * hp + 1
            st_ = u % 2
            pp = pt[u % 2]
            ng = len(grp)
            tmms = []
            for hh, bi_ in ((h0, 0), (h1, 1)):
                sbank = b.pb[3 + st_ * 2 + bi_]
                ch_ = hh // 2
                base = (hh % 2) * 64
                tp = (base, 0)
                def scmm(e, hh=hh, sbank=sbank, grp=grp, ch_=ch_,
                         base=base, tp=tp, j=j):
                    ins = None
                    if SIM_OPEN_BANKS:
                        e.matmul(sbank[:, 0:512], res.ones[0:1, 0:P],
                                 res.zrow[0:1, 0:512], start=True, stop=False)
                    for idx, kt in enumerate(grp):
                        off = idx * P
                        ins = e.matmul(
                            sbank[:, off:off + P],
                            k_resT[ch_][base:base + 64, kt * P:(kt + 1) * P],
                            qt[j][base:base + 64, ch_ * P:(ch_ + 1) * P],
                            start=(not SIM_OPEN_BANKS and idx == 0),
                            stop=(idx == len(grp) - 1),
                            tile_position=tp)
                    return ins
                key = u - 2
                bw = [expt[(qs, key)]] if key >= 0 else \
                    ([expt[qs - 1]] if qs >= 1 else [])
                tmm = b.emit("pe", scmm, waits=[qtok[qs]] + bw, inc=smm)
                if causal and diagmask is not None and grp[-1] == qs:
                    idx = len(grp) - 1
                    off = idx * P
                    tmm = b.emit("dve", lambda e, sbank=sbank, off=off:
                        e.tensor_tensor(sbank[:, off:off + P],
                                        sbank[:, off:off + P], diagmask[:],
                                        ALU.add), waits=[tmm], inc=smk)
                if (not causal) and window_tiles is not None and diagmask:
                    for idx, kt in enumerate(grp):
                        dlt = qs - kt
                        if dlt in diagmask:
                            off = idx * P
                            tmm = b.emit("dve",
                                lambda e, sbank=sbank, off=off, dlt=dlt:
                                e.tensor_tensor(sbank[:, off:off + P],
                                                sbank[:, off:off + P],
                                                diagmask[dlt][:], ALU.add),
                                waits=[tmm], inc=smk)
                tmms.append(tmm)
            pw = [ptuse[pp.name]] if pp.name in ptuse else []
            sv = b.pbS.ap()
            src_ap = bass.AP(tensor=sv.tensor, offset=sv.offset + st_ * 1024,
                             ap=[[sv.ap[0][0], P], [512, 2], [1, ng * P]])
            ppv = pp.ap()
            dst_ap = bass.AP(tensor=ppv.tensor, offset=ppv.offset,
                             ap=[[ppv.ap[0][0], P], [512, 2], [1, ng * P]])
            tex = b.emit("act", lambda e, src_ap=src_ap, dst_ap=dst_ap, hp=hp:
                e.activation(dst_ap, src_ap, AF.Exp,
                             bias=mpair_neg[:, hp:hp + 1], scale=0.125),
                waits=tmms + pw, inc=sex)
            expt[(qs, u)] = tex

        def emit_av(u):
            gi, grp, hp = units[u]
            h0, h1 = 2 * hp, 2 * hp + 1
            pp = pt[u % 2]
            tex = expt[(qs, u)]
            tav = None
            for hh, poff in ((h0, 0), (h1, 512)):
                obank, ooff = oslot(hh)
                lu = lastuse[hh // 6]
                if SIM_OPEN_BANKS:
                    fb_open = False
                else:
                    fb_open = first_bank[hh // 6] and gi == 0
                    if gi == 0:
                        first_bank[hh // 6] = False
                def avmm(e, hh=hh, grp=grp, pp=pp, poff=poff, obank=obank,
                         ooff=ooff, gi=gi, lu=lu, fb_open=fb_open):
                    ins = None
                    for idx, kt in enumerate(grp):
                        ins = e.matmul(
                            obank[:, ooff:ooff + 65],
                            pp[:, poff + idx * P:poff + (idx + 1) * P],
                            v_res[:, kt * 1056 + 66 * hh:kt * 1056 + 66 * hh + 65],
                            start=(fb_open and idx == 0),
                            stop=(lu == (gi, hh, idx)))
                    return ins
                ow2 = [oevt[qs - 1]] if (qs >= 1 and not SIM_OPEN_BANKS and
                                         fb_open) else []
                tav = b.emit("pe", avmm, waits=[tex] + ow2, inc=sav)
                otoks[(qs, hh)] = tav
            ptuse[pp.name] = tav

        for u in range(len(units)):
            emit_scores(u)
            if u >= 1:
                emit_av(u - 1)
        emit_av(len(units) - 1)
        lw = [otoks[(qs, hh)] for hh in range(16)]
        tprev = [oevt[qs - 1]] if qs >= 1 else []
        tl = None
        for bk in range(3):
            nheads = 6 if bk < 2 else 4
            bank = b.pb[bk]
            lap = bass.AP(tensor=bank[:].tensor, offset=bank[:].offset + 64,
                          ap=[[bank[:].ap[0][0], P], [66, nheads]])
            tl = b.emit("dve", lambda e, lap=lap, bk=bk, nheads=nheads:
                e.reciprocal(linv[:, bk * 6:bk * 6 + nheads], lap),
                waits=lw + tprev + ([tl] if tl else []), inc=sov)
        tno = []
        for hh in range(16):
            obank, ooff = oslot(hh)
            tno.append(b.emit("dve", lambda e, hh=hh, obank=obank, ooff=ooff, j=j:
                e.tensor_scalar(ot[j][:, 64 * hh:64 * hh + 64],
                                obank[:, ooff:ooff + 64], linv[:, hh:hh + 1],
                                None, ALU.mult),
                waits=[tl] + ([oevt[qs - 2], otoks[qs - 2]] if qs >= 2 else []),
                inc=sov))
        oevt[qs] = tno[-1]
        otoks[qs] = cso.issue(o_buf[qs * P:(qs + 1) * P, :], ot[j][:],
                              waits=[tno[-1]])
        expt[qs] = tno[-1]
    b.barrier(toks=[otoks[i] for i in (NT - 2, NT - 1) if i in otoks])
    b.reset(m)


def bcast_row_tile(b, pref, row_hbm_1d, width, reps, dst):
    res = b.res
    m = b.mark()
    r1 = b.sbuf(pref + "_r1", [1, width], F32)
    r2 = b.sbuf(pref + "_r2", [1, width * reps], BF16)
    s = b.sem("brt_s")
    lds = b.sem("brt_dl")
    tok = b.dma("sp", r1[:], row_hbm_1d.rearrange("(a c) -> a c", a=1), inc=lds)
    src_ap = bass.AP(tensor=r1[:].tensor, offset=r1[:].offset,
                     ap=[[r1[:].ap[0][0], 1], [0, reps], [1, width]])
    t = b.emit("dve", lambda e: e.tensor_copy(
        r2.ap().rearrange("a (r w) -> a r w", r=reps), src_ap),
        waits=[tok], inc=s)
    tot = width * reps
    tm1 = b.emit("pe", lambda e: e.matmul(b.pb[7][:, 0:min(tot, 512)],
                                          res.ones[0:1, 0:P],
                                          r2[0:1, 0:min(tot, 512)],
                                          start=True, stop=True),
                 waits=[t], inc=s)
    t2 = b.emit("dve", lambda e: e.tensor_copy(dst[:, 0:min(tot, 512)],
                                               b.pb[7][:, 0:min(tot, 512)]),
                waits=[tm1], inc=s)
    if tot > 512:
        tm2 = b.emit("pe", lambda e: e.matmul(b.pb[6][:, 0:tot - 512],
                                              res.ones[0:1, 0:P],
                                              r2[0:1, 512:tot],
                                              start=True, stop=True),
                     waits=[t], inc=s)
        t2 = b.emit("dve", lambda e: e.tensor_copy(dst[:, 512:tot],
                                                   b.pb[6][:, 0:tot - 512]),
                    waits=[tm2], inc=s)
    b.barrier()
    b.reset(m)


def load_weight_chunks_cols(b, w_hbm, name, col0, sem, rows=D, cols=D):
    nr = rows // P
    tiles = []
    for c in range(nr):
        t = b.sbuf(f"{name}_{c}", [P, cols], BF16)
        b.dma("pool", t[:], w_hbm[c * P:(c + 1) * P, col0:col0 + cols], inc=sem)
        tiles.append(t)
    return tiles


def attn_softmax(b, ctx, pref, x_hbm, dst_hbm, wq, wk, wv, wo, bq, bk, bv, bo,
                 transform_q=None, transform_k=None, window_tiles=None,
                 causal=False, coef=1.0, diagmask=None, qkv_col0=None,
                 wlayout="do", out_stats=None):
    res = b.res
    n = ctx["n"]
    qtm = ctx["at_q"]
    ktm = ctx["at_k"]
    qfm = ctx["at_qf"]
    obuf = ctx["at_o"]
    am = b.mark()
    v_res = b.sbuf(f"at{n}_v", [P, b.NT * 1056], BF16)
    maxq = b.sbuf(f"at{n}_mq", [P, 16], F32)
    maxk = b.sbuf(f"at{n}_mk", [P, 16], F32)
    mhat_neg = b.sbuf(f"at{n}_mh", [P, 16], F32)
    mpair_neg = b.sbuf(f"at{n}_mp", [P, 8], F32)
    cq, ck, cv = qkv_col0 if qkv_col0 is not None else (0, 0, 0)
    m1 = b.mark()
    ldw = b.sem("atw_ld")
    wtls = {}
    btls = {}
    for (wt, bt, which) in ((wq, bq, "q"), (wk, bk, "k"), (wv, bv, "v")):
        col0 = {"q": cq, "k": ck, "v": cv}[which]
        if wlayout == "od":
            wtls[which] = transpose_weight(b, wt, f"at{n}w{which}")
        else:
            wtls[which] = load_weight_chunks_cols(b, wt, f"at{n}w{which}",
                                                  col0, ldw)
        btls[which] = bias_row(b, bt, f"at{n}b{which}", D, ldw) \
            if bt is not None else None
    b.emit("pe", lambda e, v=ldw.n: e.wait_ge(ldw.h, v))
    vz = b.sem("at_vz")
    tz = b.emit("dve", lambda e: e.memset(v_res[:], 0.0), inc=vz)
    vap = v_res[:]
    onesap = bass.AP(tensor=vap.tensor, offset=vap.offset + 64,
                     ap=[[vap.ap[0][0], P], [1056, b.NT], [66, 16]])
    tz = b.emit("dve", lambda e, onesap=onesap: e.memset(onesap, 1.0),
                waits=[tz], inc=vz)
    b.wait_toks("dve", [tz])
    epis = {"q": StoreEpi(b, f"at{n}Eq", qtm, dt=BF16),
            "k": StoreEpi(b, f"at{n}Ek", ktm, dt=BF16),
            "v": VEpi(b, f"at{n}Ev", v_res)}
    mm = b.sem("lin_mm")
    evtoks = {}
    NT_ = b.NT
    for i in range(NT_):
        for wi, which in enumerate(("q", "k", "v")):
            wtl = wtls[which]
            btl = btls[which]
            for nn in range(2):
                bank = b.pb[(i % 2) * 3 + wi][:, 0:512] if False else \
                    b.pb[(i % 2) * 4 + nn + (2 if wi == 1 else 0)] \
                    if False else b.pb[((i * 3 + wi) % 4) * 2 + nn]
                def mmf(e, i=i, nn=nn, wtl=wtl, btl=btl, bank=bank):
                    ins = None
                    for c in range(ND):
                        ins = e.matmul(bank[:], res.xT[c][:, i * P:(i + 1) * P],
                                       wtl[c][:, nn * 512:(nn + 1) * 512],
                                       start=(c == 0),
                                       stop=(btl is None and c == ND - 1))
                    if btl is not None:
                        ins = e.matmul(bank[:], res.ones[0:1, 0:P],
                                       btl[0:1, nn * 512:(nn + 1) * 512],
                                       start=False, stop=True)
                    return ins
                key = (i * 3 + wi - 4, nn)
                bw = [evtoks[key]] if key in evtoks else []
                mmtok = b.emit("pe", mmf, waits=bw, inc=mm)
                evtoks[(i * 3 + wi, nn)] = epis[which](i, nn, bank, mmtok)
    b.barrier(toks=epis["q"].finish() + epis["k"].finish())
    b.reset(m1)
    qk_fm_pass(b, qtm, f"at{n}Fq", transform_q, dst_hbm=qfm, norm_out=maxq)
    qk_fm_pass(b, ktm, f"at{n}Fk", transform_k, dst_sbufT=res.xT, norm_out=maxk)
    mhat_phase(b, f"at{n}M", maxq, maxk, mhat_neg, mpair_neg)
    attn_core(b, f"at{n}C", qfm, res.xT, v_res, mhat_neg, mpair_neg, obuf,
              window_tiles=window_tiles, causal=causal, diagmask=diagmask)
    b.reset(am)
    to_fm(b, obuf, res.xT, f"at{n}To", src_bf16=True)
    m5 = b.mark()
    ldw = b.sem("atw_ld")
    if wlayout == "od":
        wol = transpose_weight(b, wo, f"at{n}wo")
    else:
        wol = load_weight_chunks(b, wo, f"at{n}wo", D, D, ldw)
    bol = bias_row(b, bo, f"at{n}bo", D, ldw) if bo is not None else None
    b.emit("pe", lambda e, v=ldw.n: e.wait_ge(ldw.h, v))
    epi = ResidualEpi(b, f"at{n}Eo", x_hbm, dst_hbm, coef, stats=out_stats)
    linear_tm(b, f"at{n}Lo", res.xT, wol, bol, epi)
    b.barrier(toks=epi.finish())
    b.reset(m5)
    ctx["n"] += 1


def transpose_weight(b, w_hbm, name):
    res = b.res
    outs = [b.sbuf(f"{name}_{c}", [P, D], BF16) for c in range(ND)]
    m = b.mark()
    wt = [b.sbuf(f"{name}_w{j}", [P, D], BF16) for j in range(2)]
    ch = b.chan("twt_l", eng="pool")
    tr = b.sem("twt_tr")
    ev = b.sem("twt_ev")
    ttr, tev = {}, {}
    for oc in range(ND):
        j = oc % 2
        w = [ttr[(oc - 2, ND - 1)]] if oc >= 2 else []
        tok = ch.issue(wt[j][:], w_hbm[oc * P:(oc + 1) * P, :], waits=w)
        for c in range(ND):
            bank = b.pb[4 + c % 4].ap().bitcast(BF16)
            prev = (oc, c - 4) if c >= 4 else ((oc - 1, c + 4) if oc >= 1 else None)
            bw = [tev[prev]] if prev else []
            ttr[(oc, c)] = b.emit("pe", lambda e, j=j, c=c, bank=bank: e.transpose(
                bank[:, 0:P], wt[j][:, c * P:(c + 1) * P], res.ident[:]),
                waits=[tok] + bw, inc=tr)
            tev[(oc, c)] = b.emit("dve", lambda e, c=c, oc=oc, bank=bank:
                e.tensor_copy(outs[c][:, oc * P:(oc + 1) * P], bank[:, 0:P]),
                waits=[ttr[(oc, c)]], inc=ev)
    b.barrier()
    b.reset(m)
    return outs


class VPlainEpi:

    def __init__(self, b, pref, v_res):
        self.b = b
        self.v_res = v_res
        self.ev = b.sem("epi_ev")

    def __call__(self, i, n, bank, mmtok):
        t = self.b.emit("dve", lambda e, bank=bank, i=i, n=n: e.tensor_copy(
            self.v_res[:, i * D + n * 512:i * D + (n + 1) * 512], bank[:]),
            waits=[mmtok], inc=self.ev)
        return t

    def finish(self):
        return []


def retention_consts(b):
    g = 0.96875
    outs = []
    kp = np.arange(P)[:, None]
    qf = np.arange(P)[None, :]
    for d in range(9):
        expo = qf + 128 * d - kp
        v = np.where(expo >= 0, np.power(np.float64(g), expo), 0.0)
        outs.append(b.const(f"c_ret{d}", f32c(v)))
    return outs


def sublayer_l3attn(b, ctx, x_hbm, dst_hbm, out_stats=None):
    n = ctx["n"]
    res = b.res
    NT = b.NT
    qtm, ktm, qfm = ctx["at_q"], ctx["at_k"], ctx["at_qf"]
    am = b.mark()
    v_res = b.sbuf(f"r3v{n}", [P, NT * D], BF16)
    dct = [b.sbuf(f"r3d{n}_{d}", [P, P], F32) for d in range(9)]
    ldd = b.sem("r3_ldd")
    for d in range(9):
        b.dma("sp", dct[d][:], ctx["ret_consts"][d], inc=ldd)
    b.emit("dve", lambda e, v=ldd.n: e.wait_ge(ldd.h, v))
    for which, wname, bname in (("q", "L3_attn_Wq", "L3_attn_bq"),
                                ("k", "L3_attn_Wk", "L3_attn_bk"),
                                ("v", "L3_attn_Wv", None)):
        m1 = b.mark()
        wT = transpose_weight(b, ctx[wname], f"r3w{n}{which}")
        if bname is not None:
            ldb = b.sem("r3_bl")
            bt = bias_row(b, ctx[bname], f"r3b{n}{which}", D, ldb)
            b.emit("pe", lambda e, v=ldb.n: e.wait_ge(ldb.h, v))
        else:
            bt = None
        if which == "q":
            epi = StoreEpi(b, f"r3Eq{n}", qtm, dt=BF16)
        elif which == "k":
            epi = StoreEpi(b, f"r3Ek{n}", ktm, dt=BF16)
        else:
            epi = VPlainEpi(b, f"r3Ev{n}", v_res)
        linear_tm(b, f"r3L{n}{which}", res.xT, wT, bt, epi)
        b.barrier(toks=epi.finish())
        b.reset(m1)
    qk_fm_pass(b, qtm, f"r3Fq{n}", None, dst_hbm=qfm)
    qk_fm_pass(b, ktm, f"r3Fk{n}", None, dst_sbufT=res.xT)
    m2 = b.mark()
    qt = [b.sbuf(f"r3qt{n}_{j}", [P, D], BF16) for j in range(2)]
    pt = [b.sbuf(f"r3pt{n}_{j}", [P, 1280], BF16) for j in range(2)]
    xa = [b.sbuf(f"r3xa{n}_{j}", [P, D], F32) for j in range(2)]
    xn = [b.sbuf(f"r3xn{n}_{j}", [P, D], F32) for j in range(2)]
    chq = b.chan("r3_q")
    chx = b.chan("r3_x")
    cso = b.chan("r3_o", eng="act", nslots=2)
    smm = b.sem("r3_smm")
    sdk = b.sem("r3_sdk")
    sav = b.sem("r3_sav")
    sev = b.sem("r3_sev")
    qtok, xtok, stok, devt, oevt, statoks = {}, {}, {}, {}, {}, {}
    for qs in range(NT):
        j = qs % 2
        kts = list(range(max(0, qs - 8), qs + 1))
        qw = [qtok[qs - 2]] if qs >= 2 else []
        qv = qfm
        qap = bass.AP(tensor=qv.tensor, offset=qv.offset + qs * P,
                      ap=[[b.T, P], [P * b.T, ND], [1, P]])
        qtok[qs] = chq.issue(qt[j].ap().rearrange("p (c t) -> p c t", c=ND),
                             qap, waits=qw)
        xw = [oevt[qs - 2]] if qs >= 2 else []
        xtok[qs] = chx.issue(xa[j][:], x_hbm[qs * P:(qs + 1) * P, :], waits=xw)
        nk = len(kts)
        nbk = (nk + 3) // 4
        def scm(e, kts=kts, j=j, qs=qs):
            ins = None
            for bb in range((len(kts) + 3) // 4):
                if SIM_OPEN_BANKS:
                    e.matmul(b.pb[3 + bb][:, 0:512], res.ones[0:1, 0:P],
                             res.zrow[0:1, 0:512], start=True, stop=False)
            for idx, kt in enumerate(kts):
                bank = b.pb[3 + idx // 4]
                off = (idx % 4) * P
                for c in range(ND):
                    ins = e.matmul(bank[:, off:off + P],
                                   res.xT[c][:, kt * P:(kt + 1) * P],
                                   qt[j][:, c * P:(c + 1) * P],
                                   start=(not SIM_OPEN_BANKS and c == 0 and
                                          idx % 4 == 0),
                                   stop=(c == ND - 1 and (idx % 4 == 3 or
                                                          idx == len(kts) - 1)))
            return ins
        bw = [devt[qs - 1]] if qs >= 1 else []
        tmm = b.emit("pe", scm, waits=[qtok[qs]] + bw, inc=smm)
        tdk = None
        for idx, kt in enumerate(kts):
            bank = b.pb[3 + idx // 4]
            off = (idx % 4) * P
            dd = qs - kt
            pw = [devt[qs - 1]] if (qs >= 1 and idx == 0) else []
            tdk = b.emit("dve", lambda e, bank=bank, off=off, dd=dd, idx=idx, j=j:
                e.tensor_mul(pt[j][:, idx * P:(idx + 1) * P],
                             bank[:, off:off + P], dct[dd][:]),
                waits=[tmm] + pw + ([tdk] if tdk else []), inc=sdk)
        devt[qs] = tdk
        def avm(e, kts=kts, j=j):
            ins = None
            for bb in range(2):
                if SIM_OPEN_BANKS:
                    e.matmul(b.pb[6 + bb][:, 0:512], res.ones[0:1, 0:P],
                             res.zrow[0:1, 0:512], start=True, stop=False)
            for idx, kt in enumerate(kts):
                for nn in range(2):
                    ins = e.matmul(b.pb[6 + nn][:, 0:512],
                                   pt[j][:, idx * P:(idx + 1) * P],
                                   v_res[:, kt * D + nn * 512:kt * D + (nn + 1) * 512],
                                   start=(not SIM_OPEN_BANKS and idx == 0),
                                   stop=(idx == len(kts) - 1))
            return ins
        aw = [oevt[qs - 1]] if qs >= 1 else []
        tav = b.emit("pe", avm, waits=[tdk] + aw, inc=sav)
        tn = None
        for nn in range(2):
            sw = [stok[qs - 2]] if qs >= 2 else []
            if qs >= 2 and out_stats is not None:
                sw = sw + [statoks[qs - 2]]
            tn = b.emit("dve", lambda e, j=j, nn=nn: e.tensor_add(
                xn[j][:, nn * 512:(nn + 1) * 512], b.pb[6 + nn][:],
                xa[j][:, nn * 512:(nn + 1) * 512]),
                waits=[tav, xtok[qs]] + sw + ([tn] if tn else []), inc=sev)
        oevt[qs] = tn
        stok[qs] = cso.issue(dst_hbm[qs * P:(qs + 1) * P, :], xn[j][:],
                             waits=[tn])
        if out_stats is not None:
            sums, sumsq = out_stats
            trd = b.emit("dve", lambda e, j=j, qs=qs: e.reduce_sum(
                sums[:, qs:qs + 1], xn[j][:], axis=AX.X), waits=[tn],
                inc=b.sem("epi_dnd"))
            statoks[qs] = b.emit("act", lambda e, j=j, qs=qs: e.activation(
                xn[j][:], xn[j][:], AF.Square, accum_out=sumsq[:, qs:qs + 1]),
                waits=[trd, stok[qs]] + ([statoks[qs - 1]] if qs >= 1 else []),
                inc=b.sem("epi_dna"))
    b.barrier(toks=[stok[i] for i in (NT - 2, NT - 1) if i in stok])
    b.reset(m2)
    b.reset(am)
    ctx["n"] += 1


def ln_materialize(b, src_hbm, dst_hbm, pref):
    res = b.res
    NT = b.NT
    m = b.mark()
    u = b.uid()
    xa = [b.sbuf(f"lm{u}_x{j}", [P, D], F32) for j in range(2)]
    xn = [b.sbuf(f"lm{u}_n{j}", [P, D], F32) for j in range(2)]
    ch = b.chan("ld")
    cst = b.chan("est", eng="act", nslots=2)
    ap_ = b.sem("lnm_ap")
    ts, tt = {}, {}
    for i in range(NT):
        j = i % 2
        w = [tt[i - 2]] if i >= 2 else []
        tok = ch.issue(xa[j][:], src_hbm[i * P:(i + 1) * P, :], waits=w)
        sw = [ts[i - 2]] if i >= 2 else []
        tt[i] = b.emit("dve", lambda e, j=j, i=i: e.tensor_scalar(
            xn[j][:], xa[j][:], res.mu[:, i:i + 1], res.rstd[:, i:i + 1],
            ALU.subtract, ALU.mult), waits=[tok] + sw, inc=ap_)
        ts[i] = cst.issue(dst_hbm[i * P:(i + 1) * P, :], xn[j][:], waits=[tt[i]])
    b.barrier(toks=[ts[i] for i in (NT - 2, NT - 1) if i in ts])
    b.reset(m)


def sublayer_l3ffn(b, ctx, x_hbm, dst_hbm, out_stats=None):
    n = ctx["n"]
    res = b.res
    NT = b.NT
    ntt = b.T // 512
    sub_m = b.mark()
    g1 = col_layout(b, ctx["L3_ffn_ln1_g"], f"f3g1{n}")
    b1c = col_layout(b, ctx["L3_ffn_ln1_b"], f"f3b1{n}")
    g2 = col_layout(b, ctx["L3_ffn_ln2_g"], f"f3g2{n}")
    b2c = col_layout(b, ctx["L3_ffn_ln2_b"], f"f3b2{n}")
    halfmask = b.sbuf(f"f3hm{n}", [P, NT * 8], F32)
    x2 = ctx["f3_x2"]
    xh2 = ctx["f3_xh2"]
    acc = ctx["f3_acc"]
    xln = ctx["f3_xln"]
    ln_materialize(b, x_hbm, xln, f"f3L{n}")
    ln_stats(b, xln, f"f3sl{n}")
    to_fm(b, xln, res.xT, f"f3A{n}", mu=res.mu, rstd=res.rstd,
          evac=("affine", g1, b1c))
    attn_softmax(b, ctx, f"f3T{n}", xln, x2, ctx["L3_ffn_wq"],
                 ctx["L3_ffn_wk"], ctx["L3_ffn_wv"], ctx["L3_ffn_wo"],
                 ctx["L3_ffn_bq"], ctx["L3_ffn_bk"], ctx["L3_ffn_bv"],
                 ctx["L3_ffn_bo"], coef=1.0)
    ln_stats(b, x2, f"f3s{n}")
    qk_fm_pass(b, x2, f"f3F{n}", ("ln", res.mu, res.rstd), dst_hbm=xh2,
               post=("affine", g2, b2c), dst_hbm_lo=ctx["f3_xh2lo"])
    mg = b.mark()
    gw = [b.sbuf(f"f3gw{n}_{c}", [P, 8], BF16) for c in range(ND)]
    gwlo = [b.sbuf(f"f3gl{n}_{c}", [P, 8], BF16) for c in range(ND)]
    gwf = b.sbuf(f"f3gf{n}", [P, 8], F32)
    gb = b.sbuf(f"f3gb{n}", [1, 8], BF16)
    xt = [b.sbuf(f"f3xt{n}_{j}", [P, D], BF16) for j in range(2)]
    xtl = [b.sbuf(f"f3xl{n}_{j}", [P, D], BF16) for j in range(2)]
    lg = b.sbuf(f"f3lg{n}", [P, 8], F32)
    mx = b.sbuf(f"f3mx{n}", [P, 1], F32)
    ldg = b.sem("f3_ldg")
    sgp = b.sem("f3_sgp")
    chgw = b.chan("f3_gw", eng="pool", nslots=4)
    chgf = b.chan("f3_gf", eng="sp", nslots=1)
    sgp0 = sgp.n
    for c in range(ND):
        tga = chgw.issue(gw[c][:], ctx["L3_ffn_gate_w"][c * P:(c + 1) * P, :])
        tgb = chgf.issue(gwf[:], ctx["L3_ffn_gate_w"][c * P:(c + 1) * P, :],
                         waits=[(sgp, sgp0 + 2 * c)])
        t1 = b.emit("dve", lambda e, c=c: e.tensor_sub(gwf[:], gwf[:], gw[c][:]),
                    waits=[tga, tgb], inc=sgp)
        b.emit("dve", lambda e, c=c: e.tensor_copy(gwlo[c][:], gwf[:]),
               waits=[t1], inc=sgp)
    b.dma("pool", gb[:], ctx["L3_ffn_gate_b"].rearrange("(a c) -> a c", a=1),
          inc=ldg)
    ldgv = ldg.n
    sgpv = sgp.n
    gw_toks = [tga]
    chg = b.chan("f3_gq")
    sg_ = b.sem("f3_sg")
    sgd = b.sem("f3_sgd")
    gtok, gevt = {}, {}
    for i in range(NT):
        j = i % 2
        qv = xh2
        qap = bass.AP(tensor=qv.tensor, offset=qv.offset + i * P,
                      ap=[[b.T, P], [P * b.T, ND], [1, P]])
        w = [gevt[i - 2]] if i >= 2 else []
        gtok[i] = chg.issue(xt[j].ap().rearrange("p (c t) -> p c t", c=ND),
                            qap, waits=w)
        qvl = ctx["f3_xh2lo"]
        qapl = bass.AP(tensor=qvl.tensor, offset=qvl.offset + i * P,
                       ap=[[b.T, P], [P * b.T, ND], [1, P]])
        gtok2 = chg.issue(xtl[j].ap().rearrange("p (c t) -> p c t", c=ND),
                          qapl, waits=w)
        bank = b.pb[i % 2]
        def gmm(e, j=j, bank=bank):
            for c in range(ND):
                e.matmul(bank[:, 0:8], xt[j][:, c * P:(c + 1) * P], gw[c][:],
                         start=(c == 0), stop=False)
                e.matmul(bank[:, 0:8], xt[j][:, c * P:(c + 1) * P], gwlo[c][:],
                         start=False, stop=False)
                e.matmul(bank[:, 0:8], xtl[j][:, c * P:(c + 1) * P], gw[c][:],
                         start=False, stop=False)
            return e.matmul(bank[:, 0:8], res.ones[0:1, 0:P], gb[0:1, :],
                            start=False, stop=True)
        bw = [gevt[i - 2]] if i >= 2 else []
        t1 = b.emit("pe", gmm, waits=[(ldg, ldgv), (sgp, sgpv), gtok[i], gtok2]
                    + bw + gw_toks, inc=sg_)
        t2 = b.emit("dve", lambda e, bank=bank: e.tensor_copy(lg[:], bank[:, 0:8]),
                    waits=[t1] + ([gevt[i - 1]] if i >= 1 else []), inc=sgd)
        t3 = b.emit("dve", lambda e: e.reduce_max(mx[:], lg[:], axis=AX.X),
                    waits=[t2], inc=sgd)
        t4 = b.emit("dve", lambda e, i=i: e.tensor_scalar(
            halfmask[:, i * 8:(i + 1) * 8], lg[:], mx[:, 0:1], None,
            ALU.is_equal), waits=[t3], inc=sgd)
        gevt[i] = b.emit("dve", lambda e, i=i: e.tensor_scalar(
            halfmask[:, i * 8:(i + 1) * 8], halfmask[:, i * 8:(i + 1) * 8],
            0.5, None, ALU.mult), waits=[t4], inc=sgd)
    b.barrier()
    b.reset(mg)
    mm_ = b.mark()
    w2t = [b.sbuf(f"f3w2{n}_{fc}", [P, D], BF16) for fc in range(32)]
    midt = [b.sbuf(f"f3md{n}_{fc}", [P, 512], BF16) for fc in range(32)]
    b1r = b.sbuf(f"f3b1r{n}", [1, 4096], BF16)
    b2r = b.sbuf(f"f3b2r{n}", [1, D], BF16)
    xh2t = [b.sbuf(f"f3h2{n}_{jj}", [P, ND * 512], BF16) for jj in range(2)]
    xac = [b.sbuf(f"f3xa{n}_{j}", [P, D], F32) for j in range(2)]
    xno = [b.sbuf(f"f3xn{n}_{j}", [P, D], F32) for j in range(2)]
    w1t = res.xT
    chw = b.chan("f3_w", eng="pool", nslots=8)
    chh = b.chan("f3_h")
    chx = b.chan("f3_xl")
    cst = b.chan("f3_st", nslots=2)
    smm = b.sem("f3_smm")
    sev = b.sem("f3_sev")
    smid = b.sem("f3_smid")
    prev_store = {}
    statoks = {}
    prev_mid_mm = None
    prev_w1_use = None
    prev_w2_use = None
    prev_acc_tok = {}
    for e in range(8):
        w1toks = []
        for c in range(ND):
            w = [prev_w1_use] if (e > 0 and c == 0) else []
            w1toks.append(chw.issue(w1t[c][:, 0:4096],
                                    ctx["L3_ffn_w1"][e][c * P:(c + 1) * P, :],
                                    waits=w))
        w2toks = []
        for fc in range(32):
            w = [prev_w2_use] if (e > 0 and fc == 0) else []
            w2toks.append(chw.issue(w2t[fc][:],
                                    ctx["L3_ffn_w2"][e][fc * P:(fc + 1) * P, :],
                                    waits=w))
        tb1 = chw.issue(b1r[:], ctx["L3_ffn_b1"][e].rearrange("(a c) -> a c", a=1),
                        waits=[prev_w1_use] if e > 0 else [])
        tb2 = chw.issue(b2r[:], ctx["L3_ffn_b2"][e].rearrange("(a c) -> a c", a=1),
                        waits=[prev_w1_use] if e > 0 else [])
        wall = w1toks + w2toks + [tb1, tb2]
        htok, mide, ymm = {}, {}, {}
        for tt in range(ntt):
            j = tt % 2
            hw = [mide.get((tt - 2, 31))] if tt >= 2 else \
                ([prev_mid_mm] if e > 0 else [])
            hw = [t for t in hw if t]
            hv = xh2
            hap = bass.AP(tensor=hv.tensor, offset=hv.offset + tt * 512,
                          ap=[[b.T, P], [P * b.T, ND], [1, 512]])
            htok[tt] = chh.issue(
                xh2t[j].ap().rearrange("p (c t) -> p c t", c=ND), hap, waits=hw)
            hts = [htok[tt]]
            for fc in range(32):
                bank = b.pb[fc % 2]
                def mmid(e2, fc=fc, j=j, bank=bank):
                    for dc in range(ND):
                        e2.matmul(bank[:], w1t[dc][:, fc * P:(fc + 1) * P],
                                  xh2t[j][:, dc * 512:(dc + 1) * 512],
                                  start=(dc == 0), stop=False)
                    return e2.matmul(bank[:], b1r[0:1, fc * P:(fc + 1) * P],
                                     res.onesw[0:1, 0:512], start=False, stop=True)
                k = fc
                bw = [mide[(tt, fc - 2)]] if fc >= 2 else \
                    ([mide[(tt - 1, 31)]] if tt >= 1 else [])
                if tt == 0 and fc == 0:
                    wts = hts + w1toks + [tb1]
                elif fc == 0:
                    wts = hts
                else:
                    wts = []
                mwy = [ymm[tt - 1]] if (fc == 0 and tt >= 1) else \
                    ([prev_mid_mm] if (e > 0 and tt == 0 and fc == 0) else [])
                t1 = b.emit("pe", mmid, waits=wts + bw + mwy, inc=smm)
                if tt == 0 and fc == 0:
                    prev_w1_use_new = t1
                mide[(tt, fc)] = b.emit("act", lambda e2, fc=fc, bank=bank:
                    e2.activation(midt[fc][:], bank[:], AF.Relu),
                    waits=[t1], inc=smid)
            for tsub in range(4):
                ti = tt * 4 + tsub
                jj = ti % 2
                aw = [prev_acc_tok.get((ti - 2, 1))]
                aw = [t for t in aw if t]
                srcb = x2 if e == 0 else acc
                atok = chx.issue(xac[jj][:], srcb[ti * P:(ti + 1) * P, :],
                                 waits=aw)
                if e == 0:
                    atok2 = chx.issue(xno[jj][:], x_hbm[ti * P:(ti + 1) * P, :],
                                      waits=[prev_store.get(ti - 2)]
                                      if (ti - 2) in prev_store else [])
                    atok3 = b.emit("dve", lambda e2, jj=jj: e2.scalar_tensor_tensor(
                        xac[jj][:], xac[jj][:], 0.5, xno[jj][:],
                        ALU.mult, ALU.add), waits=[atok, atok2], inc=sev)
                    atok = atok3
                for nh in range(2):
                    bank = b.pb[2 + (tsub % 2) * 2 + nh]
                    def ymf(e2, tsub=tsub, nh=nh, bank=bank):
                        for fc in range(32):
                            e2.matmul(bank[:], midt[fc][:, tsub * P:(tsub + 1) * P],
                                      w2t[fc][:, nh * 512:(nh + 1) * 512],
                                      start=(fc == 0), stop=False)
                        return e2.matmul(bank[:], res.ones[0:1, 0:P],
                                         b2r[0:1, nh * 512:(nh + 1) * 512],
                                         start=False, stop=True)
                    yw = [mide[(tt, 31)]]
                    if tt == 0 and tsub == 0 and nh == 0:
                        yw += w2toks + [tb2]
                    bw = [prev_acc_tok.get((ti - 2, nh))]
                    bw = [t for t in bw if t]
                    t1 = b.emit("pe", ymf, waits=yw + bw, inc=smm)
                    ymm[tt] = t1
                    sw = [prev_store[ti - 2]] if ti >= 2 and (ti - 2) in \
                        prev_store else []
                    if e == 7 and out_stats is not None and (ti - 2) in statoks:
                        sw = sw + [statoks[ti - 2]]
                    t2 = b.emit("dve", lambda e2, jj=jj, nh=nh, bank=bank,
                                ti=ti, e=e:
                        e2.scalar_tensor_tensor(
                            xno[jj][:, nh * 512:(nh + 1) * 512], bank[:],
                            halfmask[:, ti * 8 + e:ti * 8 + e + 1],
                            xac[jj][:, nh * 512:(nh + 1) * 512],
                            ALU.mult, ALU.add),
                        waits=[t1, atok] + sw, inc=sev)
                    prev_acc_tok[(ti, nh)] = t2

                dstb = dst_hbm if e == 7 else acc
                prev_acc_tok[ti] = prev_acc_tok[(ti, 1)]
                prev_store[ti] = cst.issue(dstb[ti * P:(ti + 1) * P, :],
                                           xno[jj][:],
                                           waits=[prev_acc_tok[(ti, 0)],
                                                  prev_acc_tok[(ti, 1)]])
                if e == 7 and out_stats is not None:
                    sums, sumsq = out_stats
                    trd = b.emit("dve", lambda e2, jj=jj, ti=ti: e2.reduce_sum(
                        sums[:, ti:ti + 1], xno[jj][:], axis=AX.X),
                        waits=[prev_acc_tok[(ti, 0)], prev_acc_tok[(ti, 1)]],
                        inc=b.sem("epi_dnd"))
                    statoks[ti] = b.emit("act", lambda e2, jj=jj, ti=ti:
                        e2.activation(xno[jj][:], xno[jj][:], AF.Square,
                                      accum_out=sumsq[:, ti:ti + 1]),
                        waits=[trd, prev_store[ti]] +
                              ([statoks[ti - 1]] if (ti - 1) in statoks else []),
                        inc=b.sem("epi_dna"))
        prev_mid_mm = ymm[ntt - 1]
        prev_w1_use = prev_mid_mm
        prev_w2_use = prev_mid_mm
        b.barrier(toks=[prev_store[t] for t in (NT * 4 // 4 - 2, NT * 4 // 4 - 1)
                        if t in prev_store])
    b.barrier(toks=[prev_store[t] for t in (NT * 4 // 4 - 2, NT * 4 // 4 - 1)
                    if t in prev_store])
    b.reset(mm_)
    b.reset(sub_m)
    ctx["n"] += 1



def sublayer_l0ffn(b, ctx, x_hbm, dst_hbm, out_stats=None):
    n = ctx["n"]
    res = b.res
    ntt = b.T // 512
    sub_m = b.mark()
    maskPP = b.sbuf(f"l0mk{n}", [P, 8], F32)
    m1 = b.mark()
    ld = b.sem("l0_ld")
    loss_parts = b.sbuf(f"l0lp{n}", [P, 512], F32)
    lossPP = b.sbuf(f"l0le{n}", [P, 8], F32)
    mask8b = b.sbuf(f"l0m8{n}", [1, 8], BF16)
    he = [b.sbuf(f"l0h{n}_{hc}", [P, b.T], BF16) for hc in range(2)]
    sq_scrap = b.sbuf(f"l0sqs{n}", [P, 512], F32)
    diff = [b.sbuf(f"l0df{n}_{j}", [P, 512], F32) for j in range(2)]
    wd1 = [b.sbuf(f"l0w1{n}_{c}", [P, 256], BF16) for c in range(ND)]
    wd2 = [b.sbuf(f"l0w2{n}_{hc}", [P, D], BF16) for hc in range(2)]
    zt = b.emit("dve", lambda e: e.memset(loss_parts[:], 0.0), inc=b.sem("l0_z"))
    mmh = b.sem("l0_mmh")
    evh = b.sem("l0_evh")
    mmr = b.sem("l0_mmr")
    evr = b.sem("l0_evr")
    sqs = b.sem("l0_sqs")
    prev_sq_last = None
    prev_hev = None
    prev_dv = {}
    prev_sq = {}
    for e in range(8):
        wl1 = [prev_hev] if e > 0 else []
        wl2 = [prev_sq_last] if e > 0 else []
        for c in range(ND):
            b.dma("pool", wd1[c][:], ctx["L0_ffn_Wd1"][e][c * P:(c + 1) * P, :],
                  waits=wl1 if c == 0 else [], inc=ld)
        for hc in range(2):
            b.dma("pool", wd2[hc][:], ctx["L0_ffn_Wd2"][e][hc * P:(hc + 1) * P, :],
                  waits=wl2 if hc == 0 else [], inc=ld)
        bt1 = bias_row(b, ctx["L0_ffn_bd1"][e], f"l0b1{n}_{e}", 256, ld)
        bt2 = bias_row(b, ctx["L0_ffn_bd2"][e], f"l0b2{n}_{e}", D, ld)
        ldv = ld.n
        hev = {}
        for hc in range(2):
            for tt in range(ntt):
                k = hc * ntt + tt
                bank = b.pb[k % 2]
                def mf(e2, hc=hc, tt=tt, bank=bank, bt1=bt1):
                    for c in range(ND):
                        e2.matmul(bank[:], wd1[c][:, hc * P:(hc + 1) * P],
                                  res.xT[c][:, tt * 512:(tt + 1) * 512],
                                  start=(c == 0), stop=False)
                    return e2.matmul(bank[:], bt1[0:1, hc * P:(hc + 1) * P],
                                     res.onesw[0:1, 0:512], start=False, stop=True)
                bw = [hev[k - 2]] if k >= 2 else \
                    ([prev_hev] if e > 0 else [])
                t1 = b.emit("pe", mf, waits=[(ld, ldv)] + bw, inc=mmh)
                hev[k] = b.emit("act", lambda e2, hc=hc, tt=tt, bank=bank:
                    e2.activation(he[hc][:, tt * 512:(tt + 1) * 512], bank[:],
                                  AF.Relu), waits=[t1], inc=evh)
        prev_hev = hev[2 * ntt - 1]
        for c in range(ND):
            for tt in range(ntt):
                k = c * ntt + tt
                bank = b.pb[2 + k % 2]
                def mf2(e2, c=c, tt=tt, bank=bank, bt2=bt2):
                    for hc in range(2):
                        e2.matmul(bank[:], wd2[hc][:, c * P:(c + 1) * P],
                                  he[hc][:, tt * 512:(tt + 1) * 512],
                                  start=(hc == 0), stop=False)
                    return e2.matmul(bank[:], bt2[0:1, c * P:(c + 1) * P],
                                     res.onesw[0:1, 0:512], start=False, stop=True)
                bw = [prev_dv[divmod(k - 2, ntt)]] if k >= 2 else \
                    ([prev_dv[(ND - 1, ntt - 1)]] if e > 0 else [])
                t1 = b.emit("pe", mf2,
                            waits=[(ld, ldv), hev[2 * ntt - 1]] + bw, inc=mmr)
                j = k % 2
                dwv = [prev_sq[divmod(k - 2, ntt)]] if k >= 2 else \
                    ([prev_sq[(ND - 1, ntt - 1)]] if e > 0 else [])
                t2 = b.emit("dve", lambda e2, c=c, tt=tt, j=j, bank=bank:
                    e2.scalar_tensor_tensor(
                        diff[j][:], bank[:], 1.0,
                        res.xT[c][:, tt * 512:(tt + 1) * 512],
                        ALU.mult, ALU.subtract),
                    waits=[t1] + dwv, inc=evr)
                prev_dv[(c, tt)] = t2
                col = e * 64 + k
                sqw = [prev_sq[divmod(k - 1, ntt)]] if k >= 1 else \
                    ([prev_sq[(ND - 1, ntt - 1)]] if e > 0 else [zt])
                prev_sq[(c, tt)] = b.emit("act", lambda e2, j=j, col=col:
                    e2.activation(sq_scrap[:], diff[j][:], AF.Square,
                                  accum_out=loss_parts[:, col:col + 1]),
                    waits=[t2] + sqw, inc=sqs)
        prev_sq_last = prev_sq[(ND - 1, ntt - 1)]
    b.barrier()
    s2 = b.sem("l0_s2")
    lt8 = b.sbuf(f"l0lt{n}", [8, P], F32)
    le8 = b.sbuf(f"l0l8{n}", [8, 1], F32)
    lrow = b.sbuf(f"l0lr{n}", [1, 8], F32)
    lmin = b.sbuf(f"l0lm{n}", [1, 1], F32)
    t = None
    for e in range(8):
        t = b.emit("dve", lambda e2, e=e: e2.reduce_sum(
            lossPP[:, e:e + 1], loss_parts[:, e * 64:(e + 1) * 64], axis=AX.X),
            waits=[t] if t else [], inc=s2)
    t = b.emit("pe", lambda e2: e2.transpose(b.pb[4][0:8, 0:P], lossPP[:],
                                             res.identf[:]), waits=[t], inc=s2)
    t = b.emit("dve", lambda e2: e2.tensor_copy(lt8[:], b.pb[4][0:8, 0:P]),
               waits=[t], inc=s2)
    t = b.emit("dve", lambda e2: e2.reduce_sum(le8[:], lt8[:], axis=AX.X),
               waits=[t], inc=s2)
    t = b.emit("pe", lambda e2: e2.transpose(b.pb[5][0:1, 0:8], le8[:],
                                             res.identf[0:8, 0:8]),
               waits=[t], inc=s2)
    t = b.emit("dve", lambda e2: e2.tensor_copy(lrow[:], b.pb[5][0:1, 0:8]),
               waits=[t], inc=s2)
    t = b.emit("dve", lambda e2: e2.tensor_reduce(lmin[:], lrow[:], axis=AX.X,
                                                  op=ALU.min), waits=[t], inc=s2)
    t = b.emit("dve", lambda e2: e2.tensor_scalar(lrow[:], lrow[:],
                                                  lmin[0:1, 0:1], None,
                                                  ALU.is_equal), waits=[t], inc=s2)
    t = b.emit("dve", lambda e2: e2.tensor_copy(mask8b[:], lrow[:]),
               waits=[t], inc=s2)
    t = b.emit("pe", lambda e2: e2.matmul(b.pb[6][:, 0:8], res.ones[0:1, 0:P],
                                          mask8b[0:1, :], start=True, stop=True),
               waits=[t], inc=s2)
    t = b.emit("dve", lambda e2: e2.tensor_copy(maskPP[:], b.pb[6][:, 0:8]),
               waits=[t], inc=s2)
    b.barrier()
    b.reset(m1)
    m3 = b.mark()
    wdc = [b.sbuf(f"l0wdc{n}_{c}", [P, 64], BF16) for c in range(ND)]
    wuc = b.sbuf(f"l0wuc{n}", [64, D], BF16)
    bcomb = b.sbuf(f"l0bc{n}", [1, 64 + D], BF16)
    acc = b.sbuf(f"l0acc{n}", [P, 64], F32)
    accu = b.sbuf(f"l0accu{n}", [64, D], F32)
    accb = b.sbuf(f"l0accb{n}", [1, 64 + D], F32)
    wtile = [b.sbuf(f"l0wt{n}_{j}", [P, 64], BF16) for j in range(2)]
    utile = [b.sbuf(f"l0ut{n}_{j}", [64, D], BF16) for j in range(2)]
    btile = [b.sbuf(f"l0btl{n}_{j}", [1, 64 + D], BF16) for j in range(2)]
    bbase = b.sbuf(f"l0bb{n}", [1, D], BF16)
    ch3 = b.chan("l0_c3", eng="pool")
    s3 = b.sem("l0_s3")
    ld3 = b.sem("l0_ld3")
    tcast = {}
    for c in range(ND):
        t = None
        for e in range(8):
            j = e % 2
            lw = [tcast[c - 1]] if (e < 2 and c >= 1) else []
            tok = ch3.issue(wtile[j][:],
                            ctx["L0_ffn_Wa_down"][e][c * P:(c + 1) * P, :],
                            waits=([t] if e >= 2 else []) + lw)
            if e == 0:
                t = b.emit("dve", lambda e2, j=j, e=e: e2.tensor_scalar(
                    acc[:], wtile[j][:], maskPP[:, e:e + 1], None, ALU.mult),
                    waits=[tok] + ([tcast[c - 1]] if c >= 1 else []), inc=s3)
            else:
                t = b.emit("dve", lambda e2, j=j, e=e: e2.scalar_tensor_tensor(
                    acc[:], wtile[j][:], maskPP[:, e:e + 1], acc[:],
                    ALU.mult, ALU.add), waits=[tok, t], inc=s3)
        tcast[c] = b.emit("dve", lambda e2, c=c: e2.tensor_copy(wdc[c][:], acc[:]),
                          waits=[t], inc=s3)
    t = None
    tcastu = None
    for e in range(8):
        j = e % 2
        tok = ch3.issue(utile[j][:], ctx["L0_ffn_Wa_up"][e],
                        waits=[t] if e >= 2 else [tcast[ND - 1]])
        if e == 0:
            t = b.emit("dve", lambda e2, j=j, e=e: e2.tensor_scalar(
                accu[:], utile[j][:], maskPP[0:64, e:e + 1], None, ALU.mult),
                waits=[tok], inc=s3)
        else:
            t = b.emit("dve", lambda e2, j=j, e=e: e2.scalar_tensor_tensor(
                accu[:], utile[j][:], maskPP[0:64, e:e + 1], accu[:],
                ALU.mult, ALU.add), waits=[tok, t], inc=s3)
    tcastu = b.emit("dve", lambda e2: e2.tensor_copy(wuc[:], accu[:]),
                    waits=[t], inc=s3)
    t = None
    for e in range(8):
        j = e % 2
        tokd = ch3.issue(btile[j][0:1, 0:64],
                         ctx["L0_ffn_ba_down"][e].rearrange("(a c) -> a c", a=1),
                         waits=[t] if e >= 1 else [tcastu])
        toku = ch3.issue(btile[j][0:1, 64:64 + D],
                         ctx["L0_ffn_ba_up"][e].rearrange("(a c) -> a c", a=1),
                         waits=[t] if e >= 1 else [tcastu])
        if e == 0:
            t = b.emit("dve", lambda e2, j=j, e=e: e2.tensor_scalar(
                accb[:], btile[j][:], maskPP[0:1, e:e + 1], None, ALU.mult),
                waits=[tokd, toku], inc=s3)
        else:
            t = b.emit("dve", lambda e2, j=j, e=e: e2.scalar_tensor_tensor(
                accb[:], btile[j][:], maskPP[0:1, e:e + 1], accb[:],
                ALU.mult, ALU.add), waits=[tokd, toku, t], inc=s3)
    tbb = b.dma("pool", bbase[:], ctx["L0_ffn_b_base"].rearrange("(a c) -> a c", a=1),
                inc=ld3)
    t = b.emit("dve", lambda e2: e2.tensor_add(accb[0:1, 64:64 + D],
                                               accb[0:1, 64:64 + D], bbase[0:1, :]),
               waits=[t, tbb], inc=s3)
    t = b.emit("dve", lambda e2: e2.tensor_copy(bcomb[:], accb[:]),
               waits=[t], inc=s3)
    b.barrier()
    a_fm = b.sbuf(f"l0a{n}", [64, b.T], BF16)
    m4 = b.mark()
    mm4 = b.sem("l0_mm4")
    ev4 = b.sem("l0_ev4")
    tev = {}
    for tt in range(ntt):
        bank = b.pb[tt % 2]
        def mf4(e2, tt=tt, bank=bank):
            for c in range(ND):
                e2.matmul(bank[0:64, :], wdc[c][:],
                          res.xT[c][:, tt * 512:(tt + 1) * 512],
                          start=(c == 0), stop=False)
            return e2.matmul(bank[0:64, :], bcomb[0:1, 0:64],
                             res.onesw[0:1, 0:512], start=False, stop=True)
        bw = [tev[tt - 2]] if tt >= 2 else []
        t1 = b.emit("pe", mf4, waits=bw, inc=mm4)
        tev[tt] = b.emit("act", lambda e2, tt=tt, bank=bank: e2.activation(
            a_fm[:, tt * 512:(tt + 1) * 512], bank[0:64, :], AF.Relu),
            waits=[t1], inc=ev4)
    b.barrier()
    b.reset(m4)
    m5 = b.mark()
    ld5 = b.sem("l0_ld5")
    wbase = load_weight_chunks(b, ctx["L0_ffn_W_base"], f"l0wb{n}", D, D, ld5)
    b.emit("pe", lambda e, v=ld5.n: e.wait_ge(ld5.h, v))
    def extra(e2, bank, i, nn, stop):
        return e2.matmul(bank[:], a_fm[:, i * P:(i + 1) * P],
                         wuc[:, nn * 512:(nn + 1) * 512], start=False, stop=stop)
    epi = ResidualEpi(b, f"l0E{n}", x_hbm, dst_hbm, 0.5, stats=out_stats)
    linear_tm(b, f"l0E{n}", res.xT, wbase, bcomb[0:1, 64:64 + D],
              epi, extra_mm=extra)
    b.barrier(toks=epi.finish())
    b.reset(sub_m)
    ctx["n"] += 1


def dft_consts(b):
    n = np.arange(D, dtype=np.float64)
    ang = 2.0 * np.pi * np.outer(n, n) / D
    Fr = np.cos(ang)
    Fi = -np.sin(ang)
    Gr = np.cos(ang) / D
    Gi = np.sin(ang) / D
    return (b.const("c_Fr", f32c(Fr)), b.const("c_Fi", f32c(Fi)),
            b.const("c_Fin", f32c(-Fi)), b.const("c_Gr", f32c(Gr)),
            b.const("c_Gin", f32c(-Gi)))


def l2ffn_precompute_M(b, wr_hbm, wi_hbm, m_scratch):
    cFr, cFi, cFin, cGr, cGin = dft_consts(b)
    m = b.mark()
    ld = b.sem("l2p_ld")
    mm = b.sem("l2p_mm")
    ev = b.sem("l2p_ev")
    wrt = load_weight_chunks(b, wr_hbm, "l2p_wr", D, D, ld)
    wit = load_weight_chunks(b, wi_hbm, "l2p_wi", D, D, ld)
    Frt = load_weight_chunks(b, cFr, "l2p_Fr", D, D, ld)
    Fit = load_weight_chunks(b, cFi, "l2p_Fi", D, D, ld)
    Fnt = load_weight_chunks(b, cFin, "l2p_Fn", D, D, ld)
    ctr = [b.sbuf(f"l2p_cr{c}", [P, D], BF16) for c in range(ND)]
    cti = [b.sbuf(f"l2p_ci{c}", [P, D], BF16) for c in range(ND)]
    nld = ld.n
    tev = {}
    for o in range(ND):
        for n2 in range(2):
            bank_r = b.pb[(o % 2) * 2 + n2]
            bank_i = b.pb[4 + (o % 2) * 2 + n2]
            def mmr(e, o=o, n2=n2, bank=bank_r):
                ins = None
                for k in range(ND):
                    e.matmul(bank[:], wrt[k][:, o * P:(o + 1) * P],
                             Frt[k][:, n2 * 512:(n2 + 1) * 512], start=(k == 0), stop=False)
                    ins = e.matmul(bank[:], wit[k][:, o * P:(o + 1) * P],
                                   Fnt[k][:, n2 * 512:(n2 + 1) * 512],
                                   start=False, stop=(k == ND - 1))
                return ins
            def mmi(e, o=o, n2=n2, bank=bank_i):
                ins = None
                for k in range(ND):
                    e.matmul(bank[:], wrt[k][:, o * P:(o + 1) * P],
                             Fit[k][:, n2 * 512:(n2 + 1) * 512], start=(k == 0), stop=False)
                    ins = e.matmul(bank[:], wit[k][:, o * P:(o + 1) * P],
                                   Frt[k][:, n2 * 512:(n2 + 1) * 512],
                                   start=False, stop=(k == ND - 1))
                return ins
            bw = ([tev[(o - 2, n2, "r")], tev[(o - 2, n2, "i")]] if o >= 2 else [])
            t1 = b.emit("pe", mmr, waits=[(ld, nld)] + bw, inc=mm)
            t2 = b.emit("pe", mmi, waits=[(ld, nld)] + bw, inc=mm)
            tev[(o, n2, "r")] = b.emit("dve", lambda e, o=o, n2=n2, bank=bank_r:
                e.tensor_copy(ctr[o][:, n2 * 512:(n2 + 1) * 512], bank[:]),
                waits=[t1], inc=ev)
            tev[(o, n2, "i")] = b.emit("dve", lambda e, o=o, n2=n2, bank=bank_i:
                e.tensor_copy(cti[o][:, n2 * 512:(n2 + 1) * 512], bank[:]),
                waits=[t2], inc=ev)
    b.barrier()
    b.reset(m)
    m2 = b.mark()
    ld2 = b.sem("l2p_ld2")
    mm2 = b.sem("l2p_mm2")
    ev2 = b.sem("l2p_ev2")
    cstm = b.chan("l2p_st2", nslots=2)
    Grt = load_weight_chunks(b, cGr, "l2p_Gr", D, D, ld2)
    Gnt = load_weight_chunks(b, cGin, "l2p_Gn", D, D, ld2)
    mt = [b.sbuf(f"l2p_m{j}", [P, D], BF16) for j in range(2)]
    mtoks = {}
    tev2 = {}
    nld2 = ld2.n
    for n in range(ND):
        j = n % 2
        for m2i in range(2):
            bank = b.pb[(n % 2) * 2 + m2i]
            def mmm(e, n=n, m2i=m2i, bank=bank):
                ins = None
                for o in range(ND):
                    e.matmul(bank[:], ctr[o][:, n * P:(n + 1) * P],
                             Grt[o][:, m2i * 512:(m2i + 1) * 512], start=(o == 0), stop=False)
                    ins = e.matmul(bank[:], cti[o][:, n * P:(n + 1) * P],
                                   Gnt[o][:, m2i * 512:(m2i + 1) * 512],
                                   start=False, stop=(o == ND - 1))
                return ins
            bw = [tev2[(n - 2, m2i)]] if n >= 2 else []
            t1 = b.emit("pe", mmm, waits=[(ld2, nld2)] + bw, inc=mm2)
            sw = [mtoks[n - 2]] if n >= 2 else []
            tev2[(n, m2i)] = b.emit("dve", lambda e, j=j, m2i=m2i, bank=bank:
                e.tensor_copy(mt[j][:, m2i * 512:(m2i + 1) * 512], bank[:]),
                waits=[t1] + sw, inc=ev2)
        mtoks[n] = cstm.issue(m_scratch[n * P:(n + 1) * P, :], mt[j][:],
                              waits=[tev2[(n, 0)], tev2[(n, 1)]])
    b.barrier(toks=[mtoks[ND - 2], mtoks[ND - 1]])
    b.reset(m2)


def l2ffn_apply(b, m_scratch, bias_hbm, x_hbm, dst_hbm, tag, out_stats=None):
    mk = b.mark()
    ld = b.sem(f"{tag}_wld")
    mtiles = []
    for c in range(ND):
        t = b.sbuf(f"{tag}_m{c}", [P, D], BF16)
        b.dma("pool", t[:], m_scratch[c * P:(c + 1) * P, :], inc=ld)
        mtiles.append(t)
    bt = bias_row(b, bias_hbm, f"{tag}_b", D, ld)
    b.emit("pe", lambda e: e.wait_ge(ld.h, ld.n))
    linear_tm_epilogue(b, mtiles, bt, 0.5, x_hbm, dst_hbm, tag,
                       out_stats=out_stats)
    b.reset(mk)



def sublayer_l2ffn(b, ctx, x_hbm, dst_hbm, first):
    if first:
        l2ffn_precompute_M(b, ctx["L2_ffn_w_freq_real"], ctx["L2_ffn_w_freq_imag"],
                           ctx["m_scratch"])
    ln_stats(b, x_hbm, sem_pref=f"s{ctx['n']}")
    ln_apply_transpose(b, x_hbm, sem_pref=f"a{ctx['n']}")
    l2ffn_apply(b, ctx["m_scratch"], ctx["L2_ffn_bias"], x_hbm, dst_hbm,
                f"l2f{ctx['n']}")
    ctx["n"] += 1



def build_l2ffn_test(t_tokens):
    b = B("l2ffn", t_tokens)
    x_in = b.inp("x", [t_tokens, D], F32)
    wr = b.inp("L2_ffn_w_freq_real", [D, D], F32)
    wi = b.inp("L2_ffn_w_freq_imag", [D, D], F32)
    bias = b.inp("L2_ffn_bias", [D], F32)
    x_out = b.out("x_out", [t_tokens, D], F32)
    m_scratch = b.scratch("m_scratch", [D, D], BF16)
    b.res = Resident(b)
    ctx = {"L2_ffn_w_freq_real": wr, "L2_ffn_w_freq_imag": wi,
           "L2_ffn_bias": bias, "m_scratch": m_scratch, "n": 0}
    sublayer_l2ffn(b, ctx, x_in, x_out, first=True)
    return b


def build_l1ffn_test(t_tokens):
    b = B("l1ffn", t_tokens)
    x_in = b.inp("x", [t_tokens, D])
    x_out = b.out("x_out", [t_tokens, D])
    b.res = Resident(b)
    ctx = {
        "L1_ffn_qweights": b.inp("L1_ffn_qweights", [6, D, D], I32),
        "L1_ffn_scales": b.inp("L1_ffn_scales", [6, D, 64]),
        "L1_ffn_biases": b.inp("L1_ffn_biases", [6, D]),
        "L1_ffn_lora_a": b.inp("L1_ffn_lora_a", [6, 32, D]),
        "L1_ffn_lora_b": b.inp("L1_ffn_lora_b", [6, D, 32]),
        "L1_ffn_ln_gamma": b.inp("L1_ffn_ln_gamma", [3, D]),
        "L1_ffn_ln_beta": b.inp("L1_ffn_ln_beta", [3, D]),
        "l1f_h": b.scratch("l1f_h", [t_tokens, D], F32),
        "l1f_y": b.scratch("l1f_y", [t_tokens, D], F32),
        "n": 0,
    }
    ln_stats(b, x_in, "s0")
    ln_apply_transpose(b, x_in, "a0")
    sublayer_l1ffn(b, ctx, x_in, x_out)
    return b


def build_l0ffn_test(t_tokens):
    b = B("l0ffn", t_tokens)
    x_in = b.inp("x", [t_tokens, D])
    x_out = b.out("x_out", [t_tokens, D])
    b.res = Resident(b)
    ctx = {
        "L0_ffn_W_base": b.inp("L0_ffn_W_base", [D, D]),
        "L0_ffn_b_base": b.inp("L0_ffn_b_base", [D]),
        "L0_ffn_Wd1": b.inp("L0_ffn_Wd1", [8, D, 256]),
        "L0_ffn_bd1": b.inp("L0_ffn_bd1", [8, 256]),
        "L0_ffn_Wd2": b.inp("L0_ffn_Wd2", [8, 256, D]),
        "L0_ffn_bd2": b.inp("L0_ffn_bd2", [8, D]),
        "L0_ffn_Wa_down": b.inp("L0_ffn_Wa_down", [8, D, 64]),
        "L0_ffn_ba_down": b.inp("L0_ffn_ba_down", [8, 64]),
        "L0_ffn_Wa_up": b.inp("L0_ffn_Wa_up", [8, 64, D]),
        "L0_ffn_ba_up": b.inp("L0_ffn_ba_up", [8, D]),
        "n": 0,
    }
    ln_stats(b, x_in, "s0")
    ln_apply_transpose(b, x_in, "a0")
    sublayer_l0ffn(b, ctx, x_in, x_out)
    return b


def build_l2attn_test(t_tokens):
    b = B("l2attn", t_tokens)
    x_in = b.inp("x", [t_tokens, D])
    x_out = b.out("x_out", [t_tokens, D])
    wqkv = b.inp("L2_attn_W_qkv", [D, 3 * D])
    bqkv = b.inp("L2_attn_b_qkv", [3 * D])
    wproj = b.inp("L2_attn_W_proj", [D, D])
    bproj = b.inp("L2_attn_b_proj", [D])
    qnw = b.inp("L2_attn_qn_w", [64])
    knw = b.inp("L2_attn_kn_w", [64])
    b.res = Resident(b)
    ctx = {
        "at_q": b.scratch("at_q", [t_tokens, D], BF16),
        "at_k": b.scratch("at_k", [t_tokens, D], BF16),
        "at_qf": b.scratch("at_qf", [D, t_tokens], BF16),
        "at_o": b.scratch("at_o", [t_tokens, D], BF16),
        "n": 0,
    }
    ln_stats(b, x_in, "s0")
    ln_apply_transpose(b, x_in, "a0")
    qnb = b.sbuf("l2qnb", [P, D], F32)
    knb = b.sbuf("l2knb", [P, D], F32)
    bcast_row_tile(b, "l2qn", qnw, 64, 16, qnb)
    bcast_row_tile(b, "l2kn", knw, 64, 16, knb)
    attn_softmax(b, ctx, "a", x_in, x_out, wqkv, wqkv, wqkv, wproj,
                 bqkv[0:D], bqkv[D:2 * D], bqkv[2 * D:3 * D], bproj,
                 transform_q=("rmsnorm", qnb), transform_k=("rmsnorm", knb),
                 coef=1.0, qkv_col0=(0, D, 2 * D))
    return b


def build_l3attn_test(t_tokens):
    b = B("l3a", t_tokens)
    x_in = b.inp("x", [t_tokens, D])
    x_out = b.out("x_out", [t_tokens, D])
    ctx = {"n": 0,
           "L3_attn_Wq": b.inp("L3_attn_Wq", [D, D]),
           "L3_attn_bq": b.inp("L3_attn_bq", [D]),
           "L3_attn_Wk": b.inp("L3_attn_Wk", [D, D]),
           "L3_attn_bk": b.inp("L3_attn_bk", [D]),
           "L3_attn_Wv": b.inp("L3_attn_Wv", [D, D]),
           "at_q": b.scratch("at_q", [t_tokens, D], BF16),
           "at_k": b.scratch("at_k", [t_tokens, D], BF16),
           "at_qf": b.scratch("at_qf", [D, t_tokens], BF16)}
    b.res = Resident(b)
    ctx["ret_consts"] = retention_consts(b)
    ln_stats(b, x_in, "s0")
    ln_apply_transpose(b, x_in, "a0")
    sublayer_l3attn(b, ctx, x_in, x_out)
    return b


def build_l3ffn_test(t_tokens):
    b = B("l3f", t_tokens)
    x_in = b.inp("x", [t_tokens, D])
    x_out = b.out("x_out", [t_tokens, D])
    ctx = {"n": 0}
    for nm, shp, dt in MODULE_INPUTS:
        if nm.startswith("L3_ffn"):
            ctx[nm] = b.inp(nm, shp, dt)
    for nm, shp, dt in (("at_q", [t_tokens, D], BF16),
                        ("at_k", [t_tokens, D], BF16),
                        ("at_qf", [D, t_tokens], BF16),
                        ("at_o", [t_tokens, D], BF16),
                        ("f3_x2", [t_tokens, D], F32),
                        ("f3_acc", [t_tokens, D], F32),
                        ("f3_xh2", [D, t_tokens], BF16),
                        ("f3_xh2lo", [D, t_tokens], BF16),
                        ("f3_xln", [t_tokens, D], F32)):
        ctx[nm] = b.scratch(nm, shp, dt)
    b.res = Resident(b)
    ln_stats(b, x_in, "s0")
    sublayer_l3ffn(b, ctx, x_in, x_out)
    return b


def build_l1attn_test(t_tokens):
    b = B("l1a", t_tokens)
    x_in = b.inp("x", [t_tokens, D])
    x_out = b.out("x_out", [t_tokens, D])
    ctx = {"n": 0,
           "L1_attn_wq": b.inp("L1_attn_wq", [D, D]),
           "L1_attn_wk": b.inp("L1_attn_wk", [D, D]),
           "L1_attn_wv": b.inp("L1_attn_wv", [D, D]),
           "L1_attn_wo": b.inp("L1_attn_wo", [D, D]),
           "at_q": b.scratch("at_q", [t_tokens, D], BF16),
           "at_k": b.scratch("at_k", [t_tokens, D], BF16),
           "at_qf": b.scratch("at_qf", [D, t_tokens], BF16),
           "at_o": b.scratch("at_o", [t_tokens, D], BF16)}
    b.res = Resident(b)
    ctri_h, _ = mask_consts(b)
    crc, crs = rope_consts(b, t_tokens)
    mtri = b.sbuf("m_tri", [P, P], F32)
    ropec = b.sbuf("m_ropec", [P, b.NT * 32], F32)
    ropes = b.sbuf("m_ropes", [P, b.NT * 32], F32)
    ldm = b.sem("mod_ld")
    b.dma("sp", mtri[:], ctri_h, inc=ldm)
    b.dma("sp", ropec[:], crc, inc=ldm)
    b.dma("sp", ropes[:], crs, inc=ldm)
    b.emit("sp", lambda e, v=ldm.n: e.wait_ge(ldm.h, v))
    b.barrier()
    ln_stats(b, x_in, "s0")
    ln_apply_transpose(b, x_in, "a0")
    attn_softmax(b, ctx, "a", x_in, x_out, ctx["L1_attn_wq"], ctx["L1_attn_wk"],
                 ctx["L1_attn_wv"], ctx["L1_attn_wo"], None, None, None, None,
                 transform_q=("rope", ropec, ropes),
                 transform_k=("rope", ropec, ropes),
                 causal=True, diagmask=mtri, coef=1.0, wlayout="od")
    return b


def rope_consts(b, t_tokens):
    NT = t_tokens // P
    inv = 1.0 / (10000.0 ** (np.arange(0, 64, 2, dtype=np.float64) / 64.0))
    pos = np.arange(t_tokens, dtype=np.float64)
    fr = pos[:, None] * inv[None, :]
    cs = np.cos(fr).reshape(NT, P, 32).transpose(1, 0, 2).reshape(P, NT * 32)
    sn = np.sin(fr).reshape(NT, P, 32).transpose(1, 0, 2).reshape(P, NT * 32)
    return b.const("c_ropec", f32c(cs)), b.const("c_ropes", f32c(sn))


def mask_consts(b):
    kp = np.arange(P)[:, None]
    qf = np.arange(P)[None, :]
    tri = np.where(qf >= kp, 0.0, -2.0e38)
    low = np.where(kp >= qf, 0.0, -2.0e38)
    return b.const("c_mtri", f32c(tri)), b.const("c_mlow", f32c(low))


MODULE_INPUTS = [
    ("L0_attn_Wq", [D, D], F32), ("L0_attn_bq", [D], F32),
    ("L0_attn_Wk", [D, D], F32), ("L0_attn_bk", [D], F32),
    ("L0_attn_Wv", [D, D], F32), ("L0_attn_bv", [D], F32),
    ("L0_attn_Wo", [D, D], F32), ("L0_attn_bo", [D], F32),
    ("L0_ffn_W_base", [D, D], F32), ("L0_ffn_b_base", [D], F32),
    ("L0_ffn_Wd1", [8, D, 256], F32), ("L0_ffn_bd1", [8, 256], F32),
    ("L0_ffn_Wd2", [8, 256, D], F32), ("L0_ffn_bd2", [8, D], F32),
    ("L0_ffn_Wa_down", [8, D, 64], F32), ("L0_ffn_ba_down", [8, 64], F32),
    ("L0_ffn_Wa_up", [8, 64, D], F32), ("L0_ffn_ba_up", [8, D], F32),
    ("L1_attn_wq", [D, D], F32), ("L1_attn_wk", [D, D], F32),
    ("L1_attn_wv", [D, D], F32), ("L1_attn_wo", [D, D], F32),
    ("L1_ffn_qweights", [6, D, D], I32), ("L1_ffn_scales", [6, D, 64], F32),
    ("L1_ffn_biases", [6, D], F32), ("L1_ffn_lora_a", [6, 32, D], F32),
    ("L1_ffn_lora_b", [6, D, 32], F32), ("L1_ffn_ln_gamma", [3, D], F32),
    ("L1_ffn_ln_beta", [3, D], F32),
    ("L2_attn_W_qkv", [D, 3 * D], F32), ("L2_attn_b_qkv", [3 * D], F32),
    ("L2_attn_W_proj", [D, D], F32), ("L2_attn_b_proj", [D], F32),
    ("L2_attn_qn_w", [64], F32), ("L2_attn_kn_w", [64], F32),
    ("L2_ffn_w_freq_real", [D, D], F32), ("L2_ffn_w_freq_imag", [D, D], F32),
    ("L2_ffn_bias", [D], F32),
    ("L3_attn_Wq", [D, D], F32), ("L3_attn_bq", [D], F32),
    ("L3_attn_Wk", [D, D], F32), ("L3_attn_bk", [D], F32),
    ("L3_attn_Wv", [D, D], F32),
    ("L3_ffn_ln1_g", [D], F32), ("L3_ffn_ln1_b", [D], F32),
    ("L3_ffn_ln2_g", [D], F32), ("L3_ffn_ln2_b", [D], F32),
    ("L3_ffn_wq", [D, D], F32), ("L3_ffn_bq", [D], F32),
    ("L3_ffn_wk", [D, D], F32), ("L3_ffn_bk", [D], F32),
    ("L3_ffn_wv", [D, D], F32), ("L3_ffn_bv", [D], F32),
    ("L3_ffn_wo", [D, D], F32), ("L3_ffn_bo", [D], F32),
    ("L3_ffn_gate_w", [D, 8], F32), ("L3_ffn_gate_b", [8], F32),
    ("L3_ffn_w1", [8, D, 4096], F32), ("L3_ffn_b1", [8, 4096], F32),
    ("L3_ffn_w2", [8, 4096, D], F32), ("L3_ffn_b2", [8, D], F32),
]


def build_module(t_tokens):
    b = B("mk", t_tokens)
    x_in = b.inp("L0_attn_x", [t_tokens, D])
    out = b.out("out", [t_tokens, D])
    ctx = {"n": 0}
    for nm, shp, dt in MODULE_INPUTS:
        ctx[nm] = b.inp(nm, shp, dt)
    xs = b.scratch("x_state", [t_tokens, D], F32)
    for nm, shp, dt in (("at_q", [t_tokens, D], BF16),
                        ("at_k", [t_tokens, D], BF16),
                        ("at_qf", [D, t_tokens], BF16),
                        ("at_o", [t_tokens, D], BF16),
                        ("l1f_h", [t_tokens, D], F32),
                        ("l1f_y", [t_tokens, D], F32),
                        ("m_scratch", [D, D], BF16),
                        ("f3_x2", [t_tokens, D], F32),
                        ("f3_acc", [t_tokens, D], F32),
                        ("f3_xh2", [D, t_tokens], BF16),
                        ("f3_xh2lo", [D, t_tokens], BF16),
                        ("f3_xln", [t_tokens, D], F32)):
        ctx[nm] = b.scratch(nm, shp, dt)
    b.res = Resident(b)
    ctx["ret_consts"] = retention_consts(b)
    ctri_h, clow_h = mask_consts(b)
    crc, crs = rope_consts(b, t_tokens)

    def load_tiles(specs):
        ldm = b.sem("mod_ld")
        outs = []
        for name, shape, href in specs:
            t = b.sbuf(name + str(b.uid()), shape, F32)
            b.dma("sp", t[:], href, inc=ldm)
            outs.append(t)
        b.emit("sp", lambda e, v=ldm.n: e.wait_ge(ldm.h, v))
        b.barrier()
        return outs

    cur = x_in
    res = b.res
    OS = (res.ssum, res.ssq)

    def pre(first=False):
        if first:
            ln_stats(b, cur, "MS")
        else:
            stats_finalize(b, res.ssum, res.ssq, res.mu, res.rstd, "MSF")
        ln_apply_transpose(b, cur, "MA")

    pre(first=True)
    sublayer_l0ffn(b, ctx, cur, xs, out_stats=OS)
    cur = xs
    pre()
    mk0 = b.mark()
    mtri, mlow = load_tiles([("m_tri", [P, P], ctri_h),
                             ("m_low", [P, P], clow_h)])
    attn_softmax(b, ctx, "a0", cur, cur, ctx["L0_attn_Wq"], ctx["L0_attn_Wk"],
                 ctx["L0_attn_Wv"], ctx["L0_attn_Wo"], ctx["L0_attn_bq"],
                 ctx["L0_attn_bk"], ctx["L0_attn_bv"], ctx["L0_attn_bo"],
                 window_tiles=2, diagmask={0: mtri, 2: mlow}, coef=1.0,
                 out_stats=OS)
    b.reset(mk0)
    pre()
    sublayer_l0ffn(b, ctx, cur, cur, out_stats=OS)
    pre()
    sublayer_l1ffn(b, ctx, cur, cur, out_stats=OS)
    pre()
    mk1 = b.mark()
    mtri, ropec, ropes = load_tiles([
        ("m_tri", [P, P], ctri_h),
        ("m_ropec", [P, b.NT * 32], crc), ("m_ropes", [P, b.NT * 32], crs)])
    attn_softmax(b, ctx, "a1", cur, cur, ctx["L1_attn_wq"], ctx["L1_attn_wk"],
                 ctx["L1_attn_wv"], ctx["L1_attn_wo"], None, None, None, None,
                 transform_q=("rope", ropec, ropes),
                 transform_k=("rope", ropec, ropes),
                 causal=True, diagmask=mtri, coef=1.0, wlayout="od",
                 out_stats=OS)
    b.reset(mk1)
    pre()
    sublayer_l1ffn(b, ctx, cur, cur, out_stats=OS)
    l2ffn_precompute_M(b, ctx["L2_ffn_w_freq_real"], ctx["L2_ffn_w_freq_imag"],
                       ctx["m_scratch"])
    pre()
    l2ffn_apply(b, ctx["m_scratch"], ctx["L2_ffn_bias"], cur, cur, "l2fA",
                out_stats=OS)
    pre()
    mk2 = b.mark()
    qnb = b.sbuf("m_qnb", [P, D], F32)
    knb = b.sbuf("m_knb", [P, D], F32)
    bcast_row_tile(b, "l2qn", ctx["L2_attn_qn_w"], 64, 16, qnb)
    bcast_row_tile(b, "l2kn", ctx["L2_attn_kn_w"], 64, 16, knb)
    attn_softmax(b, ctx, "a2", cur, cur, ctx["L2_attn_W_qkv"],
                 ctx["L2_attn_W_qkv"], ctx["L2_attn_W_qkv"],
                 ctx["L2_attn_W_proj"], ctx["L2_attn_b_qkv"][0:D],
                 ctx["L2_attn_b_qkv"][D:2 * D], ctx["L2_attn_b_qkv"][2 * D:3 * D],
                 ctx["L2_attn_b_proj"],
                 transform_q=("rmsnorm", qnb), transform_k=("rmsnorm", knb),
                 coef=1.0, qkv_col0=(0, D, 2 * D), out_stats=OS)
    b.reset(mk2)
    pre()
    l2ffn_apply(b, ctx["m_scratch"], ctx["L2_ffn_bias"], cur, cur, "l2fB",
                out_stats=OS)
    stats_finalize(b, res.ssum, res.ssq, res.mu, res.rstd, "MSF")
    sublayer_l3ffn(b, ctx, cur, cur, out_stats=OS)
    pre()
    sublayer_l3attn(b, ctx, cur, cur, out_stats=OS)
    stats_finalize(b, res.ssum, res.ssq, res.mu, res.rstd, "MSF")
    sublayer_l3ffn(b, ctx, cur, out)
    return b


def kernel(**inputs):
    global SIM_OPEN_BANKS
    SIM_OPEN_BANKS = False
    t_tokens = 4096
    b = build_module(t_tokens)
    nc = b.finalize()
    base = {}
    for nm, shp, dt in MODULE_INPUTS:
        base[nm] = np.ascontiguousarray(inputs[nm])
    base.update(b.consts)
    in_maps = []
    for core in range(NCORES):
        m = dict(base)
        m["L0_attn_x"] = np.ascontiguousarray(inputs["L0_attn_x"][core])
        in_maps.append(m)
    res = run_bass_kernel_spmd(nc, in_maps, core_ids=list(range(NCORES)))
    return np.stack([np.asarray(r["out"]).astype(np.float32)
                     for r in res.results])
